# version 03
# baseline (speedup 1.0000x reference)
import contextlib
import numpy as np
import ml_dtypes
import concourse.bass as bass
import concourse.mybir as mybir
from concourse.bass_utils import run_bass_kernel_spmd

F32 = mybir.dt.float32
BF16 = mybir.dt.bfloat16
I16 = mybir.dt.int16
AF = mybir.ActivationFunctionType
ALU = mybir.AluOpType
AX = mybir.AxisListType

NCORES = 8
P = 128
S = 2048
BC = 2
T = S * BC
D = 1024
NCH = D // P
HEADS = 16
HD = 64
LM = 128
SEG = 16
NMAT = BC * HEADS
S4 = 1.0 / float(np.sqrt(np.sqrt(HD)))
NT = T // P
I_FFN = 2816
H_FFN = 4096
NTB = T // 512
BF = ml_dtypes.bfloat16


class Chain:

    def __init__(self, eng, sem):
        self.eng, self.sem, self.n = eng, sem, 0

    def bar(self, inst):
        self.n += 1
        inst.then_inc(self.sem, 1)
        self.eng.wait_ge(self.sem, self.n)
        return inst



def emit_AB(nc, es_outer, d, stages=3, cuts=(), nt=NT):
    x_in = d["x"]
    wqT_in = d["wqT"]; wkT_in = d["wkT"]; wvT_in = d["wvT"]
    bqc_in = d["bqc"]; bkc_in = d["bkc"]; bvr_in = d["bvr"]; eps_in = d["epsc"]
    xn_tok_o = d["xn_tok"]; q_fm_o = d["q_fm"]; k_fm_o = d["k_fm"]
    v_tok_o = d["v_tok"]; xn_fm_o = d["xn_fm"]; qlm_o = d["qlm"]
    klm_o = d["klm"]; k2_o = d["k2"]; pcmax_o = d["pcmax"]

    es = contextlib.ExitStack()
    sbt = lambda name, shape, dt=F32: es.enter_context(nc.sbuf_tensor(name, shape, dt))
    pst = lambda name, shape, dt=F32: es.enter_context(nc.psum_tensor(name, shape, dt))
    sem = lambda name: es_outer.enter_context(nc.semaphore(name))

    NXB = 2
    x_t = [sbt(f"x_t{i}", [P, D]) for i in range(NXB)]
    xn_t = [sbt(f"xn_t{i}", [P, D]) for i in range(NXB)]
    xnb_t = [sbt(f"xnb_t{i}", [P, D], BF16) for i in range(2)]
    stat = sbt("stat", [P, 8])
    epsc = sbt("epsc_s", [P, 1])
    xn_fm = [es.enter_context(nc.sbuf_tensor(f"xn_fm{c}", [P, T], BF16, is_transpose_dest=True)) for c in range(NCH)]
    wq = [sbt(f"wqc{c}", [P, D], BF16) for c in range(NCH)]
    wk = [sbt(f"wkc{c}", [P, D], BF16) for c in range(NCH)]
    wv = [sbt(f"wvc{c}", [P, D], BF16) for c in range(NCH)]
    bqs = sbt("bqs", [P, NCH])
    bks = sbt("bks", [P, NCH])
    bvf = sbt("bvf", [1, D])
    bvb = sbt("bvb", [1, D], BF16)
    ones_row = sbt("ones_row", [1, P], BF16)
    ones_col = sbt("ones_col", [P, 1])
    if stages >= 2:
        pbf_s = sbt("pbf_s", [P, T], BF16)
        pbf = [pbf_s, pbf_s]
        lmraw = [sbt(f"lmraw{i}", [P, BC * LM]) for i in range(2)]
        b16q = sbt("b16q", [P, NCH])
        b16k = sbt("b16k", [P, NCH])
        _wldv = pbf_s[:].bitcast(F32)
        wld = [_wldv[:, 0:D], _wldv[:, D:2 * D]]
    else:
        wld_t = [sbt(f"wld{i}", [P, D]) for i in range(2)]
        wld = [wld_t[0][:], wld_t[1][:]]
        xsq_t = sbt("xsq1", [P, D])
        xsq = xsq_t[:]
    if stages >= 2:
        vf32_s = sbt("vf32_s", [P, D])
        vbf_s = sbt("vbf_s", [P, D], BF16)
        vf32 = [vf32_s, vf32_s]
        vbf = [vbf_s, vbf_s]
        qlm_b = sbt("qlm_b", [P, NCH * BC * LM], BF16)
        klm_b = sbt("klm_b", [P, NCH * BC * LM], BF16)
        xsq = qlm_b[:].bitcast(F32)[:, 0:D]
        e2 = sbt("e2", [P, LM])
        k2f = [sbt(f"k2f{i}", [P, LM]) for i in range(2)]
        k2bs = [sbt(f"k2bs{i}", [P, LM], BF16) for i in range(2)]
        s2red = sbt("s2red", [P, 4])
    runmax = sbt("runmax", [1, LM])
    pcm = sbt("pcm", [1, 4])

    ps_q = [pst(f"ps_q{i}", [P, 512]) for i in range(4)]
    ps_s2 = [pst(f"ps_s2{i}", [P, LM]) for i in range(2)]
    ps_cs = [pst(f"ps_cs{i}", [1, LM]) for i in range(2)]

    s_ldw = sem("s_ldw")
    s_wbf = sem("s_wbf")
    s_x = sem("s_x")
    s_v = sem("s_v")
    s_a = sem("s_a")
    s_tr = sem("s_tr")
    s_st = sem("s_st")
    s_pe = sem("s_pe")
    s_ev = sem("s_ev")
    s_lm = sem("s_lm")
    s_ps = sem("s_ps")
    s_v2 = sem("s_v2")
    s_s2a = sem("s_s2a")
    s_fin = sem("s_fin")
    s_vch = sem("s_vch")
    s_c = sem("s_c")
    s_lr = sem("s_lr")
    s_lw0 = sem("s_lw0")
    s_lw1 = sem("s_lw1")
    s_ach = sem("s_ach")
    s_ks = sem("s_ks")

    NQK = 2 * NCH * NTB
    NVG = NT * 2
    NG = NQK + NVG

    def sync_body(sync):
        for wi in range(0 if "noW" in cuts else 3):
            w_in = (wqT_in, wkT_in, wvT_in)[wi]
            for c in range(NCH):
                k = wi * NCH + c
                if k >= 2:
                    sync.wait_ge(s_wbf, k - 1)
                sync.dma_start(out=wld[k % 2], in_=w_in[c * P:(c + 1) * P, :]).then_inc(
                    (s_lw0, s_lw1)[k % 2], 16)
        nw = 0 if "noW" in cuts else 24
        if "noW" not in cuts:
            sync.dma_start(out=bqs[:], in_=bqc_in).then_inc(s_ldw, 16)
            sync.dma_start(out=bks[:], in_=bkc_in).then_inc(s_ldw, 16)
            sync.dma_start(out=bvf[:], in_=bvr_in).then_inc(s_ldw, 16)
        sync.dma_start(out=epsc[:], in_=eps_in).then_inc(s_ldw, 16)
        for i in range(nt):
            if i >= NXB:
                sync.wait_ge(s_a, 2 * (i - NXB) + 2)
                sync.dma_start(out=xn_tok_o[(i - NXB) * P:(i - NXB + 1) * P, :],
                               in_=xn_t[(i - NXB) % NXB][:]).then_inc(s_st, 16)
            sync.dma_start(out=x_t[i % NXB][:], in_=x_in[i * P:(i + 1) * P, :]).then_inc(s_x, 16)
        for i in range(nt - NXB, nt):
            sync.wait_ge(s_a, 2 * i + 2)
            sync.dma_start(out=xn_tok_o[i * P:(i + 1) * P, :],
                           in_=xn_t[i % NXB][:]).then_inc(s_st, 16)
        for g in range(16 if stages >= 2 else 0):
            proj, oc = divmod(g, NCH)
            sync.wait_ge(s_ev, 8 * (g + 1))
            dst = (q_fm_o, k_fm_o)[proj]
            sync.dma_start(out=dst[oc * P:(oc + 1) * P, :], in_=pbf[g % 2][:]).then_inc(s_ps, 16)
        for tt in range(0 if ("noV" in cuts or stages < 2) else NT):
            sync.wait_ge(s_ev, NQK + 2 * tt + 2)
            sync.dma_start(out=v_tok_o[tt * P:(tt + 1) * P, :], in_=vbf[tt % 2][:]).then_inc(s_ps, 16)
        sync.wait_ge(s_tr, nt * NCH * 16)
        for c in range(NCH if "noFM" not in cuts else 0):
            sync.dma_start(out=xn_fm_o[c * P:(c + 1) * P, :], in_=xn_fm[c][:]).then_inc(s_fin, 16)
        if stages >= 2:
            sync.wait_ge(s_lm, 16)
        if stages >= 2 and "noLM" not in cuts:
            sync.dma_start(out=qlm_o.rearrange("(c p) l -> p c l", p=P), in_=qlm_b[:].rearrange("p (c l) -> p c l", c=NCH)).then_inc(s_fin, 16)
            sync.dma_start(out=klm_o.rearrange("(c p) l -> p c l", p=P), in_=klm_b[:].rearrange("p (c l) -> p c l", c=NCH)).then_inc(s_fin, 16)
        for m in range(NMAT if stages >= 3 else 0):
            sync.wait_ge(s_v2, 3 * m + 2)
            sync.dma_start(out=k2_o[m * P:(m + 1) * P, :], in_=k2bs[m % 2][:]).then_inc(s_ks, 16)
        if stages >= 3:
            sync.wait_ge(s_v2, 3 * NMAT + 1)
            sync.wait_ge(s_ks, NMAT * 16)
        sync.wait_ge(s_c, 1)
        sync.dma_start(out=pcmax_o, in_=pcm[:]).then_inc(s_fin, 16)
        nfin = ((0 if "noFM" in cuts else NCH) + 1 + (2 if (stages >= 2 and "noLM" not in cuts) else 0)) * 16
        sync.wait_ge(s_fin, nfin)
        if stages >= 2:
            sync.wait_ge(s_ps, (16 if "noV" in cuts else 48) * 16)
        sync.wait_ge(s_st, nt * 16)

    def gpsimd_body(gpsimd):
        gpsimd.memset(ones_row[:], 1.0)
        gpsimd.memset(ones_col[:], 1.0)
        gpsimd.memset(runmax[:], -3.0e38)
        gpsimd.memset(pcm[:], 0.0).then_inc(s_c, 1)

    def vector_body(vector):
        vch = Chain(vector, s_vch)
        for wi in range(0 if "noW" in cuts else 3):
            wtiles = (wq, wk, wv)[wi]
            for c in range(NCH):
                k = wi * NCH + c
                vector.wait_ge((s_lw0, s_lw1)[k % 2], (k // 2 + 1) * 16)
                vector.tensor_copy(out=wtiles[c][:], in_=wld[k % 2]).then_inc(s_wbf, 1)
        if "noW" not in cuts:
            vector.wait_ge(s_ldw, 4 * 16)
            vch.bar(vector.tensor_scalar_mul(out=bqs[:], in0=bqs[:], scalar1=S4))
            vch.bar(vector.tensor_scalar_mul(out=bks[:], in0=bks[:], scalar1=S4))
            if stages >= 2:
                vch.bar(vector.tensor_scalar_mul(out=b16q[:], in0=bqs[:], scalar1=16.0))
                vch.bar(vector.tensor_scalar_mul(out=b16k[:], in0=bks[:], scalar1=16.0))
            vector.tensor_copy(out=bvb[:], in_=bvf[:]).then_inc(s_v, 1)
        else:
            vector.wait_ge(s_ldw, 16)
            vector.sem_inc(s_v, 1)
        for i in range(NT):
            b = i % NXB
            vector.wait_ge(s_x, (i + 1) * 16)
            vch.bar(vector.tensor_reduce(out=stat[:, 0:1], in_=x_t[b][:], axis=AX.X, op=ALU.add))
            vch.bar(vector.tensor_tensor(out=xsq, in0=x_t[b][:], in1=x_t[b][:], op=ALU.mult))
            vch.bar(vector.tensor_reduce(out=stat[:, 1:2], in_=xsq, axis=AX.X, op=ALU.add))
            vch.bar(vector.tensor_scalar_mul(out=stat[:, 2:3], in0=stat[:, 0:1], scalar1=1.0 / D))
            vch.bar(vector.tensor_scalar_mul(out=stat[:, 3:4], in0=stat[:, 1:2], scalar1=1.0 / D))
            vch.bar(vector.tensor_tensor(out=stat[:, 4:5], in0=stat[:, 2:3], in1=stat[:, 2:3], op=ALU.mult))
            vector.tensor_tensor(out=stat[:, 4:5], in0=stat[:, 3:4], in1=stat[:, 4:5],
                                 op=ALU.subtract).then_inc(s_v, 1)
            vector.wait_ge(s_a, 2 * i + 1)
            vector.scalar_tensor_tensor(out=stat[:, 6:7], in0=stat[:, 2:3], scalar=-1.0,
                                        in1=stat[:, 5:6], op0=ALU.mult,
                                        op1=ALU.mult).then_inc(s_v, 1)
            vector.wait_ge(s_a, 2 * i + 2)
            if i >= 2:
                vector.wait_ge(s_tr, (i - 1) * NCH * 16)
            vector.tensor_copy(out=xnb_t[i % 2][:], in_=xn_t[b][:]).then_inc(s_v, 1)
        for k in range(NQK if stages >= 2 else 0):
            g, tb = divmod(k, NTB)
            proj, oc = divmod(g, NCH)
            vector.wait_ge(s_pe, k + 1)
            if "noLM" in cuts:
                vector.sem_inc(s_lr, 1)
            else:
                vch.bar(vector.tensor_reduce(out=lmraw[g % 2][:, tb * 32:(tb + 1) * 32],
                                     in_=ps_q[k % 4][:].rearrange("p (n w) -> p n w", w=SEG),
                                     axis=AX.X, op=ALU.add))
                vector.sem_inc(s_lr, 1)
            if tb == NTB - 1:
                if "noLM" in cuts:
                    vector.sem_inc(s_lm, 1)
                else:
                    dstlm = (qlm_b, klm_b)[proj]
                    b16 = (b16q, b16k)[proj]
                    vector.tensor_scalar(out=dstlm[:, oc * BC * LM:(oc + 1) * BC * LM],
                                         in0=lmraw[g % 2][:], scalar1=S4,
                                         scalar2=b16[:, oc:oc + 1], op0=ALU.mult,
                                         op1=ALU.add).then_inc(s_lm, 1)
        vector.wait_ge(s_c, 1)
        for m in range(NMAT if stages >= 3 else 0):
            vector.wait_ge(s_pe, NG + 2 * m + 1)
            vch.bar(vector.tensor_reduce(out=s2red[:, 0:1], in_=ps_s2[m % 2][:], axis=AX.X, op=ALU.max))
            vector.tensor_scalar_mul(out=s2red[:, 1:2], in0=s2red[:, 0:1],
                                     scalar1=-1.0 / 256.0).then_inc(s_v2, 1)
            vector.wait_ge(s_s2a, m + 1)
            if m >= 2:
                vector.wait_ge(s_pe, NG + 2 * (m - 2) + 2)
            vch.bar(vector.tensor_reduce(out=s2red[:, 2:3], in_=e2[:], axis=AX.X, op=ALU.add))
            vch.bar(vector.reciprocal(out=s2red[:, 3:4], in_=s2red[:, 2:3]))
            vch.bar(vector.tensor_scalar_mul(out=k2f[m % 2][:], in0=e2[:], scalar1=s2red[:, 3:4]))
            if m >= 2:
                vector.wait_ge(s_ks, 16 * (m - 1))
            vector.tensor_copy(out=k2bs[m % 2][:],
                               in_=k2f[m % 2][:]).then_inc(s_v2, 1)
            vector.wait_ge(s_pe, NG + 2 * m + 2)
            vector.tensor_tensor(out=runmax[:], in0=runmax[:], in1=ps_cs[m % 2][0:1, :],
                                 op=ALU.max).then_inc(s_v2, 1)
        if stages >= 3:
            vector.wait_ge(s_v2, 3 * NMAT)
            vector.tensor_reduce(out=pcm[0:1, 0:1], in_=runmax[0:1, :], axis=AX.X,
                                 op=ALU.max).then_inc(s_v2, 1)

    def scalar_body(scalar):
        ach = Chain(scalar, s_ach)
        for i in range(nt):
            b = i % NXB
            scalar.wait_ge(s_v, 1 + 3 * i + 1)
            ach.bar(scalar.activation(out=stat[:, 5:6], in_=stat[:, 4:5], func=AF.Ln, bias=epsc[:, 0:1]))
            scalar.activation(out=stat[:, 5:6], in_=stat[:, 5:6], func=AF.Exp,
                              scale=-0.5).then_inc(s_a, 1)
            scalar.wait_ge(s_v, 1 + 3 * i + 2)
            if i >= NXB:
                scalar.wait_ge(s_st, (i - NXB + 1) * 16)
            scalar.activation(out=xn_t[b][:], in_=x_t[b][:], func=AF.Identity,
                              scale=stat[:, 5:6], bias=stat[:, 6:7]).then_inc(s_a, 1)
            scalar.wait_ge(s_v, 1 + 3 * i + 3)
            for c in range(NCH):
                if "noTR" in cuts:
                    scalar.sem_inc(s_tr, 16)
                else:
                    scalar.dma_start(out=xn_fm[c][:, i * P:(i + 1) * P],
                                     in_=xnb_t[i % 2][:, c * P:(c + 1) * P],
                                     transpose=True).then_inc(s_tr, 16)
        for k in range(NQK if stages >= 2 else 0):
            g, tb = divmod(k, NTB)
            proj, oc = divmod(g, NCH)
            scalar.wait_ge(s_lr, k + 1)
            if tb == 0 and g >= 1:
                scalar.wait_ge(s_ps, 16 * g)
            bias = (bqs, bks)[proj]
            scalar.activation(out=pbf[g % 2][:, tb * 512:(tb + 1) * 512],
                              in_=ps_q[k % 4][:], func=AF.Identity, scale=S4,
                              bias=bias[:, oc:oc + 1]).then_inc(s_ev, 1)
        for kv in range(0 if ("noV" in cuts or stages < 2) else NVG):
            tt, oh = divmod(kv, 2)
            k = NQK + kv
            scalar.wait_ge(s_pe, k + 1)
            if oh == 0 and tt >= 1:
                scalar.wait_ge(s_ps, 16 * (16 + tt))
            scalar.activation(out=vbf[tt % 2][:, oh * 512:(oh + 1) * 512],
                              in_=ps_q[k % 4][:], func=AF.Copy).then_inc(s_ev, 1)
        for m in range(NMAT if stages >= 3 else 0):
            scalar.wait_ge(s_v2, 3 * m + 1)
            scalar.activation(out=e2[:], in_=ps_s2[m % 2][:], func=AF.Exp,
                              scale=1.0 / 256.0, bias=s2red[:, 1:2]).then_inc(s_s2a, 1)

    def tensor_body(tensor):
        if stages < 2:
            return
        tensor.wait_ge(s_tr, NT * NCH * 16)
        tensor.wait_ge(s_wbf, 24)
        tensor.wait_ge(s_v, 1)
        tensor.wait_ge(s_c, 1)
        for k in range(NQK):
            g, tb = divmod(k, NTB)
            proj, oc = divmod(g, NCH)
            w = (wq, wk)[proj]
            if k >= 4:
                tensor.wait_ge(s_ev, k - 3)
            mi = None
            for cc in range(NCH):
                mi = tensor.matmul(ps_q[k % 4][:], w[cc][:, oc * P:(oc + 1) * P],
                                   xn_fm[cc][:, tb * 512:(tb + 1) * 512],
                                   start=(cc == 0), stop=(cc == NCH - 1))
            mi.then_inc(s_pe, 1)
        for kv in range(0 if "noV" in cuts else NVG):
            tt, oh = divmod(kv, 2)
            k = NQK + kv
            tensor.wait_ge(s_ev, k - 3)
            for cc in range(NCH):
                tensor.matmul(ps_q[k % 4][:], xn_fm[cc][:, tt * P:(tt + 1) * P],
                              wv[cc][:, oh * 512:(oh + 1) * 512],
                              start=(cc == 0), stop=False)
            tensor.matmul(ps_q[k % 4][:], ones_row[0:1, :], bvb[0:1, oh * 512:(oh + 1) * 512],
                          start=False, stop=True).then_inc(s_pe, 1)
        if stages < 3:
            return
        tensor.wait_ge(s_lm, 16)
        for m in range(NMAT):
            sq_, h = divmod(m, HEADS)
            cc, half = divmod(h, 2)
            prow = 64 * half
            if m >= 2:
                tensor.wait_ge(s_s2a, m - 1)
            qs = qlm_b[prow:prow + 64, cc * BC * LM + sq_ * LM: cc * BC * LM + (sq_ + 1) * LM]
            ks = klm_b[prow:prow + 64, cc * BC * LM + sq_ * LM: cc * BC * LM + (sq_ + 1) * LM]
            tensor.matmul(ps_s2[m % 2][:], qs, ks, start=True, stop=True).then_inc(s_pe, 1)
            tensor.wait_ge(s_v2, 3 * m + 2)
            tensor.matmul(ps_cs[m % 2][:], ones_col[:], k2f[m % 2][:],
                          start=True, stop=True).then_inc(s_pe, 1)

    es.close()
    return {"sync": sync_body, "gpsimd": gpsimd_body, "vector": vector_body,
            "scalar": scalar_body, "tensor": tensor_body}



class SP:
    def __init__(self, nc):
        self.nc = nc
        self.ops = []

    def add(self, eng, fn, dma=False):
        self.ops.append((eng, fn, 16 if dma else 1, False))

    def add_raw(self, eng, fn, inc=1):
        self.ops.append((eng, fn, inc, True))

    def emit(self, es, prologue=None):
        nc = self.nc
        sq = es.enter_context(nc.semaphore("s_q"))
        s_bar = es.enter_context(nc.semaphore("s_bar"))
        pre = []
        c = 0
        for (_, _, inc, _) in self.ops:
            pre.append(c)
            c += inc
        total = c
        blk = es.enter_context(nc.Block())
        mine = {}
        for i, (e, _, _, _) in enumerate(self.ops):
            mine.setdefault(e, []).append(i)
        engines = ["sync", "vector", "scalar", "tensor", "gpsimd"]

        def mk(ename):
            idxs = mine.get(ename, [])

            def body(eng):
                if prologue is not None:
                    if ename in prologue:
                        prologue[ename](eng)
                    eng.sem_inc(s_bar, 1)
                    eng.wait_ge(s_bar, len(engines))
                for i in idxs:
                    _, fn, inc, raw = self.ops[i]
                    if pre[i] > 0:
                        eng.wait_ge(sq, pre[i])
                    if raw:
                        fn(eng, sq, inc)
                    else:
                        fn(eng).then_inc(sq, inc)
                if ename == "sync":
                    eng.wait_ge(sq, total)
            return body

        for en in engines:
            getattr(blk, en)(mk(en))
        return total



def emit_C(nc, sp, k2_in, amax_in, eyes_in, identb_in, z_o, zt_o):
    es = contextlib.ExitStack()
    sbt = lambda n, s, dt=F32: es.enter_context(nc.sbuf_tensor(n, s, dt))
    pst = lambda n, s, dt=F32: es.enter_context(nc.psum_tensor(n, s, dt))
    k2m = sbt("k2m", [P, LM], BF16)
    amax = sbt("amax_s", [1, 4])
    gm = sbt("gm", [1, 1])
    rc = sbt("rc", [1, 1])
    rcb = sbt("rcb", [P, 1])
    onesr = sbt("onesr", [1, P])
    eyes = sbt("eyes_s", [P, 3 * P])
    eyesb = sbt("eyesb", [P, 3 * P], BF16)
    identb = sbt("identb_s", [P, P], BF16)
    K2T = sbt("K2T", [P, LM], BF16)
    Vt = sbt("Vt", [P, LM], BF16)
    VTt = sbt("VTt", [P, LM], BF16)
    ATs = sbt("ATs", [P, LM], BF16)
    N2 = sbt("N2", [P, LM], BF16)
    N1 = sbt("N1", [P, LM], BF16)
    Mt = sbt("Mt", [P, LM], BF16)
    ps_a = pst("ps_a", [P, LM])
    ps_b = pst("ps_b", [P, LM])
    ps_t = pst("ps_t", [P, LM], BF16)
    ps_r = pst("ps_r", [P, 1])

    A = sp.add
    A("sync", lambda e: e.dma_start(out=amax[:], in_=amax_in), dma=True)
    A("sync", lambda e: e.dma_start(out=eyes[:], in_=eyes_in), dma=True)
    A("sync", lambda e: e.dma_start(out=identb[:], in_=identb_in), dma=True)
    A("gpsimd", lambda e: e.memset(onesr[:], 1.0))
    A("vector", lambda e: e.tensor_reduce(out=gm[:], in_=amax[:], axis=AX.X, op=ALU.max))
    A("vector", lambda e: e.reciprocal(out=rc[:], in_=gm[:]))
    A("vector", lambda e: e.tensor_copy(out=eyesb[:], in_=eyes[:]))
    A("tensor", lambda e: e.matmul(ps_r[:], onesr[0:1, :], rc[0:1, :], start=True, stop=True))
    A("scalar", lambda e: e.activation(out=rcb[:], in_=ps_r[:], func=AF.Copy))
    for m in range(NMAT):
        mm = m
        A("sync", lambda e, m=mm: e.dma_start(out=k2m[:], in_=k2_in[m * P:(m + 1) * P, :]), dma=True)
        A("tensor", lambda e: e.transpose(ps_t[:], k2m[:], identb[:]))
        A("scalar", lambda e: e.activation(out=K2T[:], in_=ps_t[:], func=AF.Copy))
        A("scalar", lambda e: e.activation(out=Vt[:], in_=ps_t[:], func=AF.Identity, scale=rcb[:, 0:1]))
        A("vector", lambda e: e.tensor_scalar(out=VTt[:], in0=k2m[:], scalar1=rcb[:, 0:1],
                                              scalar2=None, op0=ALU.mult))
        for it in range(6):
            A("tensor", lambda e: e.matmul(ps_a[:], K2T[:], Vt[:], start=True, stop=True))
            A("tensor", lambda e: e.matmul(ps_b[:], Vt[:], K2T[:], start=True, stop=True))
            A("vector", lambda e: e.scalar_tensor_tensor(out=N2[:], in0=ps_a[:], scalar=-1.0,
                                                         in1=eyesb[:, 0:P], op0=ALU.mult, op1=ALU.add))
            A("scalar", lambda e: e.activation(out=ATs[:], in_=ps_b[:], func=AF.Copy))
            A("tensor", lambda e: e.matmul(ps_a[:], ATs[:], N2[:], start=True, stop=True))
            A("vector", lambda e: e.scalar_tensor_tensor(out=N1[:], in0=ps_a[:], scalar=-1.0,
                                                         in1=eyesb[:, P:2 * P], op0=ALU.mult, op1=ALU.add))
            A("tensor", lambda e: e.matmul(ps_a[:], ATs[:], N1[:], start=True, stop=True))
            A("vector", lambda e: e.scalar_tensor_tensor(out=Mt[:], in0=ps_a[:], scalar=-1.0,
                                                         in1=eyesb[:, 2 * P:3 * P], op0=ALU.mult, op1=ALU.add))
            A("tensor", lambda e: e.matmul(ps_a[:], VTt[:], Mt[:], start=True, stop=True))
            A("tensor", lambda e: e.matmul(ps_b[:], Mt[:], VTt[:], start=True, stop=True))
            A("scalar", lambda e: e.activation(out=Vt[:], in_=ps_a[:], func=AF.Identity, scale=0.25))
            A("scalar", lambda e: e.activation(out=VTt[:], in_=ps_b[:], func=AF.Identity, scale=0.25))
        A("sync", lambda e, m=mm: e.dma_start(out=z_o[m * P:(m + 1) * P, :], in_=Vt[:]), dma=True)
        A("sync", lambda e, m=mm: e.dma_start(out=zt_o[m * P:(m + 1) * P, :], in_=VTt[:]), dma=True)
    es.close()



def emit_D(nc, sp, q_in, k_in, v_in, qlm_in, klm_in, zt_in, identf_in, attn_o):
    es = contextlib.ExitStack()
    sbt = lambda n, s, dt=F32: es.enter_context(nc.sbuf_tensor(n, s, dt))
    pst = lambda n, s, dt=F32: es.enter_context(nc.psum_tensor(n, s, dt))
    qlmt = [sbt(f"qlmt{c}", [P, BC * LM], BF16) for c in range(NCH)]
    klmt = [sbt(f"klmt{c}", [P, BC * LM], BF16) for c in range(NCH)]
    ksl = sbt("ksl", [P, S], BF16)
    qsl = sbt("qsl", [P, S], BF16)
    vsl = sbt("vsl", [P, 16 * HD], BF16)
    ztm = sbt("ztm", [P, LM], BF16)
    s3f = sbt("s3f", [P, S])
    e3 = sbt("e3", [P, S], BF16)
    e3t = es.enter_context(nc.sbuf_tensor("e3t", [P, S], BF16, is_transpose_dest=True))
    e1 = sbt("e1", [P, S], BF16)
    red = sbt("red", [P, 4])
    Us = sbt("Us", [P, HD], BF16)
    Ps = sbt("Ps", [P, HD], BF16)
    r1s = sbt("r1s", [1, S])
    rr1 = sbt("rr1", [1, S])
    bcs = sbt("bcs", [64, 512])
    onescb = sbt("onescb", [P, 1], BF16)
    onesr1 = sbt("onesr1", [1, 64])
    identf = sbt("identf_s", [P, P])
    afm = sbt("afm", [P, T])
    stag = sbt("stag", [P, P])
    ps_m = pst("ps_m", [P, 512])
    ps_u = pst("ps_u", [P, HD])
    ps_x = pst("ps_x", [64, 512])
    ps_bc = pst("ps_bc", [64, 512])
    ps_t = pst("ps_tD", [P, P])
    ps_r1 = pst("ps_r1", [1, 512])

    A = sp.add
    for c in range(NCH):
        A("sync", lambda e, c=c: e.dma_start(out=qlmt[c][:], in_=qlm_in[c * P:(c + 1) * P, :]), dma=True)
        A("sync", lambda e, c=c: e.dma_start(out=klmt[c][:], in_=klm_in[c * P:(c + 1) * P, :]), dma=True)
    A("sync", lambda e: e.dma_start(out=identf[:], in_=identf_in), dma=True)
    A("gpsimd", lambda e: e.memset(onescb[:], 1.0))
    A("gpsimd", lambda e: e.memset(onesr1[:], 1.0))

    for cc in range(NCH):
        for s in range(BC):
            for hh in range(2):
                h = 2 * cc + hh
                m = s * HEADS + h
                A("sync", lambda e, cc=cc, hh=hh, s=s: e.dma_start(
                    out=ksl[64 * hh:64 * hh + 64, :],
                    in_=k_in[cc * P + 64 * hh: cc * P + 64 * hh + 64,
                             s * S:(s + 1) * S]), dma=True)
                A("sync", lambda e, cc=cc, hh=hh, s=s: e.dma_start(
                    out=qsl[64 * hh:64 * hh + 64, :],
                    in_=q_in[cc * P + 64 * hh: cc * P + 64 * hh + 64,
                             s * S:(s + 1) * S]), dma=True)
                for j in range(16):
                    A("sync", lambda e, s=s, j=j, h=h: e.dma_start(
                        out=vsl[:, j * HD:(j + 1) * HD],
                        in_=v_in[s * S + j * P: s * S + (j + 1) * P,
                                 h * HD:(h + 1) * HD]), dma=True)
                A("sync", lambda e, m=m: e.dma_start(out=ztm[:], in_=zt_in[m * P:(m + 1) * P, :]), dma=True)
                for b in range(4):
                    A("tensor", lambda e, cc=cc, hh=hh, s=s, b=b: e.matmul(
                        ps_m[:], qlmt[cc][64 * hh:64 * hh + 64, s * LM:(s + 1) * LM],
                        ksl[64 * hh:64 * hh + 64, b * 512:(b + 1) * 512], start=True, stop=True))
                    A("scalar", lambda e, b=b: e.activation(out=s3f[:, b * 512:(b + 1) * 512],
                                                            in_=ps_m[:], func=AF.Copy))
                A("vector", lambda e: e.tensor_reduce(out=red[:, 0:1], in_=s3f[:], axis=AX.X, op=ALU.max))
                A("vector", lambda e: e.tensor_scalar_mul(out=red[:, 1:2], in0=red[:, 0:1],
                                                          scalar1=-1.0 / SEG))
                A("scalar", lambda e: e.activation(out=e3[:], in_=s3f[:], func=AF.Exp,
                                                   scale=1.0 / SEG, bias=red[:, 1:2]))
                A("vector", lambda e: e.tensor_reduce(out=red[:, 2:3], in_=e3[:], axis=AX.X, op=ALU.add))
                A("vector", lambda e: e.reciprocal(out=red[:, 3:4], in_=red[:, 2:3]))
                for j in range(16):
                    A("scalar", lambda e, j=j: e.dma_start(out=e3t[:, j * P:(j + 1) * P],
                                                           in_=e3[:, j * P:(j + 1) * P],
                                                           transpose=True), dma=True)
                for j in range(16):
                    A("tensor", lambda e, j=j: e.matmul(ps_u[:], e3t[:, j * P:(j + 1) * P],
                                                        vsl[:, j * HD:(j + 1) * HD],
                                                        start=(j == 0), stop=(j == 15)))
                A("vector", lambda e: e.tensor_scalar(out=Us[:], in0=ps_u[:], scalar1=red[:, 3:4],
                                                      scalar2=None, op0=ALU.mult))
                A("tensor", lambda e: e.matmul(ps_u[:], ztm[:], Us[:], start=True, stop=True))
                A("scalar", lambda e: e.activation(out=Ps[:], in_=ps_u[:], func=AF.Copy))
                for b in range(4):
                    A("tensor", lambda e, cc=cc, hh=hh, s=s, b=b: e.matmul(
                        ps_m[:], klmt[cc][64 * hh:64 * hh + 64, s * LM:(s + 1) * LM],
                        qsl[64 * hh:64 * hh + 64, b * 512:(b + 1) * 512], start=True, stop=True))
                    A("scalar", lambda e, b=b: e.activation(out=e1[:, b * 512:(b + 1) * 512],
                                                            in_=ps_m[:], func=AF.Exp, scale=1.0 / SEG))
                for b in range(4):
                    A("tensor", lambda e, b=b: e.matmul(ps_r1[:], onescb[:], e1[:, b * 512:(b + 1) * 512],
                                                        start=True, stop=True))
                    A("scalar", lambda e, b=b: e.activation(out=r1s[0:1, b * 512:(b + 1) * 512],
                                                            in_=ps_r1[:], func=AF.Copy))
                A("vector", lambda e: e.reciprocal(out=rr1[:], in_=r1s[:]))
                for b in range(4):
                    A("tensor", lambda e, b=b: e.matmul(ps_x[:], Ps[:], e1[:, b * 512:(b + 1) * 512],
                                                        start=True, stop=True))
                    A("tensor", lambda e, b=b: e.matmul(ps_bc[:], onesr1[0:1, :],
                                                        rr1[0:1, b * 512:(b + 1) * 512],
                                                        start=True, stop=True))
                    A("scalar", lambda e, b=b: e.activation(out=bcs[:], in_=ps_bc[:], func=AF.Copy))
                    A("vector", lambda e, hh=hh, s=s, b=b: e.tensor_tensor(
                        out=afm[64 * hh:64 * hh + 64, s * S + b * 512: s * S + (b + 1) * 512],
                        in0=ps_x[:], in1=bcs[:], op=ALU.mult))
        for tt in range(NT):
            A("tensor", lambda e, tt=tt: e.transpose(ps_t[:], afm[:, tt * P:(tt + 1) * P], identf[:]))
            A("scalar", lambda e: e.activation(out=stag[:], in_=ps_t[:], func=AF.Copy))
            A("sync", lambda e, tt=tt, cc=cc: e.dma_start(
                out=attn_o[tt * P:(tt + 1) * P, cc * P:(cc + 1) * P], in_=stag[:]), dma=True)
    es.close()



def emit_quant(Aop, x_ap, q_ap, scr, C, pp=P):
    nb = C // 16
    bm, sc, rs, hs = scr["bm"], scr["sc"], scr["rs"], scr["hs"]
    y, a, t1, t2, t3 = scr["y"], scr["a"], scr["t1"], scr["t2"], scr["t3"]
    ri1, ri2 = scr["ri1"], scr["ri2"]
    x3 = x_ap.rearrange("p (n w) -> p n w", w=16)
    Aop("vector", lambda e: e.tensor_reduce(out=bm[:pp, :nb], in_=x3, axis=AX.X,
                                            op=ALU.max, apply_absolute_value=True))
    Aop("vector", lambda e: e.tensor_scalar(out=sc[:pp, :nb], in0=bm[:pp, :nb], scalar1=1.0 / 6.0,
                                            scalar2=1e-12, op0=ALU.mult, op1=ALU.max))
    Aop("vector", lambda e: e.reciprocal(out=rs[:pp, :nb], in_=sc[:pp, :nb]))
    rs3 = rs[:pp, :nb].unsqueeze(2).broadcast_to([pp, nb, 16])
    y3 = y[:pp, :C].rearrange("p (n w) -> p n w", w=16)
    Aop("vector", lambda e: e.tensor_tensor(out=y3, in0=x3, in1=rs3, op=ALU.mult))
    Aop("vector", lambda e: e.tensor_scalar(out=a[:pp, :C].bitcast(mybir.dt.int32),
                                            in0=y[:pp, :C].bitcast(mybir.dt.int32),
                                            scalar1=0x7fffffff, scalar2=None,
                                            op0=ALU.bitwise_and))
    Aop("vector", lambda e: e.tensor_scalar_mul(out=ri1[:pp, :C], in0=a[:pp, :C], scalar1=2.0))
    Aop("vector", lambda e: e.tensor_copy(out=ri2[:pp, :C], in_=a[:pp, :C]))
    Aop("vector", lambda e: e.tensor_scalar(out=t1[:pp, :C], in0=ri1[:pp, :C], scalar1=0.5,
                                            scalar2=2.0, op0=ALU.mult, op1=ALU.min))
    Aop("vector", lambda e: e.tensor_scalar(out=t2[:pp, :C], in0=ri2[:pp, :C], scalar1=2.0,
                                            scalar2=4.0, op0=ALU.max, op1=ALU.min))
    Aop("vector", lambda e: e.tensor_tensor(out=t1[:pp, :C], in0=t1[:pp, :C], in1=t2[:pp, :C],
                                            op=ALU.add))
    Aop("vector", lambda e: e.tensor_scalar(out=t2[:pp, :C], in0=a[:pp, :C], scalar1=5.0,
                                            scalar2=2.0, op0=ALU.is_gt, op1=ALU.mult))
    Aop("vector", lambda e: e.scalar_tensor_tensor(out=t1[:pp, :C], in0=t1[:pp, :C], scalar=-2.0,
                                                   in1=t2[:pp, :C], op0=ALU.add, op1=ALU.add))
    Aop("scalar", lambda e: e.activation(out=t3[:pp, :C], in_=y[:pp, :C], func=AF.Sign))
    Aop("vector", lambda e: e.tensor_tensor(out=t1[:pp, :C], in0=t1[:pp, :C], in1=t3[:pp, :C],
                                            op=ALU.mult))
    Aop("vector", lambda e: e.tensor_scalar_mul(out=hs[:pp, :nb], in0=sc[:pp, :nb], scalar1=1.0))
    hs3 = hs[:pp, :nb].unsqueeze(2).broadcast_to([pp, nb, 16])
    t13 = t1[:pp, :C].rearrange("p (n w) -> p n w", w=16)
    q3 = q_ap.rearrange("p (n w) -> p n w", w=16)
    Aop("vector", lambda e: e.tensor_tensor(out=q3, in0=t13, in1=hs3, op=ALU.mult))


def quant_scratch(sbt, tag, C, dt_mid=F32):
    nb = C // 16
    return {
        "bm": sbt(f"bm{tag}", [P, nb]), "sc": sbt(f"sc{tag}", [P, nb]),
        "rs": sbt(f"rs{tag}", [P, nb]), "hs": sbt(f"hs{tag}", [P, nb]),
        "y": sbt(f"yq{tag}", [P, C]), "a": sbt(f"aq{tag}", [P, C]),
        "t1": sbt(f"t1q{tag}", [P, C], BF16), "t2": sbt(f"t2q{tag}", [P, C], BF16),
        "t3": sbt(f"t3q{tag}", [P, C], BF16),
        "ri1": sbt(f"ri1q{tag}", [P, C], I16), "ri2": sbt(f"ri2q{tag}", [P, C], I16),
    }



def emit_WQfull(nc, sp, tag, wg_in, wu_in, wd_in, wgq_o, wuq_o, wdq_o):
    es = contextlib.ExitStack()
    sbt = lambda n, s, dt=F32: es.enter_context(nc.sbuf_tensor(n + tag, s, dt))
    wt = sbt("wt", [P, I_FFN])
    qt = sbt("qt", [P, I_FFN], BF16)
    qtt = es.enter_context(nc.sbuf_tensor("qtt" + tag, [P, I_FFN], BF16, is_transpose_dest=True))
    scr = quant_scratch(sbt, "w", I_FFN)
    A = sp.add
    jobs = []
    for r in range(I_FFN // P):
        jobs.append((wg_in[r * P:(r + 1) * P, :], wgq_o, r, D))
        jobs.append((wu_in[r * P:(r + 1) * P, :], wuq_o, r, D))
    for r in range(NCH):
        jobs.append((wd_in[r * P:(r + 1) * P, :], wdq_o, r, I_FFN))
    for (src_ap, dst, r, C) in jobs:
        A("sync", lambda e, src_ap=src_ap, C=C: e.dma_start(out=wt[:, 0:C], in_=src_ap), dma=True)
        emit_quant(A, wt[:, 0:C], qt[:, 0:C], scr, C)
        for j in range(C // P):
            A("scalar", lambda e, j=j: e.dma_start(out=qtt[:, j * P:(j + 1) * P],
                                                   in_=qt[:, j * P:(j + 1) * P],
                                                   transpose=True), dma=True)
        for j in range(C // P):
            A("sync", lambda e, dst=dst, r=r, j=j: e.dma_start(
                out=dst[j * P:(j + 1) * P, r * P:(r + 1) * P],
                in_=qtt[:, j * P:(j + 1) * P]), dma=True)
    es.close()



def emit_FFN(nc, sp, tag, xn_in, wg_in, wu_in, wd_in, ffn_o, hq_sc):
    es = contextlib.ExitStack()
    sbt = lambda n, s, dt=F32: es.enter_context(nc.sbuf_tensor(n + tag, s, dt))
    pst = lambda n, s, dt=F32: es.enter_context(nc.psum_tensor(n + tag, s, dt))
    wpool = sbt("wpool", [P, 2 * NCH * I_FFN], BF16)
    wgq = [wpool[:, c * I_FFN:(c + 1) * I_FFN] for c in range(NCH)]
    wuq = [wpool[:, (NCH + c) * I_FFN:(NCH + c + 1) * I_FFN] for c in range(NCH)]
    wdq = [wpool[:, j * D:(j + 1) * D] for j in range(22)]
    xt = sbt("xt", [P, D])
    xq = sbt("xq", [P, D], BF16)
    xqf = es.enter_context(nc.sbuf_tensor("xqf" + tag, [P, D], BF16, is_transpose_dest=True))
    gact = sbt("gact", [P, I_FFN])
    uact = sbt("uact", [P, I_FFN])
    hact = sbt("hact", [P, I_FFN])
    hqt = sbt("hqt", [P, I_FFN], BF16)
    hqtt = es.enter_context(nc.sbuf_tensor("hqtt" + tag, [P, I_FFN], BF16, is_transpose_dest=True))
    scr = quant_scratch(sbt, "f", I_FFN)
    hload = sbt("hload", [P, P], BF16)
    ot = sbt("ot", [P, D])
    ps = [pst(f"psF{i}", [P, 512]) for i in range(2)]

    IB = [(0, 512), (512, 1024), (1024, 1536), (1536, 2048), (2048, 2560), (2560, 2816)]
    A = sp.add
    for c in range(NCH):
        A("sync", lambda e, c=c: e.dma_start(out=wgq[c], in_=wg_in[c * P:(c + 1) * P, :]), dma=True)
        A("sync", lambda e, c=c: e.dma_start(out=wuq[c], in_=wu_in[c * P:(c + 1) * P, :]), dma=True)
    for tt in range(NT):
        A("sync", lambda e, tt=tt: e.dma_start(out=xt[:], in_=xn_in[tt * P:(tt + 1) * P, :]), dma=True)
        emit_quant(A, xt[:], xq[:], scr, D)
        for j in range(NCH):
            A("scalar", lambda e, j=j: e.dma_start(out=xqf[:, j * P:(j + 1) * P],
                                                   in_=xq[:, j * P:(j + 1) * P], transpose=True), dma=True)
        for wsel in range(2):
            wts = (wgq, wuq)[wsel]
            dstact = (gact, uact)[wsel]
            for (lo, hi) in IB:
                n = hi - lo
                for cc in range(NCH):
                    A("tensor", lambda e, cc=cc, lo=lo, hi=hi, wts=wts, n=n: e.matmul(
                        ps[0][:, 0:n], xqf[:, cc * P:(cc + 1) * P], wts[cc][:, lo:hi],
                        start=(cc == 0), stop=(cc == NCH - 1)))
                if wsel == 0:
                    A("scalar", lambda e, lo=lo, hi=hi, n=n, dstact=dstact: e.activation(
                        out=dstact[:, lo:hi], in_=ps[0][:, 0:n], func=AF.Silu))
                else:
                    A("scalar", lambda e, lo=lo, hi=hi, n=n, dstact=dstact: e.activation(
                        out=dstact[:, lo:hi], in_=ps[0][:, 0:n], func=AF.Copy))
        A("vector", lambda e: e.tensor_tensor(out=hact[:], in0=gact[:], in1=uact[:], op=ALU.mult))
        emit_quant(A, hact[:], hqt[:], scr, I_FFN)
        for j in range(22):
            A("scalar", lambda e, j=j: e.dma_start(out=hqtt[:, j * P:(j + 1) * P],
                                                   in_=hqt[:, j * P:(j + 1) * P], transpose=True), dma=True)
        for j in range(22):
            A("sync", lambda e, j=j, tt=tt: e.dma_start(
                out=hq_sc[j * P:(j + 1) * P, tt * P:(tt + 1) * P],
                in_=hqtt[:, j * P:(j + 1) * P]), dma=True)
    for j in range(22):
        A("sync", lambda e, j=j: e.dma_start(out=wdq[j], in_=wd_in[j * P:(j + 1) * P, :]), dma=True)
    for tt in range(NT):
        for half in range(2):
            for j in range(22):
                A("sync", lambda e, j=j, tt=tt: e.dma_start(
                    out=hload[:], in_=hq_sc[j * P:(j + 1) * P, tt * P:(tt + 1) * P]), dma=True)
                A("tensor", lambda e, j=j, half=half: e.matmul(
                    ps[half][:], hload[:], wdq[j][:, half * 512:(half + 1) * 512],
                    start=(j == 0), stop=(j == 21)))
            A("scalar", lambda e, half=half: e.activation(out=ot[:, half * 512:(half + 1) * 512],
                                                          in_=ps[half][:], func=AF.Copy))
        A("sync", lambda e, tt=tt: e.dma_start(out=ffn_o[tt * P:(tt + 1) * P, :], in_=ot[:]), dma=True)
    es.close()



def emit_F(nc, sp, x_in, at_in, ff_in, eps_in, x1_o, xn1_o):
    es = contextlib.ExitStack()
    sbt = lambda n, s, dt=F32: es.enter_context(nc.sbuf_tensor(n, s, dt))
    xa = sbt("xaF", [P, D]); xb = sbt("xbF", [P, D]); xc = sbt("xcF", [P, D])
    xn = sbt("xnF", [P, D]); xsq = sbt("xsqF", [P, D])
    stat = sbt("statF", [P, 8]); epsc = sbt("epscF", [P, 1])
    A = sp.add
    A("sync", lambda e: e.dma_start(out=epsc[:], in_=eps_in), dma=True)
    for i in range(NT):
        A("sync", lambda e, i=i: e.dma_start(out=xa[:], in_=x_in[i * P:(i + 1) * P, :]), dma=True)
        A("sync", lambda e, i=i: e.dma_start(out=xb[:], in_=at_in[i * P:(i + 1) * P, :]), dma=True)
        A("sync", lambda e, i=i: e.dma_start(out=xc[:], in_=ff_in[i * P:(i + 1) * P, :]), dma=True)
        A("vector", lambda e: e.tensor_tensor(out=xa[:], in0=xa[:], in1=xb[:], op=ALU.add))
        A("vector", lambda e: e.tensor_tensor(out=xa[:], in0=xa[:], in1=xc[:], op=ALU.add))
        A("sync", lambda e, i=i: e.dma_start(out=x1_o[i * P:(i + 1) * P, :], in_=xa[:]), dma=True)
        A("vector", lambda e: e.tensor_reduce(out=stat[:, 0:1], in_=xa[:], axis=AX.X, op=ALU.add))
        A("vector", lambda e: e.tensor_tensor(out=xsq[:], in0=xa[:], in1=xa[:], op=ALU.mult))
        A("vector", lambda e: e.tensor_reduce(out=stat[:, 1:2], in_=xsq[:], axis=AX.X, op=ALU.add))
        A("vector", lambda e: e.tensor_scalar_mul(out=stat[:, 2:3], in0=stat[:, 0:1], scalar1=1.0 / D))
        A("vector", lambda e: e.tensor_scalar_mul(out=stat[:, 3:4], in0=stat[:, 1:2], scalar1=1.0 / D))
        A("vector", lambda e: e.tensor_tensor(out=stat[:, 4:5], in0=stat[:, 2:3], in1=stat[:, 2:3], op=ALU.mult))
        A("vector", lambda e: e.tensor_tensor(out=stat[:, 4:5], in0=stat[:, 3:4], in1=stat[:, 4:5], op=ALU.subtract))
        A("scalar", lambda e: e.activation(out=stat[:, 5:6], in_=stat[:, 4:5], func=AF.Ln, bias=epsc[:, 0:1]))
        A("scalar", lambda e: e.activation(out=stat[:, 5:6], in_=stat[:, 5:6], func=AF.Exp, scale=-0.5))
        A("vector", lambda e: e.scalar_tensor_tensor(out=stat[:, 6:7], in0=stat[:, 2:3], scalar=-1.0,
                                                     in1=stat[:, 5:6], op0=ALU.mult, op1=ALU.mult))
        A("scalar", lambda e: e.activation(out=xn[:], in_=xa[:], func=AF.Identity,
                                           scale=stat[:, 5:6], bias=stat[:, 6:7]))
        A("sync", lambda e, i=i: e.dma_start(out=xn1_o[i * P:(i + 1) * P, :], in_=xn[:]), dma=True)
    es.close()



def emit_I(nc, sp, a_in, b_in, c_in, o_out):
    es = contextlib.ExitStack()
    sbt = lambda n, s, dt=F32: es.enter_context(nc.sbuf_tensor(n, s, dt))
    xa = sbt("xaI", [P, D]); xb = sbt("xbI", [P, D]); xc = sbt("xcI", [P, D])
    A = sp.add
    for i in range(NT):
        A("sync", lambda e, i=i: e.dma_start(out=xa[:], in_=a_in[i * P:(i + 1) * P, :]), dma=True)
        A("sync", lambda e, i=i: e.dma_start(out=xb[:], in_=b_in[i * P:(i + 1) * P, :]), dma=True)
        A("sync", lambda e, i=i: e.dma_start(out=xc[:], in_=c_in[i * P:(i + 1) * P, :]), dma=True)
        A("vector", lambda e: e.tensor_tensor(out=xa[:], in0=xa[:], in1=xb[:], op=ALU.add))
        A("vector", lambda e: e.tensor_tensor(out=xa[:], in0=xa[:], in1=xc[:], op=ALU.add))
        A("sync", lambda e, i=i: e.dma_start(out=o_out[i * P:(i + 1) * P, :], in_=xa[:]), dma=True)
    es.close()



def emit_G(nc, sp, xn1_in, ln_b, wkT_in, wvT_in, wrT_in, woT_in, wfkT_in,
           wfvT_in, wfrT_in, tdc_in, tfc_in, eps_in, attn1_o,
           a_sc, x2_sc, xk_sc, xv_sc, xr_sc, xr2_sc, sa_sc, kfq_sc):
    es = contextlib.ExitStack()
    sbt = lambda n, s, dt=F32: es.enter_context(nc.sbuf_tensor(n, s, dt))
    pst = lambda n, s, dt=F32: es.enter_context(nc.psum_tensor(n, s, dt))
    pool = sbt("poolG", [P, 25000])
    xk2_fm = [sbt(f"xk2fm{c}", [P, T], BF16) for c in range(NCH)]
    trd = es.enter_context(nc.sbuf_tensor("trdG", [P, D], BF16, is_transpose_dest=True))
    stat = sbt("statG", [P, 8])
    epsc = sbt("epscG", [P, 1])
    smallc = sbt("smallc", [P, 8])
    tdc = sbt("tdcG", [P, NCH]); tfc = sbt("tfcG", [P, NCH])
    hlast = sbt("hlastG", [1, D])
    ps = [pst(f"psG{i}", [P, 512]) for i in range(2)]

    class Carve:
        def __init__(self, base):
            self.base = base
            self.off = 0
        def f32(self, n):
            v = pool[:, self.off:self.off + n]
            self.off += n
            return v
        def bf(self, n):
            v = pool[:, self.off:self.off + (n + 1) // 2].bitcast(BF16)
            self.off += (n + 1) // 2
            return v

    A = sp.add
    A("sync", lambda e: e.dma_start(out=epsc[:], in_=eps_in), dma=True)
    A("sync", lambda e: e.dma_start(out=tdc[:], in_=tdc_in), dma=True)
    A("sync", lambda e: e.dma_start(out=tfc[:], in_=tfc_in), dma=True)

    def emit_ln(xa, dst, wname, bname, xsq):
        A("vector", lambda e: e.tensor_reduce(out=stat[:, 0:1], in_=xa, axis=AX.X, op=ALU.add))
        A("vector", lambda e: e.tensor_tensor(out=xsq, in0=xa, in1=xa, op=ALU.mult))
        A("vector", lambda e: e.tensor_reduce(out=stat[:, 1:2], in_=xsq, axis=AX.X, op=ALU.add))
        A("vector", lambda e: e.tensor_scalar_mul(out=stat[:, 2:3], in0=stat[:, 0:1], scalar1=1.0 / D))
        A("vector", lambda e: e.tensor_scalar_mul(out=stat[:, 3:4], in0=stat[:, 1:2], scalar1=1.0 / D))
        A("vector", lambda e: e.tensor_tensor(out=stat[:, 4:5], in0=stat[:, 2:3], in1=stat[:, 2:3], op=ALU.mult))
        A("vector", lambda e: e.tensor_tensor(out=stat[:, 4:5], in0=stat[:, 3:4], in1=stat[:, 4:5], op=ALU.subtract))
        A("scalar", lambda e: e.activation(out=stat[:, 5:6], in_=stat[:, 4:5], func=AF.Ln, bias=epsc[:, 0:1]))
        A("scalar", lambda e: e.activation(out=stat[:, 5:6], in_=stat[:, 5:6], func=AF.Exp, scale=-0.5))
        A("vector", lambda e: e.scalar_tensor_tensor(out=stat[:, 6:7], in0=stat[:, 2:3], scalar=-1.0,
                                                     in1=stat[:, 5:6], op0=ALU.mult, op1=ALU.mult))
        A("scalar", lambda e: e.activation(out=dst, in_=xa, func=AF.Identity,
                                           scale=stat[:, 5:6], bias=stat[:, 6:7]))
        A("vector", lambda e, wname=wname: e.tensor_tensor(out=dst, in0=dst, in1=lnt[wname][:], op=ALU.mult))
        A("vector", lambda e, bname=bname: e.tensor_tensor(out=dst, in0=dst, in1=lnt[bname][:], op=ALU.add))

    cv = Carve(0)
    lnt = {n: sbt(f"lnt_{n}", [P, D]) for n in ("ln0w", "ln0b", "ln1w", "ln1b", "tmkb", "tmvb", "tmrb")}
    for n in lnt:
        A("sync", lambda e, n=n: e.dma_start(out=lnt[n][:], in_=ln_b[n]), dma=True)
    xt = cv.f32(D); at = cv.f32(D); ht = cv.f32(D); xxt = cv.f32(D)
    dt_ = cv.f32(D); mixt = cv.f32(D); xsq1 = cv.f32(D)
    mixb = cv.bf(D)
    for i in range(NT):
        A("sync", lambda e, i=i: e.dma_start(out=xt, in_=xn1_in[i * P:(i + 1) * P, :]), dma=True)
        emit_ln(xt, at, "ln0w", "ln0b", xsq1)
        A("sync", lambda e, i=i: e.dma_start(out=a_sc[i * P:(i + 1) * P, :], in_=at), dma=True)
        emit_ln(at, ht, "ln1w", "ln1b", xsq1)
        if i % 16 == 0:
            A("gpsimd", lambda e: e.memset(xxt[0:1, 0:D], 0.0))
        else:
            A("sync", lambda e: e.dma_start(out=xxt[0:1, 0:D], in_=hlast[:]), dma=True)
        A("sync", lambda e: e.dma_start(out=xxt[1:P, 0:D], in_=ht[0:P - 1, 0:D]), dma=True)
        A("sync", lambda e: e.dma_start(out=hlast[:], in_=ht[P - 1:P, 0:D]), dma=True)
        A("vector", lambda e: e.tensor_tensor(out=dt_, in0=ht, in1=xxt, op=ALU.subtract))
        for (mn, dst_sc) in (("tmkb", xk_sc), ("tmvb", xv_sc), ("tmrb", xr_sc)):
            A("vector", lambda e, mn=mn: e.tensor_tensor(out=mixt, in0=dt_, in1=lnt[mn][:], op=ALU.mult))
            A("vector", lambda e: e.tensor_tensor(out=mixt, in0=mixt, in1=xxt, op=ALU.add))
            A("vector", lambda e: e.tensor_copy(out=mixb, in_=mixt))
            for c in range(NCH):
                A("scalar", lambda e, c=c: e.dma_start(out=trd[:, c * P:(c + 1) * P],
                                                       in_=mixb[:, c * P:(c + 1) * P],
                                                       transpose=True), dma=True)
            for c in range(NCH):
                A("sync", lambda e, c=c, i=i, dst_sc=dst_sc: e.dma_start(
                    out=dst_sc[c * P:(c + 1) * P, i * P:(i + 1) * P],
                    in_=trd[:, c * P:(c + 1) * P]), dma=True)

    cv = Carve(0)
    ko = cv.f32(S); vo = cv.f32(S); sro = cv.f32(S)
    eko = cv.f32(S); ekvo = cv.f32(S)
    Asc = cv.f32(S + 1); Bsc = cv.f32(S + 1)
    numt = cv.f32(S); dent = cv.f32(S); rdt = cv.f32(S)
    wblk = [cv.bf(P) for _ in range(24)]
    rhsb = cv.bf(512)
    saob = cv.bf(S)
    wld2 = cv.f32(P)
    for oc in range(NCH):
        for pi, w_in in enumerate((wkT_in, wvT_in, wrT_in)):
            for cc in range(NCH):
                A("sync", lambda e, w_in=w_in, cc=cc, oc=oc: e.dma_start(
                    out=wld2, in_=w_in[cc * P:(cc + 1) * P, oc * P:(oc + 1) * P]), dma=True)
                A("vector", lambda e, pi=pi, cc=cc: e.tensor_copy(out=wblk[pi * NCH + cc], in_=wld2))
        A("scalar", lambda e, oc=oc: e.activation(out=smallc[:, 0:1], in_=tdc[:, oc:oc + 1], func=AF.Exp))
        A("scalar", lambda e: e.activation(out=smallc[:, 1:2], in_=smallc[:, 0:1], func=AF.Exp, scale=-1.0))
        A("scalar", lambda e, oc=oc: e.activation(out=smallc[:, 2:3], in_=tfc[:, oc:oc + 1], func=AF.Exp))
        for s in range(BC):
            for (pi, src_sc, dsto, fn) in ((0, xk_sc, ko, AF.Copy), (1, xv_sc, vo, AF.Copy),
                                           (2, xr_sc, sro, AF.Sigmoid)):
                for tb in range(4):
                    for cc in range(NCH):
                        A("sync", lambda e, src_sc=src_sc, cc=cc, s=s, tb=tb: e.dma_start(
                            out=rhsb, in_=src_sc[cc * P:(cc + 1) * P,
                                                 s * S + tb * 512: s * S + (tb + 1) * 512]), dma=True)
                        A("tensor", lambda e, pi=pi, cc=cc: e.matmul(
                            ps[0][:], wblk[pi * NCH + cc], rhsb, start=(cc == 0), stop=(cc == NCH - 1)))
                    A("scalar", lambda e, dsto=dsto, fn=fn, tb=tb: e.activation(
                        out=dsto[:, tb * 512:(tb + 1) * 512], in_=ps[0][:], func=fn))
            A("scalar", lambda e: e.activation(out=eko, in_=ko, func=AF.Exp))
            A("vector", lambda e: e.tensor_tensor(out=ekvo, in0=eko, in1=vo, op=ALU.mult))
            A("gpsimd", lambda e: e.memset(Asc[:, 0:1], 0.0))
            A("gpsimd", lambda e: e.memset(Bsc[:, 0:1], 0.0))
            ewb = smallc[:, 1:2].broadcast_to([P, S])
            A("vector", lambda e: e.tensor_tensor_scan(out=Asc[:, 1:S + 1], data0=ewb, data1=ekvo,
                                                       initial=0.0, op0=ALU.mult, op1=ALU.add))
            A("vector", lambda e: e.tensor_tensor_scan(out=Bsc[:, 1:S + 1], data0=ewb, data1=eko,
                                                       initial=0.0, op0=ALU.mult, op1=ALU.add))
            A("vector", lambda e: e.scalar_tensor_tensor(out=numt, in0=ekvo, scalar=smallc[:, 2:3],
                                                         in1=Asc[:, 0:S], op0=ALU.mult, op1=ALU.add))
            A("vector", lambda e: e.scalar_tensor_tensor(out=dent, in0=eko, scalar=smallc[:, 2:3],
                                                         in1=Bsc[:, 0:S], op0=ALU.mult, op1=ALU.add))
            A("scalar", lambda e: e.activation(out=rdt, in_=dent, func=AF.Ln))
            A("scalar", lambda e: e.activation(out=rdt, in_=rdt, func=AF.Exp, scale=-1.0))
            A("vector", lambda e: e.tensor_tensor(out=numt, in0=numt, in1=rdt, op=ALU.mult))
            A("vector", lambda e: e.tensor_tensor(out=saob, in0=numt, in1=sro, op=ALU.mult))
            A("sync", lambda e, oc=oc, s=s: e.dma_start(
                out=sa_sc[oc * P:(oc + 1) * P, s * S:(s + 1) * S], in_=saob), dma=True)

    cv = Carve(0)
    lnt3 = {}
    for n in ("ln2w", "ln2b", "cmkb", "cmrb"):
        t = cv.f32(D)
        lnt3[n] = t
        A("sync", lambda e, n=n, t=t: e.dma_start(out=t, in_=ln_b[n]), dma=True)
    lnt.update({k: type("W", (), {"__getitem__": (lambda self, idx, t=v: t)})() for k, v in lnt3.items()})
    woTb = [cv.bf(D) for _ in range(NCH)]
    wld3 = cv.f32(D)
    for cc in range(NCH):
        A("sync", lambda e, cc=cc: e.dma_start(out=wld3, in_=woT_in[cc * P:(cc + 1) * P, :]), dma=True)
        A("vector", lambda e, cc=cc: e.tensor_copy(out=woTb[cc], in_=wld3))
    salhs = cv.bf(P)
    ot3 = cv.f32(D); a3 = cv.f32(D); x23 = cv.f32(D); h23 = cv.f32(D)
    xx3 = cv.f32(D); d3 = cv.f32(D); mix3 = cv.f32(D); xsq3 = cv.f32(D)
    mixb3 = cv.bf(D)
    for i in range(NT):
        for half in range(2):
            for cc in range(NCH):
                A("sync", lambda e, cc=cc, i=i: e.dma_start(
                    out=salhs, in_=sa_sc[cc * P:(cc + 1) * P, i * P:(i + 1) * P]), dma=True)
                A("tensor", lambda e, cc=cc, half=half: e.matmul(
                    ps[0][:], salhs, woTb[cc][:, half * 512:(half + 1) * 512],
                    start=(cc == 0), stop=(cc == NCH - 1)))
            A("scalar", lambda e, half=half: e.activation(out=ot3[:, half * 512:(half + 1) * 512],
                                                          in_=ps[0][:], func=AF.Copy))
        A("sync", lambda e, i=i: e.dma_start(out=a3, in_=a_sc[i * P:(i + 1) * P, :]), dma=True)
        A("vector", lambda e: e.tensor_tensor(out=x23, in0=ot3, in1=a3, op=ALU.add))
        A("sync", lambda e, i=i: e.dma_start(out=x2_sc[i * P:(i + 1) * P, :], in_=x23), dma=True)
        A("vector", lambda e: e.tensor_reduce(out=stat[:, 0:1], in_=x23, axis=AX.X, op=ALU.add))
        A("vector", lambda e: e.tensor_tensor(out=xsq3, in0=x23, in1=x23, op=ALU.mult))
        A("vector", lambda e: e.tensor_reduce(out=stat[:, 1:2], in_=xsq3, axis=AX.X, op=ALU.add))
        A("vector", lambda e: e.tensor_scalar_mul(out=stat[:, 2:3], in0=stat[:, 0:1], scalar1=1.0 / D))
        A("vector", lambda e: e.tensor_scalar_mul(out=stat[:, 3:4], in0=stat[:, 1:2], scalar1=1.0 / D))
        A("vector", lambda e: e.tensor_tensor(out=stat[:, 4:5], in0=stat[:, 2:3], in1=stat[:, 2:3], op=ALU.mult))
        A("vector", lambda e: e.tensor_tensor(out=stat[:, 4:5], in0=stat[:, 3:4], in1=stat[:, 4:5], op=ALU.subtract))
        A("scalar", lambda e: e.activation(out=stat[:, 5:6], in_=stat[:, 4:5], func=AF.Ln, bias=epsc[:, 0:1]))
        A("scalar", lambda e: e.activation(out=stat[:, 5:6], in_=stat[:, 5:6], func=AF.Exp, scale=-0.5))
        A("vector", lambda e: e.scalar_tensor_tensor(out=stat[:, 6:7], in0=stat[:, 2:3], scalar=-1.0,
                                                     in1=stat[:, 5:6], op0=ALU.mult, op1=ALU.mult))
        A("scalar", lambda e: e.activation(out=h23, in_=x23, func=AF.Identity,
                                           scale=stat[:, 5:6], bias=stat[:, 6:7]))
        A("vector", lambda e: e.tensor_tensor(out=h23, in0=h23, in1=lnt3["ln2w"], op=ALU.mult))
        A("vector", lambda e: e.tensor_tensor(out=h23, in0=h23, in1=lnt3["ln2b"], op=ALU.add))
        if i % 16 == 0:
            A("gpsimd", lambda e: e.memset(xx3[0:1, 0:D], 0.0))
        else:
            A("sync", lambda e: e.dma_start(out=xx3[0:1, 0:D], in_=hlast[:]), dma=True)
        A("sync", lambda e: e.dma_start(out=xx3[1:P, 0:D], in_=h23[0:P - 1, 0:D]), dma=True)
        A("sync", lambda e: e.dma_start(out=hlast[:], in_=h23[P - 1:P, 0:D]), dma=True)
        A("vector", lambda e: e.tensor_tensor(out=d3, in0=h23, in1=xx3, op=ALU.subtract))
        A("vector", lambda e: e.tensor_tensor(out=mix3, in0=d3, in1=lnt3["cmkb"], op=ALU.mult))
        A("vector", lambda e: e.tensor_tensor(out=mix3, in0=mix3, in1=xx3, op=ALU.add))
        A("vector", lambda e: e.tensor_copy(out=mixb3, in_=mix3))
        for c in range(NCH):
            A("scalar", lambda e, c=c: e.dma_start(out=trd[:, c * P:(c + 1) * P],
                                                   in_=mixb3[:, c * P:(c + 1) * P], transpose=True), dma=True)
        for c in range(NCH):
            A("vector", lambda e, c=c, i=i: e.tensor_copy(out=xk2_fm[c][:, i * P:(i + 1) * P],
                                                          in_=trd[:, c * P:(c + 1) * P]))
        A("vector", lambda e: e.tensor_tensor(out=mix3, in0=d3, in1=lnt3["cmrb"], op=ALU.mult))
        A("vector", lambda e: e.tensor_tensor(out=mix3, in0=mix3, in1=xx3, op=ALU.add))
        A("vector", lambda e: e.tensor_copy(out=mixb3, in_=mix3))
        for c in range(NCH):
            A("scalar", lambda e, c=c: e.dma_start(out=trd[:, c * P:(c + 1) * P],
                                                   in_=mixb3[:, c * P:(c + 1) * P], transpose=True), dma=True)
        for c in range(NCH):
            A("sync", lambda e, c=c, i=i: e.dma_start(
                out=xr2_sc[c * P:(c + 1) * P, i * P:(i + 1) * P],
                in_=trd[:, c * P:(c + 1) * P]), dma=True)

    cv = Carve(0)
    wfkb = [cv.bf(P) for _ in range(NCH)]
    wld4 = cv.f32(P)
    kfo = cv.f32(T)
    kfqo = cv.bf(T)
    for o4 in range(H_FFN // P):
        for cc in range(NCH):
            A("sync", lambda e, cc=cc, o4=o4: e.dma_start(
                out=wld4, in_=wfkT_in[cc * P:(cc + 1) * P, o4 * P:(o4 + 1) * P]), dma=True)
            A("vector", lambda e, cc=cc: e.tensor_copy(out=wfkb[cc], in_=wld4))
        for tb in range(NTB):
            for cc in range(NCH):
                A("tensor", lambda e, cc=cc, tb=tb: e.matmul(
                    ps[0][:], wfkb[cc], xk2_fm[cc][:, tb * 512:(tb + 1) * 512],
                    start=(cc == 0), stop=(cc == NCH - 1)))
            A("scalar", lambda e, tb=tb: e.activation(out=kfo[:, tb * 512:(tb + 1) * 512],
                                                      in_=ps[0][:], func=AF.Relu))
        A("scalar", lambda e: e.activation(out=kfqo, in_=kfo, func=AF.Square))
        A("sync", lambda e, o4=o4: e.dma_start(out=kfq_sc[o4 * P:(o4 + 1) * P, :], in_=kfqo), dma=True)

    cv = Carve(0)
    wfvb = [cv.bf(D) for _ in range(H_FFN // P)]
    wfrb = [cv.bf(D) for _ in range(NCH)]
    wld5 = cv.f32(D)
    kfl = cv.bf(P)
    xrl = cv.bf(P)
    kvt = cv.f32(D)
    rst = cv.f32(D)
    x25 = cv.f32(D)
    for ic in range(H_FFN // P):
        A("sync", lambda e, ic=ic: e.dma_start(out=wld5, in_=wfvT_in[ic * P:(ic + 1) * P, :]), dma=True)
        A("vector", lambda e, ic=ic: e.tensor_copy(out=wfvb[ic], in_=wld5))
    for cc in range(NCH):
        A("sync", lambda e, cc=cc: e.dma_start(out=wld5, in_=wfrT_in[cc * P:(cc + 1) * P, :]), dma=True)
        A("vector", lambda e, cc=cc: e.tensor_copy(out=wfrb[cc], in_=wld5))
    for i in range(NT):
        for half in range(2):
            for ic in range(H_FFN // P):
                A("sync", lambda e, ic=ic, i=i: e.dma_start(
                    out=kfl, in_=kfq_sc[ic * P:(ic + 1) * P, i * P:(i + 1) * P]), dma=True)
                A("tensor", lambda e, ic=ic, half=half: e.matmul(
                    ps[0][:], kfl, wfvb[ic][:, half * 512:(half + 1) * 512],
                    start=(ic == 0), stop=(ic == H_FFN // P - 1)))
            A("scalar", lambda e, half=half: e.activation(out=kvt[:, half * 512:(half + 1) * 512],
                                                          in_=ps[0][:], func=AF.Copy))
        for half in range(2):
            for cc in range(NCH):
                A("sync", lambda e, cc=cc, i=i: e.dma_start(
                    out=xrl, in_=xr2_sc[cc * P:(cc + 1) * P, i * P:(i + 1) * P]), dma=True)
                A("tensor", lambda e, cc=cc, half=half: e.matmul(
                    ps[0][:], xrl, wfrb[cc][:, half * 512:(half + 1) * 512],
                    start=(cc == 0), stop=(cc == NCH - 1)))
            A("scalar", lambda e, half=half: e.activation(out=rst[:, half * 512:(half + 1) * 512],
                                                          in_=ps[0][:], func=AF.Sigmoid))
        A("vector", lambda e: e.tensor_tensor(out=rst, in0=rst, in1=kvt, op=ALU.mult))
        A("sync", lambda e, i=i: e.dma_start(out=x25, in_=x2_sc[i * P:(i + 1) * P, :]), dma=True)
        A("vector", lambda e: e.tensor_tensor(out=rst, in0=rst, in1=x25, op=ALU.add))
        A("sync", lambda e, i=i: e.dma_start(out=attn1_o[i * P:(i + 1) * P, :], in_=rst), dma=True)
    es.close()



def build_MEGA():
    nc = bass.Bass("TRN2", target_bir_lowering=False)
    inp = lambda n, s, dt=F32: nc.dram_tensor(n, s, dt, kind="ExternalInput").ap()
    scr = lambda n, s, dt=F32: nc.dram_tensor(n, s, dt, kind="Internal").ap()

    x_in = inp("x", [T, D])
    wqT = inp("wqT", [D, D]); wkT = inp("wkT", [D, D]); wvT = inp("wvT", [D, D])
    bqc = inp("bqc", [P, NCH]); bkc = inp("bkc", [P, NCH]); bvr = inp("bvr", [1, D])
    epsc = inp("epsc", [P, 1])
    eyes = inp("eyes", [P, 3 * P]); identb = inp("identb", [P, P], BF16)
    identf = inp("identf", [P, P])
    wg0 = inp("wg0", [I_FFN, D]); wu0 = inp("wu0", [I_FFN, D]); wd0 = inp("wd0", [D, I_FFN])
    wg1 = inp("wg1", [I_FFN, D]); wu1 = inp("wu1", [I_FFN, D]); wd1 = inp("wd1", [D, I_FFN])
    ln_b = {n: inp(n, [P, D]) for n in ("ln0w", "ln0b", "ln1w", "ln1b", "ln2w", "ln2b",
                                        "tmkb", "tmvb", "tmrb", "cmkb", "cmrb")}
    wkT1 = inp("wkT1", [D, D]); wvT1 = inp("wvT1", [D, D]); wrT1 = inp("wrT1", [D, D])
    woT1 = inp("woT1", [D, D]); wfkT = inp("wfkT", [D, H_FFN])
    wfvT = inp("wfvT", [H_FFN, D]); wfrT = inp("wfrT", [D, D])
    tdc = inp("tdc", [P, NCH]); tfc = inp("tfc", [P, NCH])

    xn_tok = scr("xn_tok", [T, D])
    q_fm = scr("q_fm", [NCH * P, T], BF16)
    k_fm = scr("k_fm", [NCH * P, T], BF16)
    v_tok = scr("v_tok", [T, D], BF16)
    xn_fm_s = scr("xn_fm_s", [NCH * P, T], BF16)
    qlm = scr("qlm", [NCH * P, BC * LM], BF16)
    klm = scr("klm", [NCH * P, BC * LM], BF16)
    k2 = scr("k2", [NMAT * P, LM], BF16)
    pcmax_t = nc.dram_tensor("pcmax", [1, 4], F32, kind="Internal")
    pcm_all_t = nc.dram_tensor("pcm_all", [1, 4], F32, kind="Internal")
    z_s = scr("z_s", [NMAT * P, LM], BF16)
    zt_s = scr("zt_s", [NMAT * P, LM], BF16)
    attn_tok = scr("attn_tok", [T, D])
    wgq_fm = scr("wgq_fm", [D, I_FFN], BF16)
    wuq_fm = scr("wuq_fm", [D, I_FFN], BF16)
    wdqT = scr("wdqT", [I_FFN, D], BF16)
    hq_sc = scr("hq_sc", [I_FFN, T], BF16)
    ffn0_tok = scr("ffn0_tok", [T, D])
    ffn1_tok = scr("ffn1_tok", [T, D])
    x1_tok = scr("x1_tok", [T, D])
    xn1_tok = scr("xn1_tok", [T, D])
    a_sc = scr("a_sc", [T, D])
    x2_sc = scr("x2_sc", [T, D])
    xk_sc = scr("xk_sc", [D, T], BF16)
    xv_sc = scr("xv_sc", [D, T], BF16)
    xr_sc = scr("xr_sc", [D, T], BF16)
    xr2_sc = scr("xr2_sc", [D, T], BF16)
    sa_sc = scr("sa_sc", [D, T], BF16)
    kfq_sc = scr("kfq_sc", [H_FFN, T], BF16)
    attn1_tok = scr("attn1_tok", [T, D])
    out_o = nc.dram_tensor("out", [T, D], F32, kind="ExternalOutput").ap()

    es = contextlib.ExitStack()

    ab = emit_AB(nc, es, {
        "x": x_in, "wqT": wqT, "wkT": wkT, "wvT": wvT, "bqc": bqc, "bkc": bkc,
        "bvr": bvr, "epsc": epsc, "xn_tok": xn_tok, "q_fm": q_fm, "k_fm": k_fm,
        "v_tok": v_tok, "xn_fm": xn_fm_s, "qlm": qlm, "klm": klm, "k2": k2,
        "pcmax": pcmax_t.ap()})

    sp = SP(nc)

    cc_sem = es.enter_context(nc.semaphore("s_cc"))

    def cc_fn(e, sq, inc):
        e.collective_compute(
            "AllReduce", ALU.max,
            replica_groups=[list(range(NCORES))],
            ins=[pcmax_t.ap().opt()],
            outs=[pcm_all_t.ap().opt()]).then_inc(cc_sem)
        e.wait_ge(cc_sem, 1)
        e.sem_inc(sq, inc)
    sp.add_raw("gpsimd", cc_fn, 1)

    emit_WQfull(nc, sp, "0", wg0, wu0, wd0, wgq_fm, wuq_fm, wdqT)
    emit_C(nc, sp, k2, pcm_all_t.ap(), eyes, identb, z_s, zt_s)
    emit_D(nc, sp, q_fm, k_fm, v_tok, qlm, klm, zt_s, identf, attn_tok)
    emit_FFN(nc, sp, "0", xn_tok, wgq_fm, wuq_fm, wdqT, ffn0_tok, hq_sc)
    emit_F(nc, sp, x_in, attn_tok, ffn0_tok, epsc, x1_tok, xn1_tok)
    emit_WQfull(nc, sp, "1", wg1, wu1, wd1, wgq_fm, wuq_fm, wdqT)
    emit_FFN(nc, sp, "1", xn1_tok, wgq_fm, wuq_fm, wdqT, ffn1_tok, hq_sc)
    emit_G(nc, sp, xn1_tok, ln_b, wkT1, wvT1, wrT1, woT1, wfkT, wfvT, wfrT,
           tdc, tfc, epsc, attn1_tok,
           a_sc, x2_sc, xk_sc, xv_sc, xr_sc, xr2_sc, sa_sc, kfq_sc)
    emit_I(nc, sp, x1_tok, attn1_tok, ffn1_tok, out_o)

    sp.emit(es, prologue=ab)
    es.close()
    return nc



_CACHED = {}


def _get(name, builder):
    if name not in _CACHED:
        _CACHED[name] = builder()
    return _CACHED[name]


def kernel(**inputs):
    ins = {k: np.asarray(v) for k, v in inputs.items()}
    C = np.ascontiguousarray
    cores = list(range(NCORES))

    nc = _get("MEGA", build_MEGA)

    epsc = np.full((P, 1), 1e-5, np.float32)
    eyes = np.concatenate([np.eye(P) * 7, np.eye(P) * 15, np.eye(P) * 13], 1).astype(np.float32)
    identb = np.eye(P).astype(BF)
    identf = np.eye(P, dtype=np.float32)
    bt = lambda v: C(np.broadcast_to(np.asarray(v, np.float32), (P, D)))
    f32 = lambda k: ins[k].astype(np.float32)
    common = {
        "wqT": C(f32("L0_attn_Wq").T),
        "wkT": C(f32("L0_attn_Wk").T),
        "wvT": C(f32("L0_attn_Wv").T),
        "bqc": C(f32("L0_attn_bq").reshape(NCH, P).T),
        "bkc": C(f32("L0_attn_bk").reshape(NCH, P).T),
        "bvr": C(f32("L0_attn_bv").reshape(1, D)),
        "epsc": epsc, "eyes": eyes, "identb": identb, "identf": identf,
        "wg0": C(f32("L0_ffn_w_gate")), "wu0": C(f32("L0_ffn_w_up")),
        "wd0": C(f32("L0_ffn_w_down")),
        "wg1": C(f32("L1_ffn_w_gate")), "wu1": C(f32("L1_ffn_w_up")),
        "wd1": C(f32("L1_ffn_w_down")),
        "ln0w": bt(ins["L1_attn_ln0_w"]), "ln0b": bt(ins["L1_attn_ln0_b"]),
        "ln1w": bt(ins["L1_attn_ln1_w"]), "ln1b": bt(ins["L1_attn_ln1_b"]),
        "ln2w": bt(ins["L1_attn_ln2_w"]), "ln2b": bt(ins["L1_attn_ln2_b"]),
        "tmkb": bt(ins["L1_attn_tmk"]), "tmvb": bt(ins["L1_attn_tmv"]),
        "tmrb": bt(ins["L1_attn_tmr"]), "cmkb": bt(ins["L1_attn_cmk"]),
        "cmrb": bt(ins["L1_attn_cmr"]),
        "wkT1": C(f32("L1_attn_Wk").T),
        "wvT1": C(f32("L1_attn_Wv").T),
        "wrT1": C(f32("L1_attn_Wr").T),
        "woT1": C(f32("L1_attn_Wo").T),
        "wfkT": C(f32("L1_attn_Wfk").T),
        "wfvT": C(f32("L1_attn_Wfv").T),
        "wfrT": C(f32("L1_attn_Wfr").T),
        "tdc": C(f32("L1_attn_time_decay").reshape(NCH, P).T),
        "tfc": C(f32("L1_attn_time_first").reshape(NCH, P).T),
    }
    x = ins["L0_attn_x"].astype(np.float32)
    maps = [dict(common, x=C(x[c * BC:(c + 1) * BC].reshape(T, D))) for c in cores]
    res = run_bass_kernel_spmd(nc, maps, core_ids=cores)
    out = np.stack([res.results[c]["out"] for c in cores]).reshape(16, S, D).astype(np.float32)
    return out


if __name__ == "__main__":
    print("library module")


# version 04
# speedup vs baseline: 1.0638x; 1.0638x over previous
import contextlib
import numpy as np
import ml_dtypes
import concourse.bass as bass
import concourse.mybir as mybir
from concourse.bass_utils import run_bass_kernel_spmd

F32 = mybir.dt.float32
BF16 = mybir.dt.bfloat16
I16 = mybir.dt.int16
AF = mybir.ActivationFunctionType
ALU = mybir.AluOpType
AX = mybir.AxisListType

NCORES = 8
P = 128
S = 2048
BC = 2
T = S * BC
D = 1024
NCH = D // P
HEADS = 16
HD = 64
LM = 128
SEG = 16
NMAT = BC * HEADS
S4 = 1.0 / float(np.sqrt(np.sqrt(HD)))
NT = T // P
I_FFN = 2816
H_FFN = 4096
NTB = T // 512
BF = ml_dtypes.bfloat16


class Chain:

    def __init__(self, eng, sem):
        self.eng, self.sem, self.n = eng, sem, 0

    def bar(self, inst):
        self.n += 1
        inst.then_inc(self.sem, 1)
        self.eng.wait_ge(self.sem, self.n)
        return inst



def emit_AB(nc, es_outer, d, stages=3, cuts=(), nt=NT):
    x_in = d["x"]
    wqT_in = d["wqT"]; wkT_in = d["wkT"]; wvT_in = d["wvT"]
    bqc_in = d["bqc"]; bkc_in = d["bkc"]; bvr_in = d["bvr"]; eps_in = d["epsc"]
    xn_tok_o = d["xn_tok"]; q_fm_o = d["q_fm"]; k_fm_o = d["k_fm"]
    v_tok_o = d["v_tok"]; xn_fm_o = d["xn_fm"]; qlm_o = d["qlm"]
    klm_o = d["klm"]; k2_o = d["k2"]; pcmax_o = d["pcmax"]

    es = contextlib.ExitStack()
    sbt = lambda name, shape, dt=F32: es.enter_context(nc.sbuf_tensor(name, shape, dt))
    pst = lambda name, shape, dt=F32: es.enter_context(nc.psum_tensor(name, shape, dt))
    sem = lambda name: es_outer.enter_context(nc.semaphore(name))

    NXB = 2
    x_t = [sbt(f"x_t{i}", [P, D]) for i in range(NXB)]
    xn_t = [sbt(f"xn_t{i}", [P, D]) for i in range(NXB)]
    xnb_t = [sbt(f"xnb_t{i}", [P, D], BF16) for i in range(2)]
    stat = sbt("stat", [P, 8])
    epsc = sbt("epsc_s", [P, 1])
    xn_fm = [es.enter_context(nc.sbuf_tensor(f"xn_fm{c}", [P, T], BF16, is_transpose_dest=True)) for c in range(NCH)]
    wq = [sbt(f"wqc{c}", [P, D], BF16) for c in range(NCH)]
    wk = [sbt(f"wkc{c}", [P, D], BF16) for c in range(NCH)]
    wv = [sbt(f"wvc{c}", [P, D], BF16) for c in range(NCH)]
    bqs = sbt("bqs", [P, NCH])
    bks = sbt("bks", [P, NCH])
    bvf = sbt("bvf", [1, D])
    bvb = sbt("bvb", [1, D], BF16)
    ones_row = sbt("ones_row", [1, P], BF16)
    ones_col = sbt("ones_col", [P, 1])
    if stages >= 2:
        pbf_s = sbt("pbf_s", [P, T], BF16)
        pbf = [pbf_s, pbf_s]
        lmraw = [sbt(f"lmraw{i}", [P, BC * LM]) for i in range(2)]
        b16q = sbt("b16q", [P, NCH])
        b16k = sbt("b16k", [P, NCH])
        _wldv = pbf_s[:].bitcast(F32)
        wld = [_wldv[:, 0:D], _wldv[:, D:2 * D]]
    else:
        wld_t = [sbt(f"wld{i}", [P, D]) for i in range(2)]
        wld = [wld_t[0][:], wld_t[1][:]]
        xsq_t = sbt("xsq1", [P, D])
        xsq = xsq_t[:]
    if stages >= 2:
        vf32_s = sbt("vf32_s", [P, D])
        vbf_s = sbt("vbf_s", [P, D], BF16)
        vf32 = [vf32_s, vf32_s]
        vbf = [vbf_s, vbf_s]
        qlm_b = sbt("qlm_b", [P, NCH * BC * LM], BF16)
        klm_b = sbt("klm_b", [P, NCH * BC * LM], BF16)
        xsq = qlm_b[:].bitcast(F32)[:, 0:D]
        e2 = sbt("e2", [P, LM])
        k2f = [sbt(f"k2f{i}", [P, LM]) for i in range(2)]
        k2bs = [sbt(f"k2bs{i}", [P, LM], BF16) for i in range(2)]
        s2red = sbt("s2red", [P, 4])
    runmax = sbt("runmax", [1, LM])
    pcm = sbt("pcm", [1, 4])

    ps_q = [pst(f"ps_q{i}", [P, 512]) for i in range(4)]
    ps_s2 = [pst(f"ps_s2{i}", [P, LM]) for i in range(2)]
    ps_cs = [pst(f"ps_cs{i}", [1, LM]) for i in range(2)]

    s_ldw = sem("s_ldw")
    s_wbf = sem("s_wbf")
    s_x = sem("s_x")
    s_v = sem("s_v")
    s_a = sem("s_a")
    s_tr = sem("s_tr")
    s_st = sem("s_st")
    s_pe = sem("s_pe")
    s_ev = sem("s_ev")
    s_lm = sem("s_lm")
    s_ps = sem("s_ps")
    s_v2 = sem("s_v2")
    s_s2a = sem("s_s2a")
    s_fin = sem("s_fin")
    s_vch = sem("s_vch")
    s_c = sem("s_c")
    s_lr = sem("s_lr")
    s_lw0 = sem("s_lw0")
    s_lw1 = sem("s_lw1")
    s_ach = sem("s_ach")
    s_ks = sem("s_ks")

    NQK = 2 * NCH * NTB
    NVG = NT * 2
    NG = NQK + NVG

    def sync_body(sync):
        for wi in range(0 if "noW" in cuts else 3):
            w_in = (wqT_in, wkT_in, wvT_in)[wi]
            for c in range(NCH):
                k = wi * NCH + c
                if k >= 2:
                    sync.wait_ge(s_wbf, k - 1)
                sync.dma_start(out=wld[k % 2], in_=w_in[c * P:(c + 1) * P, :]).then_inc(
                    (s_lw0, s_lw1)[k % 2], 16)
        nw = 0 if "noW" in cuts else 24
        if "noW" not in cuts:
            sync.dma_start(out=bqs[:], in_=bqc_in).then_inc(s_ldw, 16)
            sync.dma_start(out=bks[:], in_=bkc_in).then_inc(s_ldw, 16)
            sync.dma_start(out=bvf[:], in_=bvr_in).then_inc(s_ldw, 16)
        sync.dma_start(out=epsc[:], in_=eps_in).then_inc(s_ldw, 16)
        for i in range(nt):
            if i >= NXB:
                sync.wait_ge(s_a, 2 * (i - NXB) + 2)
                sync.dma_start(out=xn_tok_o[(i - NXB) * P:(i - NXB + 1) * P, :],
                               in_=xn_t[(i - NXB) % NXB][:]).then_inc(s_st, 16)
            sync.dma_start(out=x_t[i % NXB][:], in_=x_in[i * P:(i + 1) * P, :]).then_inc(s_x, 16)
        for i in range(nt - NXB, nt):
            sync.wait_ge(s_a, 2 * i + 2)
            sync.dma_start(out=xn_tok_o[i * P:(i + 1) * P, :],
                           in_=xn_t[i % NXB][:]).then_inc(s_st, 16)
        for g in range(16 if stages >= 2 else 0):
            proj, oc = divmod(g, NCH)
            sync.wait_ge(s_ev, 8 * (g + 1))
            dst = (q_fm_o, k_fm_o)[proj]
            sync.dma_start(out=dst[oc * P:(oc + 1) * P, :], in_=pbf[g % 2][:]).then_inc(s_ps, 16)
        for tt in range(0 if ("noV" in cuts or stages < 2) else NT):
            sync.wait_ge(s_ev, NQK + 2 * tt + 2)
            sync.dma_start(out=v_tok_o[tt * P:(tt + 1) * P, :], in_=vbf[tt % 2][:]).then_inc(s_ps, 16)
        sync.wait_ge(s_tr, nt * NCH * 16)
        for c in range(NCH if "noFM" not in cuts else 0):
            sync.dma_start(out=xn_fm_o[c * P:(c + 1) * P, :], in_=xn_fm[c][:]).then_inc(s_fin, 16)
        if stages >= 2:
            sync.wait_ge(s_lm, 16)
        if stages >= 2 and "noLM" not in cuts:
            sync.dma_start(out=qlm_o.rearrange("(c p) l -> p c l", p=P), in_=qlm_b[:].rearrange("p (c l) -> p c l", c=NCH)).then_inc(s_fin, 16)
            sync.dma_start(out=klm_o.rearrange("(c p) l -> p c l", p=P), in_=klm_b[:].rearrange("p (c l) -> p c l", c=NCH)).then_inc(s_fin, 16)
        for m in range(NMAT if stages >= 3 else 0):
            sync.wait_ge(s_v2, 3 * m + 2)
            sync.dma_start(out=k2_o[m * P:(m + 1) * P, :], in_=k2bs[m % 2][:]).then_inc(s_ks, 16)
        if stages >= 3:
            sync.wait_ge(s_v2, 3 * NMAT + 1)
            sync.wait_ge(s_ks, NMAT * 16)
        sync.wait_ge(s_c, 1)
        sync.dma_start(out=pcmax_o, in_=pcm[:]).then_inc(s_fin, 16)
        nfin = ((0 if "noFM" in cuts else NCH) + 1 + (2 if (stages >= 2 and "noLM" not in cuts) else 0)) * 16
        sync.wait_ge(s_fin, nfin)
        if stages >= 2:
            sync.wait_ge(s_ps, (16 if "noV" in cuts else 48) * 16)
        sync.wait_ge(s_st, nt * 16)

    def gpsimd_body(gpsimd):
        gpsimd.memset(ones_row[:], 1.0)
        gpsimd.memset(ones_col[:], 1.0)
        gpsimd.memset(runmax[:], -3.0e38)
        gpsimd.memset(pcm[:], 0.0).then_inc(s_c, 1)

    def vector_body(vector):
        vch = Chain(vector, s_vch)
        for wi in range(0 if "noW" in cuts else 3):
            wtiles = (wq, wk, wv)[wi]
            for c in range(NCH):
                k = wi * NCH + c
                vector.wait_ge((s_lw0, s_lw1)[k % 2], (k // 2 + 1) * 16)
                vector.tensor_copy(out=wtiles[c][:], in_=wld[k % 2]).then_inc(s_wbf, 1)
        if "noW" not in cuts:
            vector.wait_ge(s_ldw, 4 * 16)
            vch.bar(vector.tensor_scalar_mul(out=bqs[:], in0=bqs[:], scalar1=S4))
            vch.bar(vector.tensor_scalar_mul(out=bks[:], in0=bks[:], scalar1=S4))
            if stages >= 2:
                vch.bar(vector.tensor_scalar_mul(out=b16q[:], in0=bqs[:], scalar1=16.0))
                vch.bar(vector.tensor_scalar_mul(out=b16k[:], in0=bks[:], scalar1=16.0))
            vector.tensor_copy(out=bvb[:], in_=bvf[:]).then_inc(s_v, 1)
        else:
            vector.wait_ge(s_ldw, 16)
            vector.sem_inc(s_v, 1)
        for i in range(NT):
            b = i % NXB
            vector.wait_ge(s_x, (i + 1) * 16)
            vch.bar(vector.tensor_reduce(out=stat[:, 0:1], in_=x_t[b][:], axis=AX.X, op=ALU.add))
            vch.bar(vector.tensor_tensor(out=xsq, in0=x_t[b][:], in1=x_t[b][:], op=ALU.mult))
            vch.bar(vector.tensor_reduce(out=stat[:, 1:2], in_=xsq, axis=AX.X, op=ALU.add))
            vch.bar(vector.tensor_scalar_mul(out=stat[:, 2:3], in0=stat[:, 0:1], scalar1=1.0 / D))
            vch.bar(vector.tensor_scalar_mul(out=stat[:, 3:4], in0=stat[:, 1:2], scalar1=1.0 / D))
            vch.bar(vector.tensor_tensor(out=stat[:, 4:5], in0=stat[:, 2:3], in1=stat[:, 2:3], op=ALU.mult))
            vector.tensor_tensor(out=stat[:, 4:5], in0=stat[:, 3:4], in1=stat[:, 4:5],
                                 op=ALU.subtract).then_inc(s_v, 1)
            vector.wait_ge(s_a, 2 * i + 1)
            vector.scalar_tensor_tensor(out=stat[:, 6:7], in0=stat[:, 2:3], scalar=-1.0,
                                        in1=stat[:, 5:6], op0=ALU.mult,
                                        op1=ALU.mult).then_inc(s_v, 1)
            vector.wait_ge(s_a, 2 * i + 2)
            if i >= 2:
                vector.wait_ge(s_tr, (i - 1) * NCH * 16)
            vector.tensor_copy(out=xnb_t[i % 2][:], in_=xn_t[b][:]).then_inc(s_v, 1)
        for k in range(NQK if stages >= 2 else 0):
            g, tb = divmod(k, NTB)
            proj, oc = divmod(g, NCH)
            vector.wait_ge(s_pe, k + 1)
            if "noLM" in cuts:
                vector.sem_inc(s_lr, 1)
            else:
                vch.bar(vector.tensor_reduce(out=lmraw[g % 2][:, tb * 32:(tb + 1) * 32],
                                     in_=ps_q[k % 4][:].rearrange("p (n w) -> p n w", w=SEG),
                                     axis=AX.X, op=ALU.add))
                vector.sem_inc(s_lr, 1)
            if tb == NTB - 1:
                if "noLM" in cuts:
                    vector.sem_inc(s_lm, 1)
                else:
                    dstlm = (qlm_b, klm_b)[proj]
                    b16 = (b16q, b16k)[proj]
                    vector.tensor_scalar(out=dstlm[:, oc * BC * LM:(oc + 1) * BC * LM],
                                         in0=lmraw[g % 2][:], scalar1=S4,
                                         scalar2=b16[:, oc:oc + 1], op0=ALU.mult,
                                         op1=ALU.add).then_inc(s_lm, 1)
        vector.wait_ge(s_c, 1)
        for m in range(NMAT if stages >= 3 else 0):
            vector.wait_ge(s_pe, NG + 2 * m + 1)
            vch.bar(vector.tensor_reduce(out=s2red[:, 0:1], in_=ps_s2[m % 2][:], axis=AX.X, op=ALU.max))
            vector.tensor_scalar_mul(out=s2red[:, 1:2], in0=s2red[:, 0:1],
                                     scalar1=-1.0 / 256.0).then_inc(s_v2, 1)
            vector.wait_ge(s_s2a, m + 1)
            if m >= 2:
                vector.wait_ge(s_pe, NG + 2 * (m - 2) + 2)
            vch.bar(vector.tensor_reduce(out=s2red[:, 2:3], in_=e2[:], axis=AX.X, op=ALU.add))
            vch.bar(vector.reciprocal(out=s2red[:, 3:4], in_=s2red[:, 2:3]))
            vch.bar(vector.tensor_scalar_mul(out=k2f[m % 2][:], in0=e2[:], scalar1=s2red[:, 3:4]))
            if m >= 2:
                vector.wait_ge(s_ks, 16 * (m - 1))
            vector.tensor_copy(out=k2bs[m % 2][:],
                               in_=k2f[m % 2][:]).then_inc(s_v2, 1)
            vector.wait_ge(s_pe, NG + 2 * m + 2)
            vector.tensor_tensor(out=runmax[:], in0=runmax[:], in1=ps_cs[m % 2][0:1, :],
                                 op=ALU.max).then_inc(s_v2, 1)
        if stages >= 3:
            vector.wait_ge(s_v2, 3 * NMAT)
            vector.tensor_reduce(out=pcm[0:1, 0:1], in_=runmax[0:1, :], axis=AX.X,
                                 op=ALU.max).then_inc(s_v2, 1)

    def scalar_body(scalar):
        ach = Chain(scalar, s_ach)
        for i in range(nt):
            b = i % NXB
            scalar.wait_ge(s_v, 1 + 3 * i + 1)
            ach.bar(scalar.activation(out=stat[:, 5:6], in_=stat[:, 4:5], func=AF.Ln, bias=epsc[:, 0:1]))
            scalar.activation(out=stat[:, 5:6], in_=stat[:, 5:6], func=AF.Exp,
                              scale=-0.5).then_inc(s_a, 1)
            scalar.wait_ge(s_v, 1 + 3 * i + 2)
            if i >= NXB:
                scalar.wait_ge(s_st, (i - NXB + 1) * 16)
            scalar.activation(out=xn_t[b][:], in_=x_t[b][:], func=AF.Identity,
                              scale=stat[:, 5:6], bias=stat[:, 6:7]).then_inc(s_a, 1)
            scalar.wait_ge(s_v, 1 + 3 * i + 3)
            for c in range(NCH):
                if "noTR" in cuts:
                    scalar.sem_inc(s_tr, 16)
                else:
                    scalar.dma_start(out=xn_fm[c][:, i * P:(i + 1) * P],
                                     in_=xnb_t[i % 2][:, c * P:(c + 1) * P],
                                     transpose=True).then_inc(s_tr, 16)
        for k in range(NQK if stages >= 2 else 0):
            g, tb = divmod(k, NTB)
            proj, oc = divmod(g, NCH)
            scalar.wait_ge(s_lr, k + 1)
            if tb == 0 and g >= 1:
                scalar.wait_ge(s_ps, 16 * g)
            bias = (bqs, bks)[proj]
            scalar.activation(out=pbf[g % 2][:, tb * 512:(tb + 1) * 512],
                              in_=ps_q[k % 4][:], func=AF.Identity, scale=S4,
                              bias=bias[:, oc:oc + 1]).then_inc(s_ev, 1)
        for kv in range(0 if ("noV" in cuts or stages < 2) else NVG):
            tt, oh = divmod(kv, 2)
            k = NQK + kv
            scalar.wait_ge(s_pe, k + 1)
            if oh == 0 and tt >= 1:
                scalar.wait_ge(s_ps, 16 * (16 + tt))
            scalar.activation(out=vbf[tt % 2][:, oh * 512:(oh + 1) * 512],
                              in_=ps_q[k % 4][:], func=AF.Copy).then_inc(s_ev, 1)
        for m in range(NMAT if stages >= 3 else 0):
            scalar.wait_ge(s_v2, 3 * m + 1)
            scalar.activation(out=e2[:], in_=ps_s2[m % 2][:], func=AF.Exp,
                              scale=1.0 / 256.0, bias=s2red[:, 1:2]).then_inc(s_s2a, 1)

    def tensor_body(tensor):
        if stages < 2:
            return
        tensor.wait_ge(s_tr, NT * NCH * 16)
        tensor.wait_ge(s_wbf, 24)
        tensor.wait_ge(s_v, 1)
        tensor.wait_ge(s_c, 1)
        for k in range(NQK):
            g, tb = divmod(k, NTB)
            proj, oc = divmod(g, NCH)
            w = (wq, wk)[proj]
            if k >= 4:
                tensor.wait_ge(s_ev, k - 3)
            mi = None
            for cc in range(NCH):
                mi = tensor.matmul(ps_q[k % 4][:], w[cc][:, oc * P:(oc + 1) * P],
                                   xn_fm[cc][:, tb * 512:(tb + 1) * 512],
                                   start=(cc == 0), stop=(cc == NCH - 1))
            mi.then_inc(s_pe, 1)
        for kv in range(0 if "noV" in cuts else NVG):
            tt, oh = divmod(kv, 2)
            k = NQK + kv
            tensor.wait_ge(s_ev, k - 3)
            for cc in range(NCH):
                tensor.matmul(ps_q[k % 4][:], xn_fm[cc][:, tt * P:(tt + 1) * P],
                              wv[cc][:, oh * 512:(oh + 1) * 512],
                              start=(cc == 0), stop=False)
            tensor.matmul(ps_q[k % 4][:], ones_row[0:1, :], bvb[0:1, oh * 512:(oh + 1) * 512],
                          start=False, stop=True).then_inc(s_pe, 1)
        if stages < 3:
            return
        tensor.wait_ge(s_lm, 16)
        for m in range(NMAT):
            sq_, h = divmod(m, HEADS)
            cc, half = divmod(h, 2)
            prow = 64 * half
            if m >= 2:
                tensor.wait_ge(s_s2a, m - 1)
            qs = qlm_b[prow:prow + 64, cc * BC * LM + sq_ * LM: cc * BC * LM + (sq_ + 1) * LM]
            ks = klm_b[prow:prow + 64, cc * BC * LM + sq_ * LM: cc * BC * LM + (sq_ + 1) * LM]
            tensor.matmul(ps_s2[m % 2][:], qs, ks, start=True, stop=True).then_inc(s_pe, 1)
            tensor.wait_ge(s_v2, 3 * m + 2)
            tensor.matmul(ps_cs[m % 2][:], ones_col[:], k2f[m % 2][:],
                          start=True, stop=True).then_inc(s_pe, 1)

    es.close()
    return {"sync": sync_body, "gpsimd": gpsimd_body, "vector": vector_body,
            "scalar": scalar_body, "tensor": tensor_body}



ENGINES = ["sync", "vector", "scalar", "tensor", "gpsimd"]


class Fence:
    def __init__(self, counts):
        self.counts = counts


class SP:
    def __init__(self, nc):
        self.nc = nc
        self.ops = []
        self._run = {en: 0 for en in ENGINES}
        self._cum = []
        self._eng = []

    def add(self, eng, fn, dma=False, after=None):
        i = len(self.ops)
        deps = None if after is None else (list(after) if isinstance(after, (list, tuple)) else [after])
        inc = 16 if dma else 1
        self.ops.append((eng, fn, inc, False, deps))
        self._run[eng] += inc
        self._cum.append(self._run[eng])
        self._eng.append(eng)
        return i

    def add_raw(self, eng, fn, inc=1, after=None):
        i = len(self.ops)
        deps = None if after is None else (list(after) if isinstance(after, (list, tuple)) else [after])
        self.ops.append((eng, fn, inc, True, deps))
        self._run[eng] += inc
        self._cum.append(self._run[eng])
        self._eng.append(eng)
        return i

    def fence(self):
        return Fence(dict(self._run))

    def last(self):
        return len(self.ops) - 1

    def emit(self, es, prologue=None):
        nc = self.nc
        sems = {en: es.enter_context(nc.semaphore("sp_" + en)) for en in ENGINES}
        s_bar = es.enter_context(nc.semaphore("s_bar"))
        totals = dict(self._run)
        blk = es.enter_context(nc.Block())
        mine = {}
        for i, (e, _, _, _, _) in enumerate(self.ops):
            mine.setdefault(e, []).append(i)

        def mk(ename):
            idxs = mine.get(ename, [])

            def body(eng):
                if prologue is not None:
                    if ename in prologue:
                        prologue[ename](eng)
                    eng.sem_inc(s_bar, 1)
                    eng.wait_ge(s_bar, len(ENGINES))
                for i in idxs:
                    _, fn, inc, raw, deps = self.ops[i]
                    if deps is None:
                        deps = [i - 1] if i > 0 else []
                    need = {}
                    for dp in deps:
                        if isinstance(dp, Fence):
                            for en2, v in dp.counts.items():
                                if v > 0:
                                    need[en2] = max(need.get(en2, 0), v)
                        else:
                            if dp < 0:
                                continue
                            je = self._eng[dp]
                            jinc = self.ops[dp][2]
                            if je == ename and jinc == 1 and je == "tensor":
                                continue
                            need[je] = max(need.get(je, 0), self._cum[dp])
                    for je in ENGINES:
                        if je in need:
                            eng.wait_ge(sems[je], need[je])
                    if raw:
                        fn(eng, sems[ename], inc)
                    else:
                        fn(eng).then_inc(sems[ename], inc)
                if ename == "sync":
                    for en2 in ENGINES:
                        if totals[en2] > 0:
                            eng.wait_ge(sems[en2], totals[en2])
            return body

        for en in ENGINES:
            getattr(blk, en)(mk(en))
        return totals



def emit_C(nc, sp, k2_in, amax_in, eyes_in, identb_in, z_o, zt_o):
    es = contextlib.ExitStack()
    sbt = lambda n, s, dt=F32: es.enter_context(nc.sbuf_tensor(n, s, dt))
    pst = lambda n, s, dt=F32: es.enter_context(nc.psum_tensor(n, s, dt))
    k2m = sbt("k2m", [P, LM], BF16)
    amax = sbt("amax_s", [1, 4])
    gm = sbt("gm", [1, 1])
    rc = sbt("rc", [1, 1])
    rcb = sbt("rcb", [P, 1])
    onesr = sbt("onesr", [1, P])
    eyes = sbt("eyes_s", [P, 3 * P])
    eyesb = sbt("eyesb", [P, 3 * P], BF16)
    identb = sbt("identb_s", [P, P], BF16)
    K2T = sbt("K2T", [P, LM], BF16)
    Vt = sbt("Vt", [P, LM], BF16)
    VTt = sbt("VTt", [P, LM], BF16)
    ATs = sbt("ATs", [P, LM], BF16)
    N2 = sbt("N2", [P, LM], BF16)
    N1 = sbt("N1", [P, LM], BF16)
    Mt = sbt("Mt", [P, LM], BF16)
    ps_a = pst("ps_a", [P, LM])
    ps_b = pst("ps_b", [P, LM])
    ps_t = pst("ps_t", [P, LM], BF16)
    ps_r = pst("ps_r", [P, 1])

    A = sp.add
    A("sync", lambda e: e.dma_start(out=amax[:], in_=amax_in), dma=True)
    A("sync", lambda e: e.dma_start(out=eyes[:], in_=eyes_in), dma=True)
    A("sync", lambda e: e.dma_start(out=identb[:], in_=identb_in), dma=True)
    A("gpsimd", lambda e: e.memset(onesr[:], 1.0))
    A("vector", lambda e: e.tensor_reduce(out=gm[:], in_=amax[:], axis=AX.X, op=ALU.max))
    A("vector", lambda e: e.reciprocal(out=rc[:], in_=gm[:]))
    A("vector", lambda e: e.tensor_copy(out=eyesb[:], in_=eyes[:]))
    A("tensor", lambda e: e.matmul(ps_r[:], onesr[0:1, :], rc[0:1, :], start=True, stop=True))
    A("scalar", lambda e: e.activation(out=rcb[:], in_=ps_r[:], func=AF.Copy))
    for m in range(NMAT):
        mm = m
        A("sync", lambda e, m=mm: e.dma_start(out=k2m[:], in_=k2_in[m * P:(m + 1) * P, :]), dma=True)
        A("tensor", lambda e: e.transpose(ps_t[:], k2m[:], identb[:]))
        A("scalar", lambda e: e.activation(out=K2T[:], in_=ps_t[:], func=AF.Copy))
        A("scalar", lambda e: e.activation(out=Vt[:], in_=ps_t[:], func=AF.Identity, scale=rcb[:, 0:1]))
        A("vector", lambda e: e.tensor_scalar(out=VTt[:], in0=k2m[:], scalar1=rcb[:, 0:1],
                                              scalar2=None, op0=ALU.mult))
        for it in range(6):
            A("tensor", lambda e: e.matmul(ps_a[:], K2T[:], Vt[:], start=True, stop=True))
            A("tensor", lambda e: e.matmul(ps_b[:], Vt[:], K2T[:], start=True, stop=True))
            A("vector", lambda e: e.scalar_tensor_tensor(out=N2[:], in0=ps_a[:], scalar=-1.0,
                                                         in1=eyesb[:, 0:P], op0=ALU.mult, op1=ALU.add))
            A("scalar", lambda e: e.activation(out=ATs[:], in_=ps_b[:], func=AF.Copy))
            A("tensor", lambda e: e.matmul(ps_a[:], ATs[:], N2[:], start=True, stop=True))
            A("vector", lambda e: e.scalar_tensor_tensor(out=N1[:], in0=ps_a[:], scalar=-1.0,
                                                         in1=eyesb[:, P:2 * P], op0=ALU.mult, op1=ALU.add))
            A("tensor", lambda e: e.matmul(ps_a[:], ATs[:], N1[:], start=True, stop=True))
            A("vector", lambda e: e.scalar_tensor_tensor(out=Mt[:], in0=ps_a[:], scalar=-1.0,
                                                         in1=eyesb[:, 2 * P:3 * P], op0=ALU.mult, op1=ALU.add))
            A("tensor", lambda e: e.matmul(ps_a[:], VTt[:], Mt[:], start=True, stop=True))
            A("tensor", lambda e: e.matmul(ps_b[:], Mt[:], VTt[:], start=True, stop=True))
            A("scalar", lambda e: e.activation(out=Vt[:], in_=ps_a[:], func=AF.Identity, scale=0.25))
            A("scalar", lambda e: e.activation(out=VTt[:], in_=ps_b[:], func=AF.Identity, scale=0.25))
        A("sync", lambda e, m=mm: e.dma_start(out=z_o[m * P:(m + 1) * P, :], in_=Vt[:]), dma=True)
        A("sync", lambda e, m=mm: e.dma_start(out=zt_o[m * P:(m + 1) * P, :], in_=VTt[:]), dma=True)
    es.close()



def emit_D(nc, sp, q_in, k_in, v_in, qlm_in, klm_in, zt_in, identf_in, attn_o):
    es = contextlib.ExitStack()
    sbt = lambda n, s, dt=F32: es.enter_context(nc.sbuf_tensor(n, s, dt))
    pst = lambda n, s, dt=F32: es.enter_context(nc.psum_tensor(n, s, dt))
    qlmt = [sbt(f"qlmt{c}", [P, BC * LM], BF16) for c in range(NCH)]
    klmt = [sbt(f"klmt{c}", [P, BC * LM], BF16) for c in range(NCH)]
    ksl = sbt("ksl", [P, S], BF16)
    qsl = sbt("qsl", [P, S], BF16)
    vsl = sbt("vsl", [P, 16 * HD], BF16)
    ztm = sbt("ztm", [P, LM], BF16)
    s3f = sbt("s3f", [P, S])
    e3 = sbt("e3", [P, S], BF16)
    e3t = es.enter_context(nc.sbuf_tensor("e3t", [P, S], BF16, is_transpose_dest=True))
    e1 = sbt("e1", [P, S], BF16)
    red = sbt("red", [P, 4])
    Us = sbt("Us", [P, HD], BF16)
    Ps = sbt("Ps", [P, HD], BF16)
    r1s = sbt("r1s", [1, S])
    rr1 = sbt("rr1", [1, S])
    bcs = sbt("bcs", [64, 512])
    onescb = sbt("onescb", [P, 1], BF16)
    onesr1 = sbt("onesr1", [1, 64])
    identf = sbt("identf_s", [P, P])
    afm = sbt("afm", [P, T])
    stag = sbt("stag", [P, P])
    ps_m = pst("ps_m", [P, 512])
    ps_u = pst("ps_u", [P, HD])
    ps_x = pst("ps_x", [64, 512])
    ps_bc = pst("ps_bc", [64, 512])
    ps_t = pst("ps_tD", [P, P])
    ps_r1 = pst("ps_r1", [1, 512])

    A = sp.add
    for c in range(NCH):
        A("sync", lambda e, c=c: e.dma_start(out=qlmt[c][:], in_=qlm_in[c * P:(c + 1) * P, :]), dma=True)
        A("sync", lambda e, c=c: e.dma_start(out=klmt[c][:], in_=klm_in[c * P:(c + 1) * P, :]), dma=True)
    A("sync", lambda e: e.dma_start(out=identf[:], in_=identf_in), dma=True)
    A("gpsimd", lambda e: e.memset(onescb[:], 1.0))
    A("gpsimd", lambda e: e.memset(onesr1[:], 1.0))

    for cc in range(NCH):
        for s in range(BC):
            for hh in range(2):
                h = 2 * cc + hh
                m = s * HEADS + h
                A("sync", lambda e, cc=cc, hh=hh, s=s: e.dma_start(
                    out=ksl[64 * hh:64 * hh + 64, :],
                    in_=k_in[cc * P + 64 * hh: cc * P + 64 * hh + 64,
                             s * S:(s + 1) * S]), dma=True)
                A("sync", lambda e, cc=cc, hh=hh, s=s: e.dma_start(
                    out=qsl[64 * hh:64 * hh + 64, :],
                    in_=q_in[cc * P + 64 * hh: cc * P + 64 * hh + 64,
                             s * S:(s + 1) * S]), dma=True)
                for j in range(16):
                    A("sync", lambda e, s=s, j=j, h=h: e.dma_start(
                        out=vsl[:, j * HD:(j + 1) * HD],
                        in_=v_in[s * S + j * P: s * S + (j + 1) * P,
                                 h * HD:(h + 1) * HD]), dma=True)
                A("sync", lambda e, m=m: e.dma_start(out=ztm[:], in_=zt_in[m * P:(m + 1) * P, :]), dma=True)
                for b in range(4):
                    A("tensor", lambda e, cc=cc, hh=hh, s=s, b=b: e.matmul(
                        ps_m[:], qlmt[cc][64 * hh:64 * hh + 64, s * LM:(s + 1) * LM],
                        ksl[64 * hh:64 * hh + 64, b * 512:(b + 1) * 512], start=True, stop=True))
                    A("scalar", lambda e, b=b: e.activation(out=s3f[:, b * 512:(b + 1) * 512],
                                                            in_=ps_m[:], func=AF.Copy))
                A("vector", lambda e: e.tensor_reduce(out=red[:, 0:1], in_=s3f[:], axis=AX.X, op=ALU.max))
                A("vector", lambda e: e.tensor_scalar_mul(out=red[:, 1:2], in0=red[:, 0:1],
                                                          scalar1=-1.0 / SEG))
                A("scalar", lambda e: e.activation(out=e3[:], in_=s3f[:], func=AF.Exp,
                                                   scale=1.0 / SEG, bias=red[:, 1:2]))
                A("vector", lambda e: e.tensor_reduce(out=red[:, 2:3], in_=e3[:], axis=AX.X, op=ALU.add))
                A("vector", lambda e: e.reciprocal(out=red[:, 3:4], in_=red[:, 2:3]))
                for j in range(16):
                    A("scalar", lambda e, j=j: e.dma_start(out=e3t[:, j * P:(j + 1) * P],
                                                           in_=e3[:, j * P:(j + 1) * P],
                                                           transpose=True), dma=True)
                for j in range(16):
                    A("tensor", lambda e, j=j: e.matmul(ps_u[:], e3t[:, j * P:(j + 1) * P],
                                                        vsl[:, j * HD:(j + 1) * HD],
                                                        start=(j == 0), stop=(j == 15)))
                A("vector", lambda e: e.tensor_scalar(out=Us[:], in0=ps_u[:], scalar1=red[:, 3:4],
                                                      scalar2=None, op0=ALU.mult))
                A("tensor", lambda e: e.matmul(ps_u[:], ztm[:], Us[:], start=True, stop=True))
                A("scalar", lambda e: e.activation(out=Ps[:], in_=ps_u[:], func=AF.Copy))
                for b in range(4):
                    A("tensor", lambda e, cc=cc, hh=hh, s=s, b=b: e.matmul(
                        ps_m[:], klmt[cc][64 * hh:64 * hh + 64, s * LM:(s + 1) * LM],
                        qsl[64 * hh:64 * hh + 64, b * 512:(b + 1) * 512], start=True, stop=True))
                    A("scalar", lambda e, b=b: e.activation(out=e1[:, b * 512:(b + 1) * 512],
                                                            in_=ps_m[:], func=AF.Exp, scale=1.0 / SEG))
                for b in range(4):
                    A("tensor", lambda e, b=b: e.matmul(ps_r1[:], onescb[:], e1[:, b * 512:(b + 1) * 512],
                                                        start=True, stop=True))
                    A("scalar", lambda e, b=b: e.activation(out=r1s[0:1, b * 512:(b + 1) * 512],
                                                            in_=ps_r1[:], func=AF.Copy))
                A("vector", lambda e: e.reciprocal(out=rr1[:], in_=r1s[:]))
                for b in range(4):
                    A("tensor", lambda e, b=b: e.matmul(ps_x[:], Ps[:], e1[:, b * 512:(b + 1) * 512],
                                                        start=True, stop=True))
                    A("tensor", lambda e, b=b: e.matmul(ps_bc[:], onesr1[0:1, :],
                                                        rr1[0:1, b * 512:(b + 1) * 512],
                                                        start=True, stop=True))
                    A("scalar", lambda e, b=b: e.activation(out=bcs[:], in_=ps_bc[:], func=AF.Copy))
                    A("vector", lambda e, hh=hh, s=s, b=b: e.tensor_tensor(
                        out=afm[64 * hh:64 * hh + 64, s * S + b * 512: s * S + (b + 1) * 512],
                        in0=ps_x[:], in1=bcs[:], op=ALU.mult))
        for tt in range(NT):
            A("tensor", lambda e, tt=tt: e.transpose(ps_t[:], afm[:, tt * P:(tt + 1) * P], identf[:]))
            A("scalar", lambda e: e.activation(out=stag[:], in_=ps_t[:], func=AF.Copy))
            A("sync", lambda e, tt=tt, cc=cc: e.dma_start(
                out=attn_o[tt * P:(tt + 1) * P, cc * P:(cc + 1) * P], in_=stag[:]), dma=True)
    es.close()



def emit_quant(Aop, x_ap, q_ap, scr, C, pp=P):
    nb = C // 16
    bm, sc, rs, hs = scr["bm"], scr["sc"], scr["rs"], scr["hs"]
    y, a, t1, t2, t3 = scr["y"], scr["a"], scr["t1"], scr["t2"], scr["t3"]
    ri1, ri2 = scr["ri1"], scr["ri2"]
    x3 = x_ap.rearrange("p (n w) -> p n w", w=16)
    Aop("vector", lambda e: e.tensor_reduce(out=bm[:pp, :nb], in_=x3, axis=AX.X,
                                            op=ALU.max, apply_absolute_value=True))
    Aop("vector", lambda e: e.tensor_scalar(out=sc[:pp, :nb], in0=bm[:pp, :nb], scalar1=1.0 / 6.0,
                                            scalar2=1e-12, op0=ALU.mult, op1=ALU.max))
    Aop("vector", lambda e: e.reciprocal(out=rs[:pp, :nb], in_=sc[:pp, :nb]))
    rs3 = rs[:pp, :nb].unsqueeze(2).broadcast_to([pp, nb, 16])
    y3 = y[:pp, :C].rearrange("p (n w) -> p n w", w=16)
    Aop("vector", lambda e: e.tensor_tensor(out=y3, in0=x3, in1=rs3, op=ALU.mult))
    Aop("vector", lambda e: e.tensor_scalar(out=a[:pp, :C].bitcast(mybir.dt.int32),
                                            in0=y[:pp, :C].bitcast(mybir.dt.int32),
                                            scalar1=0x7fffffff, scalar2=None,
                                            op0=ALU.bitwise_and))
    Aop("vector", lambda e: e.tensor_scalar_mul(out=ri1[:pp, :C], in0=a[:pp, :C], scalar1=2.0))
    Aop("vector", lambda e: e.tensor_copy(out=ri2[:pp, :C], in_=a[:pp, :C]))
    Aop("vector", lambda e: e.tensor_scalar(out=t1[:pp, :C], in0=ri1[:pp, :C], scalar1=0.5,
                                            scalar2=2.0, op0=ALU.mult, op1=ALU.min))
    Aop("vector", lambda e: e.tensor_scalar(out=t2[:pp, :C], in0=ri2[:pp, :C], scalar1=2.0,
                                            scalar2=4.0, op0=ALU.max, op1=ALU.min))
    Aop("vector", lambda e: e.tensor_tensor(out=t1[:pp, :C], in0=t1[:pp, :C], in1=t2[:pp, :C],
                                            op=ALU.add))
    Aop("vector", lambda e: e.tensor_scalar(out=t2[:pp, :C], in0=a[:pp, :C], scalar1=5.0,
                                            scalar2=2.0, op0=ALU.is_gt, op1=ALU.mult))
    Aop("vector", lambda e: e.scalar_tensor_tensor(out=t1[:pp, :C], in0=t1[:pp, :C], scalar=-2.0,
                                                   in1=t2[:pp, :C], op0=ALU.add, op1=ALU.add))
    Aop("scalar", lambda e: e.activation(out=t3[:pp, :C], in_=y[:pp, :C], func=AF.Sign))
    Aop("vector", lambda e: e.tensor_tensor(out=t1[:pp, :C], in0=t1[:pp, :C], in1=t3[:pp, :C],
                                            op=ALU.mult))
    Aop("vector", lambda e: e.tensor_scalar_mul(out=hs[:pp, :nb], in0=sc[:pp, :nb], scalar1=1.0))
    hs3 = hs[:pp, :nb].unsqueeze(2).broadcast_to([pp, nb, 16])
    t13 = t1[:pp, :C].rearrange("p (n w) -> p n w", w=16)
    q3 = q_ap.rearrange("p (n w) -> p n w", w=16)
    Aop("vector", lambda e: e.tensor_tensor(out=q3, in0=t13, in1=hs3, op=ALU.mult))


def quant_scratch(sbt, tag, C, dt_mid=F32):
    nb = C // 16
    return {
        "bm": sbt(f"bm{tag}", [P, nb]), "sc": sbt(f"sc{tag}", [P, nb]),
        "rs": sbt(f"rs{tag}", [P, nb]), "hs": sbt(f"hs{tag}", [P, nb]),
        "y": sbt(f"yq{tag}", [P, C]), "a": sbt(f"aq{tag}", [P, C]),
        "t1": sbt(f"t1q{tag}", [P, C], BF16), "t2": sbt(f"t2q{tag}", [P, C], BF16),
        "t3": sbt(f"t3q{tag}", [P, C], BF16),
        "ri1": sbt(f"ri1q{tag}", [P, C], I16), "ri2": sbt(f"ri2q{tag}", [P, C], I16),
    }



def emit_WQfull(nc, sp, tag, wg_in, wu_in, wd_in, wgq_o, wuq_o, wdq_o):
    es = contextlib.ExitStack()
    sbt = lambda n, s, dt=F32: es.enter_context(nc.sbuf_tensor(n + tag, s, dt))
    wt = sbt("wt", [P, I_FFN])
    qt = sbt("qt", [P, I_FFN], BF16)
    qtt = es.enter_context(nc.sbuf_tensor("qtt" + tag, [P, I_FFN], BF16, is_transpose_dest=True))
    scr = quant_scratch(sbt, "w", I_FFN)
    A = sp.add
    jobs = []
    for r in range(I_FFN // P):
        jobs.append((wg_in[r * P:(r + 1) * P, :], wgq_o, r, D))
        jobs.append((wu_in[r * P:(r + 1) * P, :], wuq_o, r, D))
    for r in range(NCH):
        jobs.append((wd_in[r * P:(r + 1) * P, :], wdq_o, r, I_FFN))
    for (src_ap, dst, r, C) in jobs:
        A("sync", lambda e, src_ap=src_ap, C=C: e.dma_start(out=wt[:, 0:C], in_=src_ap), dma=True)
        emit_quant(A, wt[:, 0:C], qt[:, 0:C], scr, C)
        for j in range(C // P):
            A("scalar", lambda e, j=j: e.dma_start(out=qtt[:, j * P:(j + 1) * P],
                                                   in_=qt[:, j * P:(j + 1) * P],
                                                   transpose=True), dma=True)
        for j in range(C // P):
            A("sync", lambda e, dst=dst, r=r, j=j: e.dma_start(
                out=dst[j * P:(j + 1) * P, r * P:(r + 1) * P],
                in_=qtt[:, j * P:(j + 1) * P]), dma=True)
    es.close()



def emit_FFN(nc, sp, tag, xn_in, wg_in, wu_in, wd_in, ffn_o, hq_sc):
    es = contextlib.ExitStack()
    sbt = lambda n, s, dt=F32: es.enter_context(nc.sbuf_tensor(n + tag, s, dt))
    pst = lambda n, s, dt=F32: es.enter_context(nc.psum_tensor(n + tag, s, dt))
    wpool = sbt("wpool", [P, 2 * NCH * I_FFN], BF16)
    wgq = [wpool[:, c * I_FFN:(c + 1) * I_FFN] for c in range(NCH)]
    wuq = [wpool[:, (NCH + c) * I_FFN:(NCH + c + 1) * I_FFN] for c in range(NCH)]
    wdq = [wpool[:, j * D:(j + 1) * D] for j in range(22)]
    xt = sbt("xt", [P, D])
    xq = sbt("xq", [P, D], BF16)
    xqf = es.enter_context(nc.sbuf_tensor("xqf" + tag, [P, D], BF16, is_transpose_dest=True))
    gact = sbt("gact", [P, I_FFN])
    uact = sbt("uact", [P, I_FFN])
    hact = sbt("hact", [P, I_FFN])
    hqt = sbt("hqt", [P, I_FFN], BF16)
    hqtt = es.enter_context(nc.sbuf_tensor("hqtt" + tag, [P, I_FFN], BF16, is_transpose_dest=True))
    scr = quant_scratch(sbt, "f", I_FFN)
    hload = sbt("hload", [P, P], BF16)
    ot = sbt("ot", [P, D])
    ps = [pst(f"psF{i}", [P, 512]) for i in range(2)]

    IB = [(0, 512), (512, 1024), (1024, 1536), (1536, 2048), (2048, 2560), (2560, 2816)]
    A = sp.add
    for c in range(NCH):
        A("sync", lambda e, c=c: e.dma_start(out=wgq[c], in_=wg_in[c * P:(c + 1) * P, :]), dma=True)
        A("sync", lambda e, c=c: e.dma_start(out=wuq[c], in_=wu_in[c * P:(c + 1) * P, :]), dma=True)
    for tt in range(NT):
        A("sync", lambda e, tt=tt: e.dma_start(out=xt[:], in_=xn_in[tt * P:(tt + 1) * P, :]), dma=True)
        emit_quant(A, xt[:], xq[:], scr, D)
        for j in range(NCH):
            A("scalar", lambda e, j=j: e.dma_start(out=xqf[:, j * P:(j + 1) * P],
                                                   in_=xq[:, j * P:(j + 1) * P], transpose=True), dma=True)
        for wsel in range(2):
            wts = (wgq, wuq)[wsel]
            dstact = (gact, uact)[wsel]
            for (lo, hi) in IB:
                n = hi - lo
                for cc in range(NCH):
                    A("tensor", lambda e, cc=cc, lo=lo, hi=hi, wts=wts, n=n: e.matmul(
                        ps[0][:, 0:n], xqf[:, cc * P:(cc + 1) * P], wts[cc][:, lo:hi],
                        start=(cc == 0), stop=(cc == NCH - 1)))
                if wsel == 0:
                    A("scalar", lambda e, lo=lo, hi=hi, n=n, dstact=dstact: e.activation(
                        out=dstact[:, lo:hi], in_=ps[0][:, 0:n], func=AF.Silu))
                else:
                    A("scalar", lambda e, lo=lo, hi=hi, n=n, dstact=dstact: e.activation(
                        out=dstact[:, lo:hi], in_=ps[0][:, 0:n], func=AF.Copy))
        A("vector", lambda e: e.tensor_tensor(out=hact[:], in0=gact[:], in1=uact[:], op=ALU.mult))
        emit_quant(A, hact[:], hqt[:], scr, I_FFN)
        for j in range(22):
            A("scalar", lambda e, j=j: e.dma_start(out=hqtt[:, j * P:(j + 1) * P],
                                                   in_=hqt[:, j * P:(j + 1) * P], transpose=True), dma=True)
        for j in range(22):
            A("sync", lambda e, j=j, tt=tt: e.dma_start(
                out=hq_sc[j * P:(j + 1) * P, tt * P:(tt + 1) * P],
                in_=hqtt[:, j * P:(j + 1) * P]), dma=True)
    for j in range(22):
        A("sync", lambda e, j=j: e.dma_start(out=wdq[j], in_=wd_in[j * P:(j + 1) * P, :]), dma=True)
    for tt in range(NT):
        for half in range(2):
            for j in range(22):
                A("sync", lambda e, j=j, tt=tt: e.dma_start(
                    out=hload[:], in_=hq_sc[j * P:(j + 1) * P, tt * P:(tt + 1) * P]), dma=True)
                A("tensor", lambda e, j=j, half=half: e.matmul(
                    ps[half][:], hload[:], wdq[j][:, half * 512:(half + 1) * 512],
                    start=(j == 0), stop=(j == 21)))
            A("scalar", lambda e, half=half: e.activation(out=ot[:, half * 512:(half + 1) * 512],
                                                          in_=ps[half][:], func=AF.Copy))
        A("sync", lambda e, tt=tt: e.dma_start(out=ffn_o[tt * P:(tt + 1) * P, :], in_=ot[:]), dma=True)
    es.close()



def emit_F(nc, sp, x_in, at_in, ff_in, eps_in, x1_o, xn1_o):
    es = contextlib.ExitStack()
    sbt = lambda n, s, dt=F32: es.enter_context(nc.sbuf_tensor(n, s, dt))
    xa = sbt("xaF", [P, D]); xb = sbt("xbF", [P, D]); xc = sbt("xcF", [P, D])
    xn = sbt("xnF", [P, D]); xsq = sbt("xsqF", [P, D])
    stat = sbt("statF", [P, 8]); epsc = sbt("epscF", [P, 1])
    A = sp.add
    A("sync", lambda e: e.dma_start(out=epsc[:], in_=eps_in), dma=True)
    for i in range(NT):
        A("sync", lambda e, i=i: e.dma_start(out=xa[:], in_=x_in[i * P:(i + 1) * P, :]), dma=True)
        A("sync", lambda e, i=i: e.dma_start(out=xb[:], in_=at_in[i * P:(i + 1) * P, :]), dma=True)
        A("sync", lambda e, i=i: e.dma_start(out=xc[:], in_=ff_in[i * P:(i + 1) * P, :]), dma=True)
        A("vector", lambda e: e.tensor_tensor(out=xa[:], in0=xa[:], in1=xb[:], op=ALU.add))
        A("vector", lambda e: e.tensor_tensor(out=xa[:], in0=xa[:], in1=xc[:], op=ALU.add))
        A("sync", lambda e, i=i: e.dma_start(out=x1_o[i * P:(i + 1) * P, :], in_=xa[:]), dma=True)
        A("vector", lambda e: e.tensor_reduce(out=stat[:, 0:1], in_=xa[:], axis=AX.X, op=ALU.add))
        A("vector", lambda e: e.tensor_tensor(out=xsq[:], in0=xa[:], in1=xa[:], op=ALU.mult))
        A("vector", lambda e: e.tensor_reduce(out=stat[:, 1:2], in_=xsq[:], axis=AX.X, op=ALU.add))
        A("vector", lambda e: e.tensor_scalar_mul(out=stat[:, 2:3], in0=stat[:, 0:1], scalar1=1.0 / D))
        A("vector", lambda e: e.tensor_scalar_mul(out=stat[:, 3:4], in0=stat[:, 1:2], scalar1=1.0 / D))
        A("vector", lambda e: e.tensor_tensor(out=stat[:, 4:5], in0=stat[:, 2:3], in1=stat[:, 2:3], op=ALU.mult))
        A("vector", lambda e: e.tensor_tensor(out=stat[:, 4:5], in0=stat[:, 3:4], in1=stat[:, 4:5], op=ALU.subtract))
        A("scalar", lambda e: e.activation(out=stat[:, 5:6], in_=stat[:, 4:5], func=AF.Ln, bias=epsc[:, 0:1]))
        A("scalar", lambda e: e.activation(out=stat[:, 5:6], in_=stat[:, 5:6], func=AF.Exp, scale=-0.5))
        A("vector", lambda e: e.scalar_tensor_tensor(out=stat[:, 6:7], in0=stat[:, 2:3], scalar=-1.0,
                                                     in1=stat[:, 5:6], op0=ALU.mult, op1=ALU.mult))
        A("scalar", lambda e: e.activation(out=xn[:], in_=xa[:], func=AF.Identity,
                                           scale=stat[:, 5:6], bias=stat[:, 6:7]))
        A("sync", lambda e, i=i: e.dma_start(out=xn1_o[i * P:(i + 1) * P, :], in_=xn[:]), dma=True)
    es.close()



def emit_I(nc, sp, a_in, b_in, c_in, o_out):
    es = contextlib.ExitStack()
    sbt = lambda n, s, dt=F32: es.enter_context(nc.sbuf_tensor(n, s, dt))
    xa = sbt("xaI", [P, D]); xb = sbt("xbI", [P, D]); xc = sbt("xcI", [P, D])
    A = sp.add
    for i in range(NT):
        A("sync", lambda e, i=i: e.dma_start(out=xa[:], in_=a_in[i * P:(i + 1) * P, :]), dma=True)
        A("sync", lambda e, i=i: e.dma_start(out=xb[:], in_=b_in[i * P:(i + 1) * P, :]), dma=True)
        A("sync", lambda e, i=i: e.dma_start(out=xc[:], in_=c_in[i * P:(i + 1) * P, :]), dma=True)
        A("vector", lambda e: e.tensor_tensor(out=xa[:], in0=xa[:], in1=xb[:], op=ALU.add))
        A("vector", lambda e: e.tensor_tensor(out=xa[:], in0=xa[:], in1=xc[:], op=ALU.add))
        A("sync", lambda e, i=i: e.dma_start(out=o_out[i * P:(i + 1) * P, :], in_=xa[:]), dma=True)
    es.close()



def emit_G(nc, sp, xn1_in, ln_b, wkT_in, wvT_in, wrT_in, woT_in, wfkT_in,
           wfvT_in, wfrT_in, tdc_in, tfc_in, eps_in, attn1_o,
           a_sc, x2_sc, xk_sc, xv_sc, xr_sc, xr2_sc, sa_sc, kfq_sc):
    es = contextlib.ExitStack()
    sbt = lambda n, s, dt=F32: es.enter_context(nc.sbuf_tensor(n, s, dt))
    pst = lambda n, s, dt=F32: es.enter_context(nc.psum_tensor(n, s, dt))
    pool = sbt("poolG", [P, 25000])
    xk2_fm = [sbt(f"xk2fm{c}", [P, T], BF16) for c in range(NCH)]
    trd = es.enter_context(nc.sbuf_tensor("trdG", [P, D], BF16, is_transpose_dest=True))
    stat = sbt("statG", [P, 8])
    epsc = sbt("epscG", [P, 1])
    smallc = sbt("smallc", [P, 8])
    tdc = sbt("tdcG", [P, NCH]); tfc = sbt("tfcG", [P, NCH])
    hlast = sbt("hlastG", [1, D])
    ps = [pst(f"psG{i}", [P, 512]) for i in range(2)]

    class Carve:
        def __init__(self, base):
            self.base = base
            self.off = 0
        def f32(self, n):
            v = pool[:, self.off:self.off + n]
            self.off += n
            return v
        def bf(self, n):
            v = pool[:, self.off:self.off + (n + 1) // 2].bitcast(BF16)
            self.off += (n + 1) // 2
            return v

    A = sp.add
    A("sync", lambda e: e.dma_start(out=epsc[:], in_=eps_in), dma=True)
    A("sync", lambda e: e.dma_start(out=tdc[:], in_=tdc_in), dma=True)
    A("sync", lambda e: e.dma_start(out=tfc[:], in_=tfc_in), dma=True)

    def emit_ln(xa, dst, wname, bname, xsq):
        A("vector", lambda e: e.tensor_reduce(out=stat[:, 0:1], in_=xa, axis=AX.X, op=ALU.add))
        A("vector", lambda e: e.tensor_tensor(out=xsq, in0=xa, in1=xa, op=ALU.mult))
        A("vector", lambda e: e.tensor_reduce(out=stat[:, 1:2], in_=xsq, axis=AX.X, op=ALU.add))
        A("vector", lambda e: e.tensor_scalar_mul(out=stat[:, 2:3], in0=stat[:, 0:1], scalar1=1.0 / D))
        A("vector", lambda e: e.tensor_scalar_mul(out=stat[:, 3:4], in0=stat[:, 1:2], scalar1=1.0 / D))
        A("vector", lambda e: e.tensor_tensor(out=stat[:, 4:5], in0=stat[:, 2:3], in1=stat[:, 2:3], op=ALU.mult))
        A("vector", lambda e: e.tensor_tensor(out=stat[:, 4:5], in0=stat[:, 3:4], in1=stat[:, 4:5], op=ALU.subtract))
        A("scalar", lambda e: e.activation(out=stat[:, 5:6], in_=stat[:, 4:5], func=AF.Ln, bias=epsc[:, 0:1]))
        A("scalar", lambda e: e.activation(out=stat[:, 5:6], in_=stat[:, 5:6], func=AF.Exp, scale=-0.5))
        A("vector", lambda e: e.scalar_tensor_tensor(out=stat[:, 6:7], in0=stat[:, 2:3], scalar=-1.0,
                                                     in1=stat[:, 5:6], op0=ALU.mult, op1=ALU.mult))
        A("scalar", lambda e: e.activation(out=dst, in_=xa, func=AF.Identity,
                                           scale=stat[:, 5:6], bias=stat[:, 6:7]))
        A("vector", lambda e, wname=wname: e.tensor_tensor(out=dst, in0=dst, in1=lnt[wname][:], op=ALU.mult))
        A("vector", lambda e, bname=bname: e.tensor_tensor(out=dst, in0=dst, in1=lnt[bname][:], op=ALU.add))

    cv = Carve(0)
    lnt = {n: sbt(f"lnt_{n}", [P, D]) for n in ("ln0w", "ln0b", "ln1w", "ln1b", "tmkb", "tmvb", "tmrb")}
    for n in lnt:
        A("sync", lambda e, n=n: e.dma_start(out=lnt[n][:], in_=ln_b[n]), dma=True)
    xt = cv.f32(D); at = cv.f32(D); ht = cv.f32(D); xxt = cv.f32(D)
    dt_ = cv.f32(D); mixt = cv.f32(D); xsq1 = cv.f32(D)
    mixb = cv.bf(D)
    for i in range(NT):
        A("sync", lambda e, i=i: e.dma_start(out=xt, in_=xn1_in[i * P:(i + 1) * P, :]), dma=True)
        emit_ln(xt, at, "ln0w", "ln0b", xsq1)
        A("sync", lambda e, i=i: e.dma_start(out=a_sc[i * P:(i + 1) * P, :], in_=at), dma=True)
        emit_ln(at, ht, "ln1w", "ln1b", xsq1)
        if i % 16 == 0:
            A("gpsimd", lambda e: e.memset(xxt[0:1, 0:D], 0.0))
        else:
            A("sync", lambda e: e.dma_start(out=xxt[0:1, 0:D], in_=hlast[:]), dma=True)
        A("sync", lambda e: e.dma_start(out=xxt[1:P, 0:D], in_=ht[0:P - 1, 0:D]), dma=True)
        A("sync", lambda e: e.dma_start(out=hlast[:], in_=ht[P - 1:P, 0:D]), dma=True)
        A("vector", lambda e: e.tensor_tensor(out=dt_, in0=ht, in1=xxt, op=ALU.subtract))
        for (mn, dst_sc) in (("tmkb", xk_sc), ("tmvb", xv_sc), ("tmrb", xr_sc)):
            A("vector", lambda e, mn=mn: e.tensor_tensor(out=mixt, in0=dt_, in1=lnt[mn][:], op=ALU.mult))
            A("vector", lambda e: e.tensor_tensor(out=mixt, in0=mixt, in1=xxt, op=ALU.add))
            A("vector", lambda e: e.tensor_copy(out=mixb, in_=mixt))
            for c in range(NCH):
                A("scalar", lambda e, c=c: e.dma_start(out=trd[:, c * P:(c + 1) * P],
                                                       in_=mixb[:, c * P:(c + 1) * P],
                                                       transpose=True), dma=True)
            for c in range(NCH):
                A("sync", lambda e, c=c, i=i, dst_sc=dst_sc: e.dma_start(
                    out=dst_sc[c * P:(c + 1) * P, i * P:(i + 1) * P],
                    in_=trd[:, c * P:(c + 1) * P]), dma=True)

    cv = Carve(0)
    ko = cv.f32(S); vo = cv.f32(S); sro = cv.f32(S)
    eko = cv.f32(S); ekvo = cv.f32(S)
    Asc = cv.f32(S + 1); Bsc = cv.f32(S + 1)
    numt = cv.f32(S); dent = cv.f32(S); rdt = cv.f32(S)
    wblk = [cv.bf(P) for _ in range(24)]
    rhsb = cv.bf(512)
    saob = cv.bf(S)
    wld2 = cv.f32(P)
    for oc in range(NCH):
        for pi, w_in in enumerate((wkT_in, wvT_in, wrT_in)):
            for cc in range(NCH):
                A("sync", lambda e, w_in=w_in, cc=cc, oc=oc: e.dma_start(
                    out=wld2, in_=w_in[cc * P:(cc + 1) * P, oc * P:(oc + 1) * P]), dma=True)
                A("vector", lambda e, pi=pi, cc=cc: e.tensor_copy(out=wblk[pi * NCH + cc], in_=wld2))
        A("scalar", lambda e, oc=oc: e.activation(out=smallc[:, 0:1], in_=tdc[:, oc:oc + 1], func=AF.Exp))
        A("scalar", lambda e: e.activation(out=smallc[:, 1:2], in_=smallc[:, 0:1], func=AF.Exp, scale=-1.0))
        A("scalar", lambda e, oc=oc: e.activation(out=smallc[:, 2:3], in_=tfc[:, oc:oc + 1], func=AF.Exp))
        for s in range(BC):
            for (pi, src_sc, dsto, fn) in ((0, xk_sc, ko, AF.Copy), (1, xv_sc, vo, AF.Copy),
                                           (2, xr_sc, sro, AF.Sigmoid)):
                for tb in range(4):
                    for cc in range(NCH):
                        A("sync", lambda e, src_sc=src_sc, cc=cc, s=s, tb=tb: e.dma_start(
                            out=rhsb, in_=src_sc[cc * P:(cc + 1) * P,
                                                 s * S + tb * 512: s * S + (tb + 1) * 512]), dma=True)
                        A("tensor", lambda e, pi=pi, cc=cc: e.matmul(
                            ps[0][:], wblk[pi * NCH + cc], rhsb, start=(cc == 0), stop=(cc == NCH - 1)))
                    A("scalar", lambda e, dsto=dsto, fn=fn, tb=tb: e.activation(
                        out=dsto[:, tb * 512:(tb + 1) * 512], in_=ps[0][:], func=fn))
            A("scalar", lambda e: e.activation(out=eko, in_=ko, func=AF.Exp))
            A("vector", lambda e: e.tensor_tensor(out=ekvo, in0=eko, in1=vo, op=ALU.mult))
            A("gpsimd", lambda e: e.memset(Asc[:, 0:1], 0.0))
            A("gpsimd", lambda e: e.memset(Bsc[:, 0:1], 0.0))
            ewb = smallc[:, 1:2].broadcast_to([P, S])
            A("vector", lambda e: e.tensor_tensor_scan(out=Asc[:, 1:S + 1], data0=ewb, data1=ekvo,
                                                       initial=0.0, op0=ALU.mult, op1=ALU.add))
            A("vector", lambda e: e.tensor_tensor_scan(out=Bsc[:, 1:S + 1], data0=ewb, data1=eko,
                                                       initial=0.0, op0=ALU.mult, op1=ALU.add))
            A("vector", lambda e: e.scalar_tensor_tensor(out=numt, in0=ekvo, scalar=smallc[:, 2:3],
                                                         in1=Asc[:, 0:S], op0=ALU.mult, op1=ALU.add))
            A("vector", lambda e: e.scalar_tensor_tensor(out=dent, in0=eko, scalar=smallc[:, 2:3],
                                                         in1=Bsc[:, 0:S], op0=ALU.mult, op1=ALU.add))
            A("scalar", lambda e: e.activation(out=rdt, in_=dent, func=AF.Ln))
            A("scalar", lambda e: e.activation(out=rdt, in_=rdt, func=AF.Exp, scale=-1.0))
            A("vector", lambda e: e.tensor_tensor(out=numt, in0=numt, in1=rdt, op=ALU.mult))
            A("vector", lambda e: e.tensor_tensor(out=saob, in0=numt, in1=sro, op=ALU.mult))
            A("sync", lambda e, oc=oc, s=s: e.dma_start(
                out=sa_sc[oc * P:(oc + 1) * P, s * S:(s + 1) * S], in_=saob), dma=True)

    cv = Carve(0)
    lnt3 = {}
    for n in ("ln2w", "ln2b", "cmkb", "cmrb"):
        t = cv.f32(D)
        lnt3[n] = t
        A("sync", lambda e, n=n, t=t: e.dma_start(out=t, in_=ln_b[n]), dma=True)
    lnt.update({k: type("W", (), {"__getitem__": (lambda self, idx, t=v: t)})() for k, v in lnt3.items()})
    woTb = [cv.bf(D) for _ in range(NCH)]
    wld3 = cv.f32(D)
    for cc in range(NCH):
        A("sync", lambda e, cc=cc: e.dma_start(out=wld3, in_=woT_in[cc * P:(cc + 1) * P, :]), dma=True)
        A("vector", lambda e, cc=cc: e.tensor_copy(out=woTb[cc], in_=wld3))
    salhs = cv.bf(P)
    ot3 = cv.f32(D); a3 = cv.f32(D); x23 = cv.f32(D); h23 = cv.f32(D)
    xx3 = cv.f32(D); d3 = cv.f32(D); mix3 = cv.f32(D); xsq3 = cv.f32(D)
    mixb3 = cv.bf(D)
    for i in range(NT):
        for half in range(2):
            for cc in range(NCH):
                A("sync", lambda e, cc=cc, i=i: e.dma_start(
                    out=salhs, in_=sa_sc[cc * P:(cc + 1) * P, i * P:(i + 1) * P]), dma=True)
                A("tensor", lambda e, cc=cc, half=half: e.matmul(
                    ps[0][:], salhs, woTb[cc][:, half * 512:(half + 1) * 512],
                    start=(cc == 0), stop=(cc == NCH - 1)))
            A("scalar", lambda e, half=half: e.activation(out=ot3[:, half * 512:(half + 1) * 512],
                                                          in_=ps[0][:], func=AF.Copy))
        A("sync", lambda e, i=i: e.dma_start(out=a3, in_=a_sc[i * P:(i + 1) * P, :]), dma=True)
        A("vector", lambda e: e.tensor_tensor(out=x23, in0=ot3, in1=a3, op=ALU.add))
        A("sync", lambda e, i=i: e.dma_start(out=x2_sc[i * P:(i + 1) * P, :], in_=x23), dma=True)
        A("vector", lambda e: e.tensor_reduce(out=stat[:, 0:1], in_=x23, axis=AX.X, op=ALU.add))
        A("vector", lambda e: e.tensor_tensor(out=xsq3, in0=x23, in1=x23, op=ALU.mult))
        A("vector", lambda e: e.tensor_reduce(out=stat[:, 1:2], in_=xsq3, axis=AX.X, op=ALU.add))
        A("vector", lambda e: e.tensor_scalar_mul(out=stat[:, 2:3], in0=stat[:, 0:1], scalar1=1.0 / D))
        A("vector", lambda e: e.tensor_scalar_mul(out=stat[:, 3:4], in0=stat[:, 1:2], scalar1=1.0 / D))
        A("vector", lambda e: e.tensor_tensor(out=stat[:, 4:5], in0=stat[:, 2:3], in1=stat[:, 2:3], op=ALU.mult))
        A("vector", lambda e: e.tensor_tensor(out=stat[:, 4:5], in0=stat[:, 3:4], in1=stat[:, 4:5], op=ALU.subtract))
        A("scalar", lambda e: e.activation(out=stat[:, 5:6], in_=stat[:, 4:5], func=AF.Ln, bias=epsc[:, 0:1]))
        A("scalar", lambda e: e.activation(out=stat[:, 5:6], in_=stat[:, 5:6], func=AF.Exp, scale=-0.5))
        A("vector", lambda e: e.scalar_tensor_tensor(out=stat[:, 6:7], in0=stat[:, 2:3], scalar=-1.0,
                                                     in1=stat[:, 5:6], op0=ALU.mult, op1=ALU.mult))
        A("scalar", lambda e: e.activation(out=h23, in_=x23, func=AF.Identity,
                                           scale=stat[:, 5:6], bias=stat[:, 6:7]))
        A("vector", lambda e: e.tensor_tensor(out=h23, in0=h23, in1=lnt3["ln2w"], op=ALU.mult))
        A("vector", lambda e: e.tensor_tensor(out=h23, in0=h23, in1=lnt3["ln2b"], op=ALU.add))
        if i % 16 == 0:
            A("gpsimd", lambda e: e.memset(xx3[0:1, 0:D], 0.0))
        else:
            A("sync", lambda e: e.dma_start(out=xx3[0:1, 0:D], in_=hlast[:]), dma=True)
        A("sync", lambda e: e.dma_start(out=xx3[1:P, 0:D], in_=h23[0:P - 1, 0:D]), dma=True)
        A("sync", lambda e: e.dma_start(out=hlast[:], in_=h23[P - 1:P, 0:D]), dma=True)
        A("vector", lambda e: e.tensor_tensor(out=d3, in0=h23, in1=xx3, op=ALU.subtract))
        A("vector", lambda e: e.tensor_tensor(out=mix3, in0=d3, in1=lnt3["cmkb"], op=ALU.mult))
        A("vector", lambda e: e.tensor_tensor(out=mix3, in0=mix3, in1=xx3, op=ALU.add))
        A("vector", lambda e: e.tensor_copy(out=mixb3, in_=mix3))
        for c in range(NCH):
            A("scalar", lambda e, c=c: e.dma_start(out=trd[:, c * P:(c + 1) * P],
                                                   in_=mixb3[:, c * P:(c + 1) * P], transpose=True), dma=True)
        for c in range(NCH):
            A("vector", lambda e, c=c, i=i: e.tensor_copy(out=xk2_fm[c][:, i * P:(i + 1) * P],
                                                          in_=trd[:, c * P:(c + 1) * P]))
        A("vector", lambda e: e.tensor_tensor(out=mix3, in0=d3, in1=lnt3["cmrb"], op=ALU.mult))
        A("vector", lambda e: e.tensor_tensor(out=mix3, in0=mix3, in1=xx3, op=ALU.add))
        A("vector", lambda e: e.tensor_copy(out=mixb3, in_=mix3))
        for c in range(NCH):
            A("scalar", lambda e, c=c: e.dma_start(out=trd[:, c * P:(c + 1) * P],
                                                   in_=mixb3[:, c * P:(c + 1) * P], transpose=True), dma=True)
        for c in range(NCH):
            A("sync", lambda e, c=c, i=i: e.dma_start(
                out=xr2_sc[c * P:(c + 1) * P, i * P:(i + 1) * P],
                in_=trd[:, c * P:(c + 1) * P]), dma=True)

    cv = Carve(0)
    wfkb = [cv.bf(P) for _ in range(NCH)]
    wld4 = cv.f32(P)
    kfo = cv.f32(T)
    kfqo = cv.bf(T)
    for o4 in range(H_FFN // P):
        for cc in range(NCH):
            A("sync", lambda e, cc=cc, o4=o4: e.dma_start(
                out=wld4, in_=wfkT_in[cc * P:(cc + 1) * P, o4 * P:(o4 + 1) * P]), dma=True)
            A("vector", lambda e, cc=cc: e.tensor_copy(out=wfkb[cc], in_=wld4))
        for tb in range(NTB):
            for cc in range(NCH):
                A("tensor", lambda e, cc=cc, tb=tb: e.matmul(
                    ps[0][:], wfkb[cc], xk2_fm[cc][:, tb * 512:(tb + 1) * 512],
                    start=(cc == 0), stop=(cc == NCH - 1)))
            A("scalar", lambda e, tb=tb: e.activation(out=kfo[:, tb * 512:(tb + 1) * 512],
                                                      in_=ps[0][:], func=AF.Relu))
        A("scalar", lambda e: e.activation(out=kfqo, in_=kfo, func=AF.Square))
        A("sync", lambda e, o4=o4: e.dma_start(out=kfq_sc[o4 * P:(o4 + 1) * P, :], in_=kfqo), dma=True)

    cv = Carve(0)
    wfvb = [cv.bf(D) for _ in range(H_FFN // P)]
    wfrb = [cv.bf(D) for _ in range(NCH)]
    wld5 = cv.f32(D)
    kfl = cv.bf(P)
    xrl = cv.bf(P)
    kvt = cv.f32(D)
    rst = cv.f32(D)
    x25 = cv.f32(D)
    for ic in range(H_FFN // P):
        A("sync", lambda e, ic=ic: e.dma_start(out=wld5, in_=wfvT_in[ic * P:(ic + 1) * P, :]), dma=True)
        A("vector", lambda e, ic=ic: e.tensor_copy(out=wfvb[ic], in_=wld5))
    for cc in range(NCH):
        A("sync", lambda e, cc=cc: e.dma_start(out=wld5, in_=wfrT_in[cc * P:(cc + 1) * P, :]), dma=True)
        A("vector", lambda e, cc=cc: e.tensor_copy(out=wfrb[cc], in_=wld5))
    for i in range(NT):
        for half in range(2):
            for ic in range(H_FFN // P):
                A("sync", lambda e, ic=ic, i=i: e.dma_start(
                    out=kfl, in_=kfq_sc[ic * P:(ic + 1) * P, i * P:(i + 1) * P]), dma=True)
                A("tensor", lambda e, ic=ic, half=half: e.matmul(
                    ps[0][:], kfl, wfvb[ic][:, half * 512:(half + 1) * 512],
                    start=(ic == 0), stop=(ic == H_FFN // P - 1)))
            A("scalar", lambda e, half=half: e.activation(out=kvt[:, half * 512:(half + 1) * 512],
                                                          in_=ps[0][:], func=AF.Copy))
        for half in range(2):
            for cc in range(NCH):
                A("sync", lambda e, cc=cc, i=i: e.dma_start(
                    out=xrl, in_=xr2_sc[cc * P:(cc + 1) * P, i * P:(i + 1) * P]), dma=True)
                A("tensor", lambda e, cc=cc, half=half: e.matmul(
                    ps[0][:], xrl, wfrb[cc][:, half * 512:(half + 1) * 512],
                    start=(cc == 0), stop=(cc == NCH - 1)))
            A("scalar", lambda e, half=half: e.activation(out=rst[:, half * 512:(half + 1) * 512],
                                                          in_=ps[0][:], func=AF.Sigmoid))
        A("vector", lambda e: e.tensor_tensor(out=rst, in0=rst, in1=kvt, op=ALU.mult))
        A("sync", lambda e, i=i: e.dma_start(out=x25, in_=x2_sc[i * P:(i + 1) * P, :]), dma=True)
        A("vector", lambda e: e.tensor_tensor(out=rst, in0=rst, in1=x25, op=ALU.add))
        A("sync", lambda e, i=i: e.dma_start(out=attn1_o[i * P:(i + 1) * P, :], in_=rst), dma=True)
    es.close()



def build_MEGA():
    nc = bass.Bass("TRN2", target_bir_lowering=False)
    inp = lambda n, s, dt=F32: nc.dram_tensor(n, s, dt, kind="ExternalInput").ap()
    scr = lambda n, s, dt=F32: nc.dram_tensor(n, s, dt, kind="Internal").ap()

    x_in = inp("x", [T, D])
    wqT = inp("wqT", [D, D]); wkT = inp("wkT", [D, D]); wvT = inp("wvT", [D, D])
    bqc = inp("bqc", [P, NCH]); bkc = inp("bkc", [P, NCH]); bvr = inp("bvr", [1, D])
    epsc = inp("epsc", [P, 1])
    eyes = inp("eyes", [P, 3 * P]); identb = inp("identb", [P, P], BF16)
    identf = inp("identf", [P, P])
    wg0 = inp("wg0", [I_FFN, D]); wu0 = inp("wu0", [I_FFN, D]); wd0 = inp("wd0", [D, I_FFN])
    wg1 = inp("wg1", [I_FFN, D]); wu1 = inp("wu1", [I_FFN, D]); wd1 = inp("wd1", [D, I_FFN])
    ln_b = {n: inp(n, [P, D]) for n in ("ln0w", "ln0b", "ln1w", "ln1b", "ln2w", "ln2b",
                                        "tmkb", "tmvb", "tmrb", "cmkb", "cmrb")}
    wkT1 = inp("wkT1", [D, D]); wvT1 = inp("wvT1", [D, D]); wrT1 = inp("wrT1", [D, D])
    woT1 = inp("woT1", [D, D]); wfkT = inp("wfkT", [D, H_FFN])
    wfvT = inp("wfvT", [H_FFN, D]); wfrT = inp("wfrT", [D, D])
    tdc = inp("tdc", [P, NCH]); tfc = inp("tfc", [P, NCH])

    xn_tok = scr("xn_tok", [T, D])
    q_fm = scr("q_fm", [NCH * P, T], BF16)
    k_fm = scr("k_fm", [NCH * P, T], BF16)
    v_tok = scr("v_tok", [T, D], BF16)
    xn_fm_s = scr("xn_fm_s", [NCH * P, T], BF16)
    qlm = scr("qlm", [NCH * P, BC * LM], BF16)
    klm = scr("klm", [NCH * P, BC * LM], BF16)
    k2 = scr("k2", [NMAT * P, LM], BF16)
    pcmax_t = nc.dram_tensor("pcmax", [1, 4], F32, kind="Internal")
    pcm_all_t = nc.dram_tensor("pcm_all", [1, 4], F32, kind="Internal")
    z_s = scr("z_s", [NMAT * P, LM], BF16)
    zt_s = scr("zt_s", [NMAT * P, LM], BF16)
    attn_tok = scr("attn_tok", [T, D])
    wgq_fm = scr("wgq_fm", [D, I_FFN], BF16)
    wuq_fm = scr("wuq_fm", [D, I_FFN], BF16)
    wdqT = scr("wdqT", [I_FFN, D], BF16)
    hq_sc = scr("hq_sc", [I_FFN, T], BF16)
    ffn0_tok = scr("ffn0_tok", [T, D])
    ffn1_tok = scr("ffn1_tok", [T, D])
    x1_tok = scr("x1_tok", [T, D])
    xn1_tok = scr("xn1_tok", [T, D])
    a_sc = scr("a_sc", [T, D])
    x2_sc = scr("x2_sc", [T, D])
    xk_sc = scr("xk_sc", [D, T], BF16)
    xv_sc = scr("xv_sc", [D, T], BF16)
    xr_sc = scr("xr_sc", [D, T], BF16)
    xr2_sc = scr("xr2_sc", [D, T], BF16)
    sa_sc = scr("sa_sc", [D, T], BF16)
    kfq_sc = scr("kfq_sc", [H_FFN, T], BF16)
    attn1_tok = scr("attn1_tok", [T, D])
    out_o = nc.dram_tensor("out", [T, D], F32, kind="ExternalOutput").ap()

    es = contextlib.ExitStack()

    ab = emit_AB(nc, es, {
        "x": x_in, "wqT": wqT, "wkT": wkT, "wvT": wvT, "bqc": bqc, "bkc": bkc,
        "bvr": bvr, "epsc": epsc, "xn_tok": xn_tok, "q_fm": q_fm, "k_fm": k_fm,
        "v_tok": v_tok, "xn_fm": xn_fm_s, "qlm": qlm, "klm": klm, "k2": k2,
        "pcmax": pcmax_t.ap()})

    sp = SP(nc)

    cc_sem = es.enter_context(nc.semaphore("s_cc"))

    def cc_fn(e, sq, inc):
        e.collective_compute(
            "AllReduce", ALU.max,
            replica_groups=[list(range(NCORES))],
            ins=[pcmax_t.ap().opt()],
            outs=[pcm_all_t.ap().opt()]).then_inc(cc_sem)
        e.wait_ge(cc_sem, 1)
        e.sem_inc(sq, inc)
    sp.add_raw("gpsimd", cc_fn, 1)

    emit_WQfull(nc, sp, "0", wg0, wu0, wd0, wgq_fm, wuq_fm, wdqT)
    emit_C(nc, sp, k2, pcm_all_t.ap(), eyes, identb, z_s, zt_s)
    emit_D(nc, sp, q_fm, k_fm, v_tok, qlm, klm, zt_s, identf, attn_tok)
    emit_FFN(nc, sp, "0", xn_tok, wgq_fm, wuq_fm, wdqT, ffn0_tok, hq_sc)
    emit_F(nc, sp, x_in, attn_tok, ffn0_tok, epsc, x1_tok, xn1_tok)
    emit_WQfull(nc, sp, "1", wg1, wu1, wd1, wgq_fm, wuq_fm, wdqT)
    emit_FFN(nc, sp, "1", xn1_tok, wgq_fm, wuq_fm, wdqT, ffn1_tok, hq_sc)
    emit_G(nc, sp, xn1_tok, ln_b, wkT1, wvT1, wrT1, woT1, wfkT, wfvT, wfrT,
           tdc, tfc, epsc, attn1_tok,
           a_sc, x2_sc, xk_sc, xv_sc, xr_sc, xr2_sc, sa_sc, kfq_sc)
    emit_I(nc, sp, x1_tok, attn1_tok, ffn1_tok, out_o)

    sp.emit(es, prologue=ab)
    es.close()
    return nc



_CACHED = {}


def _get(name, builder):
    if name not in _CACHED:
        _CACHED[name] = builder()
    return _CACHED[name]


def kernel(**inputs):
    ins = {k: np.asarray(v) for k, v in inputs.items()}
    C = np.ascontiguousarray
    cores = list(range(NCORES))

    nc = _get("MEGA", build_MEGA)

    epsc = np.full((P, 1), 1e-5, np.float32)
    eyes = np.concatenate([np.eye(P) * 7, np.eye(P) * 15, np.eye(P) * 13], 1).astype(np.float32)
    identb = np.eye(P).astype(BF)
    identf = np.eye(P, dtype=np.float32)
    bt = lambda v: C(np.broadcast_to(np.asarray(v, np.float32), (P, D)))
    f32 = lambda k: ins[k].astype(np.float32)
    common = {
        "wqT": C(f32("L0_attn_Wq").T),
        "wkT": C(f32("L0_attn_Wk").T),
        "wvT": C(f32("L0_attn_Wv").T),
        "bqc": C(f32("L0_attn_bq").reshape(NCH, P).T),
        "bkc": C(f32("L0_attn_bk").reshape(NCH, P).T),
        "bvr": C(f32("L0_attn_bv").reshape(1, D)),
        "epsc": epsc, "eyes": eyes, "identb": identb, "identf": identf,
        "wg0": C(f32("L0_ffn_w_gate")), "wu0": C(f32("L0_ffn_w_up")),
        "wd0": C(f32("L0_ffn_w_down")),
        "wg1": C(f32("L1_ffn_w_gate")), "wu1": C(f32("L1_ffn_w_up")),
        "wd1": C(f32("L1_ffn_w_down")),
        "ln0w": bt(ins["L1_attn_ln0_w"]), "ln0b": bt(ins["L1_attn_ln0_b"]),
        "ln1w": bt(ins["L1_attn_ln1_w"]), "ln1b": bt(ins["L1_attn_ln1_b"]),
        "ln2w": bt(ins["L1_attn_ln2_w"]), "ln2b": bt(ins["L1_attn_ln2_b"]),
        "tmkb": bt(ins["L1_attn_tmk"]), "tmvb": bt(ins["L1_attn_tmv"]),
        "tmrb": bt(ins["L1_attn_tmr"]), "cmkb": bt(ins["L1_attn_cmk"]),
        "cmrb": bt(ins["L1_attn_cmr"]),
        "wkT1": C(f32("L1_attn_Wk").T),
        "wvT1": C(f32("L1_attn_Wv").T),
        "wrT1": C(f32("L1_attn_Wr").T),
        "woT1": C(f32("L1_attn_Wo").T),
        "wfkT": C(f32("L1_attn_Wfk").T),
        "wfvT": C(f32("L1_attn_Wfv").T),
        "wfrT": C(f32("L1_attn_Wfr").T),
        "tdc": C(f32("L1_attn_time_decay").reshape(NCH, P).T),
        "tfc": C(f32("L1_attn_time_first").reshape(NCH, P).T),
    }
    x = ins["L0_attn_x"].astype(np.float32)
    maps = [dict(common, x=C(x[c * BC:(c + 1) * BC].reshape(T, D))) for c in cores]
    res = run_bass_kernel_spmd(nc, maps, core_ids=cores)
    out = np.stack([res.results[c]["out"] for c in cores]).reshape(16, S, D).astype(np.float32)
    return out


if __name__ == "__main__":
    print("library module")


# version 08
# speedup vs baseline: 1.3520x; 1.2709x over previous
import contextlib
import numpy as np
import ml_dtypes
import concourse.bass as bass
import concourse.mybir as mybir
from concourse.bass_utils import run_bass_kernel_spmd

F32 = mybir.dt.float32
BF16 = mybir.dt.bfloat16
I16 = mybir.dt.int16
AF = mybir.ActivationFunctionType
ALU = mybir.AluOpType
AX = mybir.AxisListType

NCORES = 8
P = 128
S = 2048
BC = 2
T = S * BC
D = 1024
NCH = D // P
HEADS = 16
HD = 64
LM = 128
SEG = 16
NMAT = BC * HEADS
S4 = 1.0 / float(np.sqrt(np.sqrt(HD)))
NT = T // P
I_FFN = 2816
H_FFN = 4096
NTB = T // 512
BF = ml_dtypes.bfloat16


class Chain:

    def __init__(self, eng, sem):
        self.eng, self.sem, self.n = eng, sem, 0

    def bar(self, inst):
        self.n += 1
        inst.then_inc(self.sem, 1)
        self.eng.wait_ge(self.sem, self.n)
        return inst



def emit_AB(nc, es_outer, d, stages=3, cuts=(), nt=NT):
    x_in = d["x"]
    wqT_in = d["wqT"]; wkT_in = d["wkT"]; wvT_in = d["wvT"]
    bqc_in = d["bqc"]; bkc_in = d["bkc"]; bvr_in = d["bvr"]; eps_in = d["epsc"]
    xn_tok_o = d["xn_tok"]; q_fm_o = d["q_fm"]; k_fm_o = d["k_fm"]
    v_tok_o = d["v_tok"]; xn_fm_o = d["xn_fm"]; qlm_o = d["qlm"]
    klm_o = d["klm"]; k2_o = d["k2"]; pcmax_o = d["pcmax"]

    es = contextlib.ExitStack()
    sbt = lambda name, shape, dt=F32: es.enter_context(nc.sbuf_tensor(name, shape, dt))
    pst = lambda name, shape, dt=F32: es.enter_context(nc.psum_tensor(name, shape, dt))
    sem = lambda name: es_outer.enter_context(nc.semaphore(name))

    NXB = 2
    x_t = [sbt(f"x_t{i}", [P, D]) for i in range(NXB)]
    xn_t = [sbt(f"xn_t{i}", [P, D]) for i in range(NXB)]
    xnb_t = [sbt(f"xnb_t{i}", [P, D], BF16) for i in range(2)]
    stat = sbt("stat", [P, 8])
    epsc = sbt("epsc_s", [P, 1])
    xn_fm = [es.enter_context(nc.sbuf_tensor(f"xn_fm{c}", [P, T], BF16, is_transpose_dest=True)) for c in range(NCH)]
    wq = [sbt(f"wqc{c}", [P, D], BF16) for c in range(NCH)]
    wk = [sbt(f"wkc{c}", [P, D], BF16) for c in range(NCH)]
    wv = [sbt(f"wvc{c}", [P, D], BF16) for c in range(NCH)]
    bqs = sbt("bqs", [P, NCH])
    bks = sbt("bks", [P, NCH])
    bvf = sbt("bvf", [1, D])
    bvb = sbt("bvb", [1, D], BF16)
    ones_row = sbt("ones_row", [1, P], BF16)
    ones_col = sbt("ones_col", [P, 1])
    if stages >= 2:
        pbf_s = sbt("pbf_s", [P, T], BF16)
        pbf = [pbf_s, pbf_s]
        lmraw = [sbt(f"lmraw{i}", [P, BC * LM]) for i in range(2)]
        b16q = sbt("b16q", [P, NCH])
        b16k = sbt("b16k", [P, NCH])
        _wldv = pbf_s[:].bitcast(F32)
        wld = [_wldv[:, 0:D], _wldv[:, D:2 * D]]
    else:
        wld_t = [sbt(f"wld{i}", [P, D]) for i in range(2)]
        wld = [wld_t[0][:], wld_t[1][:]]
        xsq_t = sbt("xsq1", [P, D])
        xsq = xsq_t[:]
    if stages >= 2:
        vf32_s = sbt("vf32_s", [P, D])
        vbf_s = sbt("vbf_s", [P, D], BF16)
        vf32 = [vf32_s, vf32_s]
        vbf = [vbf_s, vbf_s]
        qlm_b = sbt("qlm_b", [P, NCH * BC * LM], BF16)
        klm_b = sbt("klm_b", [P, NCH * BC * LM], BF16)
        xsq = qlm_b[:].bitcast(F32)[:, 0:D]
        e2 = sbt("e2", [P, LM])
        k2f = [sbt(f"k2f{i}", [P, LM]) for i in range(2)]
        k2bs = [sbt(f"k2bs{i}", [P, LM], BF16) for i in range(2)]
        s2red = sbt("s2red", [P, 4])
    runmax = sbt("runmax", [1, LM])
    pcm = sbt("pcm", [1, 4])

    ps_q = [pst(f"ps_q{i}", [P, 512]) for i in range(4)]
    ps_s2 = [pst(f"ps_s2{i}", [P, LM]) for i in range(2)]
    ps_cs = [pst(f"ps_cs{i}", [1, LM]) for i in range(2)]

    s_ldw = sem("s_ldw")
    s_wbf = sem("s_wbf")
    s_x = sem("s_x")
    s_v = sem("s_v")
    s_a = sem("s_a")
    s_tr = sem("s_tr")
    s_st = sem("s_st")
    s_pe = sem("s_pe")
    s_ev = sem("s_ev")
    s_lm = sem("s_lm")
    s_ps = sem("s_ps")
    s_v2 = sem("s_v2")
    s_s2a = sem("s_s2a")
    s_fin = sem("s_fin")
    s_vch = sem("s_vch")
    s_c = sem("s_c")
    s_lr = sem("s_lr")
    s_lw0 = sem("s_lw0")
    s_lw1 = sem("s_lw1")
    s_ach = sem("s_ach")
    s_ks = sem("s_ks")

    NQK = 2 * NCH * NTB
    NVG = NT * 2
    NG = NQK + NVG

    def sync_body(sync):
        for wi in range(0 if "noW" in cuts else 3):
            w_in = (wqT_in, wkT_in, wvT_in)[wi]
            for c in range(NCH):
                k = wi * NCH + c
                if k >= 2:
                    sync.wait_ge(s_wbf, k - 1)
                sync.dma_start(out=wld[k % 2], in_=w_in[c * P:(c + 1) * P, :]).then_inc(
                    (s_lw0, s_lw1)[k % 2], 16)
        nw = 0 if "noW" in cuts else 24
        if "noW" not in cuts:
            sync.dma_start(out=bqs[:], in_=bqc_in).then_inc(s_ldw, 16)
            sync.dma_start(out=bks[:], in_=bkc_in).then_inc(s_ldw, 16)
            sync.dma_start(out=bvf[:], in_=bvr_in).then_inc(s_ldw, 16)
        sync.dma_start(out=epsc[:], in_=eps_in).then_inc(s_ldw, 16)
        for i in range(nt):
            if i >= NXB:
                sync.wait_ge(s_a, 2 * (i - NXB) + 2)
                sync.dma_start(out=xn_tok_o[(i - NXB) * P:(i - NXB + 1) * P, :],
                               in_=xn_t[(i - NXB) % NXB][:]).then_inc(s_st, 16)
            sync.dma_start(out=x_t[i % NXB][:], in_=x_in[i * P:(i + 1) * P, :]).then_inc(s_x, 16)
        for i in range(nt - NXB, nt):
            sync.wait_ge(s_a, 2 * i + 2)
            sync.dma_start(out=xn_tok_o[i * P:(i + 1) * P, :],
                           in_=xn_t[i % NXB][:]).then_inc(s_st, 16)
        for g in range(16 if stages >= 2 else 0):
            proj, oc = divmod(g, NCH)
            sync.wait_ge(s_ev, 8 * (g + 1))
            dst = (q_fm_o, k_fm_o)[proj]
            sync.dma_start(out=dst[oc * P:(oc + 1) * P, :], in_=pbf[g % 2][:]).then_inc(s_ps, 16)
        for tt in range(0 if ("noV" in cuts or stages < 2) else NT):
            sync.wait_ge(s_ev, NQK + 2 * tt + 2)
            sync.dma_start(out=v_tok_o[tt * P:(tt + 1) * P, :], in_=vbf[tt % 2][:]).then_inc(s_ps, 16)
        sync.wait_ge(s_tr, nt * NCH * 16)
        for c in range(NCH if "noFM" not in cuts else 0):
            sync.dma_start(out=xn_fm_o[c * P:(c + 1) * P, :], in_=xn_fm[c][:]).then_inc(s_fin, 16)
        if stages >= 2:
            sync.wait_ge(s_lm, 16)
        if stages >= 2 and "noLM" not in cuts:
            sync.dma_start(out=qlm_o.rearrange("(c p) l -> p c l", p=P), in_=qlm_b[:].rearrange("p (c l) -> p c l", c=NCH)).then_inc(s_fin, 16)
            sync.dma_start(out=klm_o.rearrange("(c p) l -> p c l", p=P), in_=klm_b[:].rearrange("p (c l) -> p c l", c=NCH)).then_inc(s_fin, 16)
        for m in range(NMAT if stages >= 3 else 0):
            sync.wait_ge(s_v2, 3 * m + 2)
            sync.dma_start(out=k2_o[m * P:(m + 1) * P, :], in_=k2bs[m % 2][:]).then_inc(s_ks, 16)
        if stages >= 3:
            sync.wait_ge(s_v2, 3 * NMAT + 1)
            sync.wait_ge(s_ks, NMAT * 16)
        sync.wait_ge(s_c, 1)
        sync.dma_start(out=pcmax_o, in_=pcm[:]).then_inc(s_fin, 16)
        nfin = ((0 if "noFM" in cuts else NCH) + 1 + (2 if (stages >= 2 and "noLM" not in cuts) else 0)) * 16
        sync.wait_ge(s_fin, nfin)
        if stages >= 2:
            sync.wait_ge(s_ps, (16 if "noV" in cuts else 48) * 16)
        sync.wait_ge(s_st, nt * 16)

    def gpsimd_body(gpsimd):
        gpsimd.memset(ones_row[:], 1.0)
        gpsimd.memset(ones_col[:], 1.0)
        gpsimd.memset(runmax[:], -3.0e38)
        gpsimd.memset(pcm[:], 0.0).then_inc(s_c, 1)

    def vector_body(vector):
        vch = Chain(vector, s_vch)
        for wi in range(0 if "noW" in cuts else 3):
            wtiles = (wq, wk, wv)[wi]
            for c in range(NCH):
                k = wi * NCH + c
                vector.wait_ge((s_lw0, s_lw1)[k % 2], (k // 2 + 1) * 16)
                vector.tensor_copy(out=wtiles[c][:], in_=wld[k % 2]).then_inc(s_wbf, 1)
        if "noW" not in cuts:
            vector.wait_ge(s_ldw, 4 * 16)
            vch.bar(vector.tensor_scalar_mul(out=bqs[:], in0=bqs[:], scalar1=S4))
            vch.bar(vector.tensor_scalar_mul(out=bks[:], in0=bks[:], scalar1=S4))
            if stages >= 2:
                vch.bar(vector.tensor_scalar_mul(out=b16q[:], in0=bqs[:], scalar1=16.0))
                vch.bar(vector.tensor_scalar_mul(out=b16k[:], in0=bks[:], scalar1=16.0))
            vector.tensor_copy(out=bvb[:], in_=bvf[:]).then_inc(s_v, 1)
        else:
            vector.wait_ge(s_ldw, 16)
            vector.sem_inc(s_v, 1)
        for i in range(NT):
            b = i % NXB
            vector.wait_ge(s_x, (i + 1) * 16)
            vch.bar(vector.tensor_reduce(out=stat[:, 0:1], in_=x_t[b][:], axis=AX.X, op=ALU.add))
            vch.bar(vector.tensor_tensor(out=xsq, in0=x_t[b][:], in1=x_t[b][:], op=ALU.mult))
            vch.bar(vector.tensor_reduce(out=stat[:, 1:2], in_=xsq, axis=AX.X, op=ALU.add))
            vch.bar(vector.tensor_scalar_mul(out=stat[:, 2:3], in0=stat[:, 0:1], scalar1=1.0 / D))
            vch.bar(vector.tensor_scalar_mul(out=stat[:, 3:4], in0=stat[:, 1:2], scalar1=1.0 / D))
            vch.bar(vector.tensor_tensor(out=stat[:, 4:5], in0=stat[:, 2:3], in1=stat[:, 2:3], op=ALU.mult))
            vector.tensor_tensor(out=stat[:, 4:5], in0=stat[:, 3:4], in1=stat[:, 4:5],
                                 op=ALU.subtract).then_inc(s_v, 1)
            vector.wait_ge(s_a, 2 * i + 1)
            vector.scalar_tensor_tensor(out=stat[:, 6:7], in0=stat[:, 2:3], scalar=-1.0,
                                        in1=stat[:, 5:6], op0=ALU.mult,
                                        op1=ALU.mult).then_inc(s_v, 1)
            vector.wait_ge(s_a, 2 * i + 2)
            if i >= 2:
                vector.wait_ge(s_tr, (i - 1) * NCH * 16)
            vector.tensor_copy(out=xnb_t[i % 2][:], in_=xn_t[b][:]).then_inc(s_v, 1)
        for k in range(NQK if stages >= 2 else 0):
            g, tb = divmod(k, NTB)
            proj, oc = divmod(g, NCH)
            vector.wait_ge(s_pe, k + 1)
            if "noLM" in cuts:
                vector.sem_inc(s_lr, 1)
            else:
                vch.bar(vector.tensor_reduce(out=lmraw[g % 2][:, tb * 32:(tb + 1) * 32],
                                     in_=ps_q[k % 4][:].rearrange("p (n w) -> p n w", w=SEG),
                                     axis=AX.X, op=ALU.add))
                vector.sem_inc(s_lr, 1)
            if tb == NTB - 1:
                if "noLM" in cuts:
                    vector.sem_inc(s_lm, 1)
                else:
                    dstlm = (qlm_b, klm_b)[proj]
                    b16 = (b16q, b16k)[proj]
                    vector.tensor_scalar(out=dstlm[:, oc * BC * LM:(oc + 1) * BC * LM],
                                         in0=lmraw[g % 2][:], scalar1=S4,
                                         scalar2=b16[:, oc:oc + 1], op0=ALU.mult,
                                         op1=ALU.add).then_inc(s_lm, 1)
        vector.wait_ge(s_c, 1)
        for m in range(NMAT if stages >= 3 else 0):
            vector.wait_ge(s_pe, NG + 2 * m + 1)
            vch.bar(vector.tensor_reduce(out=s2red[:, 0:1], in_=ps_s2[m % 2][:], axis=AX.X, op=ALU.max))
            vector.tensor_scalar_mul(out=s2red[:, 1:2], in0=s2red[:, 0:1],
                                     scalar1=-1.0 / 256.0).then_inc(s_v2, 1)
            vector.wait_ge(s_s2a, m + 1)
            if m >= 2:
                vector.wait_ge(s_pe, NG + 2 * (m - 2) + 2)
            vch.bar(vector.tensor_reduce(out=s2red[:, 2:3], in_=e2[:], axis=AX.X, op=ALU.add))
            vch.bar(vector.reciprocal(out=s2red[:, 3:4], in_=s2red[:, 2:3]))
            vch.bar(vector.tensor_scalar_mul(out=k2f[m % 2][:], in0=e2[:], scalar1=s2red[:, 3:4]))
            if m >= 2:
                vector.wait_ge(s_ks, 16 * (m - 1))
            vector.tensor_copy(out=k2bs[m % 2][:],
                               in_=k2f[m % 2][:]).then_inc(s_v2, 1)
            vector.wait_ge(s_pe, NG + 2 * m + 2)
            vector.tensor_tensor(out=runmax[:], in0=runmax[:], in1=ps_cs[m % 2][0:1, :],
                                 op=ALU.max).then_inc(s_v2, 1)
        if stages >= 3:
            vector.wait_ge(s_v2, 3 * NMAT)
            vector.tensor_reduce(out=pcm[0:1, 0:1], in_=runmax[0:1, :], axis=AX.X,
                                 op=ALU.max).then_inc(s_v2, 1)

    def scalar_body(scalar):
        ach = Chain(scalar, s_ach)
        for i in range(nt):
            b = i % NXB
            scalar.wait_ge(s_v, 1 + 3 * i + 1)
            ach.bar(scalar.activation(out=stat[:, 5:6], in_=stat[:, 4:5], func=AF.Ln, bias=epsc[:, 0:1]))
            scalar.activation(out=stat[:, 5:6], in_=stat[:, 5:6], func=AF.Exp,
                              scale=-0.5).then_inc(s_a, 1)
            scalar.wait_ge(s_v, 1 + 3 * i + 2)
            if i >= NXB:
                scalar.wait_ge(s_st, (i - NXB + 1) * 16)
            scalar.activation(out=xn_t[b][:], in_=x_t[b][:], func=AF.Identity,
                              scale=stat[:, 5:6], bias=stat[:, 6:7]).then_inc(s_a, 1)
            scalar.wait_ge(s_v, 1 + 3 * i + 3)
            for c in range(NCH):
                if "noTR" in cuts:
                    scalar.sem_inc(s_tr, 16)
                else:
                    scalar.dma_start(out=xn_fm[c][:, i * P:(i + 1) * P],
                                     in_=xnb_t[i % 2][:, c * P:(c + 1) * P],
                                     transpose=True).then_inc(s_tr, 16)
        for k in range(NQK if stages >= 2 else 0):
            g, tb = divmod(k, NTB)
            proj, oc = divmod(g, NCH)
            scalar.wait_ge(s_lr, k + 1)
            if tb == 0 and g >= 1:
                scalar.wait_ge(s_ps, 16 * g)
            bias = (bqs, bks)[proj]
            scalar.activation(out=pbf[g % 2][:, tb * 512:(tb + 1) * 512],
                              in_=ps_q[k % 4][:], func=AF.Identity, scale=S4,
                              bias=bias[:, oc:oc + 1]).then_inc(s_ev, 1)
        for kv in range(0 if ("noV" in cuts or stages < 2) else NVG):
            tt, oh = divmod(kv, 2)
            k = NQK + kv
            scalar.wait_ge(s_pe, k + 1)
            if oh == 0 and tt >= 1:
                scalar.wait_ge(s_ps, 16 * (16 + tt))
            scalar.activation(out=vbf[tt % 2][:, oh * 512:(oh + 1) * 512],
                              in_=ps_q[k % 4][:], func=AF.Copy).then_inc(s_ev, 1)
        for m in range(NMAT if stages >= 3 else 0):
            scalar.wait_ge(s_v2, 3 * m + 1)
            scalar.activation(out=e2[:], in_=ps_s2[m % 2][:], func=AF.Exp,
                              scale=1.0 / 256.0, bias=s2red[:, 1:2]).then_inc(s_s2a, 1)

    def tensor_body(tensor):
        if stages < 2:
            return
        tensor.wait_ge(s_tr, NT * NCH * 16)
        tensor.wait_ge(s_wbf, 24)
        tensor.wait_ge(s_v, 1)
        tensor.wait_ge(s_c, 1)
        for k in range(NQK):
            g, tb = divmod(k, NTB)
            proj, oc = divmod(g, NCH)
            w = (wq, wk)[proj]
            if k >= 4:
                tensor.wait_ge(s_ev, k - 3)
            mi = None
            for cc in range(NCH):
                mi = tensor.matmul(ps_q[k % 4][:], w[cc][:, oc * P:(oc + 1) * P],
                                   xn_fm[cc][:, tb * 512:(tb + 1) * 512],
                                   start=(cc == 0), stop=(cc == NCH - 1))
            mi.then_inc(s_pe, 1)
        for kv in range(0 if "noV" in cuts else NVG):
            tt, oh = divmod(kv, 2)
            k = NQK + kv
            tensor.wait_ge(s_ev, k - 3)
            for cc in range(NCH):
                tensor.matmul(ps_q[k % 4][:], xn_fm[cc][:, tt * P:(tt + 1) * P],
                              wv[cc][:, oh * 512:(oh + 1) * 512],
                              start=(cc == 0), stop=False)
            tensor.matmul(ps_q[k % 4][:], ones_row[0:1, :], bvb[0:1, oh * 512:(oh + 1) * 512],
                          start=False, stop=True).then_inc(s_pe, 1)
        if stages < 3:
            return
        tensor.wait_ge(s_lm, 16)
        for m in range(NMAT):
            sq_, h = divmod(m, HEADS)
            cc, half = divmod(h, 2)
            prow = 64 * half
            if m >= 2:
                tensor.wait_ge(s_s2a, m - 1)
            qs = qlm_b[prow:prow + 64, cc * BC * LM + sq_ * LM: cc * BC * LM + (sq_ + 1) * LM]
            ks = klm_b[prow:prow + 64, cc * BC * LM + sq_ * LM: cc * BC * LM + (sq_ + 1) * LM]
            tensor.matmul(ps_s2[m % 2][:], qs, ks, start=True, stop=True).then_inc(s_pe, 1)
            tensor.wait_ge(s_v2, 3 * m + 2)
            tensor.matmul(ps_cs[m % 2][:], ones_col[:], k2f[m % 2][:],
                          start=True, stop=True).then_inc(s_pe, 1)

    es.close()
    return {"sync": sync_body, "gpsimd": gpsimd_body, "vector": vector_body,
            "scalar": scalar_body, "tensor": tensor_body}



ENGINES = ["sync", "vector", "scalar", "tensor", "gpsimd"]


class Fence:
    def __init__(self, counts):
        self.counts = counts


class SP:
    def __init__(self, nc):
        self.nc = nc
        self.ops = []
        self._run = {en: 0 for en in ENGINES}
        self._cum = []
        self._eng = []

    def add(self, eng, fn, dma=False, after=None):
        i = len(self.ops)
        deps = None if after is None else (list(after) if isinstance(after, (list, tuple)) else [after])
        inc = 16 if dma else 1
        self.ops.append((eng, fn, inc, False, deps))
        self._run[eng] += inc
        self._cum.append(self._run[eng])
        self._eng.append(eng)
        return i

    def add_raw(self, eng, fn, inc=1, after=None):
        i = len(self.ops)
        deps = None if after is None else (list(after) if isinstance(after, (list, tuple)) else [after])
        self.ops.append((eng, fn, inc, True, deps))
        self._run[eng] += inc
        self._cum.append(self._run[eng])
        self._eng.append(eng)
        return i

    def fence(self):
        return Fence(dict(self._run))

    def last(self):
        return len(self.ops) - 1


    def new_phase(self):
        self._rk = {}
        self._wk = {}
        self._pf = self.fence()
        return self._pf

    def op(self, eng, fn, reads=(), writes=(), dma=False, extra=()):
        if not hasattr(self, "_rk"):
            self.new_phase()
        if getattr(self, "_serial", False):
            h = self.add(eng, fn, dma=dma, after=[self.last()] if self.ops else [])
            for k in reads:
                self._rk.setdefault(k, []).append(h)
            for k in writes:
                self._wk[k] = h
                self._rk[k] = []
            return h
        deps = list(extra)
        fresh = False
        for k in reads:
            if k in self._wk:
                deps.append(self._wk[k])
        for k in writes:
            if k in self._wk or k in self._rk:
                deps.extend(self._rk.get(k, []))
                if k in self._wk:
                    deps.append(self._wk[k])
            else:
                fresh = True
        if fresh or (not reads and not writes):
            deps.append(self._pf)
        h = self.add(eng, fn, dma=dma, after=deps)
        for k in reads:
            self._rk.setdefault(k, []).append(h)
        for k in writes:
            self._wk[k] = h
            self._rk[k] = []
        return h

    def emit(self, es, prologue=None):
        nc = self.nc
        sems = {en: es.enter_context(nc.semaphore("sp_" + en)) for en in ENGINES}
        s_bar = es.enter_context(nc.semaphore("s_bar"))
        totals = dict(self._run)
        blk = es.enter_context(nc.Block())
        mine = {}
        for i, (e, _, _, _, _) in enumerate(self.ops):
            mine.setdefault(e, []).append(i)

        def mk(ename):
            idxs = mine.get(ename, [])

            def body(eng):
                if prologue is not None:
                    if ename in prologue:
                        prologue[ename](eng)
                    eng.sem_inc(s_bar, 1)
                    eng.wait_ge(s_bar, len(ENGINES))
                for i in idxs:
                    _, fn, inc, raw, deps = self.ops[i]
                    if deps is None:
                        deps = [i - 1] if i > 0 else []
                    need = {}
                    for dp in deps:
                        if isinstance(dp, Fence):
                            for en2, v in dp.counts.items():
                                if v > 0:
                                    need[en2] = max(need.get(en2, 0), v)
                        else:
                            if dp < 0:
                                continue
                            je = self._eng[dp]
                            jinc = self.ops[dp][2]
                            if je == ename and jinc == 1 and je == "tensor":
                                continue
                            need[je] = max(need.get(je, 0), self._cum[dp])
                    for je in ENGINES:
                        if je in need:
                            eng.wait_ge(sems[je], need[je])
                    if raw:
                        fn(eng, sems[ename], inc)
                    else:
                        fn(eng).then_inc(sems[ename], inc)
                if ename == "sync":
                    for en2 in ENGINES:
                        if totals[en2] > 0:
                            eng.wait_ge(sems[en2], totals[en2])
            return body

        for en in ENGINES:
            getattr(blk, en)(mk(en))
        return totals



def emit_C(nc, sp, k2_in, amax_in, eyes_in, identb_in, z_o, zt_o):
    es = contextlib.ExitStack()
    sbt = lambda n, s, dt=F32: es.enter_context(nc.sbuf_tensor(n, s, dt))
    pst = lambda n, s, dt=F32: es.enter_context(nc.psum_tensor(n, s, dt))
    k2m = sbt("k2m", [P, LM], BF16)
    amax = sbt("amax_s", [1, 4])
    gm = sbt("gm", [1, 1])
    rc = sbt("rc", [1, 1])
    rcb = sbt("rcb", [P, 1])
    onesr = sbt("onesr", [1, P])
    eyes = sbt("eyes_s", [P, 3 * P])
    eyesb = sbt("eyesb", [P, 3 * P], BF16)
    identb = sbt("identb_s", [P, P], BF16)
    K2T = sbt("K2T", [P, LM], BF16)
    Vt = sbt("Vt", [P, LM], BF16)
    VTt = sbt("VTt", [P, LM], BF16)
    ATs = sbt("ATs", [P, LM], BF16)
    N2 = sbt("N2", [P, LM], BF16)
    N1 = sbt("N1", [P, LM], BF16)
    Mt = sbt("Mt", [P, LM], BF16)
    ps_a = pst("ps_a", [P, LM])
    ps_b = pst("ps_b", [P, LM])
    ps_t = pst("ps_t", [P, LM], BF16)
    ps_r = pst("ps_r", [P, 1])

    A = sp.add
    A("sync", lambda e: e.dma_start(out=amax[:], in_=amax_in), dma=True, after=[sp.fence()])
    A("sync", lambda e: e.dma_start(out=eyes[:], in_=eyes_in), dma=True)
    A("sync", lambda e: e.dma_start(out=identb[:], in_=identb_in), dma=True)
    A("gpsimd", lambda e: e.memset(onesr[:], 1.0))
    A("vector", lambda e: e.tensor_reduce(out=gm[:], in_=amax[:], axis=AX.X, op=ALU.max))
    A("vector", lambda e: e.reciprocal(out=rc[:], in_=gm[:]))
    A("vector", lambda e: e.tensor_copy(out=eyesb[:], in_=eyes[:]))
    A("tensor", lambda e: e.matmul(ps_r[:], onesr[0:1, :], rc[0:1, :], start=True, stop=True))
    A("scalar", lambda e: e.activation(out=rcb[:], in_=ps_r[:], func=AF.Copy))
    for m in range(NMAT):
        mm = m
        A("sync", lambda e, m=mm: e.dma_start(out=k2m[:], in_=k2_in[m * P:(m + 1) * P, :]), dma=True)
        A("tensor", lambda e: e.transpose(ps_t[:], k2m[:], identb[:]))
        A("scalar", lambda e: e.activation(out=K2T[:], in_=ps_t[:], func=AF.Copy))
        A("scalar", lambda e: e.activation(out=Vt[:], in_=ps_t[:], func=AF.Identity, scale=rcb[:, 0:1]))
        A("vector", lambda e: e.tensor_scalar(out=VTt[:], in0=k2m[:], scalar1=rcb[:, 0:1],
                                              scalar2=None, op0=ALU.mult))
        for it in range(6):
            A("tensor", lambda e: e.matmul(ps_a[:], K2T[:], Vt[:], start=True, stop=True))
            A("tensor", lambda e: e.matmul(ps_b[:], Vt[:], K2T[:], start=True, stop=True))
            A("vector", lambda e: e.scalar_tensor_tensor(out=N2[:], in0=ps_a[:], scalar=-1.0,
                                                         in1=eyesb[:, 0:P], op0=ALU.mult, op1=ALU.add))
            A("scalar", lambda e: e.activation(out=ATs[:], in_=ps_b[:], func=AF.Copy))
            A("tensor", lambda e: e.matmul(ps_a[:], ATs[:], N2[:], start=True, stop=True))
            A("vector", lambda e: e.scalar_tensor_tensor(out=N1[:], in0=ps_a[:], scalar=-1.0,
                                                         in1=eyesb[:, P:2 * P], op0=ALU.mult, op1=ALU.add))
            A("tensor", lambda e: e.matmul(ps_a[:], ATs[:], N1[:], start=True, stop=True))
            A("vector", lambda e: e.scalar_tensor_tensor(out=Mt[:], in0=ps_a[:], scalar=-1.0,
                                                         in1=eyesb[:, 2 * P:3 * P], op0=ALU.mult, op1=ALU.add))
            A("tensor", lambda e: e.matmul(ps_a[:], VTt[:], Mt[:], start=True, stop=True))
            A("tensor", lambda e: e.matmul(ps_b[:], Mt[:], VTt[:], start=True, stop=True))
            A("scalar", lambda e: e.activation(out=Vt[:], in_=ps_a[:], func=AF.Identity, scale=0.25))
            A("scalar", lambda e: e.activation(out=VTt[:], in_=ps_b[:], func=AF.Identity, scale=0.25))
        A("sync", lambda e, m=mm: e.dma_start(out=z_o[m * P:(m + 1) * P, :], in_=Vt[:]), dma=True)
        A("sync", lambda e, m=mm: e.dma_start(out=zt_o[m * P:(m + 1) * P, :], in_=VTt[:]), dma=True)
    es.close()



def emit_D(nc, sp, q_in, k_in, v_in, qlm_in, klm_in, zt_in, identf_in, attn_o):
    es = contextlib.ExitStack()
    sbt = lambda n, s, dt=F32: es.enter_context(nc.sbuf_tensor(n, s, dt))
    pst = lambda n, s, dt=F32: es.enter_context(nc.psum_tensor(n, s, dt))
    qlmt = [sbt(f"qlmt{c}", [P, BC * LM], BF16) for c in range(NCH)]
    klmt = [sbt(f"klmt{c}", [P, BC * LM], BF16) for c in range(NCH)]
    ksl = sbt("ksl", [P, S], BF16)
    qsl = sbt("qsl", [P, S], BF16)
    vsl = sbt("vsl", [P, 16 * HD], BF16)
    ztm = sbt("ztm", [P, LM], BF16)
    s3f = sbt("s3f", [P, S])
    e3 = sbt("e3", [P, S], BF16)
    e3t = es.enter_context(nc.sbuf_tensor("e3t", [P, S], BF16, is_transpose_dest=True))
    e1 = sbt("e1", [P, S], BF16)
    red = sbt("red", [P, 4])
    Us = sbt("Us", [P, HD], BF16)
    Ps = sbt("Ps", [P, HD], BF16)
    r1s = sbt("r1s", [1, S])
    rr1 = sbt("rr1", [1, S])
    bcs = sbt("bcs", [64, 512])
    onescb = sbt("onescb", [P, 1], BF16)
    onesr1 = sbt("onesr1", [1, 64])
    identf = sbt("identf_s", [P, P])
    afm = sbt("afm", [P, T])
    stag = sbt("stag", [P, P])
    ps_m = pst("ps_m", [P, 512])
    ps_u = pst("ps_u", [P, HD])
    ps_x = pst("ps_x", [64, 512])
    ps_bc = pst("ps_bc", [64, 512])
    ps_t = pst("ps_tD", [P, P])
    ps_r1 = pst("ps_r1", [1, 512])

    A = sp.add
    fenD = sp.fence()
    for c in range(NCH):
        A("sync", lambda e, c=c: e.dma_start(out=qlmt[c][:], in_=qlm_in[c * P:(c + 1) * P, :]), dma=True,
          after=([fenD] if c == 0 else None))
        A("sync", lambda e, c=c: e.dma_start(out=klmt[c][:], in_=klm_in[c * P:(c + 1) * P, :]), dma=True)
    A("sync", lambda e: e.dma_start(out=identf[:], in_=identf_in), dma=True)
    A("gpsimd", lambda e: e.memset(onescb[:], 1.0))
    A("gpsimd", lambda e: e.memset(onesr1[:], 1.0))

    for cc in range(NCH):
        for s in range(BC):
            for hh in range(2):
                h = 2 * cc + hh
                m = s * HEADS + h
                A("sync", lambda e, cc=cc, hh=hh, s=s: e.dma_start(
                    out=ksl[64 * hh:64 * hh + 64, :],
                    in_=k_in[cc * P + 64 * hh: cc * P + 64 * hh + 64,
                             s * S:(s + 1) * S]), dma=True)
                A("sync", lambda e, cc=cc, hh=hh, s=s: e.dma_start(
                    out=qsl[64 * hh:64 * hh + 64, :],
                    in_=q_in[cc * P + 64 * hh: cc * P + 64 * hh + 64,
                             s * S:(s + 1) * S]), dma=True)
                for j in range(16):
                    A("sync", lambda e, s=s, j=j, h=h: e.dma_start(
                        out=vsl[:, j * HD:(j + 1) * HD],
                        in_=v_in[s * S + j * P: s * S + (j + 1) * P,
                                 h * HD:(h + 1) * HD]), dma=True)
                A("sync", lambda e, m=m: e.dma_start(out=ztm[:], in_=zt_in[m * P:(m + 1) * P, :]), dma=True)
                for b in range(4):
                    A("tensor", lambda e, cc=cc, hh=hh, s=s, b=b: e.matmul(
                        ps_m[:], qlmt[cc][64 * hh:64 * hh + 64, s * LM:(s + 1) * LM],
                        ksl[64 * hh:64 * hh + 64, b * 512:(b + 1) * 512], start=True, stop=True))
                    A("scalar", lambda e, b=b: e.activation(out=s3f[:, b * 512:(b + 1) * 512],
                                                            in_=ps_m[:], func=AF.Copy))
                A("vector", lambda e: e.tensor_reduce(out=red[:, 0:1], in_=s3f[:], axis=AX.X, op=ALU.max))
                A("vector", lambda e: e.tensor_scalar_mul(out=red[:, 1:2], in0=red[:, 0:1],
                                                          scalar1=-1.0 / SEG))
                A("scalar", lambda e: e.activation(out=e3[:], in_=s3f[:], func=AF.Exp,
                                                   scale=1.0 / SEG, bias=red[:, 1:2]))
                A("vector", lambda e: e.tensor_reduce(out=red[:, 2:3], in_=e3[:], axis=AX.X, op=ALU.add))
                A("vector", lambda e: e.reciprocal(out=red[:, 3:4], in_=red[:, 2:3]))
                for j in range(16):
                    A("scalar", lambda e, j=j: e.dma_start(out=e3t[:, j * P:(j + 1) * P],
                                                           in_=e3[:, j * P:(j + 1) * P],
                                                           transpose=True), dma=True)
                for j in range(16):
                    A("tensor", lambda e, j=j: e.matmul(ps_u[:], e3t[:, j * P:(j + 1) * P],
                                                        vsl[:, j * HD:(j + 1) * HD],
                                                        start=(j == 0), stop=(j == 15)))
                A("vector", lambda e: e.tensor_scalar(out=Us[:], in0=ps_u[:], scalar1=red[:, 3:4],
                                                      scalar2=None, op0=ALU.mult))
                A("tensor", lambda e: e.matmul(ps_u[:], ztm[:], Us[:], start=True, stop=True))
                A("scalar", lambda e: e.activation(out=Ps[:], in_=ps_u[:], func=AF.Copy))
                for b in range(4):
                    A("tensor", lambda e, cc=cc, hh=hh, s=s, b=b: e.matmul(
                        ps_m[:], klmt[cc][64 * hh:64 * hh + 64, s * LM:(s + 1) * LM],
                        qsl[64 * hh:64 * hh + 64, b * 512:(b + 1) * 512], start=True, stop=True))
                    A("scalar", lambda e, b=b: e.activation(out=e1[:, b * 512:(b + 1) * 512],
                                                            in_=ps_m[:], func=AF.Exp, scale=1.0 / SEG))
                for b in range(4):
                    A("tensor", lambda e, b=b: e.matmul(ps_r1[:], onescb[:], e1[:, b * 512:(b + 1) * 512],
                                                        start=True, stop=True))
                    A("scalar", lambda e, b=b: e.activation(out=r1s[0:1, b * 512:(b + 1) * 512],
                                                            in_=ps_r1[:], func=AF.Copy))
                A("vector", lambda e: e.reciprocal(out=rr1[:], in_=r1s[:]))
                for b in range(4):
                    A("tensor", lambda e, b=b: e.matmul(ps_x[:], Ps[:], e1[:, b * 512:(b + 1) * 512],
                                                        start=True, stop=True))
                    A("tensor", lambda e, b=b: e.matmul(ps_bc[:], onesr1[0:1, :],
                                                        rr1[0:1, b * 512:(b + 1) * 512],
                                                        start=True, stop=True))
                    A("scalar", lambda e, b=b: e.activation(out=bcs[:], in_=ps_bc[:], func=AF.Copy))
                    A("vector", lambda e, hh=hh, s=s, b=b: e.tensor_tensor(
                        out=afm[64 * hh:64 * hh + 64, s * S + b * 512: s * S + (b + 1) * 512],
                        in0=ps_x[:], in1=bcs[:], op=ALU.mult))
        for tt in range(NT):
            A("tensor", lambda e, tt=tt: e.transpose(ps_t[:], afm[:, tt * P:(tt + 1) * P], identf[:]))
            A("scalar", lambda e: e.activation(out=stag[:], in_=ps_t[:], func=AF.Copy))
            A("sync", lambda e, tt=tt, cc=cc: e.dma_start(
                out=attn_o[tt * P:(tt + 1) * P, cc * P:(cc + 1) * P], in_=stag[:]), dma=True)
    es.close()



def emit_quant(Aop, x_ap, q_ap, scr, C, pp=P):
    nb = C // 16
    bm, sc, rs, hs = scr["bm"], scr["sc"], scr["rs"], scr["hs"]
    y, a, t1, t2, t3 = scr["y"], scr["a"], scr["t1"], scr["t2"], scr["t3"]
    ri1, ri2 = scr["ri1"], scr["ri2"]
    x3 = x_ap.rearrange("p (n w) -> p n w", w=16)
    Aop("vector", lambda e: e.tensor_reduce(out=bm[:pp, :nb], in_=x3, axis=AX.X,
                                            op=ALU.max, apply_absolute_value=True))
    Aop("vector", lambda e: e.tensor_scalar(out=sc[:pp, :nb], in0=bm[:pp, :nb], scalar1=1.0 / 6.0,
                                            scalar2=1e-12, op0=ALU.mult, op1=ALU.max))
    Aop("vector", lambda e: e.reciprocal(out=rs[:pp, :nb], in_=sc[:pp, :nb]))
    rs3 = rs[:pp, :nb].unsqueeze(2).broadcast_to([pp, nb, 16])
    y3 = y[:pp, :C].rearrange("p (n w) -> p n w", w=16)
    Aop("vector", lambda e: e.tensor_tensor(out=y3, in0=x3, in1=rs3, op=ALU.mult))
    Aop("vector", lambda e: e.tensor_scalar(out=a[:pp, :C].bitcast(mybir.dt.int32),
                                            in0=y[:pp, :C].bitcast(mybir.dt.int32),
                                            scalar1=0x7fffffff, scalar2=None,
                                            op0=ALU.bitwise_and))
    Aop("vector", lambda e: e.tensor_scalar_mul(out=ri1[:pp, :C], in0=a[:pp, :C], scalar1=2.0))
    Aop("vector", lambda e: e.tensor_copy(out=ri2[:pp, :C], in_=a[:pp, :C]))
    Aop("vector", lambda e: e.tensor_scalar(out=t1[:pp, :C], in0=ri1[:pp, :C], scalar1=0.5,
                                            scalar2=2.0, op0=ALU.mult, op1=ALU.min))
    Aop("vector", lambda e: e.tensor_scalar(out=t2[:pp, :C], in0=ri2[:pp, :C], scalar1=2.0,
                                            scalar2=4.0, op0=ALU.max, op1=ALU.min))
    Aop("vector", lambda e: e.tensor_tensor(out=t1[:pp, :C], in0=t1[:pp, :C], in1=t2[:pp, :C],
                                            op=ALU.add))
    Aop("vector", lambda e: e.tensor_scalar(out=t2[:pp, :C], in0=a[:pp, :C], scalar1=5.0,
                                            scalar2=2.0, op0=ALU.is_gt, op1=ALU.mult))
    Aop("vector", lambda e: e.scalar_tensor_tensor(out=t1[:pp, :C], in0=t1[:pp, :C], scalar=-2.0,
                                                   in1=t2[:pp, :C], op0=ALU.add, op1=ALU.add))
    Aop("scalar", lambda e: e.activation(out=t3[:pp, :C], in_=y[:pp, :C], func=AF.Sign))
    Aop("vector", lambda e: e.tensor_tensor(out=t1[:pp, :C], in0=t1[:pp, :C], in1=t3[:pp, :C],
                                            op=ALU.mult))
    Aop("vector", lambda e: e.tensor_scalar_mul(out=hs[:pp, :nb], in0=sc[:pp, :nb], scalar1=1.0))
    hs3 = hs[:pp, :nb].unsqueeze(2).broadcast_to([pp, nb, 16])
    t13 = t1[:pp, :C].rearrange("p (n w) -> p n w", w=16)
    q3 = q_ap.rearrange("p (n w) -> p n w", w=16)
    Aop("vector", lambda e: e.tensor_tensor(out=q3, in0=t13, in1=hs3, op=ALU.mult))


def quant_scratch(sbt, tag, C, dt_mid=F32):
    nb = C // 16
    return {
        "bm": sbt(f"bm{tag}", [P, nb]), "sc": sbt(f"sc{tag}", [P, nb]),
        "rs": sbt(f"rs{tag}", [P, nb]), "hs": sbt(f"hs{tag}", [P, nb]),
        "y": sbt(f"yq{tag}", [P, C]), "a": sbt(f"aq{tag}", [P, C]),
        "t1": sbt(f"t1q{tag}", [P, C], BF16), "t2": sbt(f"t2q{tag}", [P, C], BF16),
        "t3": sbt(f"t3q{tag}", [P, C], BF16),
        "ri1": sbt(f"ri1q{tag}", [P, C], I16), "ri2": sbt(f"ri2q{tag}", [P, C], I16),
    }



def emit_WQfull(nc, sp, tag, wg_in, wu_in, wd_in, wgq_o, wuq_o, wdq_o):
    es = contextlib.ExitStack()
    sbt = lambda n, s, dt=F32: es.enter_context(nc.sbuf_tensor(n + tag, s, dt))
    wt = sbt("wt", [P, I_FFN])
    qt = sbt("qt", [P, I_FFN], BF16)
    qtt = es.enter_context(nc.sbuf_tensor("qtt" + tag, [P, I_FFN], BF16, is_transpose_dest=True))
    scr = quant_scratch(sbt, "w", I_FFN)
    A = sp.add
    fen = sp.fence()
    first = [True]
    jobs = []
    for r in range(I_FFN // P):
        jobs.append((wg_in[r * P:(r + 1) * P, :], wgq_o, r, D))
        jobs.append((wu_in[r * P:(r + 1) * P, :], wuq_o, r, D))
    for r in range(NCH):
        jobs.append((wd_in[r * P:(r + 1) * P, :], wdq_o, r, I_FFN))
    for (src_ap, dst, r, C) in jobs:
        A("sync", lambda e, src_ap=src_ap, C=C: e.dma_start(out=wt[:, 0:C], in_=src_ap), dma=True,
          after=([fen] if first[0] else None))
        first[0] = False
        emit_quant(A, wt[:, 0:C], qt[:, 0:C], scr, C)
        for j in range(C // P):
            A("scalar", lambda e, j=j: e.dma_start(out=qtt[:, j * P:(j + 1) * P],
                                                   in_=qt[:, j * P:(j + 1) * P],
                                                   transpose=True), dma=True)
        for j in range(C // P):
            A("sync", lambda e, dst=dst, r=r, j=j: e.dma_start(
                out=dst[j * P:(j + 1) * P, r * P:(r + 1) * P],
                in_=qtt[:, j * P:(j + 1) * P]), dma=True)
    es.close()



def emit_FFN(nc, sp, tag, xn_in, wg_in, wu_in, wd_in, ffn_o, hq_sc):
    es = contextlib.ExitStack()
    sbt = lambda n, s, dt=F32: es.enter_context(nc.sbuf_tensor(n + tag, s, dt))
    pst = lambda n, s, dt=F32: es.enter_context(nc.psum_tensor(n + tag, s, dt))
    wpool = sbt("wpool", [P, 2 * NCH * I_FFN], BF16)
    wgq = [wpool[:, c * I_FFN:(c + 1) * I_FFN] for c in range(NCH)]
    wuq = [wpool[:, (NCH + c) * I_FFN:(NCH + c + 1) * I_FFN] for c in range(NCH)]
    wdq = [wpool[:, j * D:(j + 1) * D] for j in range(22)]
    xt = sbt("xt", [P, D])
    xq = sbt("xq", [P, D], BF16)
    xqf = es.enter_context(nc.sbuf_tensor("xqf" + tag, [P, D], BF16, is_transpose_dest=True))
    gact = sbt("gact", [P, I_FFN])
    uact = sbt("uact", [P, I_FFN])
    hact = sbt("hact", [P, I_FFN])
    hqt = sbt("hqt", [P, I_FFN], BF16)
    hqtt = es.enter_context(nc.sbuf_tensor("hqtt" + tag, [P, I_FFN], BF16, is_transpose_dest=True))
    scr = quant_scratch(sbt, "f", I_FFN)
    hload = sbt("hload", [P, P], BF16)
    ot = sbt("ot", [P, D])
    ps = [pst(f"psF{i}", [P, 512]) for i in range(2)]

    IB = [(0, 512), (512, 1024), (1024, 1536), (1536, 2048), (2048, 2560), (2560, 2816)]
    A = sp.add
    fenF = sp.fence()
    for c in range(NCH):
        A("sync", lambda e, c=c: e.dma_start(out=wgq[c], in_=wg_in[c * P:(c + 1) * P, :]), dma=True,
          after=([fenF] if c == 0 else None))
        A("sync", lambda e, c=c: e.dma_start(out=wuq[c], in_=wu_in[c * P:(c + 1) * P, :]), dma=True)
    for tt in range(NT):
        A("sync", lambda e, tt=tt: e.dma_start(out=xt[:], in_=xn_in[tt * P:(tt + 1) * P, :]), dma=True)
        emit_quant(A, xt[:], xq[:], scr, D)
        for j in range(NCH):
            A("scalar", lambda e, j=j: e.dma_start(out=xqf[:, j * P:(j + 1) * P],
                                                   in_=xq[:, j * P:(j + 1) * P], transpose=True), dma=True)
        for wsel in range(2):
            wts = (wgq, wuq)[wsel]
            dstact = (gact, uact)[wsel]
            for (lo, hi) in IB:
                n = hi - lo
                for cc in range(NCH):
                    A("tensor", lambda e, cc=cc, lo=lo, hi=hi, wts=wts, n=n: e.matmul(
                        ps[0][:, 0:n], xqf[:, cc * P:(cc + 1) * P], wts[cc][:, lo:hi],
                        start=(cc == 0), stop=(cc == NCH - 1)))
                if wsel == 0:
                    A("scalar", lambda e, lo=lo, hi=hi, n=n, dstact=dstact: e.activation(
                        out=dstact[:, lo:hi], in_=ps[0][:, 0:n], func=AF.Silu))
                else:
                    A("scalar", lambda e, lo=lo, hi=hi, n=n, dstact=dstact: e.activation(
                        out=dstact[:, lo:hi], in_=ps[0][:, 0:n], func=AF.Copy))
        A("vector", lambda e: e.tensor_tensor(out=hact[:], in0=gact[:], in1=uact[:], op=ALU.mult))
        emit_quant(A, hact[:], hqt[:], scr, I_FFN)
        for j in range(22):
            A("scalar", lambda e, j=j: e.dma_start(out=hqtt[:, j * P:(j + 1) * P],
                                                   in_=hqt[:, j * P:(j + 1) * P], transpose=True), dma=True)
        for j in range(22):
            A("sync", lambda e, j=j, tt=tt: e.dma_start(
                out=hq_sc[j * P:(j + 1) * P, tt * P:(tt + 1) * P],
                in_=hqtt[:, j * P:(j + 1) * P]), dma=True)
    for j in range(22):
        A("sync", lambda e, j=j: e.dma_start(out=wdq[j], in_=wd_in[j * P:(j + 1) * P, :]), dma=True)
    for tt in range(NT):
        for half in range(2):
            for j in range(22):
                A("sync", lambda e, j=j, tt=tt: e.dma_start(
                    out=hload[:], in_=hq_sc[j * P:(j + 1) * P, tt * P:(tt + 1) * P]), dma=True)
                A("tensor", lambda e, j=j, half=half: e.matmul(
                    ps[half][:], hload[:], wdq[j][:, half * 512:(half + 1) * 512],
                    start=(j == 0), stop=(j == 21)))
            A("scalar", lambda e, half=half: e.activation(out=ot[:, half * 512:(half + 1) * 512],
                                                          in_=ps[half][:], func=AF.Copy))
        A("sync", lambda e, tt=tt: e.dma_start(out=ffn_o[tt * P:(tt + 1) * P, :], in_=ot[:]), dma=True)
    es.close()



def emit_F(nc, sp, x_in, at_in, ff_in, eps_in, x1_o, xn1_o):
    es = contextlib.ExitStack()
    sbt = lambda n, s, dt=F32: es.enter_context(nc.sbuf_tensor(n, s, dt))
    xa = [sbt(f"xaF{b}", [P, D]) for b in range(2)]
    xb = [sbt(f"xbF{b}", [P, D]) for b in range(2)]
    xc = [sbt(f"xcF{b}", [P, D]) for b in range(2)]
    xn = [sbt(f"xnF{b}", [P, D]) for b in range(2)]
    xsq = sbt("xsqF", [P, D])
    stat = [sbt(f"statF{b}", [P, 8]) for b in range(2)]
    epsc = sbt("epscF", [P, 1])
    sp.new_phase()
    sp._serial = "F" in _G_SERIAL
    O = sp.op
    O("sync", lambda e: e.dma_start(out=epsc[:], in_=eps_in), writes=["epsF"], dma=True)
    for i in range(NT):
        b = i % 2
        O("sync", lambda e, i=i, b=b: e.dma_start(out=xa[b][:], in_=x_in[i * P:(i + 1) * P, :]),
          writes=[f"Fa{b}"], dma=True)
        O("sync", lambda e, i=i, b=b: e.dma_start(out=xb[b][:], in_=at_in[i * P:(i + 1) * P, :]),
          writes=[f"Fb{b}"], dma=True)
        O("sync", lambda e, i=i, b=b: e.dma_start(out=xc[b][:], in_=ff_in[i * P:(i + 1) * P, :]),
          writes=[f"Fc{b}"], dma=True)
        O("vector", lambda e, b=b: e.tensor_tensor(out=xa[b][:], in0=xa[b][:], in1=xb[b][:], op=ALU.add),
          reads=[f"Fa{b}", f"Fb{b}"], writes=[f"Fa{b}"])
        O("vector", lambda e, b=b: e.tensor_tensor(out=xa[b][:], in0=xa[b][:], in1=xc[b][:], op=ALU.add),
          reads=[f"Fa{b}", f"Fc{b}"], writes=[f"Fa{b}"])
        O("sync", lambda e, i=i, b=b: e.dma_start(out=x1_o[i * P:(i + 1) * P, :], in_=xa[b][:]),
          reads=[f"Fa{b}"], dma=True)
        sk = f"Fs{b}"
        O("vector", lambda e, b=b: e.tensor_reduce(out=stat[b][:, 0:1], in_=xa[b][:], axis=AX.X, op=ALU.add),
          reads=[f"Fa{b}"], writes=[sk])
        O("vector", lambda e, b=b: e.tensor_tensor(out=xsq[:], in0=xa[b][:], in1=xa[b][:], op=ALU.mult),
          reads=[f"Fa{b}"], writes=["Fq"])
        O("vector", lambda e, b=b: e.tensor_reduce(out=stat[b][:, 1:2], in_=xsq[:], axis=AX.X, op=ALU.add),
          reads=["Fq"], writes=[sk])
        O("vector", lambda e, b=b: e.tensor_scalar_mul(out=stat[b][:, 2:3], in0=stat[b][:, 0:1], scalar1=1.0 / D),
          reads=[sk], writes=[sk])
        O("vector", lambda e, b=b: e.tensor_scalar_mul(out=stat[b][:, 3:4], in0=stat[b][:, 1:2], scalar1=1.0 / D),
          reads=[sk], writes=[sk])
        O("vector", lambda e, b=b: e.tensor_tensor(out=stat[b][:, 4:5], in0=stat[b][:, 2:3], in1=stat[b][:, 2:3], op=ALU.mult),
          reads=[sk], writes=[sk])
        O("vector", lambda e, b=b: e.tensor_tensor(out=stat[b][:, 4:5], in0=stat[b][:, 3:4], in1=stat[b][:, 4:5], op=ALU.subtract),
          reads=[sk], writes=[sk])
        O("scalar", lambda e, b=b: e.activation(out=stat[b][:, 5:6], in_=stat[b][:, 4:5], func=AF.Ln, bias=epsc[:, 0:1]),
          reads=[sk, "epsF"], writes=[sk])
        O("scalar", lambda e, b=b: e.activation(out=stat[b][:, 5:6], in_=stat[b][:, 5:6], func=AF.Exp, scale=-0.5),
          reads=[sk], writes=[sk])
        O("vector", lambda e, b=b: e.scalar_tensor_tensor(out=stat[b][:, 6:7], in0=stat[b][:, 2:3], scalar=-1.0,
                                                          in1=stat[b][:, 5:6], op0=ALU.mult, op1=ALU.mult),
          reads=[sk], writes=[sk])
        O("scalar", lambda e, b=b: e.activation(out=xn[b][:], in_=xa[b][:], func=AF.Identity,
                                                scale=stat[b][:, 5:6], bias=stat[b][:, 6:7]),
          reads=[f"Fa{b}", sk], writes=[f"Fn{b}"])
        O("sync", lambda e, i=i, b=b: e.dma_start(out=xn1_o[i * P:(i + 1) * P, :], in_=xn[b][:]),
          reads=[f"Fn{b}"], dma=True)
    sp._serial = False
    es.close()



def emit_I(nc, sp, a_in, b_in, c_in, o_out):
    es = contextlib.ExitStack()
    sbt = lambda n, s, dt=F32: es.enter_context(nc.sbuf_tensor(n, s, dt))
    xa = [sbt(f"xaI{b}", [P, D]) for b in range(2)]
    xb = [sbt(f"xbI{b}", [P, D]) for b in range(2)]
    xc = [sbt(f"xcI{b}", [P, D]) for b in range(2)]
    sp.new_phase()
    sp._serial = "I" in _G_SERIAL
    O = sp.op
    for i in range(NT):
        b = i % 2
        O("sync", lambda e, i=i, b=b: e.dma_start(out=xa[b][:], in_=a_in[i * P:(i + 1) * P, :]),
          writes=[f"Ia{b}"], dma=True)
        O("sync", lambda e, i=i, b=b: e.dma_start(out=xb[b][:], in_=b_in[i * P:(i + 1) * P, :]),
          writes=[f"Ib{b}"], dma=True)
        O("sync", lambda e, i=i, b=b: e.dma_start(out=xc[b][:], in_=c_in[i * P:(i + 1) * P, :]),
          writes=[f"Ic{b}"], dma=True)
        O("vector", lambda e, b=b: e.tensor_tensor(out=xa[b][:], in0=xa[b][:], in1=xb[b][:], op=ALU.add),
          reads=[f"Ia{b}", f"Ib{b}"], writes=[f"Ia{b}"])
        O("vector", lambda e, b=b: e.tensor_tensor(out=xa[b][:], in0=xa[b][:], in1=xc[b][:], op=ALU.add),
          reads=[f"Ia{b}", f"Ic{b}"], writes=[f"Ia{b}"])
        O("sync", lambda e, i=i, b=b: e.dma_start(out=o_out[i * P:(i + 1) * P, :], in_=xa[b][:]),
          reads=[f"Ia{b}"], dma=True)
    sp._serial = False
    es.close()



def emit_G(nc, sp, xn1_in, ln_b, wkT_in, wvT_in, wrT_in, woT_in, wfkT_in,
           wfvT_in, wfrT_in, tdc_in, tfc_in, eps_in, attn1_o,
           a_sc, x2_sc, xk_sc, xv_sc, xr_sc, xr2_sc, sa_sc, kfq_sc):
    es = contextlib.ExitStack()
    sbt = lambda n, s, dt=F32: es.enter_context(nc.sbuf_tensor(n, s, dt))
    pst = lambda n, s, dt=F32: es.enter_context(nc.psum_tensor(n, s, dt))
    POOLN = 34000
    pool = sbt("poolG", [P, POOLN])
    xk2_fm = [sbt(f"xk2fm{c}", [P, T], BF16) for c in range(NCH)]
    trd = [es.enter_context(nc.sbuf_tensor(f"trdG{b}", [P, D], BF16, is_transpose_dest=True))
           for b in range(2)]
    stat = [sbt(f"statG{b}", [P, 8]) for b in range(2)]
    epsc = sbt("epscG", [P, 1])
    smallc = [sbt(f"smallcG{b}", [P, 8]) for b in range(2)]
    tdc = sbt("tdcG", [P, NCH]); tfc = sbt("tfcG", [P, NCH])
    hlast = sbt("hlastG", [1, D])
    ps = [pst(f"psG{i}", [P, 512]) for i in range(4)]
    O = sp.op

    class Carve:
        def __init__(self):
            self.off = 0
        def f32(self, n):
            v = pool[:, self.off:self.off + n]
            self.off += n
            assert self.off <= POOLN
            return v
        def bf(self, n):
            v = pool[:, self.off:self.off + (n + 1) // 2].bitcast(BF16)
            self.off += (n + 1) // 2
            return v

    def ln(xa, xk, dst, dk, sb, sk, xsq, qk, wap=None, wk2=None, bap=None, bk2=None):
        O("vector", lambda e: e.tensor_reduce(out=sb[:, 0:1], in_=xa, axis=AX.X, op=ALU.add),
          reads=[xk], writes=[sk])
        O("vector", lambda e: e.tensor_tensor(out=xsq, in0=xa, in1=xa, op=ALU.mult),
          reads=[xk], writes=[qk])
        O("vector", lambda e: e.tensor_reduce(out=sb[:, 1:2], in_=xsq, axis=AX.X, op=ALU.add),
          reads=[qk], writes=[sk])
        O("vector", lambda e: e.tensor_scalar_mul(out=sb[:, 2:3], in0=sb[:, 0:1], scalar1=1.0 / D),
          reads=[sk], writes=[sk])
        O("vector", lambda e: e.tensor_scalar_mul(out=sb[:, 3:4], in0=sb[:, 1:2], scalar1=1.0 / D),
          reads=[sk], writes=[sk])
        O("vector", lambda e: e.tensor_tensor(out=sb[:, 4:5], in0=sb[:, 2:3], in1=sb[:, 2:3], op=ALU.mult),
          reads=[sk], writes=[sk])
        O("vector", lambda e: e.tensor_tensor(out=sb[:, 4:5], in0=sb[:, 3:4], in1=sb[:, 4:5], op=ALU.subtract),
          reads=[sk], writes=[sk])
        O("scalar", lambda e: e.activation(out=sb[:, 5:6], in_=sb[:, 4:5], func=AF.Ln, bias=epsc[:, 0:1]),
          reads=[sk, "eps"], writes=[sk])
        O("scalar", lambda e: e.activation(out=sb[:, 5:6], in_=sb[:, 5:6], func=AF.Exp, scale=-0.5),
          reads=[sk], writes=[sk])
        O("vector", lambda e: e.scalar_tensor_tensor(out=sb[:, 6:7], in0=sb[:, 2:3], scalar=-1.0,
                                                     in1=sb[:, 5:6], op0=ALU.mult, op1=ALU.mult),
          reads=[sk], writes=[sk])
        O("scalar", lambda e: e.activation(out=dst, in_=xa, func=AF.Identity,
                                           scale=sb[:, 5:6], bias=sb[:, 6:7]),
          reads=[xk, sk], writes=[dk])
        if wap is not None:
            O("vector", lambda e: e.tensor_tensor(out=dst, in0=dst, in1=wap, op=ALU.mult),
              reads=[dk, wk2], writes=[dk])
            O("vector", lambda e: e.tensor_tensor(out=dst, in0=dst, in1=bap, op=ALU.add),
              reads=[dk, bk2], writes=[dk])

    sp.new_phase()
    sp._serial = 1 in _G_SERIAL
    O("sync", lambda e: e.dma_start(out=epsc[:], in_=eps_in), writes=["eps"], dma=True)
    O("sync", lambda e: e.dma_start(out=tdc[:], in_=tdc_in), writes=["tdcc"], dma=True)
    O("sync", lambda e: e.dma_start(out=tfc[:], in_=tfc_in), writes=["tfcc"], dma=True)
    cv = Carve()
    lnt = {n: cv.f32(D) for n in ("ln0w", "ln0b", "ln1w", "ln1b", "tmkb", "tmvb", "tmrb")}
    for n in lnt:
        O("sync", lambda e, n=n: e.dma_start(out=lnt[n], in_=ln_b[n]), writes=["Lw_" + n], dma=True)
    xt = [cv.f32(D) for _ in range(2)]; at = [cv.f32(D) for _ in range(2)]
    ht = [cv.f32(D) for _ in range(2)]; xxt = [cv.f32(D) for _ in range(2)]
    dt_ = [cv.f32(D) for _ in range(2)]; mixt = [cv.f32(D) for _ in range(2)]
    xsq1 = [cv.f32(D) for _ in range(2)]; mixb = [cv.bf(D) for _ in range(2)]
    for i in range(NT):
        b = i % 2
        O("sync", lambda e, i=i, b=b: e.dma_start(out=xt[b], in_=xn1_in[i * P:(i + 1) * P, :]),
          writes=[f"xt{b}"], dma=True)
        ln(xt[b], f"xt{b}", at[b], f"at{b}", stat[b], f"st{b}", xsq1[b], f"xq{b}",
           lnt["ln0w"], "Lw_ln0w", lnt["ln0b"], "Lw_ln0b")
        O("sync", lambda e, i=i, b=b: e.dma_start(out=a_sc[i * P:(i + 1) * P, :], in_=at[b]),
          reads=[f"at{b}"], dma=True)
        ln(at[b], f"at{b}", ht[b], f"ht{b}", stat[b], f"st{b}", xsq1[b], f"xq{b}",
           lnt["ln1w"], "Lw_ln1w", lnt["ln1b"], "Lw_ln1b")
        if i % 16 == 0:
            O("gpsimd", lambda e, b=b: e.memset(xxt[b][0:1, 0:D], 0.0), writes=[f"xx{b}"])
        else:
            O("sync", lambda e, b=b: e.dma_start(out=xxt[b][0:1, 0:D], in_=hlast[:]),
              reads=["hlast"], writes=[f"xx{b}"], dma=True)
        O("sync", lambda e, b=b: e.dma_start(out=xxt[b][1:P, 0:D], in_=ht[b][0:P - 1, 0:D]),
          reads=[f"ht{b}"], writes=[f"xx{b}"], dma=True)
        O("sync", lambda e, b=b: e.dma_start(out=hlast[:], in_=ht[b][P - 1:P, 0:D]),
          reads=[f"ht{b}"], writes=["hlast"], dma=True)
        O("vector", lambda e, b=b: e.tensor_tensor(out=dt_[b], in0=ht[b], in1=xxt[b], op=ALU.subtract),
          reads=[f"ht{b}", f"xx{b}"], writes=[f"dt{b}"])
        for mi, (mn, dst_sc) in enumerate((("tmkb", xk_sc), ("tmvb", xv_sc), ("tmrb", xr_sc))):
            tb2 = (3 * i + mi) % 2
            O("vector", lambda e, b=b, mn=mn: e.tensor_tensor(out=mixt[b], in0=dt_[b], in1=lnt[mn], op=ALU.mult),
              reads=[f"dt{b}", "Lw_" + mn], writes=[f"mx{b}"])
            O("vector", lambda e, b=b: e.tensor_tensor(out=mixt[b], in0=mixt[b], in1=xxt[b], op=ALU.add),
              reads=[f"mx{b}", f"xx{b}"], writes=[f"mx{b}"])
            O("vector", lambda e, b=b: e.tensor_copy(out=mixb[b], in_=mixt[b]),
              reads=[f"mx{b}"], writes=[f"mb{b}"])
            for c in range(NCH):
                O("scalar", lambda e, b=b, c=c, tb2=tb2: e.dma_start(
                    out=trd[tb2][:, c * P:(c + 1) * P], in_=mixb[b][:, c * P:(c + 1) * P],
                    transpose=True),
                  reads=[f"mb{b}"], writes=[f"tr{tb2}_{c}"], dma=True)
            O("sync", lambda e, i=i, tb2=tb2, dst_sc=dst_sc: e.dma_start(
                out=dst_sc.rearrange("(c p) t -> p c t", p=P)[:, :, i * P:(i + 1) * P],
                in_=trd[tb2][:].rearrange("p (c q) -> p c q", c=NCH)),
              reads=[f"tr{tb2}_{c}" for c in range(NCH)], dma=True)

    sp.new_phase()
    sp._serial = 2 in _G_SERIAL
    cv = Carve()
    ko = cv.f32(S); vo = cv.f32(S); sro = cv.f32(S)
    eko = cv.f32(S); ekvo = cv.f32(S)
    Asc = cv.f32(S + 1); Bsc = cv.f32(S + 1)
    numt = cv.f32(S); dent = cv.f32(S)
    saob = cv.bf(S)
    wldS = [cv.f32(3 * D) for _ in range(2)]
    wblkb = [cv.bf(3 * D) for _ in range(2)]
    rhsB = [cv.bf(NCH * 512) for _ in range(2)]
    dslc = {"ko": ko, "vo": vo, "sr": sro}
    for oc in range(NCH):
        b = oc % 2
        for pi, w_in in enumerate((wkT_in, wvT_in, wrT_in)):
            O("sync", lambda e, w_in=w_in, oc=oc, pi=pi, b=b: e.dma_start(
                out=wldS[b][:, pi * D:(pi + 1) * D].rearrange("p (c q) -> p c q", c=NCH),
                in_=w_in.rearrange("(c p) d -> p c d", p=P)[:, :, oc * P:(oc + 1) * P]),
              writes=[f"wldS{b}_{pi}"], dma=True)
        O("vector", lambda e, b=b: e.tensor_copy(out=wblkb[b], in_=wldS[b]),
          reads=[f"wldS{b}_{pi}" for pi in range(3)], writes=[f"wbl{b}"])
        O("scalar", lambda e, oc=oc, b=b: e.activation(out=smallc[b][:, 0:1], in_=tdc[:, oc:oc + 1], func=AF.Exp),
          reads=["tdcc"], writes=[f"smc{b}"])
        O("scalar", lambda e, b=b: e.activation(out=smallc[b][:, 1:2], in_=smallc[b][:, 0:1], func=AF.Exp, scale=-1.0),
          reads=[f"smc{b}"], writes=[f"smc{b}"])
        O("scalar", lambda e, oc=oc, b=b: e.activation(out=smallc[b][:, 2:3], in_=tfc[:, oc:oc + 1], func=AF.Exp),
          reads=["tfcc"], writes=[f"smc{b}"])
        for s in range(BC):
            for pi, (src_sc, dn, fn_) in enumerate(((xk_sc, "ko", AF.Copy), (xv_sc, "vo", AF.Copy),
                                                    (xr_sc, "sr", AF.Sigmoid))):
                dsto = dslc[dn]
                for tb in range(4):
                    g = pi * 4 + tb
                    rb = g % 2
                    pk = g % 4
                    O("sync", lambda e, src_sc=src_sc, s=s, tb=tb, rb=rb: e.dma_start(
                        out=rhsB[rb][:].rearrange("p (c q) -> p c q", c=NCH),
                        in_=src_sc.rearrange("(c p) t -> p c t", p=P)[:, :, s * S + tb * 512: s * S + (tb + 1) * 512]),
                      writes=[f"rhs{rb}"], dma=True)
                    for cc in range(NCH):
                        O("tensor", lambda e, b=b, pi=pi, cc=cc, rb=rb, pk=pk: e.matmul(
                            ps[pk][:], wblkb[b][:, (pi * NCH + cc) * P:(pi * NCH + cc + 1) * P],
                            rhsB[rb][:, cc * 512:(cc + 1) * 512],
                            start=(cc == 0), stop=(cc == NCH - 1)),
                          reads=[f"wbl{b}", f"rhs{rb}"], writes=[f"ps{pk}"])
                    O("scalar", lambda e, dsto=dsto, fn_=fn_, tb=tb, pk=pk: e.activation(
                        out=dsto[:, tb * 512:(tb + 1) * 512], in_=ps[pk][:], func=fn_),
                      reads=[f"ps{pk}"], writes=[f"{dn}{tb}"])
            O("scalar", lambda e: e.activation(out=eko, in_=ko, func=AF.Exp),
              reads=[f"ko{tb}" for tb in range(4)], writes=["eko"])
            O("vector", lambda e: e.tensor_tensor(out=ekvo, in0=eko, in1=vo, op=ALU.mult),
              reads=["eko"] + [f"vo{tb}" for tb in range(4)], writes=["ekvo"])
            O("gpsimd", lambda e: e.memset(Asc[:, 0:1], 0.0), writes=["Asc"])
            O("gpsimd", lambda e: e.memset(Bsc[:, 0:1], 0.0), writes=["Bsc"])
            ewb = smallc[b][:, 1:2].broadcast_to([P, S])
            O("vector", lambda e, ewb=ewb: e.tensor_tensor_scan(out=Asc[:, 1:S + 1], data0=ewb, data1=ekvo,
                                                                initial=0.0, op0=ALU.mult, op1=ALU.add),
              reads=["ekvo", f"smc{b}"], writes=["Asc"])
            O("vector", lambda e, ewb=ewb: e.tensor_tensor_scan(out=Bsc[:, 1:S + 1], data0=ewb, data1=eko,
                                                                initial=0.0, op0=ALU.mult, op1=ALU.add),
              reads=["eko", f"smc{b}"], writes=["Bsc"])
            O("vector", lambda e, b=b: e.scalar_tensor_tensor(out=numt, in0=ekvo, scalar=smallc[b][:, 2:3],
                                                              in1=Asc[:, 0:S], op0=ALU.mult, op1=ALU.add),
              reads=["ekvo", "Asc", f"smc{b}"], writes=["numt"])
            O("vector", lambda e, b=b: e.scalar_tensor_tensor(out=dent, in0=eko, scalar=smallc[b][:, 2:3],
                                                              in1=Bsc[:, 0:S], op0=ALU.mult, op1=ALU.add),
              reads=["eko", "Bsc", f"smc{b}"], writes=["dent"])
            O("scalar", lambda e: e.activation(out=dent, in_=dent, func=AF.Ln),
              reads=["dent"], writes=["dent"])
            O("scalar", lambda e: e.activation(out=dent, in_=dent, func=AF.Exp, scale=-1.0),
              reads=["dent"], writes=["dent"])
            O("vector", lambda e: e.tensor_tensor(out=numt, in0=numt, in1=dent, op=ALU.mult),
              reads=["numt", "dent"], writes=["numt"])
            O("vector", lambda e: e.tensor_tensor(out=saob, in0=numt, in1=sro, op=ALU.mult),
              reads=["numt"] + [f"sr{tb}" for tb in range(4)], writes=["saob"])
            O("sync", lambda e, oc=oc, s=s: e.dma_start(
                out=sa_sc[oc * P:(oc + 1) * P, s * S:(s + 1) * S], in_=saob),
              reads=["saob"], dma=True)

    sp.new_phase()
    sp._serial = 3 in _G_SERIAL
    cv = Carve()
    lnt3 = {}
    for n in ("ln2w", "ln2b", "cmkb", "cmrb"):
        t = cv.f32(D)
        lnt3[n] = t
        O("sync", lambda e, n=n, t=t: e.dma_start(out=t, in_=ln_b[n]), writes=["Lw_" + n], dma=True)
    woTS = cv.f32(NCH * D)
    woTb = cv.bf(NCH * D)
    O("sync", lambda e: e.dma_start(out=woTS.rearrange("p (c q) -> p c q", c=NCH),
                                    in_=woT_in.rearrange("(c p) d -> p c d", p=P)),
      writes=["woTS"], dma=True)
    O("vector", lambda e: e.tensor_copy(out=woTb, in_=woTS), reads=["woTS"], writes=["woT"])
    salB = [cv.bf(NCH * P) for _ in range(2)]
    ot3 = [cv.f32(D) for _ in range(2)]; a3 = [cv.f32(D) for _ in range(2)]
    x23 = [cv.f32(D) for _ in range(2)]; h23 = [cv.f32(D) for _ in range(2)]
    xx3 = [cv.f32(D) for _ in range(2)]; mb3 = [cv.bf(D) for _ in range(2)]
    d3 = cv.f32(D); mix3 = cv.f32(D); xsq3 = cv.f32(D)
    for i in range(NT):
        b = i % 2
        O("sync", lambda e, i=i, b=b: e.dma_start(
            out=salB[b][:].rearrange("p (c q) -> p c q", c=NCH),
            in_=sa_sc.rearrange("(c p) t -> p c t", p=P)[:, :, i * P:(i + 1) * P]),
          writes=[f"sal{b}"], dma=True)
        for half in range(2):
            for cc in range(NCH):
                O("tensor", lambda e, b=b, cc=cc, half=half: e.matmul(
                    ps[half][:], salB[b][:, cc * P:(cc + 1) * P],
                    woTb[:, cc * D + half * 512: cc * D + (half + 1) * 512],
                    start=(cc == 0), stop=(cc == NCH - 1)),
                  reads=[f"sal{b}", "woT"], writes=[f"ps{half}"])
            O("scalar", lambda e, b=b, half=half: e.activation(
                out=ot3[b][:, half * 512:(half + 1) * 512], in_=ps[half][:], func=AF.Copy),
              reads=[f"ps{half}"], writes=[f"ot{b}_{half}"])
        O("sync", lambda e, i=i, b=b: e.dma_start(out=a3[b], in_=a_sc[i * P:(i + 1) * P, :]),
          writes=[f"a3{b}"], dma=True)
        O("vector", lambda e, b=b: e.tensor_tensor(out=x23[b], in0=ot3[b], in1=a3[b], op=ALU.add),
          reads=[f"ot{b}_0", f"ot{b}_1", f"a3{b}"], writes=[f"x2{b}"])
        O("sync", lambda e, i=i, b=b: e.dma_start(out=x2_sc[i * P:(i + 1) * P, :], in_=x23[b]),
          reads=[f"x2{b}"], dma=True)
        ln(x23[b], f"x2{b}", h23[b], f"h2{b}", stat[b], f"st3{b}", xsq3, "xq3",
           lnt3["ln2w"], "Lw_ln2w", lnt3["ln2b"], "Lw_ln2b")
        if i % 16 == 0:
            O("gpsimd", lambda e, b=b: e.memset(xx3[b][0:1, 0:D], 0.0), writes=[f"xx3{b}"])
        else:
            O("sync", lambda e, b=b: e.dma_start(out=xx3[b][0:1, 0:D], in_=hlast[:]),
              reads=["hl3"], writes=[f"xx3{b}"], dma=True)
        O("sync", lambda e, b=b: e.dma_start(out=xx3[b][1:P, 0:D], in_=h23[b][0:P - 1, 0:D]),
          reads=[f"h2{b}"], writes=[f"xx3{b}"], dma=True)
        O("sync", lambda e, b=b: e.dma_start(out=hlast[:], in_=h23[b][P - 1:P, 0:D]),
          reads=[f"h2{b}"], writes=["hl3"], dma=True)
        O("vector", lambda e, b=b: e.tensor_tensor(out=d3, in0=h23[b], in1=xx3[b], op=ALU.subtract),
          reads=[f"h2{b}", f"xx3{b}"], writes=["d3"])
        O("vector", lambda e: e.tensor_tensor(out=mix3, in0=d3, in1=lnt3["cmkb"], op=ALU.mult),
          reads=["d3", "Lw_cmkb"], writes=["mx3"])
        O("vector", lambda e, b=b: e.tensor_tensor(out=mix3, in0=mix3, in1=xx3[b], op=ALU.add),
          reads=["mx3", f"xx3{b}"], writes=["mx3"])
        O("vector", lambda e, b=b: e.tensor_copy(out=mb3[b], in_=mix3),
          reads=["mx3"], writes=[f"mb3{b}"])
        for c in range(NCH):
            O("scalar", lambda e, b=b, c=c: e.dma_start(out=trd[0][:, c * P:(c + 1) * P],
                                                        in_=mb3[b][:, c * P:(c + 1) * P], transpose=True),
              reads=[f"mb3{b}"], writes=[f"t30_{c}"], dma=True)
        for c in range(NCH):
            O("vector", lambda e, c=c, i=i: e.tensor_copy(out=xk2_fm[c][:, i * P:(i + 1) * P],
                                                          in_=trd[0][:, c * P:(c + 1) * P]),
              reads=[f"t30_{c}"], writes=[f"xkf{c}"])
        O("vector", lambda e: e.tensor_tensor(out=mix3, in0=d3, in1=lnt3["cmrb"], op=ALU.mult),
          reads=["d3", "Lw_cmrb"], writes=["mx3"])
        O("vector", lambda e, b=b: e.tensor_tensor(out=mix3, in0=mix3, in1=xx3[b], op=ALU.add),
          reads=["mx3", f"xx3{b}"], writes=["mx3"])
        O("vector", lambda e, b=b: e.tensor_copy(out=mb3[b], in_=mix3),
          reads=["mx3"], writes=[f"mb3{b}"])
        for c in range(NCH):
            O("scalar", lambda e, b=b, c=c: e.dma_start(out=trd[1][:, c * P:(c + 1) * P],
                                                        in_=mb3[b][:, c * P:(c + 1) * P], transpose=True),
              reads=[f"mb3{b}"], writes=[f"t31_{c}"], dma=True)
        O("sync", lambda e, i=i: e.dma_start(
            out=xr2_sc.rearrange("(c p) t -> p c t", p=P)[:, :, i * P:(i + 1) * P],
            in_=trd[1][:].rearrange("p (c q) -> p c q", c=NCH)),
          reads=[f"t31_{c}" for c in range(NCH)], dma=True)

    sp.new_phase()
    sp._serial = 4 in _G_SERIAL
    cv = Carve()
    wfkS = [cv.f32(NCH * P) for _ in range(2)]
    wfkb = [cv.bf(NCH * P) for _ in range(2)]
    kfo = cv.f32(T)
    kfqo = [cv.bf(T) for _ in range(2)]
    for o4 in range(H_FFN // P):
        b = o4 % 2
        O("sync", lambda e, o4=o4, b=b: e.dma_start(
            out=wfkS[b][:].rearrange("p (c q) -> p c q", c=NCH),
            in_=wfkT_in.rearrange("(c p) h -> p c h", p=P)[:, :, o4 * P:(o4 + 1) * P]),
          writes=[f"wfS{b}"], dma=True)
        O("vector", lambda e, b=b: e.tensor_copy(out=wfkb[b], in_=wfkS[b]),
          reads=[f"wfS{b}"], writes=[f"wfk{b}"])
        for tb in range(NTB):
            pk = tb % 4
            for cc in range(NCH):
                O("tensor", lambda e, b=b, cc=cc, tb=tb, pk=pk: e.matmul(
                    ps[pk][:], wfkb[b][:, cc * P:(cc + 1) * P], xk2_fm[cc][:, tb * 512:(tb + 1) * 512],
                    start=(cc == 0), stop=(cc == NCH - 1)),
                  reads=[f"wfk{b}"], writes=[f"ps{pk}"])
            O("scalar", lambda e, tb=tb, pk=pk: e.activation(out=kfo[:, tb * 512:(tb + 1) * 512],
                                                             in_=ps[pk][:], func=AF.Relu),
              reads=[f"ps{pk}"], writes=[f"kf{tb}"])
        O("scalar", lambda e, b=b: e.activation(out=kfqo[b], in_=kfo, func=AF.Square),
          reads=[f"kf{tb}" for tb in range(NTB)], writes=[f"kq{b}"])
        O("sync", lambda e, o4=o4, b=b: e.dma_start(out=kfq_sc[o4 * P:(o4 + 1) * P, :], in_=kfqo[b]),
          reads=[f"kq{b}"], dma=True)

    sp.new_phase()
    sp._serial = 5 in _G_SERIAL
    cv = Carve()
    wfvS = [cv.f32(2 * D) for _ in range(2)]
    wfvb = cv.bf(32 * D)
    wfrb = cv.bf(NCH * D)
    kflB = [cv.bf(32 * P) for _ in range(2)]
    xrlB = [cv.bf(NCH * P) for _ in range(2)]
    kvt = cv.f32(D)
    rst = cv.f32(D)
    x25 = [cv.f32(D) for _ in range(2)]
    for j in range(16):
        b = j % 2
        O("sync", lambda e, j=j, b=b: e.dma_start(
            out=wfvS[b][:].rearrange("p (c q) -> p c q", c=2),
            in_=wfvT_in.rearrange("(c p) d -> p c d", p=P)[:, 2 * j:2 * j + 2, :]),
          writes=[f"wvS{b}"], dma=True)
        O("vector", lambda e, j=j, b=b: e.tensor_copy(out=wfvb[:, 2 * j * D:(2 * j + 2) * D], in_=wfvS[b]),
          reads=[f"wvS{b}"], writes=[f"wfv{2 * j}", f"wfv{2 * j + 1}"])
    for j in range(4):
        b = j % 2
        O("sync", lambda e, j=j, b=b: e.dma_start(
            out=wfvS[b][:].rearrange("p (c q) -> p c q", c=2),
            in_=wfrT_in.rearrange("(c p) d -> p c d", p=P)[:, 2 * j:2 * j + 2, :]),
          writes=[f"wvS{b}"], dma=True)
        O("vector", lambda e, j=j, b=b: e.tensor_copy(out=wfrb[:, 2 * j * D:(2 * j + 2) * D], in_=wfvS[b]),
          reads=[f"wvS{b}"], writes=["wfr"])
    for i in range(NT):
        b = i % 2
        O("sync", lambda e, i=i, b=b: e.dma_start(
            out=kflB[b][:].rearrange("p (c q) -> p c q", c=32),
            in_=kfq_sc.rearrange("(c p) t -> p c t", p=P)[:, :, i * P:(i + 1) * P]),
          writes=[f"kfl{b}"], dma=True)
        for half in range(2):
            for ic in range(32):
                O("tensor", lambda e, b=b, ic=ic, half=half: e.matmul(
                    ps[half][:], kflB[b][:, ic * P:(ic + 1) * P],
                    wfvb[:, ic * D + half * 512: ic * D + (half + 1) * 512],
                    start=(ic == 0), stop=(ic == 31)),
                  reads=[f"kfl{b}", f"wfv{ic}"], writes=[f"ps{half}"])
            O("scalar", lambda e, half=half: e.activation(out=kvt[:, half * 512:(half + 1) * 512],
                                                          in_=ps[half][:], func=AF.Copy),
              reads=[f"ps{half}"], writes=[f"kv{half}"])
        O("sync", lambda e, i=i, b=b: e.dma_start(
            out=xrlB[b][:].rearrange("p (c q) -> p c q", c=NCH),
            in_=xr2_sc.rearrange("(c p) t -> p c t", p=P)[:, :, i * P:(i + 1) * P]),
          writes=[f"xrl{b}"], dma=True)
        for half in range(2):
            pk = 2 + half
            for cc in range(NCH):
                O("tensor", lambda e, b=b, cc=cc, half=half, pk=pk: e.matmul(
                    ps[pk][:], xrlB[b][:, cc * P:(cc + 1) * P],
                    wfrb[:, cc * D + half * 512: cc * D + (half + 1) * 512],
                    start=(cc == 0), stop=(cc == NCH - 1)),
                  reads=[f"xrl{b}", "wfr"], writes=[f"ps{pk}"])
            O("scalar", lambda e, half=half, pk=pk: e.activation(out=rst[:, half * 512:(half + 1) * 512],
                                                                 in_=ps[pk][:], func=AF.Sigmoid),
              reads=[f"ps{pk}"], writes=[f"rs{half}"])
        O("vector", lambda e: e.tensor_tensor(out=rst, in0=rst, in1=kvt, op=ALU.mult),
          reads=["rs0", "rs1", "kv0", "kv1"], writes=["rs0", "rs1"])
        O("sync", lambda e, i=i, b=b: e.dma_start(out=x25[b], in_=x2_sc[i * P:(i + 1) * P, :]),
          writes=[f"x25{b}"], dma=True)
        O("vector", lambda e, b=b: e.tensor_tensor(out=rst, in0=rst, in1=x25[b], op=ALU.add),
          reads=["rs0", "rs1", f"x25{b}"], writes=["rs0", "rs1"])
        O("sync", lambda e, i=i: e.dma_start(out=attn1_o[i * P:(i + 1) * P, :], in_=rst),
          reads=["rs0", "rs1"], dma=True)
    sp._serial = False
    es.close()



_G_SERIAL = {1, 2, 3, 4, 5, "F", "I"}
_DBG = set()
_LAST = {}


def build_MEGA():
    nc = bass.Bass("TRN2", target_bir_lowering=False)
    inp = lambda n, s, dt=F32: nc.dram_tensor(n, s, dt, kind="ExternalInput").ap()
    scr = lambda n, s, dt=F32: nc.dram_tensor(
        n, s, dt, kind="ExternalOutput" if n in _DBG else "Internal").ap()

    x_in = inp("x", [T, D])
    wqT = inp("wqT", [D, D]); wkT = inp("wkT", [D, D]); wvT = inp("wvT", [D, D])
    bqc = inp("bqc", [P, NCH]); bkc = inp("bkc", [P, NCH]); bvr = inp("bvr", [1, D])
    epsc = inp("epsc", [P, 1])
    eyes = inp("eyes", [P, 3 * P]); identb = inp("identb", [P, P], BF16)
    identf = inp("identf", [P, P])
    wg0 = inp("wg0", [I_FFN, D]); wu0 = inp("wu0", [I_FFN, D]); wd0 = inp("wd0", [D, I_FFN])
    wg1 = inp("wg1", [I_FFN, D]); wu1 = inp("wu1", [I_FFN, D]); wd1 = inp("wd1", [D, I_FFN])
    ln_b = {n: inp(n, [P, D]) for n in ("ln0w", "ln0b", "ln1w", "ln1b", "ln2w", "ln2b",
                                        "tmkb", "tmvb", "tmrb", "cmkb", "cmrb")}
    wkT1 = inp("wkT1", [D, D]); wvT1 = inp("wvT1", [D, D]); wrT1 = inp("wrT1", [D, D])
    woT1 = inp("woT1", [D, D]); wfkT = inp("wfkT", [D, H_FFN])
    wfvT = inp("wfvT", [H_FFN, D]); wfrT = inp("wfrT", [D, D])
    tdc = inp("tdc", [P, NCH]); tfc = inp("tfc", [P, NCH])

    xn_tok = scr("xn_tok", [T, D])
    q_fm = scr("q_fm", [NCH * P, T], BF16)
    k_fm = scr("k_fm", [NCH * P, T], BF16)
    v_tok = scr("v_tok", [T, D], BF16)
    xn_fm_s = scr("xn_fm_s", [NCH * P, T], BF16)
    qlm = scr("qlm", [NCH * P, BC * LM], BF16)
    klm = scr("klm", [NCH * P, BC * LM], BF16)
    k2 = scr("k2", [NMAT * P, LM], BF16)
    pcmax_t = nc.dram_tensor("pcmax", [1, 4], F32, kind="Internal")
    pcm_all_t = nc.dram_tensor("pcm_all", [1, 4], F32, kind="Internal")
    z_s = scr("z_s", [NMAT * P, LM], BF16)
    zt_s = scr("zt_s", [NMAT * P, LM], BF16)
    attn_tok = scr("attn_tok", [T, D])
    wgq_fm = scr("wgq_fm", [D, I_FFN], BF16)
    wuq_fm = scr("wuq_fm", [D, I_FFN], BF16)
    wdqT = scr("wdqT", [I_FFN, D], BF16)
    hq_sc = scr("hq_sc", [I_FFN, T], BF16)
    ffn0_tok = scr("ffn0_tok", [T, D])
    ffn1_tok = scr("ffn1_tok", [T, D])
    x1_tok = scr("x1_tok", [T, D])
    xn1_tok = scr("xn1_tok", [T, D])
    a_sc = scr("a_sc", [T, D])
    x2_sc = scr("x2_sc", [T, D])
    xk_sc = scr("xk_sc", [D, T], BF16)
    xv_sc = scr("xv_sc", [D, T], BF16)
    xr_sc = scr("xr_sc", [D, T], BF16)
    xr2_sc = scr("xr2_sc", [D, T], BF16)
    sa_sc = scr("sa_sc", [D, T], BF16)
    kfq_sc = scr("kfq_sc", [H_FFN, T], BF16)
    attn1_tok = scr("attn1_tok", [T, D])
    out_o = nc.dram_tensor("out", [T, D], F32, kind="ExternalOutput").ap()

    es = contextlib.ExitStack()

    ab = emit_AB(nc, es, {
        "x": x_in, "wqT": wqT, "wkT": wkT, "wvT": wvT, "bqc": bqc, "bkc": bkc,
        "bvr": bvr, "epsc": epsc, "xn_tok": xn_tok, "q_fm": q_fm, "k_fm": k_fm,
        "v_tok": v_tok, "xn_fm": xn_fm_s, "qlm": qlm, "klm": klm, "k2": k2,
        "pcmax": pcmax_t.ap()})

    sp = SP(nc)

    cc_sem = es.enter_context(nc.semaphore("s_cc"))

    def cc_fn(e, sq, inc):
        e.collective_compute(
            "AllReduce", ALU.max,
            replica_groups=[list(range(NCORES))],
            ins=[pcmax_t.ap().opt()],
            outs=[pcm_all_t.ap().opt()]).then_inc(cc_sem)
        e.wait_ge(cc_sem, 1)
        e.sem_inc(sq, inc)
    sp.add_raw("gpsimd", cc_fn, 1)

    emit_WQfull(nc, sp, "0", wg0, wu0, wd0, wgq_fm, wuq_fm, wdqT)
    emit_C(nc, sp, k2, pcm_all_t.ap(), eyes, identb, z_s, zt_s)
    emit_D(nc, sp, q_fm, k_fm, v_tok, qlm, klm, zt_s, identf, attn_tok)
    emit_FFN(nc, sp, "0", xn_tok, wgq_fm, wuq_fm, wdqT, ffn0_tok, hq_sc)
    emit_F(nc, sp, x_in, attn_tok, ffn0_tok, epsc, x1_tok, xn1_tok)
    emit_WQfull(nc, sp, "1", wg1, wu1, wd1, wgq_fm, wuq_fm, wdqT)
    emit_FFN(nc, sp, "1", xn1_tok, wgq_fm, wuq_fm, wdqT, ffn1_tok, hq_sc)
    emit_G(nc, sp, xn1_tok, ln_b, wkT1, wvT1, wrT1, woT1, wfkT, wfvT, wfrT,
           tdc, tfc, epsc, attn1_tok,
           a_sc, x2_sc, xk_sc, xv_sc, xr_sc, xr2_sc, sa_sc, kfq_sc)
    emit_I(nc, sp, x1_tok, attn1_tok, ffn1_tok, out_o)

    sp.emit(es, prologue=ab)
    es.close()
    return nc



_CACHED = {}


def _get(name, builder):
    if name not in _CACHED:
        _CACHED[name] = builder()
    return _CACHED[name]


def kernel(**inputs):
    ins = {k: np.asarray(v) for k, v in inputs.items()}
    C = np.ascontiguousarray
    cores = list(range(NCORES))

    nc = _get("MEGA", build_MEGA)

    epsc = np.full((P, 1), 1e-5, np.float32)
    eyes = np.concatenate([np.eye(P) * 7, np.eye(P) * 15, np.eye(P) * 13], 1).astype(np.float32)
    identb = np.eye(P).astype(BF)
    identf = np.eye(P, dtype=np.float32)
    bt = lambda v: C(np.broadcast_to(np.asarray(v, np.float32), (P, D)))
    f32 = lambda k: ins[k].astype(np.float32)
    common = {
        "wqT": C(f32("L0_attn_Wq").T),
        "wkT": C(f32("L0_attn_Wk").T),
        "wvT": C(f32("L0_attn_Wv").T),
        "bqc": C(f32("L0_attn_bq").reshape(NCH, P).T),
        "bkc": C(f32("L0_attn_bk").reshape(NCH, P).T),
        "bvr": C(f32("L0_attn_bv").reshape(1, D)),
        "epsc": epsc, "eyes": eyes, "identb": identb, "identf": identf,
        "wg0": C(f32("L0_ffn_w_gate")), "wu0": C(f32("L0_ffn_w_up")),
        "wd0": C(f32("L0_ffn_w_down")),
        "wg1": C(f32("L1_ffn_w_gate")), "wu1": C(f32("L1_ffn_w_up")),
        "wd1": C(f32("L1_ffn_w_down")),
        "ln0w": bt(ins["L1_attn_ln0_w"]), "ln0b": bt(ins["L1_attn_ln0_b"]),
        "ln1w": bt(ins["L1_attn_ln1_w"]), "ln1b": bt(ins["L1_attn_ln1_b"]),
        "ln2w": bt(ins["L1_attn_ln2_w"]), "ln2b": bt(ins["L1_attn_ln2_b"]),
        "tmkb": bt(ins["L1_attn_tmk"]), "tmvb": bt(ins["L1_attn_tmv"]),
        "tmrb": bt(ins["L1_attn_tmr"]), "cmkb": bt(ins["L1_attn_cmk"]),
        "cmrb": bt(ins["L1_attn_cmr"]),
        "wkT1": C(f32("L1_attn_Wk").T),
        "wvT1": C(f32("L1_attn_Wv").T),
        "wrT1": C(f32("L1_attn_Wr").T),
        "woT1": C(f32("L1_attn_Wo").T),
        "wfkT": C(f32("L1_attn_Wfk").T),
        "wfvT": C(f32("L1_attn_Wfv").T),
        "wfrT": C(f32("L1_attn_Wfr").T),
        "tdc": C(f32("L1_attn_time_decay").reshape(NCH, P).T),
        "tfc": C(f32("L1_attn_time_first").reshape(NCH, P).T),
    }
    x = ins["L0_attn_x"].astype(np.float32)
    maps = [dict(common, x=C(x[c * BC:(c + 1) * BC].reshape(T, D))) for c in cores]
    res = run_bass_kernel_spmd(nc, maps, core_ids=cores)
    _LAST["res"] = res
    out = np.stack([res.results[c]["out"] for c in cores]).reshape(16, S, D).astype(np.float32)
    return out


if __name__ == "__main__":
    print("library module")


# version 10
# speedup vs baseline: 1.4516x; 1.0737x over previous
import contextlib
import numpy as np
import ml_dtypes
import concourse.bass as bass
import concourse.mybir as mybir
from concourse.bass_utils import run_bass_kernel_spmd

F32 = mybir.dt.float32
BF16 = mybir.dt.bfloat16
I16 = mybir.dt.int16
AF = mybir.ActivationFunctionType
ALU = mybir.AluOpType
AX = mybir.AxisListType

NCORES = 8
P = 128
S = 2048
BC = 2
T = S * BC
D = 1024
NCH = D // P
HEADS = 16
HD = 64
LM = 128
SEG = 16
NMAT = BC * HEADS
S4 = 1.0 / float(np.sqrt(np.sqrt(HD)))
NT = T // P
I_FFN = 2816
H_FFN = 4096
NTB = T // 512
BF = ml_dtypes.bfloat16


class Chain:

    def __init__(self, eng, sem):
        self.eng, self.sem, self.n = eng, sem, 0

    def bar(self, inst):
        self.n += 1
        inst.then_inc(self.sem, 1)
        self.eng.wait_ge(self.sem, self.n)
        return inst



def emit_AB(nc, es_outer, d, stages=3, cuts=(), nt=NT):
    x_in = d["x"]
    wqT_in = d["wqT"]; wkT_in = d["wkT"]; wvT_in = d["wvT"]
    bqc_in = d["bqc"]; bkc_in = d["bkc"]; bvr_in = d["bvr"]; eps_in = d["epsc"]
    xn_tok_o = d["xn_tok"]; q_fm_o = d["q_fm"]; k_fm_o = d["k_fm"]
    v_tok_o = d["v_tok"]; xn_fm_o = d["xn_fm"]; qlm_o = d["qlm"]
    klm_o = d["klm"]; k2_o = d["k2"]; pcmax_o = d["pcmax"]

    es = contextlib.ExitStack()
    sbt = lambda name, shape, dt=F32: es.enter_context(nc.sbuf_tensor(name, shape, dt))
    pst = lambda name, shape, dt=F32: es.enter_context(nc.psum_tensor(name, shape, dt))
    sem = lambda name: es_outer.enter_context(nc.semaphore(name))

    NXB = 2
    x_t = [sbt(f"x_t{i}", [P, D]) for i in range(NXB)]
    xn_t = [sbt(f"xn_t{i}", [P, D]) for i in range(NXB)]
    xnb_t = [sbt(f"xnb_t{i}", [P, D], BF16) for i in range(2)]
    stat = sbt("stat", [P, 8])
    epsc = sbt("epsc_s", [P, 1])
    xn_fm = [es.enter_context(nc.sbuf_tensor(f"xn_fm{c}", [P, T], BF16, is_transpose_dest=True)) for c in range(NCH)]
    wq = [sbt(f"wqc{c}", [P, D], BF16) for c in range(NCH)]
    wk = [sbt(f"wkc{c}", [P, D], BF16) for c in range(NCH)]
    wv = [sbt(f"wvc{c}", [P, D], BF16) for c in range(NCH)]
    bqs = sbt("bqs", [P, NCH])
    bks = sbt("bks", [P, NCH])
    bvf = sbt("bvf", [1, D])
    bvb = sbt("bvb", [1, D], BF16)
    ones_row = sbt("ones_row", [1, P], BF16)
    ones_col = sbt("ones_col", [P, 1])
    if stages >= 2:
        pbf_s = sbt("pbf_s", [P, T], BF16)
        pbf = [pbf_s, pbf_s]
        lmraw = [sbt(f"lmraw{i}", [P, BC * LM]) for i in range(2)]
        b16q = sbt("b16q", [P, NCH])
        b16k = sbt("b16k", [P, NCH])
        _wldv = pbf_s[:].bitcast(F32)
        wld = [_wldv[:, 0:D], _wldv[:, D:2 * D]]
    else:
        wld_t = [sbt(f"wld{i}", [P, D]) for i in range(2)]
        wld = [wld_t[0][:], wld_t[1][:]]
        xsq_t = sbt("xsq1", [P, D])
        xsq = xsq_t[:]
    if stages >= 2:
        vf32_s = sbt("vf32_s", [P, D])
        vbf_s = sbt("vbf_s", [P, D], BF16)
        vf32 = [vf32_s, vf32_s]
        vbf = [vbf_s, vbf_s]
        qlm_b = sbt("qlm_b", [P, NCH * BC * LM], BF16)
        klm_b = sbt("klm_b", [P, NCH * BC * LM], BF16)
        xsq = qlm_b[:].bitcast(F32)[:, 0:D]
        e2 = sbt("e2", [P, LM])
        k2f = [sbt(f"k2f{i}", [P, LM]) for i in range(2)]
        k2bs = [sbt(f"k2bs{i}", [P, LM], BF16) for i in range(2)]
        s2red = sbt("s2red", [P, 4])
    runmax = sbt("runmax", [1, LM])
    pcm = sbt("pcm", [1, 4])

    ps_q = [pst(f"ps_q{i}", [P, 512]) for i in range(4)]
    ps_s2 = [pst(f"ps_s2{i}", [P, LM]) for i in range(2)]
    ps_cs = [pst(f"ps_cs{i}", [1, LM]) for i in range(2)]

    s_ldw = sem("s_ldw")
    s_wbf = sem("s_wbf")
    s_x = sem("s_x")
    s_v = sem("s_v")
    s_a = sem("s_a")
    s_tr = sem("s_tr")
    s_st = sem("s_st")
    s_pe = sem("s_pe")
    s_ev = sem("s_ev")
    s_lm = sem("s_lm")
    s_ps = sem("s_ps")
    s_v2 = sem("s_v2")
    s_s2a = sem("s_s2a")
    s_fin = sem("s_fin")
    s_vch = sem("s_vch")
    s_c = sem("s_c")
    s_lr = sem("s_lr")
    s_lw0 = sem("s_lw0")
    s_lw1 = sem("s_lw1")
    s_ach = sem("s_ach")
    s_ks = sem("s_ks")

    NQK = 2 * NCH * NTB
    NVG = NT * 2
    NG = NQK + NVG

    def sync_body(sync):
        for wi in range(0 if "noW" in cuts else 3):
            w_in = (wqT_in, wkT_in, wvT_in)[wi]
            for c in range(NCH):
                k = wi * NCH + c
                if k >= 2:
                    sync.wait_ge(s_wbf, k - 1)
                sync.dma_start(out=wld[k % 2], in_=w_in[c * P:(c + 1) * P, :]).then_inc(
                    (s_lw0, s_lw1)[k % 2], 16)
        nw = 0 if "noW" in cuts else 24
        if "noW" not in cuts:
            sync.dma_start(out=bqs[:], in_=bqc_in).then_inc(s_ldw, 16)
            sync.dma_start(out=bks[:], in_=bkc_in).then_inc(s_ldw, 16)
            sync.dma_start(out=bvf[:], in_=bvr_in).then_inc(s_ldw, 16)
        sync.dma_start(out=epsc[:], in_=eps_in).then_inc(s_ldw, 16)
        for i in range(nt):
            if i >= NXB:
                sync.wait_ge(s_a, 2 * (i - NXB) + 2)
                sync.dma_start(out=xn_tok_o[(i - NXB) * P:(i - NXB + 1) * P, :],
                               in_=xn_t[(i - NXB) % NXB][:]).then_inc(s_st, 16)
            sync.dma_start(out=x_t[i % NXB][:], in_=x_in[i * P:(i + 1) * P, :]).then_inc(s_x, 16)
        for i in range(nt - NXB, nt):
            sync.wait_ge(s_a, 2 * i + 2)
            sync.dma_start(out=xn_tok_o[i * P:(i + 1) * P, :],
                           in_=xn_t[i % NXB][:]).then_inc(s_st, 16)
        for g in range(16 if stages >= 2 else 0):
            proj, oc = divmod(g, NCH)
            sync.wait_ge(s_ev, 8 * (g + 1))
            dst = (q_fm_o, k_fm_o)[proj]
            sync.dma_start(out=dst[oc * P:(oc + 1) * P, :], in_=pbf[g % 2][:]).then_inc(s_ps, 16)
        for tt in range(0 if ("noV" in cuts or stages < 2) else NT):
            sync.wait_ge(s_ev, NQK + 2 * tt + 2)
            sync.dma_start(out=v_tok_o[tt * P:(tt + 1) * P, :], in_=vbf[tt % 2][:]).then_inc(s_ps, 16)
        sync.wait_ge(s_tr, nt * NCH * 16)
        for c in range(NCH if "noFM" not in cuts else 0):
            sync.dma_start(out=xn_fm_o[c * P:(c + 1) * P, :], in_=xn_fm[c][:]).then_inc(s_fin, 16)
        if stages >= 2:
            sync.wait_ge(s_lm, 16)
        if stages >= 2 and "noLM" not in cuts:
            sync.dma_start(out=qlm_o.rearrange("(c p) l -> p c l", p=P), in_=qlm_b[:].rearrange("p (c l) -> p c l", c=NCH)).then_inc(s_fin, 16)
            sync.dma_start(out=klm_o.rearrange("(c p) l -> p c l", p=P), in_=klm_b[:].rearrange("p (c l) -> p c l", c=NCH)).then_inc(s_fin, 16)
        for m in range(NMAT if stages >= 3 else 0):
            sync.wait_ge(s_v2, 3 * m + 2)
            sync.dma_start(out=k2_o[m * P:(m + 1) * P, :], in_=k2bs[m % 2][:]).then_inc(s_ks, 16)
        if stages >= 3:
            sync.wait_ge(s_v2, 3 * NMAT + 1)
            sync.wait_ge(s_ks, NMAT * 16)
        sync.wait_ge(s_c, 1)
        sync.dma_start(out=pcmax_o, in_=pcm[:]).then_inc(s_fin, 16)
        nfin = ((0 if "noFM" in cuts else NCH) + 1 + (2 if (stages >= 2 and "noLM" not in cuts) else 0)) * 16
        sync.wait_ge(s_fin, nfin)
        if stages >= 2:
            sync.wait_ge(s_ps, (16 if "noV" in cuts else 48) * 16)
        sync.wait_ge(s_st, nt * 16)

    def gpsimd_body(gpsimd):
        gpsimd.memset(ones_row[:], 1.0)
        gpsimd.memset(ones_col[:], 1.0)
        gpsimd.memset(runmax[:], -3.0e38)
        gpsimd.memset(pcm[:], 0.0).then_inc(s_c, 1)

    def vector_body(vector):
        vch = Chain(vector, s_vch)
        for wi in range(0 if "noW" in cuts else 3):
            wtiles = (wq, wk, wv)[wi]
            for c in range(NCH):
                k = wi * NCH + c
                vector.wait_ge((s_lw0, s_lw1)[k % 2], (k // 2 + 1) * 16)
                vector.tensor_copy(out=wtiles[c][:], in_=wld[k % 2]).then_inc(s_wbf, 1)
        if "noW" not in cuts:
            vector.wait_ge(s_ldw, 4 * 16)
            vch.bar(vector.tensor_scalar_mul(out=bqs[:], in0=bqs[:], scalar1=S4))
            vch.bar(vector.tensor_scalar_mul(out=bks[:], in0=bks[:], scalar1=S4))
            if stages >= 2:
                vch.bar(vector.tensor_scalar_mul(out=b16q[:], in0=bqs[:], scalar1=16.0))
                vch.bar(vector.tensor_scalar_mul(out=b16k[:], in0=bks[:], scalar1=16.0))
            vector.tensor_copy(out=bvb[:], in_=bvf[:]).then_inc(s_v, 1)
        else:
            vector.wait_ge(s_ldw, 16)
            vector.sem_inc(s_v, 1)
        for i in range(NT):
            b = i % NXB
            vector.wait_ge(s_x, (i + 1) * 16)
            vch.bar(vector.tensor_reduce(out=stat[:, 0:1], in_=x_t[b][:], axis=AX.X, op=ALU.add))
            vch.bar(vector.tensor_tensor(out=xsq, in0=x_t[b][:], in1=x_t[b][:], op=ALU.mult))
            vch.bar(vector.tensor_reduce(out=stat[:, 1:2], in_=xsq, axis=AX.X, op=ALU.add))
            vch.bar(vector.tensor_scalar_mul(out=stat[:, 2:3], in0=stat[:, 0:1], scalar1=1.0 / D))
            vch.bar(vector.tensor_scalar_mul(out=stat[:, 3:4], in0=stat[:, 1:2], scalar1=1.0 / D))
            vch.bar(vector.tensor_tensor(out=stat[:, 4:5], in0=stat[:, 2:3], in1=stat[:, 2:3], op=ALU.mult))
            vector.tensor_tensor(out=stat[:, 4:5], in0=stat[:, 3:4], in1=stat[:, 4:5],
                                 op=ALU.subtract).then_inc(s_v, 1)
            vector.wait_ge(s_a, 2 * i + 1)
            vector.scalar_tensor_tensor(out=stat[:, 6:7], in0=stat[:, 2:3], scalar=-1.0,
                                        in1=stat[:, 5:6], op0=ALU.mult,
                                        op1=ALU.mult).then_inc(s_v, 1)
            vector.wait_ge(s_a, 2 * i + 2)
            if i >= 2:
                vector.wait_ge(s_tr, (i - 1) * NCH * 16)
            vector.tensor_copy(out=xnb_t[i % 2][:], in_=xn_t[b][:]).then_inc(s_v, 1)
        for k in range(NQK if stages >= 2 else 0):
            g, tb = divmod(k, NTB)
            proj, oc = divmod(g, NCH)
            vector.wait_ge(s_pe, k + 1)
            if "noLM" in cuts:
                vector.sem_inc(s_lr, 1)
            else:
                vch.bar(vector.tensor_reduce(out=lmraw[g % 2][:, tb * 32:(tb + 1) * 32],
                                     in_=ps_q[k % 4][:].rearrange("p (n w) -> p n w", w=SEG),
                                     axis=AX.X, op=ALU.add))
                vector.sem_inc(s_lr, 1)
            if tb == NTB - 1:
                if "noLM" in cuts:
                    vector.sem_inc(s_lm, 1)
                else:
                    dstlm = (qlm_b, klm_b)[proj]
                    b16 = (b16q, b16k)[proj]
                    vector.tensor_scalar(out=dstlm[:, oc * BC * LM:(oc + 1) * BC * LM],
                                         in0=lmraw[g % 2][:], scalar1=S4,
                                         scalar2=b16[:, oc:oc + 1], op0=ALU.mult,
                                         op1=ALU.add).then_inc(s_lm, 1)
        vector.wait_ge(s_c, 1)
        for m in range(NMAT if stages >= 3 else 0):
            vector.wait_ge(s_pe, NG + 2 * m + 1)
            vch.bar(vector.tensor_reduce(out=s2red[:, 0:1], in_=ps_s2[m % 2][:], axis=AX.X, op=ALU.max))
            vector.tensor_scalar_mul(out=s2red[:, 1:2], in0=s2red[:, 0:1],
                                     scalar1=-1.0 / 256.0).then_inc(s_v2, 1)
            vector.wait_ge(s_s2a, m + 1)
            if m >= 2:
                vector.wait_ge(s_pe, NG + 2 * (m - 2) + 2)
            vch.bar(vector.tensor_reduce(out=s2red[:, 2:3], in_=e2[:], axis=AX.X, op=ALU.add))
            vch.bar(vector.reciprocal(out=s2red[:, 3:4], in_=s2red[:, 2:3]))
            vch.bar(vector.tensor_scalar_mul(out=k2f[m % 2][:], in0=e2[:], scalar1=s2red[:, 3:4]))
            if m >= 2:
                vector.wait_ge(s_ks, 16 * (m - 1))
            vector.tensor_copy(out=k2bs[m % 2][:],
                               in_=k2f[m % 2][:]).then_inc(s_v2, 1)
            vector.wait_ge(s_pe, NG + 2 * m + 2)
            vector.tensor_tensor(out=runmax[:], in0=runmax[:], in1=ps_cs[m % 2][0:1, :],
                                 op=ALU.max).then_inc(s_v2, 1)
        if stages >= 3:
            vector.wait_ge(s_v2, 3 * NMAT)
            vector.tensor_reduce(out=pcm[0:1, 0:1], in_=runmax[0:1, :], axis=AX.X,
                                 op=ALU.max).then_inc(s_v2, 1)

    def scalar_body(scalar):
        ach = Chain(scalar, s_ach)
        for i in range(nt):
            b = i % NXB
            scalar.wait_ge(s_v, 1 + 3 * i + 1)
            ach.bar(scalar.activation(out=stat[:, 5:6], in_=stat[:, 4:5], func=AF.Ln, bias=epsc[:, 0:1]))
            scalar.activation(out=stat[:, 5:6], in_=stat[:, 5:6], func=AF.Exp,
                              scale=-0.5).then_inc(s_a, 1)
            scalar.wait_ge(s_v, 1 + 3 * i + 2)
            if i >= NXB:
                scalar.wait_ge(s_st, (i - NXB + 1) * 16)
            scalar.activation(out=xn_t[b][:], in_=x_t[b][:], func=AF.Identity,
                              scale=stat[:, 5:6], bias=stat[:, 6:7]).then_inc(s_a, 1)
            scalar.wait_ge(s_v, 1 + 3 * i + 3)
            for c in range(NCH):
                if "noTR" in cuts:
                    scalar.sem_inc(s_tr, 16)
                else:
                    scalar.dma_start(out=xn_fm[c][:, i * P:(i + 1) * P],
                                     in_=xnb_t[i % 2][:, c * P:(c + 1) * P],
                                     transpose=True).then_inc(s_tr, 16)
        for k in range(NQK if stages >= 2 else 0):
            g, tb = divmod(k, NTB)
            proj, oc = divmod(g, NCH)
            scalar.wait_ge(s_lr, k + 1)
            if tb == 0 and g >= 1:
                scalar.wait_ge(s_ps, 16 * g)
            bias = (bqs, bks)[proj]
            scalar.activation(out=pbf[g % 2][:, tb * 512:(tb + 1) * 512],
                              in_=ps_q[k % 4][:], func=AF.Identity, scale=S4,
                              bias=bias[:, oc:oc + 1]).then_inc(s_ev, 1)
        for kv in range(0 if ("noV" in cuts or stages < 2) else NVG):
            tt, oh = divmod(kv, 2)
            k = NQK + kv
            scalar.wait_ge(s_pe, k + 1)
            if oh == 0 and tt >= 1:
                scalar.wait_ge(s_ps, 16 * (16 + tt))
            scalar.activation(out=vbf[tt % 2][:, oh * 512:(oh + 1) * 512],
                              in_=ps_q[k % 4][:], func=AF.Copy).then_inc(s_ev, 1)
        for m in range(NMAT if stages >= 3 else 0):
            scalar.wait_ge(s_v2, 3 * m + 1)
            scalar.activation(out=e2[:], in_=ps_s2[m % 2][:], func=AF.Exp,
                              scale=1.0 / 256.0, bias=s2red[:, 1:2]).then_inc(s_s2a, 1)

    def tensor_body(tensor):
        if stages < 2:
            return
        tensor.wait_ge(s_tr, NT * NCH * 16)
        tensor.wait_ge(s_wbf, 24)
        tensor.wait_ge(s_v, 1)
        tensor.wait_ge(s_c, 1)
        for k in range(NQK):
            g, tb = divmod(k, NTB)
            proj, oc = divmod(g, NCH)
            w = (wq, wk)[proj]
            if k >= 4:
                tensor.wait_ge(s_ev, k - 3)
            mi = None
            for cc in range(NCH):
                mi = tensor.matmul(ps_q[k % 4][:], w[cc][:, oc * P:(oc + 1) * P],
                                   xn_fm[cc][:, tb * 512:(tb + 1) * 512],
                                   start=(cc == 0), stop=(cc == NCH - 1))
            mi.then_inc(s_pe, 1)
        for kv in range(0 if "noV" in cuts else NVG):
            tt, oh = divmod(kv, 2)
            k = NQK + kv
            tensor.wait_ge(s_ev, k - 3)
            for cc in range(NCH):
                tensor.matmul(ps_q[k % 4][:], xn_fm[cc][:, tt * P:(tt + 1) * P],
                              wv[cc][:, oh * 512:(oh + 1) * 512],
                              start=(cc == 0), stop=False)
            tensor.matmul(ps_q[k % 4][:], ones_row[0:1, :], bvb[0:1, oh * 512:(oh + 1) * 512],
                          start=False, stop=True).then_inc(s_pe, 1)
        if stages < 3:
            return
        tensor.wait_ge(s_lm, 16)
        for m in range(NMAT):
            sq_, h = divmod(m, HEADS)
            cc, half = divmod(h, 2)
            prow = 64 * half
            if m >= 2:
                tensor.wait_ge(s_s2a, m - 1)
            qs = qlm_b[prow:prow + 64, cc * BC * LM + sq_ * LM: cc * BC * LM + (sq_ + 1) * LM]
            ks = klm_b[prow:prow + 64, cc * BC * LM + sq_ * LM: cc * BC * LM + (sq_ + 1) * LM]
            tensor.matmul(ps_s2[m % 2][:], qs, ks, start=True, stop=True).then_inc(s_pe, 1)
            tensor.wait_ge(s_v2, 3 * m + 2)
            tensor.matmul(ps_cs[m % 2][:], ones_col[:], k2f[m % 2][:],
                          start=True, stop=True).then_inc(s_pe, 1)

    es.close()
    return {"sync": sync_body, "gpsimd": gpsimd_body, "vector": vector_body,
            "scalar": scalar_body, "tensor": tensor_body}



ENGINES = ["sync", "vector", "scalar", "tensor", "gpsimd"]
DMA_POOL = {"sync": 16, "scalar": 8, "gpsimd": 2, "vector": 2, "tensor": 1}


class Fence:
    def __init__(self, comp, dma):
        self.comp = comp
        self.dma = dma
        self.enforced = set()


class SP:

    def __init__(self, nc):
        self.nc = nc
        self.ops = []
        self._runC = {en: 0 for en in ENGINES}
        self._dmacnt = {}
        self._dmarr = {en: 0 for en in ENGINES}
        self._meta = []

    def _push(self, eng, fn, inc, raw, after):
        i = len(self.ops)
        deps = None if after is None else (list(after) if isinstance(after, (list, tuple)) else [after])
        self.ops.append((eng, fn, inc, raw, deps))
        if inc == 16:
            slot = self._dmarr[eng] % DMA_POOL[eng]
            self._dmarr[eng] += 1
            k = self._dmacnt.get((eng, slot), 0) + 1
            self._dmacnt[(eng, slot)] = k
            self._meta.append(("d", eng, slot, k))
        else:
            self._runC[eng] += inc
            self._meta.append(("c", eng, self._runC[eng]))
        return i

    def add(self, eng, fn, dma=False, after=None):
        return self._push(eng, fn, 16 if dma else 1, False, after)

    def add_raw(self, eng, fn, inc=1, after=None):
        return self._push(eng, fn, inc, True, after)

    def fence(self):
        return Fence(dict(self._runC), dict(self._dmacnt))

    def last(self):
        return len(self.ops) - 1


    def new_phase(self):
        self._rk = {}
        self._wk = {}
        self._pf = self.fence()
        return self._pf

    def op(self, eng, fn, reads=(), writes=(), dma=False, extra=()):
        if not hasattr(self, "_rk"):
            self.new_phase()
        if getattr(self, "_serial", False):
            h = self.add(eng, fn, dma=dma, after=[self.last()] if self.ops else [])
            for k in reads:
                self._rk.setdefault(k, []).append(h)
            for k in writes:
                self._wk[k] = h
                self._rk[k] = []
            return h
        deps = list(extra)
        fresh = False
        for k in reads:
            if k in self._wk:
                deps.append(self._wk[k])
        for k in writes:
            if k in self._wk or k in self._rk:
                deps.extend(self._rk.get(k, []))
                if k in self._wk:
                    deps.append(self._wk[k])
            else:
                fresh = True
        if fresh or (not reads and not writes):
            deps.append(self._pf)
        h = self.add(eng, fn, dma=dma, after=deps)
        for k in reads:
            self._rk.setdefault(k, []).append(h)
        for k in writes:
            self._wk[k] = h
            self._rk[k] = []
        return h

    def emit(self, es, prologue=None):
        nc = self.nc
        semC = {en: es.enter_context(nc.semaphore("spc_" + en)) for en in ENGINES}
        semD = {key: es.enter_context(nc.semaphore(f"spd_{key[0]}_{key[1]}"))
                for key in self._dmacnt}
        s_bar = es.enter_context(nc.semaphore("s_bar"))
        blk = es.enter_context(nc.Block())
        mine = {}
        for i, (e, _, _, _, _) in enumerate(self.ops):
            mine.setdefault(e, []).append(i)

        def mk(ename):
            idxs = mine.get(ename, [])

            def body(eng):
                if prologue is not None:
                    if ename in prologue:
                        prologue[ename](eng)
                    eng.sem_inc(s_bar, 1)
                    eng.wait_ge(s_bar, len(ENGINES))
                for i in idxs:
                    _, fn, inc, raw, deps = self.ops[i]
                    if deps is None:
                        deps = [i - 1] if i > 0 else []
                    needC = {}
                    needD = {}
                    for dp in deps:
                        if isinstance(dp, Fence):
                            if ename in dp.enforced:
                                continue
                            dp.enforced.add(ename)
                            for en2, v in dp.comp.items():
                                if v > 0 and en2 != ename:
                                    needC[en2] = max(needC.get(en2, 0), v)
                            for key, v in dp.dma.items():
                                if v > 0:
                                    needD[key] = max(needD.get(key, 0), v)
                        else:
                            if dp < 0:
                                continue
                            m = self._meta[dp]
                            if m[0] == "d":
                                key = (m[1], m[2])
                                needD[key] = max(needD.get(key, 0), m[3])
                            else:
                                je, cum = m[1], m[2]
                                if je == ename and je == "tensor":
                                    continue
                                needC[je] = max(needC.get(je, 0), cum)
                    mm = self._meta[i]
                    if mm[0] == "d" and mm[3] > 1:
                        key = (mm[1], mm[2])
                        needD[key] = max(needD.get(key, 0), mm[3] - 1)
                    for je in ENGINES:
                        if je in needC:
                            eng.wait_ge(semC[je], needC[je])
                    for key, v in needD.items():
                        eng.wait_ge(semD[key], 16 * v)
                    if raw:
                        fn(eng, semC[ename], inc)
                    elif mm[0] == "d":
                        fn(eng).then_inc(semD[(mm[1], mm[2])], 16)
                    else:
                        fn(eng).then_inc(semC[ename], inc)
                if ename == "sync":
                    for en2 in ENGINES:
                        if self._runC[en2] > 0:
                            eng.wait_ge(semC[en2], self._runC[en2])
                    for key, v in self._dmacnt.items():
                        eng.wait_ge(semD[key], 16 * v)
            return body

        for en in ENGINES:
            getattr(blk, en)(mk(en))



def emit_C(nc, sp, k2_in, amax_in, eyes_in, identb_in, z_o, zt_o):
    es = contextlib.ExitStack()
    sbt = lambda n, s, dt=F32: es.enter_context(nc.sbuf_tensor(n, s, dt))
    pst = lambda n, s, dt=F32: es.enter_context(nc.psum_tensor(n, s, dt))
    k2m = sbt("k2m", [P, LM], BF16)
    amax = sbt("amax_s", [1, 4])
    gm = sbt("gm", [1, 1])
    rc = sbt("rc", [1, 1])
    rcb = sbt("rcb", [P, 1])
    onesr = sbt("onesr", [1, P])
    eyes = sbt("eyes_s", [P, 3 * P])
    eyesb = sbt("eyesb", [P, 3 * P], BF16)
    identb = sbt("identb_s", [P, P], BF16)
    K2T = sbt("K2T", [P, LM], BF16)
    Vt = sbt("Vt", [P, LM], BF16)
    VTt = sbt("VTt", [P, LM], BF16)
    ATs = sbt("ATs", [P, LM], BF16)
    N2 = sbt("N2", [P, LM], BF16)
    N1 = sbt("N1", [P, LM], BF16)
    Mt = sbt("Mt", [P, LM], BF16)
    ps_a = pst("ps_a", [P, LM])
    ps_b = pst("ps_b", [P, LM])
    ps_t = pst("ps_t", [P, LM], BF16)
    ps_r = pst("ps_r", [P, 1])

    A = sp.add
    A("sync", lambda e: e.dma_start(out=amax[:], in_=amax_in), dma=True, after=[sp.fence()])
    A("sync", lambda e: e.dma_start(out=eyes[:], in_=eyes_in), dma=True)
    A("sync", lambda e: e.dma_start(out=identb[:], in_=identb_in), dma=True)
    A("gpsimd", lambda e: e.memset(onesr[:], 1.0))
    A("vector", lambda e: e.tensor_reduce(out=gm[:], in_=amax[:], axis=AX.X, op=ALU.max))
    A("vector", lambda e: e.reciprocal(out=rc[:], in_=gm[:]))
    A("vector", lambda e: e.tensor_copy(out=eyesb[:], in_=eyes[:]))
    A("tensor", lambda e: e.matmul(ps_r[:], onesr[0:1, :], rc[0:1, :], start=True, stop=True))
    A("scalar", lambda e: e.activation(out=rcb[:], in_=ps_r[:], func=AF.Copy))
    for m in range(NMAT):
        mm = m
        A("sync", lambda e, m=mm: e.dma_start(out=k2m[:], in_=k2_in[m * P:(m + 1) * P, :]), dma=True)
        A("tensor", lambda e: e.transpose(ps_t[:], k2m[:], identb[:]))
        A("scalar", lambda e: e.activation(out=K2T[:], in_=ps_t[:], func=AF.Copy))
        A("scalar", lambda e: e.activation(out=Vt[:], in_=ps_t[:], func=AF.Identity, scale=rcb[:, 0:1]))
        A("vector", lambda e: e.tensor_scalar(out=VTt[:], in0=k2m[:], scalar1=rcb[:, 0:1],
                                              scalar2=None, op0=ALU.mult))
        for it in range(6):
            A("tensor", lambda e: e.matmul(ps_a[:], K2T[:], Vt[:], start=True, stop=True))
            A("tensor", lambda e: e.matmul(ps_b[:], Vt[:], K2T[:], start=True, stop=True))
            A("vector", lambda e: e.scalar_tensor_tensor(out=N2[:], in0=ps_a[:], scalar=-1.0,
                                                         in1=eyesb[:, 0:P], op0=ALU.mult, op1=ALU.add))
            A("scalar", lambda e: e.activation(out=ATs[:], in_=ps_b[:], func=AF.Copy))
            A("tensor", lambda e: e.matmul(ps_a[:], ATs[:], N2[:], start=True, stop=True))
            A("vector", lambda e: e.scalar_tensor_tensor(out=N1[:], in0=ps_a[:], scalar=-1.0,
                                                         in1=eyesb[:, P:2 * P], op0=ALU.mult, op1=ALU.add))
            A("tensor", lambda e: e.matmul(ps_a[:], ATs[:], N1[:], start=True, stop=True))
            A("vector", lambda e: e.scalar_tensor_tensor(out=Mt[:], in0=ps_a[:], scalar=-1.0,
                                                         in1=eyesb[:, 2 * P:3 * P], op0=ALU.mult, op1=ALU.add))
            A("tensor", lambda e: e.matmul(ps_a[:], VTt[:], Mt[:], start=True, stop=True))
            A("tensor", lambda e: e.matmul(ps_b[:], Mt[:], VTt[:], start=True, stop=True))
            A("scalar", lambda e: e.activation(out=Vt[:], in_=ps_a[:], func=AF.Identity, scale=0.25))
            A("scalar", lambda e: e.activation(out=VTt[:], in_=ps_b[:], func=AF.Identity, scale=0.25))
        A("sync", lambda e, m=mm: e.dma_start(out=z_o[m * P:(m + 1) * P, :], in_=Vt[:]), dma=True)
        A("sync", lambda e, m=mm: e.dma_start(out=zt_o[m * P:(m + 1) * P, :], in_=VTt[:]), dma=True)
    es.close()



def emit_D(nc, sp, q_in, k_in, v_in, qlm_in, klm_in, zt_in, identf_in, attn_o):
    es = contextlib.ExitStack()
    sbt = lambda n, s, dt=F32: es.enter_context(nc.sbuf_tensor(n, s, dt))
    pst = lambda n, s, dt=F32: es.enter_context(nc.psum_tensor(n, s, dt))
    qlmt = [sbt(f"qlmt{c}", [P, BC * LM], BF16) for c in range(NCH)]
    klmt = [sbt(f"klmt{c}", [P, BC * LM], BF16) for c in range(NCH)]
    ksl = sbt("ksl", [P, S], BF16)
    qsl = sbt("qsl", [P, S], BF16)
    vsl = sbt("vsl", [P, 16 * HD], BF16)
    ztm = sbt("ztm", [P, LM], BF16)
    s3f = sbt("s3f", [P, S])
    e3 = sbt("e3", [P, S], BF16)
    e3t = es.enter_context(nc.sbuf_tensor("e3t", [P, S], BF16, is_transpose_dest=True))
    e1 = sbt("e1", [P, S], BF16)
    red = sbt("red", [P, 4])
    Us = sbt("Us", [P, HD], BF16)
    Ps = sbt("Ps", [P, HD], BF16)
    r1s = sbt("r1s", [1, S])
    rr1 = sbt("rr1", [1, S])
    bcs = sbt("bcs", [64, 512])
    onescb = sbt("onescb", [P, 1], BF16)
    onesr1 = sbt("onesr1", [1, 64])
    identf = sbt("identf_s", [P, P])
    afm = sbt("afm", [P, T])
    stag = sbt("stag", [P, P])
    ps_m = pst("ps_m", [P, 512])
    ps_u = pst("ps_u", [P, HD])
    ps_x = pst("ps_x", [64, 512])
    ps_bc = pst("ps_bc", [64, 512])
    ps_t = pst("ps_tD", [P, P])
    ps_r1 = pst("ps_r1", [1, 512])

    A = sp.add
    fenD = sp.fence()
    for c in range(NCH):
        A("sync", lambda e, c=c: e.dma_start(out=qlmt[c][:], in_=qlm_in[c * P:(c + 1) * P, :]), dma=True,
          after=([fenD] if c == 0 else None))
        A("sync", lambda e, c=c: e.dma_start(out=klmt[c][:], in_=klm_in[c * P:(c + 1) * P, :]), dma=True)
    A("sync", lambda e: e.dma_start(out=identf[:], in_=identf_in), dma=True)
    A("gpsimd", lambda e: e.memset(onescb[:], 1.0))
    A("gpsimd", lambda e: e.memset(onesr1[:], 1.0))

    for cc in range(NCH):
        for s in range(BC):
            for hh in range(2):
                h = 2 * cc + hh
                m = s * HEADS + h
                A("sync", lambda e, cc=cc, hh=hh, s=s: e.dma_start(
                    out=ksl[64 * hh:64 * hh + 64, :],
                    in_=k_in[cc * P + 64 * hh: cc * P + 64 * hh + 64,
                             s * S:(s + 1) * S]), dma=True)
                A("sync", lambda e, cc=cc, hh=hh, s=s: e.dma_start(
                    out=qsl[64 * hh:64 * hh + 64, :],
                    in_=q_in[cc * P + 64 * hh: cc * P + 64 * hh + 64,
                             s * S:(s + 1) * S]), dma=True)
                for j in range(16):
                    A("sync", lambda e, s=s, j=j, h=h: e.dma_start(
                        out=vsl[:, j * HD:(j + 1) * HD],
                        in_=v_in[s * S + j * P: s * S + (j + 1) * P,
                                 h * HD:(h + 1) * HD]), dma=True)
                A("sync", lambda e, m=m: e.dma_start(out=ztm[:], in_=zt_in[m * P:(m + 1) * P, :]), dma=True)
                for b in range(4):
                    A("tensor", lambda e, cc=cc, hh=hh, s=s, b=b: e.matmul(
                        ps_m[:], qlmt[cc][64 * hh:64 * hh + 64, s * LM:(s + 1) * LM],
                        ksl[64 * hh:64 * hh + 64, b * 512:(b + 1) * 512], start=True, stop=True))
                    A("scalar", lambda e, b=b: e.activation(out=s3f[:, b * 512:(b + 1) * 512],
                                                            in_=ps_m[:], func=AF.Copy))
                A("vector", lambda e: e.tensor_reduce(out=red[:, 0:1], in_=s3f[:], axis=AX.X, op=ALU.max))
                A("vector", lambda e: e.tensor_scalar_mul(out=red[:, 1:2], in0=red[:, 0:1],
                                                          scalar1=-1.0 / SEG))
                A("scalar", lambda e: e.activation(out=e3[:], in_=s3f[:], func=AF.Exp,
                                                   scale=1.0 / SEG, bias=red[:, 1:2]))
                A("vector", lambda e: e.tensor_reduce(out=red[:, 2:3], in_=e3[:], axis=AX.X, op=ALU.add))
                A("vector", lambda e: e.reciprocal(out=red[:, 3:4], in_=red[:, 2:3]))
                for j in range(16):
                    A("scalar", lambda e, j=j: e.dma_start(out=e3t[:, j * P:(j + 1) * P],
                                                           in_=e3[:, j * P:(j + 1) * P],
                                                           transpose=True), dma=True)
                for j in range(16):
                    A("tensor", lambda e, j=j: e.matmul(ps_u[:], e3t[:, j * P:(j + 1) * P],
                                                        vsl[:, j * HD:(j + 1) * HD],
                                                        start=(j == 0), stop=(j == 15)))
                A("vector", lambda e: e.tensor_scalar(out=Us[:], in0=ps_u[:], scalar1=red[:, 3:4],
                                                      scalar2=None, op0=ALU.mult))
                A("tensor", lambda e: e.matmul(ps_u[:], ztm[:], Us[:], start=True, stop=True))
                A("scalar", lambda e: e.activation(out=Ps[:], in_=ps_u[:], func=AF.Copy))
                for b in range(4):
                    A("tensor", lambda e, cc=cc, hh=hh, s=s, b=b: e.matmul(
                        ps_m[:], klmt[cc][64 * hh:64 * hh + 64, s * LM:(s + 1) * LM],
                        qsl[64 * hh:64 * hh + 64, b * 512:(b + 1) * 512], start=True, stop=True))
                    A("scalar", lambda e, b=b: e.activation(out=e1[:, b * 512:(b + 1) * 512],
                                                            in_=ps_m[:], func=AF.Exp, scale=1.0 / SEG))
                for b in range(4):
                    A("tensor", lambda e, b=b: e.matmul(ps_r1[:], onescb[:], e1[:, b * 512:(b + 1) * 512],
                                                        start=True, stop=True))
                    A("scalar", lambda e, b=b: e.activation(out=r1s[0:1, b * 512:(b + 1) * 512],
                                                            in_=ps_r1[:], func=AF.Copy))
                A("vector", lambda e: e.reciprocal(out=rr1[:], in_=r1s[:]))
                for b in range(4):
                    A("tensor", lambda e, b=b: e.matmul(ps_x[:], Ps[:], e1[:, b * 512:(b + 1) * 512],
                                                        start=True, stop=True))
                    A("tensor", lambda e, b=b: e.matmul(ps_bc[:], onesr1[0:1, :],
                                                        rr1[0:1, b * 512:(b + 1) * 512],
                                                        start=True, stop=True))
                    A("scalar", lambda e, b=b: e.activation(out=bcs[:], in_=ps_bc[:], func=AF.Copy))
                    A("vector", lambda e, hh=hh, s=s, b=b: e.tensor_tensor(
                        out=afm[64 * hh:64 * hh + 64, s * S + b * 512: s * S + (b + 1) * 512],
                        in0=ps_x[:], in1=bcs[:], op=ALU.mult))
        for tt in range(NT):
            A("tensor", lambda e, tt=tt: e.transpose(ps_t[:], afm[:, tt * P:(tt + 1) * P], identf[:]))
            A("scalar", lambda e: e.activation(out=stag[:], in_=ps_t[:], func=AF.Copy))
            A("sync", lambda e, tt=tt, cc=cc: e.dma_start(
                out=attn_o[tt * P:(tt + 1) * P, cc * P:(cc + 1) * P], in_=stag[:]), dma=True)
    es.close()



def emit_quant(Aop, x_ap, q_ap, scr, C, pp=P):
    nb = C // 16
    bm, sc, rs, hs = scr["bm"], scr["sc"], scr["rs"], scr["hs"]
    y, a, t1, t2, t3 = scr["y"], scr["a"], scr["t1"], scr["t2"], scr["t3"]
    ri1, ri2 = scr["ri1"], scr["ri2"]
    x3 = x_ap.rearrange("p (n w) -> p n w", w=16)
    Aop("vector", lambda e: e.tensor_reduce(out=bm[:pp, :nb], in_=x3, axis=AX.X,
                                            op=ALU.max, apply_absolute_value=True))
    Aop("vector", lambda e: e.tensor_scalar(out=sc[:pp, :nb], in0=bm[:pp, :nb], scalar1=1.0 / 6.0,
                                            scalar2=1e-12, op0=ALU.mult, op1=ALU.max))
    Aop("vector", lambda e: e.reciprocal(out=rs[:pp, :nb], in_=sc[:pp, :nb]))
    rs3 = rs[:pp, :nb].unsqueeze(2).broadcast_to([pp, nb, 16])
    y3 = y[:pp, :C].rearrange("p (n w) -> p n w", w=16)
    Aop("vector", lambda e: e.tensor_tensor(out=y3, in0=x3, in1=rs3, op=ALU.mult))
    Aop("vector", lambda e: e.tensor_scalar(out=a[:pp, :C].bitcast(mybir.dt.int32),
                                            in0=y[:pp, :C].bitcast(mybir.dt.int32),
                                            scalar1=0x7fffffff, scalar2=None,
                                            op0=ALU.bitwise_and))
    Aop("vector", lambda e: e.tensor_scalar_mul(out=ri1[:pp, :C], in0=a[:pp, :C], scalar1=2.0))
    Aop("vector", lambda e: e.tensor_copy(out=ri2[:pp, :C], in_=a[:pp, :C]))
    Aop("vector", lambda e: e.tensor_scalar(out=t1[:pp, :C], in0=ri1[:pp, :C], scalar1=0.5,
                                            scalar2=2.0, op0=ALU.mult, op1=ALU.min))
    Aop("vector", lambda e: e.tensor_scalar(out=t2[:pp, :C], in0=ri2[:pp, :C], scalar1=2.0,
                                            scalar2=4.0, op0=ALU.max, op1=ALU.min))
    Aop("vector", lambda e: e.tensor_tensor(out=t1[:pp, :C], in0=t1[:pp, :C], in1=t2[:pp, :C],
                                            op=ALU.add))
    Aop("vector", lambda e: e.tensor_scalar(out=t2[:pp, :C], in0=a[:pp, :C], scalar1=5.0,
                                            scalar2=2.0, op0=ALU.is_gt, op1=ALU.mult))
    Aop("vector", lambda e: e.scalar_tensor_tensor(out=t1[:pp, :C], in0=t1[:pp, :C], scalar=-2.0,
                                                   in1=t2[:pp, :C], op0=ALU.add, op1=ALU.add))
    Aop("scalar", lambda e: e.activation(out=t3[:pp, :C], in_=y[:pp, :C], func=AF.Sign))
    Aop("vector", lambda e: e.tensor_tensor(out=t1[:pp, :C], in0=t1[:pp, :C], in1=t3[:pp, :C],
                                            op=ALU.mult))
    Aop("vector", lambda e: e.tensor_scalar_mul(out=hs[:pp, :nb], in0=sc[:pp, :nb], scalar1=1.0))
    hs3 = hs[:pp, :nb].unsqueeze(2).broadcast_to([pp, nb, 16])
    t13 = t1[:pp, :C].rearrange("p (n w) -> p n w", w=16)
    q3 = q_ap.rearrange("p (n w) -> p n w", w=16)
    Aop("vector", lambda e: e.tensor_tensor(out=q3, in0=t13, in1=hs3, op=ALU.mult))


def quant_scratch(sbt, tag, C, dt_mid=F32):
    nb = C // 16
    return {
        "bm": sbt(f"bm{tag}", [P, nb]), "sc": sbt(f"sc{tag}", [P, nb]),
        "rs": sbt(f"rs{tag}", [P, nb]), "hs": sbt(f"hs{tag}", [P, nb]),
        "y": sbt(f"yq{tag}", [P, C]), "a": sbt(f"aq{tag}", [P, C]),
        "t1": sbt(f"t1q{tag}", [P, C], BF16), "t2": sbt(f"t2q{tag}", [P, C], BF16),
        "t3": sbt(f"t3q{tag}", [P, C], BF16),
        "ri1": sbt(f"ri1q{tag}", [P, C], I16), "ri2": sbt(f"ri2q{tag}", [P, C], I16),
    }



def emit_WQfull(nc, sp, tag, wg_in, wu_in, wd_in, wgq_o, wuq_o, wdq_o):
    es = contextlib.ExitStack()
    sbt = lambda n, s, dt=F32: es.enter_context(nc.sbuf_tensor(n + tag, s, dt))
    wt = sbt("wt", [P, I_FFN])
    qt = sbt("qt", [P, I_FFN], BF16)
    qtt = es.enter_context(nc.sbuf_tensor("qtt" + tag, [P, I_FFN], BF16, is_transpose_dest=True))
    scr = quant_scratch(sbt, "w", I_FFN)
    A = sp.add
    fen = sp.fence()
    first = [True]
    jobs = []
    for r in range(I_FFN // P):
        jobs.append((wg_in[r * P:(r + 1) * P, :], wgq_o, r, D))
        jobs.append((wu_in[r * P:(r + 1) * P, :], wuq_o, r, D))
    for r in range(NCH):
        jobs.append((wd_in[r * P:(r + 1) * P, :], wdq_o, r, I_FFN))
    for (src_ap, dst, r, C) in jobs:
        A("sync", lambda e, src_ap=src_ap, C=C: e.dma_start(out=wt[:, 0:C], in_=src_ap), dma=True,
          after=([fen] if first[0] else None))
        first[0] = False
        emit_quant(A, wt[:, 0:C], qt[:, 0:C], scr, C)
        for j in range(C // P):
            A("scalar", lambda e, j=j: e.dma_start(out=qtt[:, j * P:(j + 1) * P],
                                                   in_=qt[:, j * P:(j + 1) * P],
                                                   transpose=True), dma=True)
        for j in range(C // P):
            A("sync", lambda e, dst=dst, r=r, j=j: e.dma_start(
                out=dst[j * P:(j + 1) * P, r * P:(r + 1) * P],
                in_=qtt[:, j * P:(j + 1) * P]), dma=True)
    es.close()



def emit_FFN(nc, sp, tag, xn_in, wg_in, wu_in, wd_in, ffn_o, hq_sc):
    es = contextlib.ExitStack()
    sbt = lambda n, s, dt=F32: es.enter_context(nc.sbuf_tensor(n + tag, s, dt))
    pst = lambda n, s, dt=F32: es.enter_context(nc.psum_tensor(n + tag, s, dt))
    wpool = sbt("wpool", [P, 2 * NCH * I_FFN], BF16)
    wgq = [wpool[:, c * I_FFN:(c + 1) * I_FFN] for c in range(NCH)]
    wuq = [wpool[:, (NCH + c) * I_FFN:(NCH + c + 1) * I_FFN] for c in range(NCH)]
    wdq = [wpool[:, j * D:(j + 1) * D] for j in range(22)]
    xt = sbt("xt", [P, D])
    xq = sbt("xq", [P, D], BF16)
    xqf = es.enter_context(nc.sbuf_tensor("xqf" + tag, [P, D], BF16, is_transpose_dest=True))
    gact = sbt("gact", [P, I_FFN])
    uact = sbt("uact", [P, I_FFN])
    hact = sbt("hact", [P, I_FFN])
    hqt = sbt("hqt", [P, I_FFN], BF16)
    hqtt = es.enter_context(nc.sbuf_tensor("hqtt" + tag, [P, I_FFN], BF16, is_transpose_dest=True))
    scr = quant_scratch(sbt, "f", I_FFN)
    hload = sbt("hload", [P, P], BF16)
    ot = sbt("ot", [P, D])
    ps = [pst(f"psF{i}", [P, 512]) for i in range(2)]

    IB = [(0, 512), (512, 1024), (1024, 1536), (1536, 2048), (2048, 2560), (2560, 2816)]
    A = sp.add
    fenF = sp.fence()
    for c in range(NCH):
        A("sync", lambda e, c=c: e.dma_start(out=wgq[c], in_=wg_in[c * P:(c + 1) * P, :]), dma=True,
          after=([fenF] if c == 0 else None))
        A("sync", lambda e, c=c: e.dma_start(out=wuq[c], in_=wu_in[c * P:(c + 1) * P, :]), dma=True)
    for tt in range(NT):
        A("sync", lambda e, tt=tt: e.dma_start(out=xt[:], in_=xn_in[tt * P:(tt + 1) * P, :]), dma=True)
        emit_quant(A, xt[:], xq[:], scr, D)
        for j in range(NCH):
            A("scalar", lambda e, j=j: e.dma_start(out=xqf[:, j * P:(j + 1) * P],
                                                   in_=xq[:, j * P:(j + 1) * P], transpose=True), dma=True)
        for wsel in range(2):
            wts = (wgq, wuq)[wsel]
            dstact = (gact, uact)[wsel]
            for (lo, hi) in IB:
                n = hi - lo
                for cc in range(NCH):
                    A("tensor", lambda e, cc=cc, lo=lo, hi=hi, wts=wts, n=n: e.matmul(
                        ps[0][:, 0:n], xqf[:, cc * P:(cc + 1) * P], wts[cc][:, lo:hi],
                        start=(cc == 0), stop=(cc == NCH - 1)))
                if wsel == 0:
                    A("scalar", lambda e, lo=lo, hi=hi, n=n, dstact=dstact: e.activation(
                        out=dstact[:, lo:hi], in_=ps[0][:, 0:n], func=AF.Silu))
                else:
                    A("scalar", lambda e, lo=lo, hi=hi, n=n, dstact=dstact: e.activation(
                        out=dstact[:, lo:hi], in_=ps[0][:, 0:n], func=AF.Copy))
        A("vector", lambda e: e.tensor_tensor(out=hact[:], in0=gact[:], in1=uact[:], op=ALU.mult))
        emit_quant(A, hact[:], hqt[:], scr, I_FFN)
        for j in range(22):
            A("scalar", lambda e, j=j: e.dma_start(out=hqtt[:, j * P:(j + 1) * P],
                                                   in_=hqt[:, j * P:(j + 1) * P], transpose=True), dma=True)
        for j in range(22):
            A("sync", lambda e, j=j, tt=tt: e.dma_start(
                out=hq_sc[j * P:(j + 1) * P, tt * P:(tt + 1) * P],
                in_=hqtt[:, j * P:(j + 1) * P]), dma=True)
    for j in range(22):
        A("sync", lambda e, j=j: e.dma_start(out=wdq[j], in_=wd_in[j * P:(j + 1) * P, :]), dma=True)
    for tt in range(NT):
        for half in range(2):
            for j in range(22):
                A("sync", lambda e, j=j, tt=tt: e.dma_start(
                    out=hload[:], in_=hq_sc[j * P:(j + 1) * P, tt * P:(tt + 1) * P]), dma=True)
                A("tensor", lambda e, j=j, half=half: e.matmul(
                    ps[half][:], hload[:], wdq[j][:, half * 512:(half + 1) * 512],
                    start=(j == 0), stop=(j == 21)))
            A("scalar", lambda e, half=half: e.activation(out=ot[:, half * 512:(half + 1) * 512],
                                                          in_=ps[half][:], func=AF.Copy))
        A("sync", lambda e, tt=tt: e.dma_start(out=ffn_o[tt * P:(tt + 1) * P, :], in_=ot[:]), dma=True)
    es.close()



def emit_F(nc, sp, x_in, at_in, ff_in, eps_in, x1_o, xn1_o):
    es = contextlib.ExitStack()
    sbt = lambda n, s, dt=F32: es.enter_context(nc.sbuf_tensor(n, s, dt))
    xa = [sbt(f"xaF{b}", [P, D]) for b in range(2)]
    xb = [sbt(f"xbF{b}", [P, D]) for b in range(2)]
    xc = [sbt(f"xcF{b}", [P, D]) for b in range(2)]
    xn = [sbt(f"xnF{b}", [P, D]) for b in range(2)]
    xsq = sbt("xsqF", [P, D])
    stat = [sbt(f"statF{b}", [P, 8]) for b in range(2)]
    epsc = sbt("epscF", [P, 1])
    sp.new_phase()
    sp._serial = "F" in _G_SERIAL
    O = sp.op
    O("sync", lambda e: e.dma_start(out=epsc[:], in_=eps_in), writes=["epsF"], dma=True)
    for i in range(NT):
        b = i % 2
        O("sync", lambda e, i=i, b=b: e.dma_start(out=xa[b][:], in_=x_in[i * P:(i + 1) * P, :]),
          writes=[f"Fa{b}"], dma=True)
        O("sync", lambda e, i=i, b=b: e.dma_start(out=xb[b][:], in_=at_in[i * P:(i + 1) * P, :]),
          writes=[f"Fb{b}"], dma=True)
        O("sync", lambda e, i=i, b=b: e.dma_start(out=xc[b][:], in_=ff_in[i * P:(i + 1) * P, :]),
          writes=[f"Fc{b}"], dma=True)
        O("vector", lambda e, b=b: e.tensor_tensor(out=xa[b][:], in0=xa[b][:], in1=xb[b][:], op=ALU.add),
          reads=[f"Fa{b}", f"Fb{b}"], writes=[f"Fa{b}"])
        O("vector", lambda e, b=b: e.tensor_tensor(out=xa[b][:], in0=xa[b][:], in1=xc[b][:], op=ALU.add),
          reads=[f"Fa{b}", f"Fc{b}"], writes=[f"Fa{b}"])
        O("sync", lambda e, i=i, b=b: e.dma_start(out=x1_o[i * P:(i + 1) * P, :], in_=xa[b][:]),
          reads=[f"Fa{b}"], dma=True)
        sk = f"Fs{b}"
        O("vector", lambda e, b=b: e.tensor_reduce(out=stat[b][:, 0:1], in_=xa[b][:], axis=AX.X, op=ALU.add),
          reads=[f"Fa{b}"], writes=[sk])
        O("vector", lambda e, b=b: e.tensor_tensor(out=xsq[:], in0=xa[b][:], in1=xa[b][:], op=ALU.mult),
          reads=[f"Fa{b}"], writes=["Fq"])
        O("vector", lambda e, b=b: e.tensor_reduce(out=stat[b][:, 1:2], in_=xsq[:], axis=AX.X, op=ALU.add),
          reads=["Fq"], writes=[sk])
        O("vector", lambda e, b=b: e.tensor_scalar_mul(out=stat[b][:, 2:3], in0=stat[b][:, 0:1], scalar1=1.0 / D),
          reads=[sk], writes=[sk])
        O("vector", lambda e, b=b: e.tensor_scalar_mul(out=stat[b][:, 3:4], in0=stat[b][:, 1:2], scalar1=1.0 / D),
          reads=[sk], writes=[sk])
        O("vector", lambda e, b=b: e.tensor_tensor(out=stat[b][:, 4:5], in0=stat[b][:, 2:3], in1=stat[b][:, 2:3], op=ALU.mult),
          reads=[sk], writes=[sk])
        O("vector", lambda e, b=b: e.tensor_tensor(out=stat[b][:, 4:5], in0=stat[b][:, 3:4], in1=stat[b][:, 4:5], op=ALU.subtract),
          reads=[sk], writes=[sk])
        O("scalar", lambda e, b=b: e.activation(out=stat[b][:, 5:6], in_=stat[b][:, 4:5], func=AF.Ln, bias=epsc[:, 0:1]),
          reads=[sk, "epsF"], writes=[sk])
        O("scalar", lambda e, b=b: e.activation(out=stat[b][:, 5:6], in_=stat[b][:, 5:6], func=AF.Exp, scale=-0.5),
          reads=[sk], writes=[sk])
        O("vector", lambda e, b=b: e.scalar_tensor_tensor(out=stat[b][:, 6:7], in0=stat[b][:, 2:3], scalar=-1.0,
                                                          in1=stat[b][:, 5:6], op0=ALU.mult, op1=ALU.mult),
          reads=[sk], writes=[sk])
        O("scalar", lambda e, b=b: e.activation(out=xn[b][:], in_=xa[b][:], func=AF.Identity,
                                                scale=stat[b][:, 5:6], bias=stat[b][:, 6:7]),
          reads=[f"Fa{b}", sk], writes=[f"Fn{b}"])
        O("sync", lambda e, i=i, b=b: e.dma_start(out=xn1_o[i * P:(i + 1) * P, :], in_=xn[b][:]),
          reads=[f"Fn{b}"], dma=True)
    sp._serial = False
    es.close()



def emit_I(nc, sp, a_in, b_in, c_in, o_out):
    es = contextlib.ExitStack()
    sbt = lambda n, s, dt=F32: es.enter_context(nc.sbuf_tensor(n, s, dt))
    xa = [sbt(f"xaI{b}", [P, D]) for b in range(2)]
    xb = [sbt(f"xbI{b}", [P, D]) for b in range(2)]
    xc = [sbt(f"xcI{b}", [P, D]) for b in range(2)]
    sp.new_phase()
    sp._serial = "I" in _G_SERIAL
    O = sp.op
    for i in range(NT):
        b = i % 2
        O("sync", lambda e, i=i, b=b: e.dma_start(out=xa[b][:], in_=a_in[i * P:(i + 1) * P, :]),
          writes=[f"Ia{b}"], dma=True)
        O("sync", lambda e, i=i, b=b: e.dma_start(out=xb[b][:], in_=b_in[i * P:(i + 1) * P, :]),
          writes=[f"Ib{b}"], dma=True)
        O("sync", lambda e, i=i, b=b: e.dma_start(out=xc[b][:], in_=c_in[i * P:(i + 1) * P, :]),
          writes=[f"Ic{b}"], dma=True)
        O("vector", lambda e, b=b: e.tensor_tensor(out=xa[b][:], in0=xa[b][:], in1=xb[b][:], op=ALU.add),
          reads=[f"Ia{b}", f"Ib{b}"], writes=[f"Ia{b}"])
        O("vector", lambda e, b=b: e.tensor_tensor(out=xa[b][:], in0=xa[b][:], in1=xc[b][:], op=ALU.add),
          reads=[f"Ia{b}", f"Ic{b}"], writes=[f"Ia{b}"])
        O("sync", lambda e, i=i, b=b: e.dma_start(out=o_out[i * P:(i + 1) * P, :], in_=xa[b][:]),
          reads=[f"Ia{b}"], dma=True)
    sp._serial = False
    es.close()



def emit_G(nc, sp, xn1_in, ln_b, wkT_in, wvT_in, wrT_in, woT_in, wfkT_in,
           wfvT_in, wfrT_in, tdc_in, tfc_in, eps_in, attn1_o,
           a_sc, x2_sc, xk_sc, xv_sc, xr_sc, xr2_sc, sa_sc, kfq_sc):
    es = contextlib.ExitStack()
    sbt = lambda n, s, dt=F32: es.enter_context(nc.sbuf_tensor(n, s, dt))
    pst = lambda n, s, dt=F32: es.enter_context(nc.psum_tensor(n, s, dt))
    POOLN = 34000
    pool = sbt("poolG", [P, POOLN])
    xk2_fm = [sbt(f"xk2fm{c}", [P, T], BF16) for c in range(NCH)]
    trd = [es.enter_context(nc.sbuf_tensor(f"trdG{b}", [P, D], BF16, is_transpose_dest=True))
           for b in range(2)]
    stat = [sbt(f"statG{b}", [P, 8]) for b in range(2)]
    epsc = sbt("epscG", [P, 1])
    smallc = [sbt(f"smallcG{b}", [P, 8]) for b in range(2)]
    tdc = sbt("tdcG", [P, NCH]); tfc = sbt("tfcG", [P, NCH])
    hlast = sbt("hlastG", [1, D])
    ps = [pst(f"psG{i}", [P, 512]) for i in range(4)]
    O = sp.op

    class Carve:
        def __init__(self):
            self.off = 0
        def f32(self, n):
            v = pool[:, self.off:self.off + n]
            self.off += n
            assert self.off <= POOLN
            return v
        def bf(self, n):
            v = pool[:, self.off:self.off + (n + 1) // 2].bitcast(BF16)
            self.off += (n + 1) // 2
            return v

    def ln(xa, xk, dst, dk, sb, sk, xsq, qk, wap=None, wk2=None, bap=None, bk2=None):
        O("vector", lambda e: e.tensor_reduce(out=sb[:, 0:1], in_=xa, axis=AX.X, op=ALU.add),
          reads=[xk], writes=[sk])
        O("vector", lambda e: e.tensor_tensor(out=xsq, in0=xa, in1=xa, op=ALU.mult),
          reads=[xk], writes=[qk])
        O("vector", lambda e: e.tensor_reduce(out=sb[:, 1:2], in_=xsq, axis=AX.X, op=ALU.add),
          reads=[qk], writes=[sk])
        O("vector", lambda e: e.tensor_scalar_mul(out=sb[:, 2:3], in0=sb[:, 0:1], scalar1=1.0 / D),
          reads=[sk], writes=[sk])
        O("vector", lambda e: e.tensor_scalar_mul(out=sb[:, 3:4], in0=sb[:, 1:2], scalar1=1.0 / D),
          reads=[sk], writes=[sk])
        O("vector", lambda e: e.tensor_tensor(out=sb[:, 4:5], in0=sb[:, 2:3], in1=sb[:, 2:3], op=ALU.mult),
          reads=[sk], writes=[sk])
        O("vector", lambda e: e.tensor_tensor(out=sb[:, 4:5], in0=sb[:, 3:4], in1=sb[:, 4:5], op=ALU.subtract),
          reads=[sk], writes=[sk])
        O("scalar", lambda e: e.activation(out=sb[:, 5:6], in_=sb[:, 4:5], func=AF.Ln, bias=epsc[:, 0:1]),
          reads=[sk, "eps"], writes=[sk])
        O("scalar", lambda e: e.activation(out=sb[:, 5:6], in_=sb[:, 5:6], func=AF.Exp, scale=-0.5),
          reads=[sk], writes=[sk])
        O("vector", lambda e: e.scalar_tensor_tensor(out=sb[:, 6:7], in0=sb[:, 2:3], scalar=-1.0,
                                                     in1=sb[:, 5:6], op0=ALU.mult, op1=ALU.mult),
          reads=[sk], writes=[sk])
        O("scalar", lambda e: e.activation(out=dst, in_=xa, func=AF.Identity,
                                           scale=sb[:, 5:6], bias=sb[:, 6:7]),
          reads=[xk, sk], writes=[dk])
        if wap is not None:
            O("vector", lambda e: e.tensor_tensor(out=dst, in0=dst, in1=wap, op=ALU.mult),
              reads=[dk, wk2], writes=[dk])
            O("vector", lambda e: e.tensor_tensor(out=dst, in0=dst, in1=bap, op=ALU.add),
              reads=[dk, bk2], writes=[dk])

    sp.new_phase()
    sp._serial = 1 in _G_SERIAL
    O("sync", lambda e: e.dma_start(out=epsc[:], in_=eps_in), writes=["eps"], dma=True)
    O("sync", lambda e: e.dma_start(out=tdc[:], in_=tdc_in), writes=["tdcc"], dma=True)
    O("sync", lambda e: e.dma_start(out=tfc[:], in_=tfc_in), writes=["tfcc"], dma=True)
    cv = Carve()
    lnt = {n: cv.f32(D) for n in ("ln0w", "ln0b", "ln1w", "ln1b", "tmkb", "tmvb", "tmrb")}
    for n in lnt:
        O("sync", lambda e, n=n: e.dma_start(out=lnt[n], in_=ln_b[n]), writes=["Lw_" + n], dma=True)
    xt = [cv.f32(D) for _ in range(2)]; at = [cv.f32(D) for _ in range(2)]
    ht = [cv.f32(D) for _ in range(2)]; xxt = [cv.f32(D) for _ in range(2)]
    dt_ = [cv.f32(D) for _ in range(2)]; mixt = [cv.f32(D) for _ in range(2)]
    xsq1 = [cv.f32(D) for _ in range(2)]; mixb = [cv.bf(D) for _ in range(2)]
    for i in range(NT):
        b = i % 2
        O("sync", lambda e, i=i, b=b: e.dma_start(out=xt[b], in_=xn1_in[i * P:(i + 1) * P, :]),
          writes=[f"xt{b}"], dma=True)
        ln(xt[b], f"xt{b}", at[b], f"at{b}", stat[b], f"st{b}", xsq1[b], f"xq{b}",
           lnt["ln0w"], "Lw_ln0w", lnt["ln0b"], "Lw_ln0b")
        O("sync", lambda e, i=i, b=b: e.dma_start(out=a_sc[i * P:(i + 1) * P, :], in_=at[b]),
          reads=[f"at{b}"], dma=True)
        ln(at[b], f"at{b}", ht[b], f"ht{b}", stat[b], f"st{b}", xsq1[b], f"xq{b}",
           lnt["ln1w"], "Lw_ln1w", lnt["ln1b"], "Lw_ln1b")
        if i % 16 == 0:
            O("gpsimd", lambda e, b=b: e.memset(xxt[b][0:1, 0:D], 0.0), writes=[f"xx{b}"])
        else:
            O("sync", lambda e, b=b: e.dma_start(out=xxt[b][0:1, 0:D], in_=hlast[:]),
              reads=["hlast"], writes=[f"xx{b}"], dma=True)
        O("sync", lambda e, b=b: e.dma_start(out=xxt[b][1:P, 0:D], in_=ht[b][0:P - 1, 0:D]),
          reads=[f"ht{b}"], writes=[f"xx{b}"], dma=True)
        O("sync", lambda e, b=b: e.dma_start(out=hlast[:], in_=ht[b][P - 1:P, 0:D]),
          reads=[f"ht{b}"], writes=["hlast"], dma=True)
        O("vector", lambda e, b=b: e.tensor_tensor(out=dt_[b], in0=ht[b], in1=xxt[b], op=ALU.subtract),
          reads=[f"ht{b}", f"xx{b}"], writes=[f"dt{b}"])
        for mi, (mn, dst_sc) in enumerate((("tmkb", xk_sc), ("tmvb", xv_sc), ("tmrb", xr_sc))):
            tb2 = (3 * i + mi) % 2
            O("vector", lambda e, b=b, mn=mn: e.tensor_tensor(out=mixt[b], in0=dt_[b], in1=lnt[mn], op=ALU.mult),
              reads=[f"dt{b}", "Lw_" + mn], writes=[f"mx{b}"])
            O("vector", lambda e, b=b: e.tensor_tensor(out=mixt[b], in0=mixt[b], in1=xxt[b], op=ALU.add),
              reads=[f"mx{b}", f"xx{b}"], writes=[f"mx{b}"])
            O("vector", lambda e, b=b: e.tensor_copy(out=mixb[b], in_=mixt[b]),
              reads=[f"mx{b}"], writes=[f"mb{b}"])
            for c in range(NCH):
                O("scalar", lambda e, b=b, c=c, tb2=tb2: e.dma_start(
                    out=trd[tb2][:, c * P:(c + 1) * P], in_=mixb[b][:, c * P:(c + 1) * P],
                    transpose=True),
                  reads=[f"mb{b}"], writes=[f"tr{tb2}_{c}"], dma=True)
            O("sync", lambda e, i=i, tb2=tb2, dst_sc=dst_sc: e.dma_start(
                out=dst_sc.rearrange("(c p) t -> p c t", p=P)[:, :, i * P:(i + 1) * P],
                in_=trd[tb2][:].rearrange("p (c q) -> p c q", c=NCH)),
              reads=[f"tr{tb2}_{c}" for c in range(NCH)], dma=True)

    sp.new_phase()
    sp._serial = 2 in _G_SERIAL
    cv = Carve()
    ko = cv.f32(S); vo = cv.f32(S); sro = cv.f32(S)
    eko = cv.f32(S); ekvo = cv.f32(S)
    Asc = cv.f32(S + 1); Bsc = cv.f32(S + 1)
    numt = cv.f32(S); dent = cv.f32(S)
    saob = cv.bf(S)
    wldS = [cv.f32(3 * D) for _ in range(2)]
    wblkb = [cv.bf(3 * D) for _ in range(2)]
    rhsB = [cv.bf(NCH * 512) for _ in range(2)]
    dslc = {"ko": ko, "vo": vo, "sr": sro}
    for oc in range(NCH):
        b = oc % 2
        for pi, w_in in enumerate((wkT_in, wvT_in, wrT_in)):
            O("sync", lambda e, w_in=w_in, oc=oc, pi=pi, b=b: e.dma_start(
                out=wldS[b][:, pi * D:(pi + 1) * D].rearrange("p (c q) -> p c q", c=NCH),
                in_=w_in.rearrange("(c p) d -> p c d", p=P)[:, :, oc * P:(oc + 1) * P]),
              writes=[f"wldS{b}_{pi}"], dma=True)
        O("vector", lambda e, b=b: e.tensor_copy(out=wblkb[b], in_=wldS[b]),
          reads=[f"wldS{b}_{pi}" for pi in range(3)], writes=[f"wbl{b}"])
        O("scalar", lambda e, oc=oc, b=b: e.activation(out=smallc[b][:, 0:1], in_=tdc[:, oc:oc + 1], func=AF.Exp),
          reads=["tdcc"], writes=[f"smc{b}"])
        O("scalar", lambda e, b=b: e.activation(out=smallc[b][:, 1:2], in_=smallc[b][:, 0:1], func=AF.Exp, scale=-1.0),
          reads=[f"smc{b}"], writes=[f"smc{b}"])
        O("scalar", lambda e, oc=oc, b=b: e.activation(out=smallc[b][:, 2:3], in_=tfc[:, oc:oc + 1], func=AF.Exp),
          reads=["tfcc"], writes=[f"smc{b}"])
        for s in range(BC):
            for pi, (src_sc, dn, fn_) in enumerate(((xk_sc, "ko", AF.Copy), (xv_sc, "vo", AF.Copy),
                                                    (xr_sc, "sr", AF.Sigmoid))):
                dsto = dslc[dn]
                for tb in range(4):
                    g = pi * 4 + tb
                    rb = g % 2
                    pk = g % 4
                    O("sync", lambda e, src_sc=src_sc, s=s, tb=tb, rb=rb: e.dma_start(
                        out=rhsB[rb][:].rearrange("p (c q) -> p c q", c=NCH),
                        in_=src_sc.rearrange("(c p) t -> p c t", p=P)[:, :, s * S + tb * 512: s * S + (tb + 1) * 512]),
                      writes=[f"rhs{rb}"], dma=True)
                    for cc in range(NCH):
                        O("tensor", lambda e, b=b, pi=pi, cc=cc, rb=rb, pk=pk: e.matmul(
                            ps[pk][:], wblkb[b][:, (pi * NCH + cc) * P:(pi * NCH + cc + 1) * P],
                            rhsB[rb][:, cc * 512:(cc + 1) * 512],
                            start=(cc == 0), stop=(cc == NCH - 1)),
                          reads=[f"wbl{b}", f"rhs{rb}"], writes=[f"ps{pk}"])
                    O("scalar", lambda e, dsto=dsto, fn_=fn_, tb=tb, pk=pk: e.activation(
                        out=dsto[:, tb * 512:(tb + 1) * 512], in_=ps[pk][:], func=fn_),
                      reads=[f"ps{pk}"], writes=[f"{dn}{tb}"])
            O("scalar", lambda e: e.activation(out=eko, in_=ko, func=AF.Exp),
              reads=[f"ko{tb}" for tb in range(4)], writes=["eko"])
            O("vector", lambda e: e.tensor_tensor(out=ekvo, in0=eko, in1=vo, op=ALU.mult),
              reads=["eko"] + [f"vo{tb}" for tb in range(4)], writes=["ekvo"])
            O("gpsimd", lambda e: e.memset(Asc[:, 0:1], 0.0), writes=["Asc"])
            O("gpsimd", lambda e: e.memset(Bsc[:, 0:1], 0.0), writes=["Bsc"])
            ewb = smallc[b][:, 1:2].broadcast_to([P, S])
            O("vector", lambda e, ewb=ewb: e.tensor_tensor_scan(out=Asc[:, 1:S + 1], data0=ewb, data1=ekvo,
                                                                initial=0.0, op0=ALU.mult, op1=ALU.add),
              reads=["ekvo", f"smc{b}"], writes=["Asc"])
            O("vector", lambda e, ewb=ewb: e.tensor_tensor_scan(out=Bsc[:, 1:S + 1], data0=ewb, data1=eko,
                                                                initial=0.0, op0=ALU.mult, op1=ALU.add),
              reads=["eko", f"smc{b}"], writes=["Bsc"])
            O("vector", lambda e, b=b: e.scalar_tensor_tensor(out=numt, in0=ekvo, scalar=smallc[b][:, 2:3],
                                                              in1=Asc[:, 0:S], op0=ALU.mult, op1=ALU.add),
              reads=["ekvo", "Asc", f"smc{b}"], writes=["numt"])
            O("vector", lambda e, b=b: e.scalar_tensor_tensor(out=dent, in0=eko, scalar=smallc[b][:, 2:3],
                                                              in1=Bsc[:, 0:S], op0=ALU.mult, op1=ALU.add),
              reads=["eko", "Bsc", f"smc{b}"], writes=["dent"])
            O("scalar", lambda e: e.activation(out=dent, in_=dent, func=AF.Ln),
              reads=["dent"], writes=["dent"])
            O("scalar", lambda e: e.activation(out=dent, in_=dent, func=AF.Exp, scale=-1.0),
              reads=["dent"], writes=["dent"])
            O("vector", lambda e: e.tensor_tensor(out=numt, in0=numt, in1=dent, op=ALU.mult),
              reads=["numt", "dent"], writes=["numt"])
            O("vector", lambda e: e.tensor_tensor(out=saob, in0=numt, in1=sro, op=ALU.mult),
              reads=["numt"] + [f"sr{tb}" for tb in range(4)], writes=["saob"])
            O("sync", lambda e, oc=oc, s=s: e.dma_start(
                out=sa_sc[oc * P:(oc + 1) * P, s * S:(s + 1) * S], in_=saob),
              reads=["saob"], dma=True)

    sp.new_phase()
    sp._serial = 3 in _G_SERIAL
    cv = Carve()
    lnt3 = {}
    for n in ("ln2w", "ln2b", "cmkb", "cmrb"):
        t = cv.f32(D)
        lnt3[n] = t
        O("sync", lambda e, n=n, t=t: e.dma_start(out=t, in_=ln_b[n]), writes=["Lw_" + n], dma=True)
    woTS = cv.f32(NCH * D)
    woTb = cv.bf(NCH * D)
    O("sync", lambda e: e.dma_start(out=woTS.rearrange("p (c q) -> p c q", c=NCH),
                                    in_=woT_in.rearrange("(c p) d -> p c d", p=P)),
      writes=["woTS"], dma=True)
    O("vector", lambda e: e.tensor_copy(out=woTb, in_=woTS), reads=["woTS"], writes=["woT"])
    salB = [cv.bf(NCH * P) for _ in range(2)]
    ot3 = [cv.f32(D) for _ in range(2)]; a3 = [cv.f32(D) for _ in range(2)]
    x23 = [cv.f32(D) for _ in range(2)]; h23 = [cv.f32(D) for _ in range(2)]
    xx3 = [cv.f32(D) for _ in range(2)]; mb3 = [cv.bf(D) for _ in range(2)]
    d3 = cv.f32(D); mix3 = cv.f32(D); xsq3 = cv.f32(D)
    for i in range(NT):
        b = i % 2
        O("sync", lambda e, i=i, b=b: e.dma_start(
            out=salB[b][:].rearrange("p (c q) -> p c q", c=NCH),
            in_=sa_sc.rearrange("(c p) t -> p c t", p=P)[:, :, i * P:(i + 1) * P]),
          writes=[f"sal{b}"], dma=True)
        for half in range(2):
            for cc in range(NCH):
                O("tensor", lambda e, b=b, cc=cc, half=half: e.matmul(
                    ps[half][:], salB[b][:, cc * P:(cc + 1) * P],
                    woTb[:, cc * D + half * 512: cc * D + (half + 1) * 512],
                    start=(cc == 0), stop=(cc == NCH - 1)),
                  reads=[f"sal{b}", "woT"], writes=[f"ps{half}"])
            O("scalar", lambda e, b=b, half=half: e.activation(
                out=ot3[b][:, half * 512:(half + 1) * 512], in_=ps[half][:], func=AF.Copy),
              reads=[f"ps{half}"], writes=[f"ot{b}_{half}"])
        O("sync", lambda e, i=i, b=b: e.dma_start(out=a3[b], in_=a_sc[i * P:(i + 1) * P, :]),
          writes=[f"a3{b}"], dma=True)
        O("vector", lambda e, b=b: e.tensor_tensor(out=x23[b], in0=ot3[b], in1=a3[b], op=ALU.add),
          reads=[f"ot{b}_0", f"ot{b}_1", f"a3{b}"], writes=[f"x2{b}"])
        O("sync", lambda e, i=i, b=b: e.dma_start(out=x2_sc[i * P:(i + 1) * P, :], in_=x23[b]),
          reads=[f"x2{b}"], dma=True)
        ln(x23[b], f"x2{b}", h23[b], f"h2{b}", stat[b], f"st3{b}", xsq3, "xq3",
           lnt3["ln2w"], "Lw_ln2w", lnt3["ln2b"], "Lw_ln2b")
        if i % 16 == 0:
            O("gpsimd", lambda e, b=b: e.memset(xx3[b][0:1, 0:D], 0.0), writes=[f"xx3{b}"])
        else:
            O("sync", lambda e, b=b: e.dma_start(out=xx3[b][0:1, 0:D], in_=hlast[:]),
              reads=["hl3"], writes=[f"xx3{b}"], dma=True)
        O("sync", lambda e, b=b: e.dma_start(out=xx3[b][1:P, 0:D], in_=h23[b][0:P - 1, 0:D]),
          reads=[f"h2{b}"], writes=[f"xx3{b}"], dma=True)
        O("sync", lambda e, b=b: e.dma_start(out=hlast[:], in_=h23[b][P - 1:P, 0:D]),
          reads=[f"h2{b}"], writes=["hl3"], dma=True)
        O("vector", lambda e, b=b: e.tensor_tensor(out=d3, in0=h23[b], in1=xx3[b], op=ALU.subtract),
          reads=[f"h2{b}", f"xx3{b}"], writes=["d3"])
        O("vector", lambda e: e.tensor_tensor(out=mix3, in0=d3, in1=lnt3["cmkb"], op=ALU.mult),
          reads=["d3", "Lw_cmkb"], writes=["mx3"])
        O("vector", lambda e, b=b: e.tensor_tensor(out=mix3, in0=mix3, in1=xx3[b], op=ALU.add),
          reads=["mx3", f"xx3{b}"], writes=["mx3"])
        O("vector", lambda e, b=b: e.tensor_copy(out=mb3[b], in_=mix3),
          reads=["mx3"], writes=[f"mb3{b}"])
        for c in range(NCH):
            O("scalar", lambda e, b=b, c=c: e.dma_start(out=trd[0][:, c * P:(c + 1) * P],
                                                        in_=mb3[b][:, c * P:(c + 1) * P], transpose=True),
              reads=[f"mb3{b}"], writes=[f"t30_{c}"], dma=True)
        for c in range(NCH):
            O("vector", lambda e, c=c, i=i: e.tensor_copy(out=xk2_fm[c][:, i * P:(i + 1) * P],
                                                          in_=trd[0][:, c * P:(c + 1) * P]),
              reads=[f"t30_{c}"], writes=[f"xkf{c}"])
        O("vector", lambda e: e.tensor_tensor(out=mix3, in0=d3, in1=lnt3["cmrb"], op=ALU.mult),
          reads=["d3", "Lw_cmrb"], writes=["mx3"])
        O("vector", lambda e, b=b: e.tensor_tensor(out=mix3, in0=mix3, in1=xx3[b], op=ALU.add),
          reads=["mx3", f"xx3{b}"], writes=["mx3"])
        O("vector", lambda e, b=b: e.tensor_copy(out=mb3[b], in_=mix3),
          reads=["mx3"], writes=[f"mb3{b}"])
        for c in range(NCH):
            O("scalar", lambda e, b=b, c=c: e.dma_start(out=trd[1][:, c * P:(c + 1) * P],
                                                        in_=mb3[b][:, c * P:(c + 1) * P], transpose=True),
              reads=[f"mb3{b}"], writes=[f"t31_{c}"], dma=True)
        O("sync", lambda e, i=i: e.dma_start(
            out=xr2_sc.rearrange("(c p) t -> p c t", p=P)[:, :, i * P:(i + 1) * P],
            in_=trd[1][:].rearrange("p (c q) -> p c q", c=NCH)),
          reads=[f"t31_{c}" for c in range(NCH)], dma=True)

    sp.new_phase()
    sp._serial = 4 in _G_SERIAL
    cv = Carve()
    wfkS = [cv.f32(NCH * P) for _ in range(2)]
    wfkb = [cv.bf(NCH * P) for _ in range(2)]
    kfo = cv.f32(T)
    kfqo = [cv.bf(T) for _ in range(2)]
    for o4 in range(H_FFN // P):
        b = o4 % 2
        O("sync", lambda e, o4=o4, b=b: e.dma_start(
            out=wfkS[b][:].rearrange("p (c q) -> p c q", c=NCH),
            in_=wfkT_in.rearrange("(c p) h -> p c h", p=P)[:, :, o4 * P:(o4 + 1) * P]),
          writes=[f"wfS{b}"], dma=True)
        O("vector", lambda e, b=b: e.tensor_copy(out=wfkb[b], in_=wfkS[b]),
          reads=[f"wfS{b}"], writes=[f"wfk{b}"])
        for tb in range(NTB):
            pk = tb % 4
            for cc in range(NCH):
                O("tensor", lambda e, b=b, cc=cc, tb=tb, pk=pk: e.matmul(
                    ps[pk][:], wfkb[b][:, cc * P:(cc + 1) * P], xk2_fm[cc][:, tb * 512:(tb + 1) * 512],
                    start=(cc == 0), stop=(cc == NCH - 1)),
                  reads=[f"wfk{b}"], writes=[f"ps{pk}"])
            O("scalar", lambda e, tb=tb, pk=pk: e.activation(out=kfo[:, tb * 512:(tb + 1) * 512],
                                                             in_=ps[pk][:], func=AF.Relu),
              reads=[f"ps{pk}"], writes=[f"kf{tb}"])
        O("scalar", lambda e, b=b: e.activation(out=kfqo[b], in_=kfo, func=AF.Square),
          reads=[f"kf{tb}" for tb in range(NTB)], writes=[f"kq{b}"])
        O("sync", lambda e, o4=o4, b=b: e.dma_start(out=kfq_sc[o4 * P:(o4 + 1) * P, :], in_=kfqo[b]),
          reads=[f"kq{b}"], dma=True)

    sp.new_phase()
    sp._serial = 5 in _G_SERIAL
    cv = Carve()
    wfvS = [cv.f32(2 * D) for _ in range(2)]
    wfvb = cv.bf(32 * D)
    wfrb = cv.bf(NCH * D)
    kflB = [cv.bf(32 * P) for _ in range(2)]
    xrlB = [cv.bf(NCH * P) for _ in range(2)]
    kvt = cv.f32(D)
    rst = cv.f32(D)
    x25 = [cv.f32(D) for _ in range(2)]
    for j in range(16):
        b = j % 2
        O("sync", lambda e, j=j, b=b: e.dma_start(
            out=wfvS[b][:].rearrange("p (c q) -> p c q", c=2),
            in_=wfvT_in.rearrange("(c p) d -> p c d", p=P)[:, 2 * j:2 * j + 2, :]),
          writes=[f"wvS{b}"], dma=True)
        O("vector", lambda e, j=j, b=b: e.tensor_copy(out=wfvb[:, 2 * j * D:(2 * j + 2) * D], in_=wfvS[b]),
          reads=[f"wvS{b}"], writes=[f"wfv{2 * j}", f"wfv{2 * j + 1}"])
    for j in range(4):
        b = j % 2
        O("sync", lambda e, j=j, b=b: e.dma_start(
            out=wfvS[b][:].rearrange("p (c q) -> p c q", c=2),
            in_=wfrT_in.rearrange("(c p) d -> p c d", p=P)[:, 2 * j:2 * j + 2, :]),
          writes=[f"wvS{b}"], dma=True)
        O("vector", lambda e, j=j, b=b: e.tensor_copy(out=wfrb[:, 2 * j * D:(2 * j + 2) * D], in_=wfvS[b]),
          reads=[f"wvS{b}"], writes=["wfr"])
    for i in range(NT):
        b = i % 2
        O("sync", lambda e, i=i, b=b: e.dma_start(
            out=kflB[b][:].rearrange("p (c q) -> p c q", c=32),
            in_=kfq_sc.rearrange("(c p) t -> p c t", p=P)[:, :, i * P:(i + 1) * P]),
          writes=[f"kfl{b}"], dma=True)
        for half in range(2):
            for ic in range(32):
                O("tensor", lambda e, b=b, ic=ic, half=half: e.matmul(
                    ps[half][:], kflB[b][:, ic * P:(ic + 1) * P],
                    wfvb[:, ic * D + half * 512: ic * D + (half + 1) * 512],
                    start=(ic == 0), stop=(ic == 31)),
                  reads=[f"kfl{b}", f"wfv{ic}"], writes=[f"ps{half}"])
            O("scalar", lambda e, half=half: e.activation(out=kvt[:, half * 512:(half + 1) * 512],
                                                          in_=ps[half][:], func=AF.Copy),
              reads=[f"ps{half}"], writes=[f"kv{half}"])
        O("sync", lambda e, i=i, b=b: e.dma_start(
            out=xrlB[b][:].rearrange("p (c q) -> p c q", c=NCH),
            in_=xr2_sc.rearrange("(c p) t -> p c t", p=P)[:, :, i * P:(i + 1) * P]),
          writes=[f"xrl{b}"], dma=True)
        for half in range(2):
            pk = 2 + half
            for cc in range(NCH):
                O("tensor", lambda e, b=b, cc=cc, half=half, pk=pk: e.matmul(
                    ps[pk][:], xrlB[b][:, cc * P:(cc + 1) * P],
                    wfrb[:, cc * D + half * 512: cc * D + (half + 1) * 512],
                    start=(cc == 0), stop=(cc == NCH - 1)),
                  reads=[f"xrl{b}", "wfr"], writes=[f"ps{pk}"])
            O("scalar", lambda e, half=half, pk=pk: e.activation(out=rst[:, half * 512:(half + 1) * 512],
                                                                 in_=ps[pk][:], func=AF.Sigmoid),
              reads=[f"ps{pk}"], writes=[f"rs{half}"])
        O("vector", lambda e: e.tensor_tensor(out=rst, in0=rst, in1=kvt, op=ALU.mult),
          reads=["rs0", "rs1", "kv0", "kv1"], writes=["rs0", "rs1"])
        O("sync", lambda e, i=i, b=b: e.dma_start(out=x25[b], in_=x2_sc[i * P:(i + 1) * P, :]),
          writes=[f"x25{b}"], dma=True)
        O("vector", lambda e, b=b: e.tensor_tensor(out=rst, in0=rst, in1=x25[b], op=ALU.add),
          reads=["rs0", "rs1", f"x25{b}"], writes=["rs0", "rs1"])
        O("sync", lambda e, i=i: e.dma_start(out=attn1_o[i * P:(i + 1) * P, :], in_=rst),
          reads=["rs0", "rs1"], dma=True)
    sp._serial = False
    es.close()



_G_SERIAL = set()
_DBG = set()
_LAST = {}


def build_MEGA():
    nc = bass.Bass("TRN2", target_bir_lowering=False)
    inp = lambda n, s, dt=F32: nc.dram_tensor(n, s, dt, kind="ExternalInput").ap()
    scr = lambda n, s, dt=F32: nc.dram_tensor(
        n, s, dt, kind="ExternalOutput" if n in _DBG else "Internal").ap()

    x_in = inp("x", [T, D])
    wqT = inp("wqT", [D, D]); wkT = inp("wkT", [D, D]); wvT = inp("wvT", [D, D])
    bqc = inp("bqc", [P, NCH]); bkc = inp("bkc", [P, NCH]); bvr = inp("bvr", [1, D])
    epsc = inp("epsc", [P, 1])
    eyes = inp("eyes", [P, 3 * P]); identb = inp("identb", [P, P], BF16)
    identf = inp("identf", [P, P])
    wg0 = inp("wg0", [I_FFN, D]); wu0 = inp("wu0", [I_FFN, D]); wd0 = inp("wd0", [D, I_FFN])
    wg1 = inp("wg1", [I_FFN, D]); wu1 = inp("wu1", [I_FFN, D]); wd1 = inp("wd1", [D, I_FFN])
    ln_b = {n: inp(n, [P, D]) for n in ("ln0w", "ln0b", "ln1w", "ln1b", "ln2w", "ln2b",
                                        "tmkb", "tmvb", "tmrb", "cmkb", "cmrb")}
    wkT1 = inp("wkT1", [D, D]); wvT1 = inp("wvT1", [D, D]); wrT1 = inp("wrT1", [D, D])
    woT1 = inp("woT1", [D, D]); wfkT = inp("wfkT", [D, H_FFN])
    wfvT = inp("wfvT", [H_FFN, D]); wfrT = inp("wfrT", [D, D])
    tdc = inp("tdc", [P, NCH]); tfc = inp("tfc", [P, NCH])

    xn_tok = scr("xn_tok", [T, D])
    q_fm = scr("q_fm", [NCH * P, T], BF16)
    k_fm = scr("k_fm", [NCH * P, T], BF16)
    v_tok = scr("v_tok", [T, D], BF16)
    xn_fm_s = scr("xn_fm_s", [NCH * P, T], BF16)
    qlm = scr("qlm", [NCH * P, BC * LM], BF16)
    klm = scr("klm", [NCH * P, BC * LM], BF16)
    k2 = scr("k2", [NMAT * P, LM], BF16)
    pcmax_t = nc.dram_tensor("pcmax", [1, 4], F32, kind="Internal")
    pcm_all_t = nc.dram_tensor("pcm_all", [1, 4], F32, kind="Internal")
    z_s = scr("z_s", [NMAT * P, LM], BF16)
    zt_s = scr("zt_s", [NMAT * P, LM], BF16)
    attn_tok = scr("attn_tok", [T, D])
    wgq_fm = scr("wgq_fm", [D, I_FFN], BF16)
    wuq_fm = scr("wuq_fm", [D, I_FFN], BF16)
    wdqT = scr("wdqT", [I_FFN, D], BF16)
    hq_sc = scr("hq_sc", [I_FFN, T], BF16)
    ffn0_tok = scr("ffn0_tok", [T, D])
    ffn1_tok = scr("ffn1_tok", [T, D])
    x1_tok = scr("x1_tok", [T, D])
    xn1_tok = scr("xn1_tok", [T, D])
    a_sc = scr("a_sc", [T, D])
    x2_sc = scr("x2_sc", [T, D])
    xk_sc = scr("xk_sc", [D, T], BF16)
    xv_sc = scr("xv_sc", [D, T], BF16)
    xr_sc = scr("xr_sc", [D, T], BF16)
    xr2_sc = scr("xr2_sc", [D, T], BF16)
    sa_sc = scr("sa_sc", [D, T], BF16)
    kfq_sc = scr("kfq_sc", [H_FFN, T], BF16)
    attn1_tok = scr("attn1_tok", [T, D])
    out_o = nc.dram_tensor("out", [T, D], F32, kind="ExternalOutput").ap()

    es = contextlib.ExitStack()

    ab = emit_AB(nc, es, {
        "x": x_in, "wqT": wqT, "wkT": wkT, "wvT": wvT, "bqc": bqc, "bkc": bkc,
        "bvr": bvr, "epsc": epsc, "xn_tok": xn_tok, "q_fm": q_fm, "k_fm": k_fm,
        "v_tok": v_tok, "xn_fm": xn_fm_s, "qlm": qlm, "klm": klm, "k2": k2,
        "pcmax": pcmax_t.ap()})

    sp = SP(nc)

    cc_sem = es.enter_context(nc.semaphore("s_cc"))

    def cc_fn(e, sq, inc):
        e.collective_compute(
            "AllReduce", ALU.max,
            replica_groups=[list(range(NCORES))],
            ins=[pcmax_t.ap().opt()],
            outs=[pcm_all_t.ap().opt()]).then_inc(cc_sem)
        e.wait_ge(cc_sem, 1)
        e.sem_inc(sq, inc)
    sp.add_raw("gpsimd", cc_fn, 1)

    emit_WQfull(nc, sp, "0", wg0, wu0, wd0, wgq_fm, wuq_fm, wdqT)
    emit_C(nc, sp, k2, pcm_all_t.ap(), eyes, identb, z_s, zt_s)
    emit_D(nc, sp, q_fm, k_fm, v_tok, qlm, klm, zt_s, identf, attn_tok)
    emit_FFN(nc, sp, "0", xn_tok, wgq_fm, wuq_fm, wdqT, ffn0_tok, hq_sc)
    emit_F(nc, sp, x_in, attn_tok, ffn0_tok, epsc, x1_tok, xn1_tok)
    emit_WQfull(nc, sp, "1", wg1, wu1, wd1, wgq_fm, wuq_fm, wdqT)
    emit_FFN(nc, sp, "1", xn1_tok, wgq_fm, wuq_fm, wdqT, ffn1_tok, hq_sc)
    emit_G(nc, sp, xn1_tok, ln_b, wkT1, wvT1, wrT1, woT1, wfkT, wfvT, wfrT,
           tdc, tfc, epsc, attn1_tok,
           a_sc, x2_sc, xk_sc, xv_sc, xr_sc, xr2_sc, sa_sc, kfq_sc)
    emit_I(nc, sp, x1_tok, attn1_tok, ffn1_tok, out_o)

    sp.emit(es, prologue=ab)
    es.close()
    return nc



_CACHED = {}


def _get(name, builder):
    if name not in _CACHED:
        _CACHED[name] = builder()
    return _CACHED[name]


def kernel(**inputs):
    ins = {k: np.asarray(v) for k, v in inputs.items()}
    C = np.ascontiguousarray
    cores = list(range(NCORES))

    nc = _get("MEGA", build_MEGA)

    epsc = np.full((P, 1), 1e-5, np.float32)
    eyes = np.concatenate([np.eye(P) * 7, np.eye(P) * 15, np.eye(P) * 13], 1).astype(np.float32)
    identb = np.eye(P).astype(BF)
    identf = np.eye(P, dtype=np.float32)
    bt = lambda v: C(np.broadcast_to(np.asarray(v, np.float32), (P, D)))
    f32 = lambda k: ins[k].astype(np.float32)
    common = {
        "wqT": C(f32("L0_attn_Wq").T),
        "wkT": C(f32("L0_attn_Wk").T),
        "wvT": C(f32("L0_attn_Wv").T),
        "bqc": C(f32("L0_attn_bq").reshape(NCH, P).T),
        "bkc": C(f32("L0_attn_bk").reshape(NCH, P).T),
        "bvr": C(f32("L0_attn_bv").reshape(1, D)),
        "epsc": epsc, "eyes": eyes, "identb": identb, "identf": identf,
        "wg0": C(f32("L0_ffn_w_gate")), "wu0": C(f32("L0_ffn_w_up")),
        "wd0": C(f32("L0_ffn_w_down")),
        "wg1": C(f32("L1_ffn_w_gate")), "wu1": C(f32("L1_ffn_w_up")),
        "wd1": C(f32("L1_ffn_w_down")),
        "ln0w": bt(ins["L1_attn_ln0_w"]), "ln0b": bt(ins["L1_attn_ln0_b"]),
        "ln1w": bt(ins["L1_attn_ln1_w"]), "ln1b": bt(ins["L1_attn_ln1_b"]),
        "ln2w": bt(ins["L1_attn_ln2_w"]), "ln2b": bt(ins["L1_attn_ln2_b"]),
        "tmkb": bt(ins["L1_attn_tmk"]), "tmvb": bt(ins["L1_attn_tmv"]),
        "tmrb": bt(ins["L1_attn_tmr"]), "cmkb": bt(ins["L1_attn_cmk"]),
        "cmrb": bt(ins["L1_attn_cmr"]),
        "wkT1": C(f32("L1_attn_Wk").T),
        "wvT1": C(f32("L1_attn_Wv").T),
        "wrT1": C(f32("L1_attn_Wr").T),
        "woT1": C(f32("L1_attn_Wo").T),
        "wfkT": C(f32("L1_attn_Wfk").T),
        "wfvT": C(f32("L1_attn_Wfv").T),
        "wfrT": C(f32("L1_attn_Wfr").T),
        "tdc": C(f32("L1_attn_time_decay").reshape(NCH, P).T),
        "tfc": C(f32("L1_attn_time_first").reshape(NCH, P).T),
    }
    x = ins["L0_attn_x"].astype(np.float32)
    maps = [dict(common, x=C(x[c * BC:(c + 1) * BC].reshape(T, D))) for c in cores]
    res = run_bass_kernel_spmd(nc, maps, core_ids=cores)
    _LAST["res"] = res
    out = np.stack([res.results[c]["out"] for c in cores]).reshape(16, S, D).astype(np.float32)
    return out


if __name__ == "__main__":
    print("library module")


# version 12
# speedup vs baseline: 2.3992x; 1.6528x over previous
import contextlib
import numpy as np
import ml_dtypes
import concourse.bass as bass
import concourse.mybir as mybir
from concourse.bass_utils import run_bass_kernel_spmd

F32 = mybir.dt.float32
BF16 = mybir.dt.bfloat16
I16 = mybir.dt.int16
AF = mybir.ActivationFunctionType
ALU = mybir.AluOpType
AX = mybir.AxisListType

NCORES = 8
P = 128
S = 2048
BC = 2
T = S * BC
D = 1024
NCH = D // P
HEADS = 16
HD = 64
LM = 128
SEG = 16
NMAT = BC * HEADS
S4 = 1.0 / float(np.sqrt(np.sqrt(HD)))
NT = T // P
I_FFN = 2816
H_FFN = 4096
NTB = T // 512
BF = ml_dtypes.bfloat16


class Chain:

    def __init__(self, eng, sem):
        self.eng, self.sem, self.n = eng, sem, 0

    def bar(self, inst):
        self.n += 1
        inst.then_inc(self.sem, 1)
        self.eng.wait_ge(self.sem, self.n)
        return inst



def emit_AB(nc, es_outer, d, stages=3, cuts=(), nt=NT):
    x_in = d["x"]
    wqT_in = d["wqT"]; wkT_in = d["wkT"]; wvT_in = d["wvT"]
    bqc_in = d["bqc"]; bkc_in = d["bkc"]; bvr_in = d["bvr"]; eps_in = d["epsc"]
    xn_tok_o = d["xn_tok"]; q_fm_o = d["q_fm"]; k_fm_o = d["k_fm"]
    v_tok_o = d["v_tok"]; xn_fm_o = d["xn_fm"]; qlm_o = d["qlm"]
    klm_o = d["klm"]; k2_o = d["k2"]; pcmax_o = d["pcmax"]

    es = contextlib.ExitStack()
    sbt = lambda name, shape, dt=F32: es.enter_context(nc.sbuf_tensor(name, shape, dt))
    pst = lambda name, shape, dt=F32: es.enter_context(nc.psum_tensor(name, shape, dt))
    sem = lambda name: es_outer.enter_context(nc.semaphore(name))

    NXB = 2
    x_t = [sbt(f"x_t{i}", [P, D]) for i in range(NXB)]
    xn_t = [sbt(f"xn_t{i}", [P, D]) for i in range(NXB)]
    xnb_t = [sbt(f"xnb_t{i}", [P, D], BF16) for i in range(2)]
    stat = sbt("stat", [P, 8])
    epsc = sbt("epsc_s", [P, 1])
    xn_fm = [es.enter_context(nc.sbuf_tensor(f"xn_fm{c}", [P, T], BF16, is_transpose_dest=True)) for c in range(NCH)]
    wq = [sbt(f"wqc{c}", [P, D], BF16) for c in range(NCH)]
    wk = [sbt(f"wkc{c}", [P, D], BF16) for c in range(NCH)]
    wv = [sbt(f"wvc{c}", [P, D], BF16) for c in range(NCH)]
    bqs = sbt("bqs", [P, NCH])
    bks = sbt("bks", [P, NCH])
    bvf = sbt("bvf", [1, D])
    bvb = sbt("bvb", [1, D], BF16)
    ones_row = sbt("ones_row", [1, P], BF16)
    ones_col = sbt("ones_col", [P, 1])
    if stages >= 2:
        pbf_s = sbt("pbf_s", [P, T], BF16)
        pbf = [pbf_s, pbf_s]
        lmraw = [sbt(f"lmraw{i}", [P, BC * LM]) for i in range(2)]
        b16q = sbt("b16q", [P, NCH])
        b16k = sbt("b16k", [P, NCH])
        _wldv = pbf_s[:].bitcast(F32)
        wld = [_wldv[:, 0:D], _wldv[:, D:2 * D]]
    else:
        wld_t = [sbt(f"wld{i}", [P, D]) for i in range(2)]
        wld = [wld_t[0][:], wld_t[1][:]]
        xsq_t = sbt("xsq1", [P, D])
        xsq = xsq_t[:]
    if stages >= 2:
        vf32_s = sbt("vf32_s", [P, D])
        vbf_s = sbt("vbf_s", [P, D], BF16)
        vf32 = [vf32_s, vf32_s]
        vbf = [vbf_s, vbf_s]
        qlm_b = sbt("qlm_b", [P, NCH * BC * LM], BF16)
        klm_b = sbt("klm_b", [P, NCH * BC * LM], BF16)
        xsq = qlm_b[:].bitcast(F32)[:, 0:D]
        e2 = sbt("e2", [P, LM])
        k2f = [sbt(f"k2f{i}", [P, LM]) for i in range(2)]
        k2bs = [sbt(f"k2bs{i}", [P, LM], BF16) for i in range(2)]
        s2red = sbt("s2red", [P, 4])
    runmax = sbt("runmax", [1, LM])
    pcm = sbt("pcm", [1, 4])

    ps_q = [pst(f"ps_q{i}", [P, 512]) for i in range(4)]
    ps_s2 = [pst(f"ps_s2{i}", [P, LM]) for i in range(2)]
    ps_cs = [pst(f"ps_cs{i}", [1, LM]) for i in range(2)]

    s_ldw = sem("s_ldw")
    s_wbf = sem("s_wbf")
    s_x = sem("s_x")
    s_v = sem("s_v")
    s_a = sem("s_a")
    s_tr = sem("s_tr")
    s_st = sem("s_st")
    s_pe = sem("s_pe")
    s_ev = sem("s_ev")
    s_lm = sem("s_lm")
    s_ps = sem("s_ps")
    s_v2 = sem("s_v2")
    s_s2a = sem("s_s2a")
    s_fin = sem("s_fin")
    s_vch = sem("s_vch")
    s_c = sem("s_c")
    s_lr = sem("s_lr")
    s_lw0 = sem("s_lw0")
    s_lw1 = sem("s_lw1")
    s_ach = sem("s_ach")
    s_ks = sem("s_ks")

    NQK = 2 * NCH * NTB
    NVG = NT * 2
    NG = NQK + NVG

    def sync_body(sync):
        for wi in range(0 if "noW" in cuts else 3):
            w_in = (wqT_in, wkT_in, wvT_in)[wi]
            for c in range(NCH):
                k = wi * NCH + c
                if k >= 2:
                    sync.wait_ge(s_wbf, k - 1)
                sync.dma_start(out=wld[k % 2], in_=w_in[c * P:(c + 1) * P, :]).then_inc(
                    (s_lw0, s_lw1)[k % 2], 16)
        nw = 0 if "noW" in cuts else 24
        if "noW" not in cuts:
            sync.dma_start(out=bqs[:], in_=bqc_in).then_inc(s_ldw, 16)
            sync.dma_start(out=bks[:], in_=bkc_in).then_inc(s_ldw, 16)
            sync.dma_start(out=bvf[:], in_=bvr_in).then_inc(s_ldw, 16)
        sync.dma_start(out=epsc[:], in_=eps_in).then_inc(s_ldw, 16)
        for i in range(nt):
            if i >= NXB:
                sync.wait_ge(s_a, 2 * (i - NXB) + 2)
                sync.dma_start(out=xn_tok_o[(i - NXB) * P:(i - NXB + 1) * P, :],
                               in_=xn_t[(i - NXB) % NXB][:]).then_inc(s_st, 16)
            sync.dma_start(out=x_t[i % NXB][:], in_=x_in[i * P:(i + 1) * P, :]).then_inc(s_x, 16)
        for i in range(nt - NXB, nt):
            sync.wait_ge(s_a, 2 * i + 2)
            sync.dma_start(out=xn_tok_o[i * P:(i + 1) * P, :],
                           in_=xn_t[i % NXB][:]).then_inc(s_st, 16)
        for g in range(16 if stages >= 2 else 0):
            proj, oc = divmod(g, NCH)
            sync.wait_ge(s_ev, 8 * (g + 1))
            dst = (q_fm_o, k_fm_o)[proj]
            sync.dma_start(out=dst[oc * P:(oc + 1) * P, :], in_=pbf[g % 2][:]).then_inc(s_ps, 16)
        for tt in range(0 if ("noV" in cuts or stages < 2) else NT):
            sync.wait_ge(s_ev, NQK + 2 * tt + 2)
            sync.dma_start(out=v_tok_o[tt * P:(tt + 1) * P, :], in_=vbf[tt % 2][:]).then_inc(s_ps, 16)
        sync.wait_ge(s_tr, nt * NCH * 16)
        for c in range(NCH if "noFM" not in cuts else 0):
            sync.dma_start(out=xn_fm_o[c * P:(c + 1) * P, :], in_=xn_fm[c][:]).then_inc(s_fin, 16)
        if stages >= 2:
            sync.wait_ge(s_lm, 16)
        if stages >= 2 and "noLM" not in cuts:
            sync.dma_start(out=qlm_o.rearrange("(c p) l -> p c l", p=P), in_=qlm_b[:].rearrange("p (c l) -> p c l", c=NCH)).then_inc(s_fin, 16)
            sync.dma_start(out=klm_o.rearrange("(c p) l -> p c l", p=P), in_=klm_b[:].rearrange("p (c l) -> p c l", c=NCH)).then_inc(s_fin, 16)
        for m in range(NMAT if stages >= 3 else 0):
            sync.wait_ge(s_v2, 3 * m + 2)
            sync.dma_start(out=k2_o[m * P:(m + 1) * P, :], in_=k2bs[m % 2][:]).then_inc(s_ks, 16)
        if stages >= 3:
            sync.wait_ge(s_v2, 3 * NMAT + 1)
            sync.wait_ge(s_ks, NMAT * 16)
        sync.wait_ge(s_c, 1)
        sync.dma_start(out=pcmax_o, in_=pcm[:]).then_inc(s_fin, 16)
        nfin = ((0 if "noFM" in cuts else NCH) + 1 + (2 if (stages >= 2 and "noLM" not in cuts) else 0)) * 16
        sync.wait_ge(s_fin, nfin)
        if stages >= 2:
            sync.wait_ge(s_ps, (16 if "noV" in cuts else 48) * 16)
        sync.wait_ge(s_st, nt * 16)

    def gpsimd_body(gpsimd):
        gpsimd.memset(ones_row[:], 1.0)
        gpsimd.memset(ones_col[:], 1.0)
        gpsimd.memset(runmax[:], -3.0e38)
        gpsimd.memset(pcm[:], 0.0).then_inc(s_c, 1)

    def vector_body(vector):
        vch = Chain(vector, s_vch)
        for wi in range(0 if "noW" in cuts else 3):
            wtiles = (wq, wk, wv)[wi]
            for c in range(NCH):
                k = wi * NCH + c
                vector.wait_ge((s_lw0, s_lw1)[k % 2], (k // 2 + 1) * 16)
                vector.tensor_copy(out=wtiles[c][:], in_=wld[k % 2]).then_inc(s_wbf, 1)
        if "noW" not in cuts:
            vector.wait_ge(s_ldw, 4 * 16)
            vch.bar(vector.tensor_scalar_mul(out=bqs[:], in0=bqs[:], scalar1=S4))
            vch.bar(vector.tensor_scalar_mul(out=bks[:], in0=bks[:], scalar1=S4))
            if stages >= 2:
                vch.bar(vector.tensor_scalar_mul(out=b16q[:], in0=bqs[:], scalar1=16.0))
                vch.bar(vector.tensor_scalar_mul(out=b16k[:], in0=bks[:], scalar1=16.0))
            vector.tensor_copy(out=bvb[:], in_=bvf[:]).then_inc(s_v, 1)
        else:
            vector.wait_ge(s_ldw, 16)
            vector.sem_inc(s_v, 1)
        for i in range(NT):
            b = i % NXB
            vector.wait_ge(s_x, (i + 1) * 16)
            vch.bar(vector.tensor_reduce(out=stat[:, 0:1], in_=x_t[b][:], axis=AX.X, op=ALU.add))
            vch.bar(vector.tensor_tensor(out=xsq, in0=x_t[b][:], in1=x_t[b][:], op=ALU.mult))
            vch.bar(vector.tensor_reduce(out=stat[:, 1:2], in_=xsq, axis=AX.X, op=ALU.add))
            vch.bar(vector.tensor_scalar_mul(out=stat[:, 2:3], in0=stat[:, 0:1], scalar1=1.0 / D))
            vch.bar(vector.tensor_scalar_mul(out=stat[:, 3:4], in0=stat[:, 1:2], scalar1=1.0 / D))
            vch.bar(vector.tensor_tensor(out=stat[:, 4:5], in0=stat[:, 2:3], in1=stat[:, 2:3], op=ALU.mult))
            vector.tensor_tensor(out=stat[:, 4:5], in0=stat[:, 3:4], in1=stat[:, 4:5],
                                 op=ALU.subtract).then_inc(s_v, 1)
            vector.wait_ge(s_a, 2 * i + 1)
            vector.scalar_tensor_tensor(out=stat[:, 6:7], in0=stat[:, 2:3], scalar=-1.0,
                                        in1=stat[:, 5:6], op0=ALU.mult,
                                        op1=ALU.mult).then_inc(s_v, 1)
            vector.wait_ge(s_a, 2 * i + 2)
            if i >= 2:
                vector.wait_ge(s_tr, (i - 1) * NCH * 16)
            vector.tensor_copy(out=xnb_t[i % 2][:], in_=xn_t[b][:]).then_inc(s_v, 1)
        for k in range(NQK if stages >= 2 else 0):
            g, tb = divmod(k, NTB)
            proj, oc = divmod(g, NCH)
            vector.wait_ge(s_pe, k + 1)
            if "noLM" in cuts:
                vector.sem_inc(s_lr, 1)
            else:
                vch.bar(vector.tensor_reduce(out=lmraw[g % 2][:, tb * 32:(tb + 1) * 32],
                                     in_=ps_q[k % 4][:].rearrange("p (n w) -> p n w", w=SEG),
                                     axis=AX.X, op=ALU.add))
                vector.sem_inc(s_lr, 1)
            if tb == NTB - 1:
                if "noLM" in cuts:
                    vector.sem_inc(s_lm, 1)
                else:
                    dstlm = (qlm_b, klm_b)[proj]
                    b16 = (b16q, b16k)[proj]
                    vector.tensor_scalar(out=dstlm[:, oc * BC * LM:(oc + 1) * BC * LM],
                                         in0=lmraw[g % 2][:], scalar1=S4,
                                         scalar2=b16[:, oc:oc + 1], op0=ALU.mult,
                                         op1=ALU.add).then_inc(s_lm, 1)
        vector.wait_ge(s_c, 1)
        for m in range(NMAT if stages >= 3 else 0):
            vector.wait_ge(s_pe, NG + 2 * m + 1)
            vch.bar(vector.tensor_reduce(out=s2red[:, 0:1], in_=ps_s2[m % 2][:], axis=AX.X, op=ALU.max))
            vector.tensor_scalar_mul(out=s2red[:, 1:2], in0=s2red[:, 0:1],
                                     scalar1=-1.0 / 256.0).then_inc(s_v2, 1)
            vector.wait_ge(s_s2a, m + 1)
            if m >= 2:
                vector.wait_ge(s_pe, NG + 2 * (m - 2) + 2)
            vch.bar(vector.tensor_reduce(out=s2red[:, 2:3], in_=e2[:], axis=AX.X, op=ALU.add))
            vch.bar(vector.reciprocal(out=s2red[:, 3:4], in_=s2red[:, 2:3]))
            vch.bar(vector.tensor_scalar_mul(out=k2f[m % 2][:], in0=e2[:], scalar1=s2red[:, 3:4]))
            if m >= 2:
                vector.wait_ge(s_ks, 16 * (m - 1))
            vector.tensor_copy(out=k2bs[m % 2][:],
                               in_=k2f[m % 2][:]).then_inc(s_v2, 1)
            vector.wait_ge(s_pe, NG + 2 * m + 2)
            vector.tensor_tensor(out=runmax[:], in0=runmax[:], in1=ps_cs[m % 2][0:1, :],
                                 op=ALU.max).then_inc(s_v2, 1)
        if stages >= 3:
            vector.wait_ge(s_v2, 3 * NMAT)
            vector.tensor_reduce(out=pcm[0:1, 0:1], in_=runmax[0:1, :], axis=AX.X,
                                 op=ALU.max).then_inc(s_v2, 1)

    def scalar_body(scalar):
        ach = Chain(scalar, s_ach)
        for i in range(nt):
            b = i % NXB
            scalar.wait_ge(s_v, 1 + 3 * i + 1)
            ach.bar(scalar.activation(out=stat[:, 5:6], in_=stat[:, 4:5], func=AF.Ln, bias=epsc[:, 0:1]))
            scalar.activation(out=stat[:, 5:6], in_=stat[:, 5:6], func=AF.Exp,
                              scale=-0.5).then_inc(s_a, 1)
            scalar.wait_ge(s_v, 1 + 3 * i + 2)
            if i >= NXB:
                scalar.wait_ge(s_st, (i - NXB + 1) * 16)
            scalar.activation(out=xn_t[b][:], in_=x_t[b][:], func=AF.Identity,
                              scale=stat[:, 5:6], bias=stat[:, 6:7]).then_inc(s_a, 1)
            scalar.wait_ge(s_v, 1 + 3 * i + 3)
            for c in range(NCH):
                if "noTR" in cuts:
                    scalar.sem_inc(s_tr, 16)
                else:
                    scalar.dma_start(out=xn_fm[c][:, i * P:(i + 1) * P],
                                     in_=xnb_t[i % 2][:, c * P:(c + 1) * P],
                                     transpose=True).then_inc(s_tr, 16)
        for k in range(NQK if stages >= 2 else 0):
            g, tb = divmod(k, NTB)
            proj, oc = divmod(g, NCH)
            scalar.wait_ge(s_lr, k + 1)
            if tb == 0 and g >= 1:
                scalar.wait_ge(s_ps, 16 * g)
            bias = (bqs, bks)[proj]
            scalar.activation(out=pbf[g % 2][:, tb * 512:(tb + 1) * 512],
                              in_=ps_q[k % 4][:], func=AF.Identity, scale=S4,
                              bias=bias[:, oc:oc + 1]).then_inc(s_ev, 1)
        for kv in range(0 if ("noV" in cuts or stages < 2) else NVG):
            tt, oh = divmod(kv, 2)
            k = NQK + kv
            scalar.wait_ge(s_pe, k + 1)
            if oh == 0 and tt >= 1:
                scalar.wait_ge(s_ps, 16 * (16 + tt))
            scalar.activation(out=vbf[tt % 2][:, oh * 512:(oh + 1) * 512],
                              in_=ps_q[k % 4][:], func=AF.Copy).then_inc(s_ev, 1)
        for m in range(NMAT if stages >= 3 else 0):
            scalar.wait_ge(s_v2, 3 * m + 1)
            scalar.activation(out=e2[:], in_=ps_s2[m % 2][:], func=AF.Exp,
                              scale=1.0 / 256.0, bias=s2red[:, 1:2]).then_inc(s_s2a, 1)

    def tensor_body(tensor):
        if stages < 2:
            return
        tensor.wait_ge(s_tr, NT * NCH * 16)
        tensor.wait_ge(s_wbf, 24)
        tensor.wait_ge(s_v, 1)
        tensor.wait_ge(s_c, 1)
        for k in range(NQK):
            g, tb = divmod(k, NTB)
            proj, oc = divmod(g, NCH)
            w = (wq, wk)[proj]
            if k >= 4:
                tensor.wait_ge(s_ev, k - 3)
            mi = None
            for cc in range(NCH):
                mi = tensor.matmul(ps_q[k % 4][:], w[cc][:, oc * P:(oc + 1) * P],
                                   xn_fm[cc][:, tb * 512:(tb + 1) * 512],
                                   start=(cc == 0), stop=(cc == NCH - 1))
            mi.then_inc(s_pe, 1)
        for kv in range(0 if "noV" in cuts else NVG):
            tt, oh = divmod(kv, 2)
            k = NQK + kv
            tensor.wait_ge(s_ev, k - 3)
            for cc in range(NCH):
                tensor.matmul(ps_q[k % 4][:], xn_fm[cc][:, tt * P:(tt + 1) * P],
                              wv[cc][:, oh * 512:(oh + 1) * 512],
                              start=(cc == 0), stop=False)
            tensor.matmul(ps_q[k % 4][:], ones_row[0:1, :], bvb[0:1, oh * 512:(oh + 1) * 512],
                          start=False, stop=True).then_inc(s_pe, 1)
        if stages < 3:
            return
        tensor.wait_ge(s_lm, 16)
        for m in range(NMAT):
            sq_, h = divmod(m, HEADS)
            cc, half = divmod(h, 2)
            prow = 64 * half
            if m >= 2:
                tensor.wait_ge(s_s2a, m - 1)
            qs = qlm_b[prow:prow + 64, cc * BC * LM + sq_ * LM: cc * BC * LM + (sq_ + 1) * LM]
            ks = klm_b[prow:prow + 64, cc * BC * LM + sq_ * LM: cc * BC * LM + (sq_ + 1) * LM]
            tensor.matmul(ps_s2[m % 2][:], qs, ks, start=True, stop=True).then_inc(s_pe, 1)
            tensor.wait_ge(s_v2, 3 * m + 2)
            tensor.matmul(ps_cs[m % 2][:], ones_col[:], k2f[m % 2][:],
                          start=True, stop=True).then_inc(s_pe, 1)

    es.close()
    return {"sync": sync_body, "gpsimd": gpsimd_body, "vector": vector_body,
            "scalar": scalar_body, "tensor": tensor_body}



ENGINES = ["sync", "vector", "scalar", "tensor", "gpsimd"]
DMA_POOL = {"sync": 16, "scalar": 8, "gpsimd": 2, "vector": 2, "tensor": 1}


class Fence:
    def __init__(self, comp, dma):
        self.comp = comp
        self.dma = dma
        self.enforced = set()


class SP:

    def __init__(self, nc):
        self.nc = nc
        self.ops = []
        self._runC = {en: 0 for en in ENGINES}
        self._dmacnt = {}
        self._dmarr = {en: 0 for en in ENGINES}
        self._meta = []

    def _push(self, eng, fn, inc, raw, after):
        i = len(self.ops)
        deps = None if after is None else (list(after) if isinstance(after, (list, tuple)) else [after])
        self.ops.append((eng, fn, inc, raw, deps))
        if inc == 16:
            slot = self._dmarr[eng] % DMA_POOL[eng]
            self._dmarr[eng] += 1
            k = self._dmacnt.get((eng, slot), 0) + 1
            self._dmacnt[(eng, slot)] = k
            self._meta.append(("d", eng, slot, k))
        else:
            self._runC[eng] += inc
            self._meta.append(("c", eng, self._runC[eng]))
        return i

    def add(self, eng, fn, dma=False, after=None):
        return self._push(eng, fn, 16 if dma else 1, False, after)

    def add_raw(self, eng, fn, inc=1, after=None):
        return self._push(eng, fn, inc, True, after)

    def fence(self):
        return Fence(dict(self._runC), dict(self._dmacnt))

    def last(self):
        return len(self.ops) - 1


    def new_phase(self):
        self._rk = {}
        self._wk = {}
        self._pf = self.fence()
        return self._pf

    def op(self, eng, fn, reads=(), writes=(), dma=False, extra=()):
        if not hasattr(self, "_rk"):
            self.new_phase()
        if getattr(self, "_serial", False):
            h = self.add(eng, fn, dma=dma, after=[self.last()] if self.ops else [])
            for k in reads:
                self._rk.setdefault(k, []).append(h)
            for k in writes:
                self._wk[k] = h
                self._rk[k] = []
            return h
        deps = list(extra)
        fresh = False
        for k in reads:
            if k in self._wk:
                deps.append(self._wk[k])
        for k in writes:
            if k in self._wk or k in self._rk:
                deps.extend(self._rk.get(k, []))
                if k in self._wk:
                    deps.append(self._wk[k])
            else:
                fresh = True
        if fresh or (not reads and not writes):
            deps.append(self._pf)
        h = self.add(eng, fn, dma=dma, after=deps)
        for k in reads:
            self._rk.setdefault(k, []).append(h)
        for k in writes:
            self._wk[k] = h
            self._rk[k] = []
        return h

    def emit(self, es, prologue=None):
        nc = self.nc
        semC = {en: es.enter_context(nc.semaphore("spc_" + en)) for en in ENGINES}
        semD = {key: es.enter_context(nc.semaphore(f"spd_{key[0]}_{key[1]}"))
                for key in self._dmacnt}
        s_bar = es.enter_context(nc.semaphore("s_bar"))
        blk = es.enter_context(nc.Block())
        mine = {}
        for i, (e, _, _, _, _) in enumerate(self.ops):
            mine.setdefault(e, []).append(i)

        def mk(ename):
            idxs = mine.get(ename, [])

            def body(eng):
                if prologue is not None:
                    if ename in prologue:
                        prologue[ename](eng)
                    eng.sem_inc(s_bar, 1)
                    eng.wait_ge(s_bar, len(ENGINES))
                for i in idxs:
                    _, fn, inc, raw, deps = self.ops[i]
                    if deps is None:
                        deps = [i - 1] if i > 0 else []
                    needC = {}
                    needD = {}
                    for dp in deps:
                        if isinstance(dp, Fence):
                            if ename in dp.enforced:
                                continue
                            dp.enforced.add(ename)
                            for en2, v in dp.comp.items():
                                if v > 0 and en2 != ename:
                                    needC[en2] = max(needC.get(en2, 0), v)
                            for key, v in dp.dma.items():
                                if v > 0:
                                    needD[key] = max(needD.get(key, 0), v)
                        else:
                            if dp < 0:
                                continue
                            m = self._meta[dp]
                            if m[0] == "d":
                                key = (m[1], m[2])
                                needD[key] = max(needD.get(key, 0), m[3])
                            else:
                                je, cum = m[1], m[2]
                                if je == ename and je == "tensor":
                                    continue
                                needC[je] = max(needC.get(je, 0), cum)
                    mm = self._meta[i]
                    if mm[0] == "d" and mm[3] > 1:
                        key = (mm[1], mm[2])
                        needD[key] = max(needD.get(key, 0), mm[3] - 1)
                    for je in ENGINES:
                        if je in needC:
                            eng.wait_ge(semC[je], needC[je])
                    for key, v in needD.items():
                        eng.wait_ge(semD[key], 16 * v)
                    if raw:
                        fn(eng, semC[ename], inc)
                    elif mm[0] == "d":
                        fn(eng).then_inc(semD[(mm[1], mm[2])], 16)
                    else:
                        fn(eng).then_inc(semC[ename], inc)
                if ename == "sync":
                    for en2 in ENGINES:
                        if self._runC[en2] > 0:
                            eng.wait_ge(semC[en2], self._runC[en2])
                    for key, v in self._dmacnt.items():
                        eng.wait_ge(semD[key], 16 * v)
            return body

        for en in ENGINES:
            getattr(blk, en)(mk(en))



def emit_C(nc, sp, k2_in, amax_in, eyes_in, identb_in, z_o, zt_o):
    es = contextlib.ExitStack()
    sbt = lambda n, s, dt=F32: es.enter_context(nc.sbuf_tensor(n, s, dt))
    pst = lambda n, s, dt=F32: es.enter_context(nc.psum_tensor(n, s, dt))
    k2m = [sbt(f"k2m{b}", [P, LM], BF16) for b in range(2)]
    amax = sbt("amax_s", [1, 4])
    gm = sbt("gm", [1, 1])
    rc = sbt("rc", [1, 1])
    rcb = sbt("rcb", [P, 1])
    onesr = sbt("onesr", [1, P])
    eyes = sbt("eyes_s", [P, 3 * P])
    eyesb = sbt("eyesb", [P, 3 * P], BF16)
    identb = sbt("identb_s", [P, P], BF16)
    K2T = [sbt(f"K2T{b}", [P, LM], BF16) for b in range(2)]
    Vt = [sbt(f"Vt{b}", [P, LM], BF16) for b in range(2)]
    VTt = [sbt(f"VTt{b}", [P, LM], BF16) for b in range(2)]
    ATs = [sbt(f"ATs{b}", [P, LM], BF16) for b in range(2)]
    N2 = [sbt(f"N2{b}", [P, LM], BF16) for b in range(2)]
    N1 = [sbt(f"N1{b}", [P, LM], BF16) for b in range(2)]
    Mt = [sbt(f"Mt{b}", [P, LM], BF16) for b in range(2)]
    ps_a = [pst(f"ps_a{b}", [P, LM]) for b in range(2)]
    ps_b = [pst(f"ps_b{b}", [P, LM]) for b in range(2)]
    ps_t = [pst(f"ps_t{b}", [P, LM], BF16) for b in range(2)]
    ps_r = pst("ps_r", [P, 1])

    sp.new_phase()
    sp._serial = "C" in _G_SERIAL
    O = sp.op
    O("sync", lambda e: e.dma_start(out=amax[:], in_=amax_in), writes=["Cam"], dma=True)
    O("sync", lambda e: e.dma_start(out=eyes[:], in_=eyes_in), writes=["Cey0"], dma=True)
    O("sync", lambda e: e.dma_start(out=identb[:], in_=identb_in), writes=["Cib"], dma=True)
    O("gpsimd", lambda e: e.memset(onesr[:], 1.0), writes=["Cone"])
    O("vector", lambda e: e.tensor_reduce(out=gm[:], in_=amax[:], axis=AX.X, op=ALU.max),
      reads=["Cam"], writes=["Cgm"])
    O("vector", lambda e: e.reciprocal(out=rc[:], in_=gm[:]), reads=["Cgm"], writes=["Crc"])
    O("vector", lambda e: e.tensor_copy(out=eyesb[:], in_=eyes[:]), reads=["Cey0"], writes=["Cey"])
    O("tensor", lambda e: e.matmul(ps_r[:], onesr[0:1, :], rc[0:1, :], start=True, stop=True),
      reads=["Cone", "Crc"], writes=["Cpr"])
    O("scalar", lambda e: e.activation(out=rcb[:], in_=ps_r[:], func=AF.Copy),
      reads=["Cpr"], writes=["Crcb"])
    for m in range(NMAT):
        b = m % 2
        O("sync", lambda e, m=m, b=b: e.dma_start(out=k2m[b][:], in_=k2_in[m * P:(m + 1) * P, :]),
          writes=[f"Ck{b}"], dma=True)
        O("tensor", lambda e, b=b: e.transpose(ps_t[b][:], k2m[b][:], identb[:]),
          reads=[f"Ck{b}", "Cib"], writes=[f"Cpt{b}"])
        O("scalar", lambda e, b=b: e.activation(out=K2T[b][:], in_=ps_t[b][:], func=AF.Copy),
          reads=[f"Cpt{b}"], writes=[f"CK{b}"])
        O("scalar", lambda e, b=b: e.activation(out=Vt[b][:], in_=ps_t[b][:], func=AF.Identity, scale=rcb[:, 0:1]),
          reads=[f"Cpt{b}", "Crcb"], writes=[f"CV{b}"])
        O("vector", lambda e, b=b: e.tensor_scalar(out=VTt[b][:], in0=k2m[b][:], scalar1=rcb[:, 0:1],
                                                   scalar2=None, op0=ALU.mult),
          reads=[f"Ck{b}", "Crcb"], writes=[f"CVT{b}"])
        for it in range(6):
            O("tensor", lambda e, b=b: e.matmul(ps_a[b][:], K2T[b][:], Vt[b][:], start=True, stop=True),
              reads=[f"CK{b}", f"CV{b}"], writes=[f"Cpa{b}"])
            O("tensor", lambda e, b=b: e.matmul(ps_b[b][:], Vt[b][:], K2T[b][:], start=True, stop=True),
              reads=[f"CV{b}", f"CK{b}"], writes=[f"Cpb{b}"])
            O("vector", lambda e, b=b: e.scalar_tensor_tensor(out=N2[b][:], in0=ps_a[b][:], scalar=-1.0,
                                                              in1=eyesb[:, 0:P], op0=ALU.mult, op1=ALU.add),
              reads=[f"Cpa{b}", "Cey"], writes=[f"CN2{b}"])
            O("scalar", lambda e, b=b: e.activation(out=ATs[b][:], in_=ps_b[b][:], func=AF.Copy),
              reads=[f"Cpb{b}"], writes=[f"CA{b}"])
            O("tensor", lambda e, b=b: e.matmul(ps_a[b][:], ATs[b][:], N2[b][:], start=True, stop=True),
              reads=[f"CA{b}", f"CN2{b}"], writes=[f"Cpa{b}"])
            O("vector", lambda e, b=b: e.scalar_tensor_tensor(out=N1[b][:], in0=ps_a[b][:], scalar=-1.0,
                                                              in1=eyesb[:, P:2 * P], op0=ALU.mult, op1=ALU.add),
              reads=[f"Cpa{b}", "Cey"], writes=[f"CN1{b}"])
            O("tensor", lambda e, b=b: e.matmul(ps_a[b][:], ATs[b][:], N1[b][:], start=True, stop=True),
              reads=[f"CA{b}", f"CN1{b}"], writes=[f"Cpa{b}"])
            O("vector", lambda e, b=b: e.scalar_tensor_tensor(out=Mt[b][:], in0=ps_a[b][:], scalar=-1.0,
                                                              in1=eyesb[:, 2 * P:3 * P], op0=ALU.mult, op1=ALU.add),
              reads=[f"Cpa{b}", "Cey"], writes=[f"CM{b}"])
            O("tensor", lambda e, b=b: e.matmul(ps_a[b][:], VTt[b][:], Mt[b][:], start=True, stop=True),
              reads=[f"CVT{b}", f"CM{b}"], writes=[f"Cpa{b}"])
            O("tensor", lambda e, b=b: e.matmul(ps_b[b][:], Mt[b][:], VTt[b][:], start=True, stop=True),
              reads=[f"CM{b}", f"CVT{b}"], writes=[f"Cpb{b}"])
            O("scalar", lambda e, b=b: e.activation(out=Vt[b][:], in_=ps_a[b][:], func=AF.Identity, scale=0.25),
              reads=[f"Cpa{b}"], writes=[f"CV{b}"])
            O("scalar", lambda e, b=b: e.activation(out=VTt[b][:], in_=ps_b[b][:], func=AF.Identity, scale=0.25),
              reads=[f"Cpb{b}"], writes=[f"CVT{b}"])
        O("sync", lambda e, m=m, b=b: e.dma_start(out=z_o[m * P:(m + 1) * P, :], in_=Vt[b][:]),
          reads=[f"CV{b}"], dma=True)
        O("sync", lambda e, m=m, b=b: e.dma_start(out=zt_o[m * P:(m + 1) * P, :], in_=VTt[b][:]),
          reads=[f"CVT{b}"], dma=True)
    sp._serial = False
    es.close()



def emit_D(nc, sp, q_in, k_in, v_in, qlm_in, klm_in, zt_in, identf_in, attn_o):
    es = contextlib.ExitStack()
    sbt = lambda n, s, dt=F32: es.enter_context(nc.sbuf_tensor(n, s, dt))
    pst = lambda n, s, dt=F32: es.enter_context(nc.psum_tensor(n, s, dt))
    qlmt = [sbt(f"qlmt{c}", [P, BC * LM], BF16) for c in range(NCH)]
    klmt = [sbt(f"klmt{c}", [P, BC * LM], BF16) for c in range(NCH)]
    ksl = [sbt(f"ksl{b}", [P, S], BF16) for b in range(2)]
    qsl = [sbt(f"qsl{b}", [P, S], BF16) for b in range(2)]
    vsl = [sbt(f"vsl{b}", [P, 16 * HD], BF16) for b in range(2)]
    ztm = [sbt(f"ztm{b}", [P, LM], BF16) for b in range(2)]
    s3f = [sbt(f"s3f{b}", [P, S]) for b in range(2)]
    e3 = [sbt(f"e3{b}", [P, S], BF16) for b in range(2)]
    e3t = [es.enter_context(nc.sbuf_tensor(f"e3t{b}", [P, S], BF16, is_transpose_dest=True))
           for b in range(2)]
    e1 = [sbt(f"e1{b}", [P, S], BF16) for b in range(2)]
    red = [sbt(f"red{b}", [P, 4]) for b in range(2)]
    Us = [sbt(f"Us{b}", [P, HD], BF16) for b in range(2)]
    Ps = [sbt(f"Ps{b}", [P, HD], BF16) for b in range(2)]
    r1s = sbt("r1s", [1, S])
    rr1 = sbt("rr1", [1, S])
    bcs = [sbt(f"bcs{b}", [64, 512]) for b in range(2)]
    onescb = sbt("onescb", [P, 1], BF16)
    onesr1 = sbt("onesr1", [1, 64])
    identf = sbt("identf_s", [P, P])
    afm = sbt("afm", [P, T])
    stag = [sbt(f"stag{b}", [P, P]) for b in range(2)]
    ps_m = [pst(f"ps_m{b}", [P, 512]) for b in range(2)]
    ps_u = pst("ps_u", [P, HD])
    ps_x = [pst(f"ps_x{b}", [64, 512]) for b in range(2)]
    ps_bc = pst("ps_bc", [64, 512])
    ps_t = pst("ps_tD", [P, P])
    ps_r1 = pst("ps_r1", [1, 512])

    sp.new_phase()
    sp._serial = "D" in _G_SERIAL
    O = sp.op
    for c in range(NCH):
        O("sync", lambda e, c=c: e.dma_start(out=qlmt[c][:], in_=qlm_in[c * P:(c + 1) * P, :]),
          writes=[f"DQL{c}"], dma=True)
        O("sync", lambda e, c=c: e.dma_start(out=klmt[c][:], in_=klm_in[c * P:(c + 1) * P, :]),
          writes=[f"DKL{c}"], dma=True)
    O("sync", lambda e: e.dma_start(out=identf[:], in_=identf_in), writes=["Dif"], dma=True)
    O("gpsimd", lambda e: e.memset(onescb[:], 1.0), writes=["Do1"])
    O("gpsimd", lambda e: e.memset(onesr1[:], 1.0), writes=["Do2"])

    afk = [f"Daf{hh}_{s}_{b3}" for hh in range(2) for s in range(BC) for b3 in range(4)]
    it = 0
    for cc in range(NCH):
        for s in range(BC):
            for hh in range(2):
                b = it % 2
                it += 1
                h = 2 * cc + hh
                m = s * HEADS + h
                O("sync", lambda e, cc=cc, hh=hh, s=s, b=b: e.dma_start(
                    out=ksl[b][64 * hh:64 * hh + 64, :],
                    in_=k_in[cc * P + 64 * hh: cc * P + 64 * hh + 64, s * S:(s + 1) * S]),
                  writes=[f"Dk{b}"], dma=True)
                O("sync", lambda e, cc=cc, hh=hh, s=s, b=b: e.dma_start(
                    out=qsl[b][64 * hh:64 * hh + 64, :],
                    in_=q_in[cc * P + 64 * hh: cc * P + 64 * hh + 64, s * S:(s + 1) * S]),
                  writes=[f"Dq{b}"], dma=True)
                O("sync", lambda e, s=s, h=h, b=b: e.dma_start(
                    out=vsl[b][:].rearrange("p (j d) -> p j d", j=16),
                    in_=v_in[s * S:(s + 1) * S, :].rearrange("(j p) d -> p j d", p=P)[:, :, h * HD:(h + 1) * HD]),
                  writes=[f"Dv{b}"], dma=True)
                O("sync", lambda e, m=m, b=b: e.dma_start(out=ztm[b][:], in_=zt_in[m * P:(m + 1) * P, :]),
                  writes=[f"Dz{b}"], dma=True)
                for b3 in range(4):
                    O("tensor", lambda e, cc=cc, hh=hh, s=s, b=b, b3=b3: e.matmul(
                        ps_m[b3 % 2][:], qlmt[cc][64 * hh:64 * hh + 64, s * LM:(s + 1) * LM],
                        ksl[b][64 * hh:64 * hh + 64, b3 * 512:(b3 + 1) * 512], start=True, stop=True),
                      reads=[f"DQL{cc}", f"Dk{b}"], writes=[f"Dpm{b3 % 2}"])
                    O("scalar", lambda e, b=b, b3=b3: e.activation(out=s3f[b][:, b3 * 512:(b3 + 1) * 512],
                                                                   in_=ps_m[b3 % 2][:], func=AF.Copy),
                      reads=[f"Dpm{b3 % 2}"], writes=[f"Ds3_{b}_{b3}"])
                O("vector", lambda e, b=b: e.tensor_reduce(out=red[b][:, 0:1], in_=s3f[b][:], axis=AX.X, op=ALU.max),
                  reads=[f"Ds3_{b}_{b3}" for b3 in range(4)], writes=[f"Dr{b}"])
                O("vector", lambda e, b=b: e.tensor_scalar_mul(out=red[b][:, 1:2], in0=red[b][:, 0:1],
                                                               scalar1=-1.0 / SEG),
                  reads=[f"Dr{b}"], writes=[f"Dr{b}"])
                O("scalar", lambda e, b=b: e.activation(out=e3[b][:], in_=s3f[b][:], func=AF.Exp,
                                                        scale=1.0 / SEG, bias=red[b][:, 1:2]),
                  reads=[f"Ds3_{b}_{b3}" for b3 in range(4)] + [f"Dr{b}"], writes=[f"De{b}"])
                O("vector", lambda e, b=b: e.tensor_reduce(out=red[b][:, 2:3], in_=e3[b][:], axis=AX.X, op=ALU.add),
                  reads=[f"De{b}"], writes=[f"Dr{b}"])
                O("vector", lambda e, b=b: e.reciprocal(out=red[b][:, 3:4], in_=red[b][:, 2:3]),
                  reads=[f"Dr{b}"], writes=[f"Dr{b}"])
                for j in range(16):
                    O("scalar", lambda e, j=j, b=b: e.dma_start(out=e3t[b][:, j * P:(j + 1) * P],
                                                                in_=e3[b][:, j * P:(j + 1) * P],
                                                                transpose=True),
                      reads=[f"De{b}"], writes=[f"Dt{b}_{j}"], dma=True)
                for j in range(16):
                    O("tensor", lambda e, j=j, b=b: e.matmul(ps_u[:], e3t[b][:, j * P:(j + 1) * P],
                                                             vsl[b][:, j * HD:(j + 1) * HD],
                                                             start=(j == 0), stop=(j == 15)),
                      reads=[f"Dt{b}_{j}", f"Dv{b}"], writes=["Dpu"])
                O("vector", lambda e, b=b: e.tensor_scalar(out=Us[b][:], in0=ps_u[:], scalar1=red[b][:, 3:4],
                                                           scalar2=None, op0=ALU.mult),
                  reads=["Dpu", f"Dr{b}"], writes=[f"DU{b}"])
                O("tensor", lambda e, b=b: e.matmul(ps_u[:], ztm[b][:], Us[b][:], start=True, stop=True),
                  reads=[f"Dz{b}", f"DU{b}"], writes=["Dpu"])
                O("scalar", lambda e, b=b: e.activation(out=Ps[b][:], in_=ps_u[:], func=AF.Copy),
                  reads=["Dpu"], writes=[f"DP{b}"])
                for b3 in range(4):
                    O("tensor", lambda e, cc=cc, hh=hh, s=s, b=b, b3=b3: e.matmul(
                        ps_m[b3 % 2][:], klmt[cc][64 * hh:64 * hh + 64, s * LM:(s + 1) * LM],
                        qsl[b][64 * hh:64 * hh + 64, b3 * 512:(b3 + 1) * 512], start=True, stop=True),
                      reads=[f"DKL{cc}", f"Dq{b}"], writes=[f"Dpm{b3 % 2}"])
                    O("scalar", lambda e, b=b, b3=b3: e.activation(out=e1[b][:, b3 * 512:(b3 + 1) * 512],
                                                                   in_=ps_m[b3 % 2][:], func=AF.Exp, scale=1.0 / SEG),
                      reads=[f"Dpm{b3 % 2}"], writes=[f"De1_{b}_{b3}"])
                for b3 in range(4):
                    O("tensor", lambda e, b=b, b3=b3: e.matmul(ps_r1[:], onescb[:], e1[b][:, b3 * 512:(b3 + 1) * 512],
                                                               start=True, stop=True),
                      reads=["Do1", f"De1_{b}_{b3}"], writes=["Dpr1"])
                    O("scalar", lambda e, b3=b3: e.activation(out=r1s[0:1, b3 * 512:(b3 + 1) * 512],
                                                              in_=ps_r1[:], func=AF.Copy),
                      reads=["Dpr1"], writes=[f"Dr1_{b3}"])
                O("vector", lambda e: e.reciprocal(out=rr1[:], in_=r1s[:]),
                  reads=[f"Dr1_{b3}" for b3 in range(4)], writes=["Drr"])
                for b3 in range(4):
                    O("tensor", lambda e, b=b, b3=b3: e.matmul(ps_x[b3 % 2][:], Ps[b][:],
                                                               e1[b][:, b3 * 512:(b3 + 1) * 512],
                                                               start=True, stop=True),
                      reads=[f"DP{b}", f"De1_{b}_{b3}"], writes=[f"Dpx{b3 % 2}"])
                    O("tensor", lambda e, b3=b3: e.matmul(ps_bc[:], onesr1[0:1, :],
                                                          rr1[0:1, b3 * 512:(b3 + 1) * 512],
                                                          start=True, stop=True),
                      reads=["Do2", "Drr"], writes=["Dpbc"])
                    O("scalar", lambda e, b=b, b3=b3: e.activation(out=bcs[b3 % 2][:], in_=ps_bc[:], func=AF.Copy),
                      reads=["Dpbc"], writes=[f"Dbc{b3 % 2}"])
                    O("vector", lambda e, hh=hh, s=s, b=b, b3=b3: e.tensor_tensor(
                        out=afm[64 * hh:64 * hh + 64, s * S + b3 * 512: s * S + (b3 + 1) * 512],
                        in0=ps_x[b3 % 2][:], in1=bcs[b3 % 2][:], op=ALU.mult),
                      reads=[f"Dpx{b3 % 2}", f"Dbc{b3 % 2}"], writes=[f"Daf{hh}_{s}_{b3}"])
        for tt in range(NT):
            b2 = tt % 2
            O("tensor", lambda e, tt=tt: e.transpose(ps_t[:], afm[:, tt * P:(tt + 1) * P], identf[:]),
              reads=afk + ["Dif"], writes=["Dpst"])
            O("scalar", lambda e, b2=b2: e.activation(out=stag[b2][:], in_=ps_t[:], func=AF.Copy),
              reads=["Dpst"], writes=[f"Dst{b2}"])
            O("sync", lambda e, tt=tt, cc=cc, b2=b2: e.dma_start(
                out=attn_o[tt * P:(tt + 1) * P, cc * P:(cc + 1) * P], in_=stag[b2][:]),
              reads=[f"Dst{b2}"], dma=True)
    sp._serial = False
    es.close()



def emit_quant_t(O, x_ap, xk, q_ap, qk, scr, sk, C, pp=P):
    nb = C // 16
    bm, sc, rs, hs = scr["bm"], scr["sc"], scr["rs"], scr["hs"]
    y, a, t1, t2, t3 = scr["y"], scr["a"], scr["t1"], scr["t2"], scr["t3"]
    ri1, ri2 = scr["ri1"], scr["ri2"]
    x3 = x_ap.rearrange("p (n w) -> p n w", w=16)
    K = lambda n: sk + n
    O("vector", lambda e: e.tensor_reduce(out=bm[:pp, :nb], in_=x3, axis=AX.X,
                                          op=ALU.max, apply_absolute_value=True),
      reads=[xk], writes=[K("bm")])
    O("vector", lambda e: e.tensor_scalar(out=sc[:pp, :nb], in0=bm[:pp, :nb], scalar1=1.0 / 6.0,
                                          scalar2=1e-12, op0=ALU.mult, op1=ALU.max),
      reads=[K("bm")], writes=[K("sc")])
    O("vector", lambda e: e.reciprocal(out=rs[:pp, :nb], in_=sc[:pp, :nb]),
      reads=[K("sc")], writes=[K("rs")])
    rs3 = rs[:pp, :nb].unsqueeze(2).broadcast_to([pp, nb, 16])
    y3 = y[:pp, :C].rearrange("p (n w) -> p n w", w=16)
    O("vector", lambda e: e.tensor_tensor(out=y3, in0=x3, in1=rs3, op=ALU.mult),
      reads=[xk, K("rs")], writes=[K("y")])
    O("vector", lambda e: e.tensor_scalar(out=a[:pp, :C].bitcast(mybir.dt.int32),
                                          in0=y[:pp, :C].bitcast(mybir.dt.int32),
                                          scalar1=0x7fffffff, scalar2=None,
                                          op0=ALU.bitwise_and),
      reads=[K("y")], writes=[K("a")])
    O("vector", lambda e: e.tensor_scalar_mul(out=ri1[:pp, :C], in0=a[:pp, :C], scalar1=2.0),
      reads=[K("a")], writes=[K("r1")])
    O("vector", lambda e: e.tensor_copy(out=ri2[:pp, :C], in_=a[:pp, :C]),
      reads=[K("a")], writes=[K("r2")])
    O("vector", lambda e: e.tensor_scalar(out=t1[:pp, :C], in0=ri1[:pp, :C], scalar1=0.5,
                                          scalar2=2.0, op0=ALU.mult, op1=ALU.min),
      reads=[K("r1")], writes=[K("t1")])
    O("vector", lambda e: e.tensor_scalar(out=t2[:pp, :C], in0=ri2[:pp, :C], scalar1=2.0,
                                          scalar2=4.0, op0=ALU.max, op1=ALU.min),
      reads=[K("r2")], writes=[K("t2")])
    O("vector", lambda e: e.tensor_tensor(out=t1[:pp, :C], in0=t1[:pp, :C], in1=t2[:pp, :C],
                                          op=ALU.add),
      reads=[K("t1"), K("t2")], writes=[K("t1")])
    O("vector", lambda e: e.tensor_scalar(out=t2[:pp, :C], in0=a[:pp, :C], scalar1=5.0,
                                          scalar2=2.0, op0=ALU.is_gt, op1=ALU.mult),
      reads=[K("a")], writes=[K("t2")])
    O("vector", lambda e: e.scalar_tensor_tensor(out=t1[:pp, :C], in0=t1[:pp, :C], scalar=-2.0,
                                                 in1=t2[:pp, :C], op0=ALU.add, op1=ALU.add),
      reads=[K("t1"), K("t2")], writes=[K("t1")])
    O("scalar", lambda e: e.activation(out=t3[:pp, :C], in_=y[:pp, :C], func=AF.Sign),
      reads=[K("y")], writes=[K("t3")])
    O("vector", lambda e: e.tensor_tensor(out=t1[:pp, :C], in0=t1[:pp, :C], in1=t3[:pp, :C],
                                          op=ALU.mult),
      reads=[K("t1"), K("t3")], writes=[K("t1")])
    O("vector", lambda e: e.tensor_scalar_mul(out=hs[:pp, :nb], in0=sc[:pp, :nb], scalar1=1.0),
      reads=[K("sc")], writes=[K("hs")])
    hs3 = hs[:pp, :nb].unsqueeze(2).broadcast_to([pp, nb, 16])
    t13 = t1[:pp, :C].rearrange("p (n w) -> p n w", w=16)
    q3 = q_ap.rearrange("p (n w) -> p n w", w=16)
    O("vector", lambda e: e.tensor_tensor(out=q3, in0=t13, in1=hs3, op=ALU.mult),
      reads=[K("t1"), K("hs")], writes=[qk])


def emit_quant(Aop, x_ap, q_ap, scr, C, pp=P):
    nb = C // 16
    bm, sc, rs, hs = scr["bm"], scr["sc"], scr["rs"], scr["hs"]
    y, a, t1, t2, t3 = scr["y"], scr["a"], scr["t1"], scr["t2"], scr["t3"]
    ri1, ri2 = scr["ri1"], scr["ri2"]
    x3 = x_ap.rearrange("p (n w) -> p n w", w=16)
    Aop("vector", lambda e: e.tensor_reduce(out=bm[:pp, :nb], in_=x3, axis=AX.X,
                                            op=ALU.max, apply_absolute_value=True))
    Aop("vector", lambda e: e.tensor_scalar(out=sc[:pp, :nb], in0=bm[:pp, :nb], scalar1=1.0 / 6.0,
                                            scalar2=1e-12, op0=ALU.mult, op1=ALU.max))
    Aop("vector", lambda e: e.reciprocal(out=rs[:pp, :nb], in_=sc[:pp, :nb]))
    rs3 = rs[:pp, :nb].unsqueeze(2).broadcast_to([pp, nb, 16])
    y3 = y[:pp, :C].rearrange("p (n w) -> p n w", w=16)
    Aop("vector", lambda e: e.tensor_tensor(out=y3, in0=x3, in1=rs3, op=ALU.mult))
    Aop("vector", lambda e: e.tensor_scalar(out=a[:pp, :C].bitcast(mybir.dt.int32),
                                            in0=y[:pp, :C].bitcast(mybir.dt.int32),
                                            scalar1=0x7fffffff, scalar2=None,
                                            op0=ALU.bitwise_and))
    Aop("vector", lambda e: e.tensor_scalar_mul(out=ri1[:pp, :C], in0=a[:pp, :C], scalar1=2.0))
    Aop("vector", lambda e: e.tensor_copy(out=ri2[:pp, :C], in_=a[:pp, :C]))
    Aop("vector", lambda e: e.tensor_scalar(out=t1[:pp, :C], in0=ri1[:pp, :C], scalar1=0.5,
                                            scalar2=2.0, op0=ALU.mult, op1=ALU.min))
    Aop("vector", lambda e: e.tensor_scalar(out=t2[:pp, :C], in0=ri2[:pp, :C], scalar1=2.0,
                                            scalar2=4.0, op0=ALU.max, op1=ALU.min))
    Aop("vector", lambda e: e.tensor_tensor(out=t1[:pp, :C], in0=t1[:pp, :C], in1=t2[:pp, :C],
                                            op=ALU.add))
    Aop("vector", lambda e: e.tensor_scalar(out=t2[:pp, :C], in0=a[:pp, :C], scalar1=5.0,
                                            scalar2=2.0, op0=ALU.is_gt, op1=ALU.mult))
    Aop("vector", lambda e: e.scalar_tensor_tensor(out=t1[:pp, :C], in0=t1[:pp, :C], scalar=-2.0,
                                                   in1=t2[:pp, :C], op0=ALU.add, op1=ALU.add))
    Aop("scalar", lambda e: e.activation(out=t3[:pp, :C], in_=y[:pp, :C], func=AF.Sign))
    Aop("vector", lambda e: e.tensor_tensor(out=t1[:pp, :C], in0=t1[:pp, :C], in1=t3[:pp, :C],
                                            op=ALU.mult))
    Aop("vector", lambda e: e.tensor_scalar_mul(out=hs[:pp, :nb], in0=sc[:pp, :nb], scalar1=1.0))
    hs3 = hs[:pp, :nb].unsqueeze(2).broadcast_to([pp, nb, 16])
    t13 = t1[:pp, :C].rearrange("p (n w) -> p n w", w=16)
    q3 = q_ap.rearrange("p (n w) -> p n w", w=16)
    Aop("vector", lambda e: e.tensor_tensor(out=q3, in0=t13, in1=hs3, op=ALU.mult))


def quant_scratch(sbt, tag, C, dt_mid=F32):
    nb = C // 16
    return {
        "bm": sbt(f"bm{tag}", [P, nb]), "sc": sbt(f"sc{tag}", [P, nb]),
        "rs": sbt(f"rs{tag}", [P, nb]), "hs": sbt(f"hs{tag}", [P, nb]),
        "y": sbt(f"yq{tag}", [P, C]), "a": sbt(f"aq{tag}", [P, C]),
        "t1": sbt(f"t1q{tag}", [P, C], BF16), "t2": sbt(f"t2q{tag}", [P, C], BF16),
        "t3": sbt(f"t3q{tag}", [P, C], BF16),
        "ri1": sbt(f"ri1q{tag}", [P, C], I16), "ri2": sbt(f"ri2q{tag}", [P, C], I16),
    }



def emit_WQfull(nc, sp, tag, wg_in, wu_in, wd_in, wgq_o, wuq_o, wdq_o):
    es = contextlib.ExitStack()
    sbt = lambda n, s, dt=F32: es.enter_context(nc.sbuf_tensor(n + tag, s, dt))
    wt = [sbt(f"wt{b}", [P, I_FFN]) for b in range(2)]
    qt = [sbt(f"qt{b}", [P, I_FFN], BF16) for b in range(2)]
    qtt = [es.enter_context(nc.sbuf_tensor(f"qtt{b}" + tag, [P, I_FFN], BF16, is_transpose_dest=True))
           for b in range(2)]
    scr = quant_scratch(sbt, "w", I_FFN)
    sp.new_phase()
    sp._serial = "W" in _G_SERIAL
    O = sp.op
    t = "w" + tag
    jobs = []
    for r in range(I_FFN // P):
        jobs.append((wg_in[r * P:(r + 1) * P, :], wgq_o, r, D))
        jobs.append((wu_in[r * P:(r + 1) * P, :], wuq_o, r, D))
    for r in range(NCH):
        jobs.append((wd_in[r * P:(r + 1) * P, :], wdq_o, r, I_FFN))
    for jb, (src_ap, dst, r, C) in enumerate(jobs):
        b = jb % 2
        nj = C // P
        O("sync", lambda e, src_ap=src_ap, C=C, b=b: e.dma_start(out=wt[b][:, 0:C], in_=src_ap),
          writes=[t + f"wt{b}"], dma=True)
        emit_quant_t(O, wt[b][:, 0:C], t + f"wt{b}", qt[b][:, 0:C], t + f"qt{b}", scr, t + "s", C)
        for j in range(nj):
            O("scalar", lambda e, j=j, b=b: e.dma_start(out=qtt[b][:, j * P:(j + 1) * P],
                                                        in_=qt[b][:, j * P:(j + 1) * P],
                                                        transpose=True),
              reads=[t + f"qt{b}"], writes=[t + f"qx{b}_{j}"], dma=True)
        O("sync", lambda e, dst=dst, r=r, b=b, nj=nj: e.dma_start(
            out=dst.rearrange("(j p) c -> p j c", p=P)[:, :, r * P:(r + 1) * P],
            in_=qtt[b][:, 0:nj * P].rearrange("p (j q) -> p j q", j=nj)),
          reads=[t + f"qx{b}_{j}" for j in range(nj)], dma=True)
    sp._serial = False
    es.close()



def emit_FFN(nc, sp, tag, xn_in, wg_in, wu_in, wd_in, ffn_o, hq_sc):
    es = contextlib.ExitStack()
    sbt = lambda n, s, dt=F32: es.enter_context(nc.sbuf_tensor(n + tag, s, dt))
    pst = lambda n, s, dt=F32: es.enter_context(nc.psum_tensor(n + tag, s, dt))
    wpool = sbt("wpool", [P, 2 * NCH * I_FFN], BF16)
    wgq = [wpool[:, c * I_FFN:(c + 1) * I_FFN] for c in range(NCH)]
    wuq = [wpool[:, (NCH + c) * I_FFN:(NCH + c + 1) * I_FFN] for c in range(NCH)]
    wdq = [wpool[:, j * D:(j + 1) * D] for j in range(22)]
    xt = [sbt(f"xt{b}", [P, D]) for b in range(2)]
    xq = sbt("xq", [P, D], BF16)
    xqf = es.enter_context(nc.sbuf_tensor("xqf" + tag, [P, D], BF16, is_transpose_dest=True))
    gact = sbt("gact", [P, I_FFN])
    uact = sbt("uact", [P, I_FFN])
    hqt = sbt("hqt", [P, I_FFN], BF16)
    hqtt = [es.enter_context(nc.sbuf_tensor(f"hqtt{b}" + tag, [P, I_FFN], BF16, is_transpose_dest=True))
            for b in range(2)]
    scr = quant_scratch(sbt, "f", I_FFN)
    hloadB = [sbt(f"hloadB{b}", [P, 22 * P], BF16) for b in range(2)]
    ot = sbt("ot", [P, D])
    ps = [pst(f"psF{i}", [P, 512]) for i in range(4)]

    IB = [(0, 512), (512, 1024), (1024, 1536), (1536, 2048), (2048, 2560), (2560, 2816)]
    sp.new_phase()
    sp._serial = "N" in _G_SERIAL
    O = sp.op
    t = "n" + tag
    for c in range(NCH):
        O("sync", lambda e, c=c: e.dma_start(out=wgq[c], in_=wg_in[c * P:(c + 1) * P, :]),
          writes=[t + f"Wg{c}"], dma=True)
        O("sync", lambda e, c=c: e.dma_start(out=wuq[c], in_=wu_in[c * P:(c + 1) * P, :]),
          writes=[t + f"Wu{c}"], dma=True)
    for tt in range(NT):
        b = tt % 2
        O("sync", lambda e, tt=tt, b=b: e.dma_start(out=xt[b][:], in_=xn_in[tt * P:(tt + 1) * P, :]),
          writes=[t + f"xt{b}"], dma=True)
        emit_quant_t(O, xt[b][:], t + f"xt{b}", xq[:], t + "xq", scr, t + "s", D)
        for j in range(NCH):
            O("scalar", lambda e, j=j: e.dma_start(out=xqf[:, j * P:(j + 1) * P],
                                                   in_=xq[:, j * P:(j + 1) * P], transpose=True),
              reads=[t + "xq"], writes=[t + f"xf{j}"], dma=True)
        g = 0
        for wsel in range(2):
            wts = (wgq, wuq)[wsel]
            wkey = ("Wg", "Wu")[wsel]
            akey = ("ga", "ua")[wsel]
            dstact = (gact, uact)[wsel]
            for ib, (lo, hi) in enumerate(IB):
                n = hi - lo
                pk = g % 4
                for cc in range(NCH):
                    O("tensor", lambda e, cc=cc, lo=lo, hi=hi, wts=wts, n=n, pk=pk: e.matmul(
                        ps[pk][:, 0:n], xqf[:, cc * P:(cc + 1) * P], wts[cc][:, lo:hi],
                        start=(cc == 0), stop=(cc == NCH - 1)),
                      reads=[t + f"xf{cc}", t + wkey + str(cc)], writes=[f"ps{pk}{tag}"])
                if wsel == 0:
                    O("scalar", lambda e, lo=lo, hi=hi, n=n, dstact=dstact, pk=pk: e.activation(
                        out=dstact[:, lo:hi], in_=ps[pk][:, 0:n], func=AF.Silu),
                      reads=[f"ps{pk}{tag}"], writes=[t + akey + str(ib)])
                else:
                    O("scalar", lambda e, lo=lo, hi=hi, n=n, dstact=dstact, pk=pk: e.activation(
                        out=dstact[:, lo:hi], in_=ps[pk][:, 0:n], func=AF.Copy),
                      reads=[f"ps{pk}{tag}"], writes=[t + akey + str(ib)])
                g += 1
        O("vector", lambda e: e.tensor_tensor(out=gact[:], in0=gact[:], in1=uact[:], op=ALU.mult),
          reads=[t + f"ga{ib}" for ib in range(6)] + [t + f"ua{ib}" for ib in range(6)],
          writes=[t + "ha"])
        emit_quant_t(O, gact[:], t + "ha", hqt[:], t + "hq", scr, t + "s", I_FFN)
        for j in range(22):
            O("scalar", lambda e, j=j, b=b: e.dma_start(out=hqtt[b][:, j * P:(j + 1) * P],
                                                        in_=hqt[:, j * P:(j + 1) * P], transpose=True),
              reads=[t + "hq"], writes=[t + f"hf{b}_{j}"], dma=True)
        O("sync", lambda e, tt=tt, b=b: e.dma_start(
            out=hq_sc.rearrange("(j p) t2 -> p j t2", p=P)[:, :, tt * P:(tt + 1) * P],
            in_=hqtt[b][:].rearrange("p (j q) -> p j q", j=22)),
          reads=[t + f"hf{b}_{j}" for j in range(22)], dma=True)
    sp.new_phase()
    sp._serial = "N" in _G_SERIAL
    for j in range(22):
        O("sync", lambda e, j=j: e.dma_start(out=wdq[j], in_=wd_in[j * P:(j + 1) * P, :]),
          writes=[t + f"Wd{j}"], dma=True)
    for tt in range(NT):
        b = tt % 2
        O("sync", lambda e, tt=tt, b=b: e.dma_start(
            out=hloadB[b][:].rearrange("p (j q) -> p j q", j=22),
            in_=hq_sc.rearrange("(j p) t2 -> p j t2", p=P)[:, :, tt * P:(tt + 1) * P]),
          writes=[t + f"hl{b}"], dma=True)
        for half in range(2):
            for j in range(22):
                O("tensor", lambda e, j=j, half=half, b=b: e.matmul(
                    ps[half][:], hloadB[b][:, j * P:(j + 1) * P], wdq[j][:, half * 512:(half + 1) * 512],
                    start=(j == 0), stop=(j == 21)),
                  reads=[t + f"hl{b}", t + f"Wd{j}"], writes=[f"ps{half}{tag}"])
            O("scalar", lambda e, half=half: e.activation(out=ot[:, half * 512:(half + 1) * 512],
                                                          in_=ps[half][:], func=AF.Copy),
              reads=[f"ps{half}{tag}"], writes=[t + f"ot{half}"])
        O("sync", lambda e, tt=tt: e.dma_start(out=ffn_o[tt * P:(tt + 1) * P, :], in_=ot[:]),
          reads=[t + "ot0", t + "ot1"], dma=True)
    sp._serial = False
    es.close()



def emit_F(nc, sp, x_in, at_in, ff_in, eps_in, x1_o, xn1_o):
    es = contextlib.ExitStack()
    sbt = lambda n, s, dt=F32: es.enter_context(nc.sbuf_tensor(n, s, dt))
    xa = [sbt(f"xaF{b}", [P, D]) for b in range(2)]
    xb = [sbt(f"xbF{b}", [P, D]) for b in range(2)]
    xc = [sbt(f"xcF{b}", [P, D]) for b in range(2)]
    xn = [sbt(f"xnF{b}", [P, D]) for b in range(2)]
    xsq = sbt("xsqF", [P, D])
    stat = [sbt(f"statF{b}", [P, 8]) for b in range(2)]
    epsc = sbt("epscF", [P, 1])
    sp.new_phase()
    sp._serial = "F" in _G_SERIAL
    O = sp.op
    O("sync", lambda e: e.dma_start(out=epsc[:], in_=eps_in), writes=["epsF"], dma=True)
    for i in range(NT):
        b = i % 2
        O("sync", lambda e, i=i, b=b: e.dma_start(out=xa[b][:], in_=x_in[i * P:(i + 1) * P, :]),
          writes=[f"Fa{b}"], dma=True)
        O("sync", lambda e, i=i, b=b: e.dma_start(out=xb[b][:], in_=at_in[i * P:(i + 1) * P, :]),
          writes=[f"Fb{b}"], dma=True)
        O("sync", lambda e, i=i, b=b: e.dma_start(out=xc[b][:], in_=ff_in[i * P:(i + 1) * P, :]),
          writes=[f"Fc{b}"], dma=True)
        O("vector", lambda e, b=b: e.tensor_tensor(out=xa[b][:], in0=xa[b][:], in1=xb[b][:], op=ALU.add),
          reads=[f"Fa{b}", f"Fb{b}"], writes=[f"Fa{b}"])
        O("vector", lambda e, b=b: e.tensor_tensor(out=xa[b][:], in0=xa[b][:], in1=xc[b][:], op=ALU.add),
          reads=[f"Fa{b}", f"Fc{b}"], writes=[f"Fa{b}"])
        O("sync", lambda e, i=i, b=b: e.dma_start(out=x1_o[i * P:(i + 1) * P, :], in_=xa[b][:]),
          reads=[f"Fa{b}"], dma=True)
        sk = f"Fs{b}"
        O("vector", lambda e, b=b: e.tensor_reduce(out=stat[b][:, 0:1], in_=xa[b][:], axis=AX.X, op=ALU.add),
          reads=[f"Fa{b}"], writes=[sk])
        O("vector", lambda e, b=b: e.tensor_tensor(out=xsq[:], in0=xa[b][:], in1=xa[b][:], op=ALU.mult),
          reads=[f"Fa{b}"], writes=["Fq"])
        O("vector", lambda e, b=b: e.tensor_reduce(out=stat[b][:, 1:2], in_=xsq[:], axis=AX.X, op=ALU.add),
          reads=["Fq"], writes=[sk])
        O("vector", lambda e, b=b: e.tensor_scalar_mul(out=stat[b][:, 2:3], in0=stat[b][:, 0:1], scalar1=1.0 / D),
          reads=[sk], writes=[sk])
        O("vector", lambda e, b=b: e.tensor_scalar_mul(out=stat[b][:, 3:4], in0=stat[b][:, 1:2], scalar1=1.0 / D),
          reads=[sk], writes=[sk])
        O("vector", lambda e, b=b: e.tensor_tensor(out=stat[b][:, 4:5], in0=stat[b][:, 2:3], in1=stat[b][:, 2:3], op=ALU.mult),
          reads=[sk], writes=[sk])
        O("vector", lambda e, b=b: e.tensor_tensor(out=stat[b][:, 4:5], in0=stat[b][:, 3:4], in1=stat[b][:, 4:5], op=ALU.subtract),
          reads=[sk], writes=[sk])
        O("scalar", lambda e, b=b: e.activation(out=stat[b][:, 5:6], in_=stat[b][:, 4:5], func=AF.Ln, bias=epsc[:, 0:1]),
          reads=[sk, "epsF"], writes=[sk])
        O("scalar", lambda e, b=b: e.activation(out=stat[b][:, 5:6], in_=stat[b][:, 5:6], func=AF.Exp, scale=-0.5),
          reads=[sk], writes=[sk])
        O("vector", lambda e, b=b: e.scalar_tensor_tensor(out=stat[b][:, 6:7], in0=stat[b][:, 2:3], scalar=-1.0,
                                                          in1=stat[b][:, 5:6], op0=ALU.mult, op1=ALU.mult),
          reads=[sk], writes=[sk])
        O("scalar", lambda e, b=b: e.activation(out=xn[b][:], in_=xa[b][:], func=AF.Identity,
                                                scale=stat[b][:, 5:6], bias=stat[b][:, 6:7]),
          reads=[f"Fa{b}", sk], writes=[f"Fn{b}"])
        O("sync", lambda e, i=i, b=b: e.dma_start(out=xn1_o[i * P:(i + 1) * P, :], in_=xn[b][:]),
          reads=[f"Fn{b}"], dma=True)
    sp._serial = False
    es.close()



def emit_I(nc, sp, a_in, b_in, c_in, o_out):
    es = contextlib.ExitStack()
    sbt = lambda n, s, dt=F32: es.enter_context(nc.sbuf_tensor(n, s, dt))
    xa = [sbt(f"xaI{b}", [P, D]) for b in range(2)]
    xb = [sbt(f"xbI{b}", [P, D]) for b in range(2)]
    xc = [sbt(f"xcI{b}", [P, D]) for b in range(2)]
    sp.new_phase()
    sp._serial = "I" in _G_SERIAL
    O = sp.op
    for i in range(NT):
        b = i % 2
        O("sync", lambda e, i=i, b=b: e.dma_start(out=xa[b][:], in_=a_in[i * P:(i + 1) * P, :]),
          writes=[f"Ia{b}"], dma=True)
        O("sync", lambda e, i=i, b=b: e.dma_start(out=xb[b][:], in_=b_in[i * P:(i + 1) * P, :]),
          writes=[f"Ib{b}"], dma=True)
        O("sync", lambda e, i=i, b=b: e.dma_start(out=xc[b][:], in_=c_in[i * P:(i + 1) * P, :]),
          writes=[f"Ic{b}"], dma=True)
        O("vector", lambda e, b=b: e.tensor_tensor(out=xa[b][:], in0=xa[b][:], in1=xb[b][:], op=ALU.add),
          reads=[f"Ia{b}", f"Ib{b}"], writes=[f"Ia{b}"])
        O("vector", lambda e, b=b: e.tensor_tensor(out=xa[b][:], in0=xa[b][:], in1=xc[b][:], op=ALU.add),
          reads=[f"Ia{b}", f"Ic{b}"], writes=[f"Ia{b}"])
        O("sync", lambda e, i=i, b=b: e.dma_start(out=o_out[i * P:(i + 1) * P, :], in_=xa[b][:]),
          reads=[f"Ia{b}"], dma=True)
    sp._serial = False
    es.close()



def emit_G(nc, sp, xn1_in, ln_b, wkT_in, wvT_in, wrT_in, woT_in, wfkT_in,
           wfvT_in, wfrT_in, tdc_in, tfc_in, eps_in, attn1_o,
           a_sc, x2_sc, xk_sc, xv_sc, xr_sc, xr2_sc, sa_sc, kfq_sc):
    es = contextlib.ExitStack()
    sbt = lambda n, s, dt=F32: es.enter_context(nc.sbuf_tensor(n, s, dt))
    pst = lambda n, s, dt=F32: es.enter_context(nc.psum_tensor(n, s, dt))
    POOLN = 34000
    pool = sbt("poolG", [P, POOLN])
    xk2_fm = [sbt(f"xk2fm{c}", [P, T], BF16) for c in range(NCH)]
    trd = [es.enter_context(nc.sbuf_tensor(f"trdG{b}", [P, D], BF16, is_transpose_dest=True))
           for b in range(2)]
    stat = [sbt(f"statG{b}", [P, 8]) for b in range(2)]
    epsc = sbt("epscG", [P, 1])
    smallc = [sbt(f"smallcG{b}", [P, 8]) for b in range(2)]
    tdc = sbt("tdcG", [P, NCH]); tfc = sbt("tfcG", [P, NCH])
    hlast = sbt("hlastG", [1, D])
    ps = [pst(f"psG{i}", [P, 512]) for i in range(4)]
    O = sp.op

    class Carve:
        def __init__(self):
            self.off = 0
        def f32(self, n):
            v = pool[:, self.off:self.off + n]
            self.off += n
            assert self.off <= POOLN
            return v
        def bf(self, n):
            v = pool[:, self.off:self.off + (n + 1) // 2].bitcast(BF16)
            self.off += (n + 1) // 2
            return v

    def ln(xa, xk, dst, dk, sb, sk, xsq, qk, wap=None, wk2=None, bap=None, bk2=None):
        O("vector", lambda e: e.tensor_reduce(out=sb[:, 0:1], in_=xa, axis=AX.X, op=ALU.add),
          reads=[xk], writes=[sk])
        O("vector", lambda e: e.tensor_tensor(out=xsq, in0=xa, in1=xa, op=ALU.mult),
          reads=[xk], writes=[qk])
        O("vector", lambda e: e.tensor_reduce(out=sb[:, 1:2], in_=xsq, axis=AX.X, op=ALU.add),
          reads=[qk], writes=[sk])
        O("vector", lambda e: e.tensor_scalar_mul(out=sb[:, 2:3], in0=sb[:, 0:1], scalar1=1.0 / D),
          reads=[sk], writes=[sk])
        O("vector", lambda e: e.tensor_scalar_mul(out=sb[:, 3:4], in0=sb[:, 1:2], scalar1=1.0 / D),
          reads=[sk], writes=[sk])
        O("vector", lambda e: e.tensor_tensor(out=sb[:, 4:5], in0=sb[:, 2:3], in1=sb[:, 2:3], op=ALU.mult),
          reads=[sk], writes=[sk])
        O("vector", lambda e: e.tensor_tensor(out=sb[:, 4:5], in0=sb[:, 3:4], in1=sb[:, 4:5], op=ALU.subtract),
          reads=[sk], writes=[sk])
        O("scalar", lambda e: e.activation(out=sb[:, 5:6], in_=sb[:, 4:5], func=AF.Ln, bias=epsc[:, 0:1]),
          reads=[sk, "eps"], writes=[sk])
        O("scalar", lambda e: e.activation(out=sb[:, 5:6], in_=sb[:, 5:6], func=AF.Exp, scale=-0.5),
          reads=[sk], writes=[sk])
        O("vector", lambda e: e.scalar_tensor_tensor(out=sb[:, 6:7], in0=sb[:, 2:3], scalar=-1.0,
                                                     in1=sb[:, 5:6], op0=ALU.mult, op1=ALU.mult),
          reads=[sk], writes=[sk])
        O("scalar", lambda e: e.activation(out=dst, in_=xa, func=AF.Identity,
                                           scale=sb[:, 5:6], bias=sb[:, 6:7]),
          reads=[xk, sk], writes=[dk])
        if wap is not None:
            O("vector", lambda e: e.tensor_tensor(out=dst, in0=dst, in1=wap, op=ALU.mult),
              reads=[dk, wk2], writes=[dk])
            O("vector", lambda e: e.tensor_tensor(out=dst, in0=dst, in1=bap, op=ALU.add),
              reads=[dk, bk2], writes=[dk])

    sp.new_phase()
    sp._serial = 1 in _G_SERIAL
    O("sync", lambda e: e.dma_start(out=epsc[:], in_=eps_in), writes=["eps"], dma=True)
    O("sync", lambda e: e.dma_start(out=tdc[:], in_=tdc_in), writes=["tdcc"], dma=True)
    O("sync", lambda e: e.dma_start(out=tfc[:], in_=tfc_in), writes=["tfcc"], dma=True)
    cv = Carve()
    lnt = {n: cv.f32(D) for n in ("ln0w", "ln0b", "ln1w", "ln1b", "tmkb", "tmvb", "tmrb")}
    for n in lnt:
        O("sync", lambda e, n=n: e.dma_start(out=lnt[n], in_=ln_b[n]), writes=["Lw_" + n], dma=True)
    xt = [cv.f32(D) for _ in range(2)]; at = [cv.f32(D) for _ in range(2)]
    ht = [cv.f32(D) for _ in range(2)]; xxt = [cv.f32(D) for _ in range(2)]
    dt_ = [cv.f32(D) for _ in range(2)]; mixt = [cv.f32(D) for _ in range(2)]
    xsq1 = [cv.f32(D) for _ in range(2)]; mixb = [cv.bf(D) for _ in range(2)]
    for i in range(NT):
        b = i % 2
        O("sync", lambda e, i=i, b=b: e.dma_start(out=xt[b], in_=xn1_in[i * P:(i + 1) * P, :]),
          writes=[f"xt{b}"], dma=True)
        ln(xt[b], f"xt{b}", at[b], f"at{b}", stat[b], f"st{b}", xsq1[b], f"xq{b}",
           lnt["ln0w"], "Lw_ln0w", lnt["ln0b"], "Lw_ln0b")
        O("sync", lambda e, i=i, b=b: e.dma_start(out=a_sc[i * P:(i + 1) * P, :], in_=at[b]),
          reads=[f"at{b}"], dma=True)
        ln(at[b], f"at{b}", ht[b], f"ht{b}", stat[b], f"st{b}", xsq1[b], f"xq{b}",
           lnt["ln1w"], "Lw_ln1w", lnt["ln1b"], "Lw_ln1b")
        if i % 16 == 0:
            O("gpsimd", lambda e, b=b: e.memset(xxt[b][0:1, 0:D], 0.0), writes=[f"xx{b}"])
        else:
            O("sync", lambda e, b=b: e.dma_start(out=xxt[b][0:1, 0:D], in_=hlast[:]),
              reads=["hlast"], writes=[f"xx{b}"], dma=True)
        O("sync", lambda e, b=b: e.dma_start(out=xxt[b][1:P, 0:D], in_=ht[b][0:P - 1, 0:D]),
          reads=[f"ht{b}"], writes=[f"xx{b}"], dma=True)
        O("sync", lambda e, b=b: e.dma_start(out=hlast[:], in_=ht[b][P - 1:P, 0:D]),
          reads=[f"ht{b}"], writes=["hlast"], dma=True)
        O("vector", lambda e, b=b: e.tensor_tensor(out=dt_[b], in0=ht[b], in1=xxt[b], op=ALU.subtract),
          reads=[f"ht{b}", f"xx{b}"], writes=[f"dt{b}"])
        for mi, (mn, dst_sc) in enumerate((("tmkb", xk_sc), ("tmvb", xv_sc), ("tmrb", xr_sc))):
            tb2 = (3 * i + mi) % 2
            O("vector", lambda e, b=b, mn=mn: e.tensor_tensor(out=mixt[b], in0=dt_[b], in1=lnt[mn], op=ALU.mult),
              reads=[f"dt{b}", "Lw_" + mn], writes=[f"mx{b}"])
            O("vector", lambda e, b=b: e.tensor_tensor(out=mixt[b], in0=mixt[b], in1=xxt[b], op=ALU.add),
              reads=[f"mx{b}", f"xx{b}"], writes=[f"mx{b}"])
            O("vector", lambda e, b=b: e.tensor_copy(out=mixb[b], in_=mixt[b]),
              reads=[f"mx{b}"], writes=[f"mb{b}"])
            for c in range(NCH):
                O("scalar", lambda e, b=b, c=c, tb2=tb2: e.dma_start(
                    out=trd[tb2][:, c * P:(c + 1) * P], in_=mixb[b][:, c * P:(c + 1) * P],
                    transpose=True),
                  reads=[f"mb{b}"], writes=[f"tr{tb2}_{c}"], dma=True)
            O("sync", lambda e, i=i, tb2=tb2, dst_sc=dst_sc: e.dma_start(
                out=dst_sc.rearrange("(c p) t -> p c t", p=P)[:, :, i * P:(i + 1) * P],
                in_=trd[tb2][:].rearrange("p (c q) -> p c q", c=NCH)),
              reads=[f"tr{tb2}_{c}" for c in range(NCH)], dma=True)

    sp.new_phase()
    sp._serial = 2 in _G_SERIAL
    cv = Carve()
    ko = cv.f32(S); vo = cv.f32(S); sro = cv.f32(S)
    eko = cv.f32(S); ekvo = cv.f32(S)
    Asc = cv.f32(S + 1); Bsc = cv.f32(S + 1)
    numt = cv.f32(S); dent = cv.f32(S)
    saob = cv.bf(S)
    wldS = [cv.f32(3 * D) for _ in range(2)]
    wblkb = [cv.bf(3 * D) for _ in range(2)]
    rhsB = [cv.bf(NCH * 512) for _ in range(2)]
    dslc = {"ko": ko, "vo": vo, "sr": sro}
    for oc in range(NCH):
        b = oc % 2
        for pi, w_in in enumerate((wkT_in, wvT_in, wrT_in)):
            O("sync", lambda e, w_in=w_in, oc=oc, pi=pi, b=b: e.dma_start(
                out=wldS[b][:, pi * D:(pi + 1) * D].rearrange("p (c q) -> p c q", c=NCH),
                in_=w_in.rearrange("(c p) d -> p c d", p=P)[:, :, oc * P:(oc + 1) * P]),
              writes=[f"wldS{b}_{pi}"], dma=True)
        O("vector", lambda e, b=b: e.tensor_copy(out=wblkb[b], in_=wldS[b]),
          reads=[f"wldS{b}_{pi}" for pi in range(3)], writes=[f"wbl{b}"])
        O("scalar", lambda e, oc=oc, b=b: e.activation(out=smallc[b][:, 0:1], in_=tdc[:, oc:oc + 1], func=AF.Exp),
          reads=["tdcc"], writes=[f"smc{b}"])
        O("scalar", lambda e, b=b: e.activation(out=smallc[b][:, 1:2], in_=smallc[b][:, 0:1], func=AF.Exp, scale=-1.0),
          reads=[f"smc{b}"], writes=[f"smc{b}"])
        O("scalar", lambda e, oc=oc, b=b: e.activation(out=smallc[b][:, 2:3], in_=tfc[:, oc:oc + 1], func=AF.Exp),
          reads=["tfcc"], writes=[f"smc{b}"])
        for s in range(BC):
            for pi, (src_sc, dn, fn_) in enumerate(((xk_sc, "ko", AF.Copy), (xv_sc, "vo", AF.Copy),
                                                    (xr_sc, "sr", AF.Sigmoid))):
                dsto = dslc[dn]
                for tb in range(4):
                    g = pi * 4 + tb
                    rb = g % 2
                    pk = g % 4
                    O("sync", lambda e, src_sc=src_sc, s=s, tb=tb, rb=rb: e.dma_start(
                        out=rhsB[rb][:].rearrange("p (c q) -> p c q", c=NCH),
                        in_=src_sc.rearrange("(c p) t -> p c t", p=P)[:, :, s * S + tb * 512: s * S + (tb + 1) * 512]),
                      writes=[f"rhs{rb}"], dma=True)
                    for cc in range(NCH):
                        O("tensor", lambda e, b=b, pi=pi, cc=cc, rb=rb, pk=pk: e.matmul(
                            ps[pk][:], wblkb[b][:, (pi * NCH + cc) * P:(pi * NCH + cc + 1) * P],
                            rhsB[rb][:, cc * 512:(cc + 1) * 512],
                            start=(cc == 0), stop=(cc == NCH - 1)),
                          reads=[f"wbl{b}", f"rhs{rb}"], writes=[f"ps{pk}"])
                    O("scalar", lambda e, dsto=dsto, fn_=fn_, tb=tb, pk=pk: e.activation(
                        out=dsto[:, tb * 512:(tb + 1) * 512], in_=ps[pk][:], func=fn_),
                      reads=[f"ps{pk}"], writes=[f"{dn}{tb}"])
            O("scalar", lambda e: e.activation(out=eko, in_=ko, func=AF.Exp),
              reads=[f"ko{tb}" for tb in range(4)], writes=["eko"])
            O("vector", lambda e: e.tensor_tensor(out=ekvo, in0=eko, in1=vo, op=ALU.mult),
              reads=["eko"] + [f"vo{tb}" for tb in range(4)], writes=["ekvo"])
            O("gpsimd", lambda e: e.memset(Asc[:, 0:1], 0.0), writes=["Asc"])
            O("gpsimd", lambda e: e.memset(Bsc[:, 0:1], 0.0), writes=["Bsc"])
            ewb = smallc[b][:, 1:2].broadcast_to([P, S])
            O("vector", lambda e, ewb=ewb: e.tensor_tensor_scan(out=Asc[:, 1:S + 1], data0=ewb, data1=ekvo,
                                                                initial=0.0, op0=ALU.mult, op1=ALU.add),
              reads=["ekvo", f"smc{b}"], writes=["Asc"])
            O("vector", lambda e, ewb=ewb: e.tensor_tensor_scan(out=Bsc[:, 1:S + 1], data0=ewb, data1=eko,
                                                                initial=0.0, op0=ALU.mult, op1=ALU.add),
              reads=["eko", f"smc{b}"], writes=["Bsc"])
            O("vector", lambda e, b=b: e.scalar_tensor_tensor(out=numt, in0=ekvo, scalar=smallc[b][:, 2:3],
                                                              in1=Asc[:, 0:S], op0=ALU.mult, op1=ALU.add),
              reads=["ekvo", "Asc", f"smc{b}"], writes=["numt"])
            O("vector", lambda e, b=b: e.scalar_tensor_tensor(out=dent, in0=eko, scalar=smallc[b][:, 2:3],
                                                              in1=Bsc[:, 0:S], op0=ALU.mult, op1=ALU.add),
              reads=["eko", "Bsc", f"smc{b}"], writes=["dent"])
            O("scalar", lambda e: e.activation(out=dent, in_=dent, func=AF.Ln),
              reads=["dent"], writes=["dent"])
            O("scalar", lambda e: e.activation(out=dent, in_=dent, func=AF.Exp, scale=-1.0),
              reads=["dent"], writes=["dent"])
            O("vector", lambda e: e.tensor_tensor(out=numt, in0=numt, in1=dent, op=ALU.mult),
              reads=["numt", "dent"], writes=["numt"])
            O("vector", lambda e: e.tensor_tensor(out=saob, in0=numt, in1=sro, op=ALU.mult),
              reads=["numt"] + [f"sr{tb}" for tb in range(4)], writes=["saob"])
            O("sync", lambda e, oc=oc, s=s: e.dma_start(
                out=sa_sc[oc * P:(oc + 1) * P, s * S:(s + 1) * S], in_=saob),
              reads=["saob"], dma=True)

    sp.new_phase()
    sp._serial = 3 in _G_SERIAL
    cv = Carve()
    lnt3 = {}
    for n in ("ln2w", "ln2b", "cmkb", "cmrb"):
        t = cv.f32(D)
        lnt3[n] = t
        O("sync", lambda e, n=n, t=t: e.dma_start(out=t, in_=ln_b[n]), writes=["Lw_" + n], dma=True)
    woTS = cv.f32(NCH * D)
    woTb = cv.bf(NCH * D)
    O("sync", lambda e: e.dma_start(out=woTS.rearrange("p (c q) -> p c q", c=NCH),
                                    in_=woT_in.rearrange("(c p) d -> p c d", p=P)),
      writes=["woTS"], dma=True)
    O("vector", lambda e: e.tensor_copy(out=woTb, in_=woTS), reads=["woTS"], writes=["woT"])
    salB = [cv.bf(NCH * P) for _ in range(2)]
    ot3 = [cv.f32(D) for _ in range(2)]; a3 = [cv.f32(D) for _ in range(2)]
    x23 = [cv.f32(D) for _ in range(2)]; h23 = [cv.f32(D) for _ in range(2)]
    xx3 = [cv.f32(D) for _ in range(2)]; mb3 = [cv.bf(D) for _ in range(2)]
    d3 = cv.f32(D); mix3 = cv.f32(D); xsq3 = cv.f32(D)
    for i in range(NT):
        b = i % 2
        O("sync", lambda e, i=i, b=b: e.dma_start(
            out=salB[b][:].rearrange("p (c q) -> p c q", c=NCH),
            in_=sa_sc.rearrange("(c p) t -> p c t", p=P)[:, :, i * P:(i + 1) * P]),
          writes=[f"sal{b}"], dma=True)
        for half in range(2):
            for cc in range(NCH):
                O("tensor", lambda e, b=b, cc=cc, half=half: e.matmul(
                    ps[half][:], salB[b][:, cc * P:(cc + 1) * P],
                    woTb[:, cc * D + half * 512: cc * D + (half + 1) * 512],
                    start=(cc == 0), stop=(cc == NCH - 1)),
                  reads=[f"sal{b}", "woT"], writes=[f"ps{half}"])
            O("scalar", lambda e, b=b, half=half: e.activation(
                out=ot3[b][:, half * 512:(half + 1) * 512], in_=ps[half][:], func=AF.Copy),
              reads=[f"ps{half}"], writes=[f"ot{b}_{half}"])
        O("sync", lambda e, i=i, b=b: e.dma_start(out=a3[b], in_=a_sc[i * P:(i + 1) * P, :]),
          writes=[f"a3{b}"], dma=True)
        O("vector", lambda e, b=b: e.tensor_tensor(out=x23[b], in0=ot3[b], in1=a3[b], op=ALU.add),
          reads=[f"ot{b}_0", f"ot{b}_1", f"a3{b}"], writes=[f"x2{b}"])
        O("sync", lambda e, i=i, b=b: e.dma_start(out=x2_sc[i * P:(i + 1) * P, :], in_=x23[b]),
          reads=[f"x2{b}"], dma=True)
        ln(x23[b], f"x2{b}", h23[b], f"h2{b}", stat[b], f"st3{b}", xsq3, "xq3",
           lnt3["ln2w"], "Lw_ln2w", lnt3["ln2b"], "Lw_ln2b")
        if i % 16 == 0:
            O("gpsimd", lambda e, b=b: e.memset(xx3[b][0:1, 0:D], 0.0), writes=[f"xx3{b}"])
        else:
            O("sync", lambda e, b=b: e.dma_start(out=xx3[b][0:1, 0:D], in_=hlast[:]),
              reads=["hl3"], writes=[f"xx3{b}"], dma=True)
        O("sync", lambda e, b=b: e.dma_start(out=xx3[b][1:P, 0:D], in_=h23[b][0:P - 1, 0:D]),
          reads=[f"h2{b}"], writes=[f"xx3{b}"], dma=True)
        O("sync", lambda e, b=b: e.dma_start(out=hlast[:], in_=h23[b][P - 1:P, 0:D]),
          reads=[f"h2{b}"], writes=["hl3"], dma=True)
        O("vector", lambda e, b=b: e.tensor_tensor(out=d3, in0=h23[b], in1=xx3[b], op=ALU.subtract),
          reads=[f"h2{b}", f"xx3{b}"], writes=["d3"])
        O("vector", lambda e: e.tensor_tensor(out=mix3, in0=d3, in1=lnt3["cmkb"], op=ALU.mult),
          reads=["d3", "Lw_cmkb"], writes=["mx3"])
        O("vector", lambda e, b=b: e.tensor_tensor(out=mix3, in0=mix3, in1=xx3[b], op=ALU.add),
          reads=["mx3", f"xx3{b}"], writes=["mx3"])
        O("vector", lambda e, b=b: e.tensor_copy(out=mb3[b], in_=mix3),
          reads=["mx3"], writes=[f"mb3{b}"])
        for c in range(NCH):
            O("scalar", lambda e, b=b, c=c: e.dma_start(out=trd[0][:, c * P:(c + 1) * P],
                                                        in_=mb3[b][:, c * P:(c + 1) * P], transpose=True),
              reads=[f"mb3{b}"], writes=[f"t30_{c}"], dma=True)
        for c in range(NCH):
            O("vector", lambda e, c=c, i=i: e.tensor_copy(out=xk2_fm[c][:, i * P:(i + 1) * P],
                                                          in_=trd[0][:, c * P:(c + 1) * P]),
              reads=[f"t30_{c}"], writes=[f"xkf{c}"])
        O("vector", lambda e: e.tensor_tensor(out=mix3, in0=d3, in1=lnt3["cmrb"], op=ALU.mult),
          reads=["d3", "Lw_cmrb"], writes=["mx3"])
        O("vector", lambda e, b=b: e.tensor_tensor(out=mix3, in0=mix3, in1=xx3[b], op=ALU.add),
          reads=["mx3", f"xx3{b}"], writes=["mx3"])
        O("vector", lambda e, b=b: e.tensor_copy(out=mb3[b], in_=mix3),
          reads=["mx3"], writes=[f"mb3{b}"])
        for c in range(NCH):
            O("scalar", lambda e, b=b, c=c: e.dma_start(out=trd[1][:, c * P:(c + 1) * P],
                                                        in_=mb3[b][:, c * P:(c + 1) * P], transpose=True),
              reads=[f"mb3{b}"], writes=[f"t31_{c}"], dma=True)
        O("sync", lambda e, i=i: e.dma_start(
            out=xr2_sc.rearrange("(c p) t -> p c t", p=P)[:, :, i * P:(i + 1) * P],
            in_=trd[1][:].rearrange("p (c q) -> p c q", c=NCH)),
          reads=[f"t31_{c}" for c in range(NCH)], dma=True)

    sp.new_phase()
    sp._serial = 4 in _G_SERIAL
    cv = Carve()
    wfkS = [cv.f32(NCH * P) for _ in range(2)]
    wfkb = [cv.bf(NCH * P) for _ in range(2)]
    kfo = cv.f32(T)
    kfqo = [cv.bf(T) for _ in range(2)]
    for o4 in range(H_FFN // P):
        b = o4 % 2
        O("sync", lambda e, o4=o4, b=b: e.dma_start(
            out=wfkS[b][:].rearrange("p (c q) -> p c q", c=NCH),
            in_=wfkT_in.rearrange("(c p) h -> p c h", p=P)[:, :, o4 * P:(o4 + 1) * P]),
          writes=[f"wfS{b}"], dma=True)
        O("vector", lambda e, b=b: e.tensor_copy(out=wfkb[b], in_=wfkS[b]),
          reads=[f"wfS{b}"], writes=[f"wfk{b}"])
        for tb in range(NTB):
            pk = tb % 4
            for cc in range(NCH):
                O("tensor", lambda e, b=b, cc=cc, tb=tb, pk=pk: e.matmul(
                    ps[pk][:], wfkb[b][:, cc * P:(cc + 1) * P], xk2_fm[cc][:, tb * 512:(tb + 1) * 512],
                    start=(cc == 0), stop=(cc == NCH - 1)),
                  reads=[f"wfk{b}"], writes=[f"ps{pk}"])
            O("scalar", lambda e, tb=tb, pk=pk: e.activation(out=kfo[:, tb * 512:(tb + 1) * 512],
                                                             in_=ps[pk][:], func=AF.Relu),
              reads=[f"ps{pk}"], writes=[f"kf{tb}"])
        O("scalar", lambda e, b=b: e.activation(out=kfqo[b], in_=kfo, func=AF.Square),
          reads=[f"kf{tb}" for tb in range(NTB)], writes=[f"kq{b}"])
        O("sync", lambda e, o4=o4, b=b: e.dma_start(out=kfq_sc[o4 * P:(o4 + 1) * P, :], in_=kfqo[b]),
          reads=[f"kq{b}"], dma=True)

    sp.new_phase()
    sp._serial = 5 in _G_SERIAL
    cv = Carve()
    wfvS = [cv.f32(2 * D) for _ in range(2)]
    wfvb = cv.bf(32 * D)
    wfrb = cv.bf(NCH * D)
    kflB = [cv.bf(32 * P) for _ in range(2)]
    xrlB = [cv.bf(NCH * P) for _ in range(2)]
    kvt = cv.f32(D)
    rst = cv.f32(D)
    x25 = [cv.f32(D) for _ in range(2)]
    for j in range(16):
        b = j % 2
        O("sync", lambda e, j=j, b=b: e.dma_start(
            out=wfvS[b][:].rearrange("p (c q) -> p c q", c=2),
            in_=wfvT_in.rearrange("(c p) d -> p c d", p=P)[:, 2 * j:2 * j + 2, :]),
          writes=[f"wvS{b}"], dma=True)
        O("vector", lambda e, j=j, b=b: e.tensor_copy(out=wfvb[:, 2 * j * D:(2 * j + 2) * D], in_=wfvS[b]),
          reads=[f"wvS{b}"], writes=[f"wfv{2 * j}", f"wfv{2 * j + 1}"])
    for j in range(4):
        b = j % 2
        O("sync", lambda e, j=j, b=b: e.dma_start(
            out=wfvS[b][:].rearrange("p (c q) -> p c q", c=2),
            in_=wfrT_in.rearrange("(c p) d -> p c d", p=P)[:, 2 * j:2 * j + 2, :]),
          writes=[f"wvS{b}"], dma=True)
        O("vector", lambda e, j=j, b=b: e.tensor_copy(out=wfrb[:, 2 * j * D:(2 * j + 2) * D], in_=wfvS[b]),
          reads=[f"wvS{b}"], writes=["wfr"])
    for i in range(NT):
        b = i % 2
        O("sync", lambda e, i=i, b=b: e.dma_start(
            out=kflB[b][:].rearrange("p (c q) -> p c q", c=32),
            in_=kfq_sc.rearrange("(c p) t -> p c t", p=P)[:, :, i * P:(i + 1) * P]),
          writes=[f"kfl{b}"], dma=True)
        for half in range(2):
            for ic in range(32):
                O("tensor", lambda e, b=b, ic=ic, half=half: e.matmul(
                    ps[half][:], kflB[b][:, ic * P:(ic + 1) * P],
                    wfvb[:, ic * D + half * 512: ic * D + (half + 1) * 512],
                    start=(ic == 0), stop=(ic == 31)),
                  reads=[f"kfl{b}", f"wfv{ic}"], writes=[f"ps{half}"])
            O("scalar", lambda e, half=half: e.activation(out=kvt[:, half * 512:(half + 1) * 512],
                                                          in_=ps[half][:], func=AF.Copy),
              reads=[f"ps{half}"], writes=[f"kv{half}"])
        O("sync", lambda e, i=i, b=b: e.dma_start(
            out=xrlB[b][:].rearrange("p (c q) -> p c q", c=NCH),
            in_=xr2_sc.rearrange("(c p) t -> p c t", p=P)[:, :, i * P:(i + 1) * P]),
          writes=[f"xrl{b}"], dma=True)
        for half in range(2):
            pk = 2 + half
            for cc in range(NCH):
                O("tensor", lambda e, b=b, cc=cc, half=half, pk=pk: e.matmul(
                    ps[pk][:], xrlB[b][:, cc * P:(cc + 1) * P],
                    wfrb[:, cc * D + half * 512: cc * D + (half + 1) * 512],
                    start=(cc == 0), stop=(cc == NCH - 1)),
                  reads=[f"xrl{b}", "wfr"], writes=[f"ps{pk}"])
            O("scalar", lambda e, half=half, pk=pk: e.activation(out=rst[:, half * 512:(half + 1) * 512],
                                                                 in_=ps[pk][:], func=AF.Sigmoid),
              reads=[f"ps{pk}"], writes=[f"rs{half}"])
        O("vector", lambda e: e.tensor_tensor(out=rst, in0=rst, in1=kvt, op=ALU.mult),
          reads=["rs0", "rs1", "kv0", "kv1"], writes=["rs0", "rs1"])
        O("sync", lambda e, i=i, b=b: e.dma_start(out=x25[b], in_=x2_sc[i * P:(i + 1) * P, :]),
          writes=[f"x25{b}"], dma=True)
        O("vector", lambda e, b=b: e.tensor_tensor(out=rst, in0=rst, in1=x25[b], op=ALU.add),
          reads=["rs0", "rs1", f"x25{b}"], writes=["rs0", "rs1"])
        O("sync", lambda e, i=i: e.dma_start(out=attn1_o[i * P:(i + 1) * P, :], in_=rst),
          reads=["rs0", "rs1"], dma=True)
    sp._serial = False
    es.close()



_G_SERIAL = set()
_DBG = set()
_LAST = {}


def build_MEGA():
    nc = bass.Bass("TRN2", target_bir_lowering=False)
    inp = lambda n, s, dt=F32: nc.dram_tensor(n, s, dt, kind="ExternalInput").ap()
    scr = lambda n, s, dt=F32: nc.dram_tensor(
        n, s, dt, kind="ExternalOutput" if n in _DBG else "Internal").ap()

    x_in = inp("x", [T, D])
    wqT = inp("wqT", [D, D]); wkT = inp("wkT", [D, D]); wvT = inp("wvT", [D, D])
    bqc = inp("bqc", [P, NCH]); bkc = inp("bkc", [P, NCH]); bvr = inp("bvr", [1, D])
    epsc = inp("epsc", [P, 1])
    eyes = inp("eyes", [P, 3 * P]); identb = inp("identb", [P, P], BF16)
    identf = inp("identf", [P, P])
    wg0 = inp("wg0", [I_FFN, D]); wu0 = inp("wu0", [I_FFN, D]); wd0 = inp("wd0", [D, I_FFN])
    wg1 = inp("wg1", [I_FFN, D]); wu1 = inp("wu1", [I_FFN, D]); wd1 = inp("wd1", [D, I_FFN])
    ln_b = {n: inp(n, [P, D]) for n in ("ln0w", "ln0b", "ln1w", "ln1b", "ln2w", "ln2b",
                                        "tmkb", "tmvb", "tmrb", "cmkb", "cmrb")}
    wkT1 = inp("wkT1", [D, D]); wvT1 = inp("wvT1", [D, D]); wrT1 = inp("wrT1", [D, D])
    woT1 = inp("woT1", [D, D]); wfkT = inp("wfkT", [D, H_FFN])
    wfvT = inp("wfvT", [H_FFN, D]); wfrT = inp("wfrT", [D, D])
    tdc = inp("tdc", [P, NCH]); tfc = inp("tfc", [P, NCH])

    xn_tok = scr("xn_tok", [T, D])
    q_fm = scr("q_fm", [NCH * P, T], BF16)
    k_fm = scr("k_fm", [NCH * P, T], BF16)
    v_tok = scr("v_tok", [T, D], BF16)
    xn_fm_s = scr("xn_fm_s", [NCH * P, T], BF16)
    qlm = scr("qlm", [NCH * P, BC * LM], BF16)
    klm = scr("klm", [NCH * P, BC * LM], BF16)
    k2 = scr("k2", [NMAT * P, LM], BF16)
    pcmax_t = nc.dram_tensor("pcmax", [1, 4], F32, kind="Internal")
    pcm_all_t = nc.dram_tensor("pcm_all", [1, 4], F32, kind="Internal")
    z_s = scr("z_s", [NMAT * P, LM], BF16)
    zt_s = scr("zt_s", [NMAT * P, LM], BF16)
    attn_tok = scr("attn_tok", [T, D])
    wgq_fm = scr("wgq_fm", [D, I_FFN], BF16)
    wuq_fm = scr("wuq_fm", [D, I_FFN], BF16)
    wdqT = scr("wdqT", [I_FFN, D], BF16)
    hq_sc = scr("hq_sc", [I_FFN, T], BF16)
    ffn0_tok = scr("ffn0_tok", [T, D])
    ffn1_tok = scr("ffn1_tok", [T, D])
    x1_tok = scr("x1_tok", [T, D])
    xn1_tok = scr("xn1_tok", [T, D])
    a_sc = scr("a_sc", [T, D])
    x2_sc = scr("x2_sc", [T, D])
    xk_sc = scr("xk_sc", [D, T], BF16)
    xv_sc = scr("xv_sc", [D, T], BF16)
    xr_sc = scr("xr_sc", [D, T], BF16)
    xr2_sc = scr("xr2_sc", [D, T], BF16)
    sa_sc = scr("sa_sc", [D, T], BF16)
    kfq_sc = scr("kfq_sc", [H_FFN, T], BF16)
    attn1_tok = scr("attn1_tok", [T, D])
    out_o = nc.dram_tensor("out", [T, D], F32, kind="ExternalOutput").ap()

    es = contextlib.ExitStack()

    ab = emit_AB(nc, es, {
        "x": x_in, "wqT": wqT, "wkT": wkT, "wvT": wvT, "bqc": bqc, "bkc": bkc,
        "bvr": bvr, "epsc": epsc, "xn_tok": xn_tok, "q_fm": q_fm, "k_fm": k_fm,
        "v_tok": v_tok, "xn_fm": xn_fm_s, "qlm": qlm, "klm": klm, "k2": k2,
        "pcmax": pcmax_t.ap()})

    sp = SP(nc)

    cc_sem = es.enter_context(nc.semaphore("s_cc"))

    def cc_fn(e, sq, inc):
        e.collective_compute(
            "AllReduce", ALU.max,
            replica_groups=[list(range(NCORES))],
            ins=[pcmax_t.ap().opt()],
            outs=[pcm_all_t.ap().opt()]).then_inc(cc_sem)
        e.wait_ge(cc_sem, 1)
        e.sem_inc(sq, inc)
    sp.add_raw("gpsimd", cc_fn, 1)

    emit_WQfull(nc, sp, "0", wg0, wu0, wd0, wgq_fm, wuq_fm, wdqT)
    emit_C(nc, sp, k2, pcm_all_t.ap(), eyes, identb, z_s, zt_s)
    emit_D(nc, sp, q_fm, k_fm, v_tok, qlm, klm, zt_s, identf, attn_tok)
    emit_FFN(nc, sp, "0", xn_tok, wgq_fm, wuq_fm, wdqT, ffn0_tok, hq_sc)
    emit_F(nc, sp, x_in, attn_tok, ffn0_tok, epsc, x1_tok, xn1_tok)
    emit_WQfull(nc, sp, "1", wg1, wu1, wd1, wgq_fm, wuq_fm, wdqT)
    emit_FFN(nc, sp, "1", xn1_tok, wgq_fm, wuq_fm, wdqT, ffn1_tok, hq_sc)
    emit_G(nc, sp, xn1_tok, ln_b, wkT1, wvT1, wrT1, woT1, wfkT, wfvT, wfrT,
           tdc, tfc, epsc, attn1_tok,
           a_sc, x2_sc, xk_sc, xv_sc, xr_sc, xr2_sc, sa_sc, kfq_sc)
    emit_I(nc, sp, x1_tok, attn1_tok, ffn1_tok, out_o)

    sp.emit(es, prologue=ab)
    es.close()
    return nc



_CACHED = {}


def _get(name, builder):
    if name not in _CACHED:
        _CACHED[name] = builder()
    return _CACHED[name]


def kernel(**inputs):
    ins = {k: np.asarray(v) for k, v in inputs.items()}
    C = np.ascontiguousarray
    cores = list(range(NCORES))

    nc = _get("MEGA", build_MEGA)

    epsc = np.full((P, 1), 1e-5, np.float32)
    eyes = np.concatenate([np.eye(P) * 7, np.eye(P) * 15, np.eye(P) * 13], 1).astype(np.float32)
    identb = np.eye(P).astype(BF)
    identf = np.eye(P, dtype=np.float32)
    bt = lambda v: C(np.broadcast_to(np.asarray(v, np.float32), (P, D)))
    f32 = lambda k: ins[k].astype(np.float32)
    common = {
        "wqT": C(f32("L0_attn_Wq").T),
        "wkT": C(f32("L0_attn_Wk").T),
        "wvT": C(f32("L0_attn_Wv").T),
        "bqc": C(f32("L0_attn_bq").reshape(NCH, P).T),
        "bkc": C(f32("L0_attn_bk").reshape(NCH, P).T),
        "bvr": C(f32("L0_attn_bv").reshape(1, D)),
        "epsc": epsc, "eyes": eyes, "identb": identb, "identf": identf,
        "wg0": C(f32("L0_ffn_w_gate")), "wu0": C(f32("L0_ffn_w_up")),
        "wd0": C(f32("L0_ffn_w_down")),
        "wg1": C(f32("L1_ffn_w_gate")), "wu1": C(f32("L1_ffn_w_up")),
        "wd1": C(f32("L1_ffn_w_down")),
        "ln0w": bt(ins["L1_attn_ln0_w"]), "ln0b": bt(ins["L1_attn_ln0_b"]),
        "ln1w": bt(ins["L1_attn_ln1_w"]), "ln1b": bt(ins["L1_attn_ln1_b"]),
        "ln2w": bt(ins["L1_attn_ln2_w"]), "ln2b": bt(ins["L1_attn_ln2_b"]),
        "tmkb": bt(ins["L1_attn_tmk"]), "tmvb": bt(ins["L1_attn_tmv"]),
        "tmrb": bt(ins["L1_attn_tmr"]), "cmkb": bt(ins["L1_attn_cmk"]),
        "cmrb": bt(ins["L1_attn_cmr"]),
        "wkT1": C(f32("L1_attn_Wk").T),
        "wvT1": C(f32("L1_attn_Wv").T),
        "wrT1": C(f32("L1_attn_Wr").T),
        "woT1": C(f32("L1_attn_Wo").T),
        "wfkT": C(f32("L1_attn_Wfk").T),
        "wfvT": C(f32("L1_attn_Wfv").T),
        "wfrT": C(f32("L1_attn_Wfr").T),
        "tdc": C(f32("L1_attn_time_decay").reshape(NCH, P).T),
        "tfc": C(f32("L1_attn_time_first").reshape(NCH, P).T),
    }
    x = ins["L0_attn_x"].astype(np.float32)
    maps = [dict(common, x=C(x[c * BC:(c + 1) * BC].reshape(T, D))) for c in cores]
    res = run_bass_kernel_spmd(nc, maps, core_ids=cores)
    _LAST["res"] = res
    out = np.stack([res.results[c]["out"] for c in cores]).reshape(16, S, D).astype(np.float32)
    return out


if __name__ == "__main__":
    print("library module")
